# version 03
# baseline (speedup 1.0000x reference)
import sys
import numpy as np

sys.path.insert(0, "/opt/trn_rl_repo")

import concourse.bass as bass
import concourse.mybir as mybir
from concourse.bass_utils import run_bass_kernel_spmd

F32 = mybir.dt.float32
BF16 = mybir.dt.bfloat16
AF = mybir.ActivationFunctionType
ALU = mybir.AluOpType
AX = mybir.AxisListType

NCORES = 8
T = 1024
DM = 4096
DH = 11008
HB = DH // 128
KC = DM // 128
EPS_LN = 1e-5


class CSem:
    def __init__(self, handle):
        self.h = handle
        self.val = 0

    def inc(self, instr, n=1):
        instr.then_inc(self.h, n)
        self.val += n
        return self.val

    def wait(self, eng, val=None):
        eng.wait_ge(self.h, self.val if val is None else val)


class Prog:
    def __init__(self):
        self.nc = bass.Bass("TRN2", target_bir_lowering=False, debug=False)
        self._ctxs = []
        self.sems = {}

    def sem(self, name):
        if name not in self.sems:
            g = self.nc.semaphore(name)
            self.sems[name] = CSem(g.__enter__())
            self._ctxs.append(g)
        return self.sems[name]

    def sbuf(self, name, shape, dtype, scope=None):
        g = self.nc.sbuf_tensor(name, shape, dtype)
        t = g.__enter__()
        (self._ctxs if scope is None else scope).append(g)
        return t

    def psum(self, name, shape, dtype=F32):
        g = self.nc.psum_tensor(name, shape, dtype)
        t = g.__enter__()
        self._ctxs.append(g)
        return t

    def open_block(self):
        self._blk = self.nc.Block()
        self.block = self._blk.__enter__()

    def close_scope(self, scope):
        for g in reversed(scope):
            g.__exit__(None, None, None)

    def close(self):
        self._blk.__exit__(None, None, None)
        for g in reversed(self._ctxs):
            g.__exit__(None, None, None)

    def te(self, f):
        self.block.tensor(f)

    def dve(self, f):
        self.block.vector(f)

    def act(self, f):
        self.block.scalar(f)

    def pool(self, f):
        self.block.gpsimd(f)

    def sync(self, f):
        self.block.sync(f)



def _lhsT_slabs(W):
    o, i = W.shape
    return np.ascontiguousarray(
        W.reshape(o // 128, 128, i // 128, 128).transpose(0, 3, 2, 1))


def prep_l0ffn(inp):
    gw = inp["L0_ffn_gate_w"]; uw = inp["L0_ffn_up_w"]; dw = inp["L0_ffn_down_w"]
    d = {}
    d["gsl"] = _lhsT_slabs(np.asarray(gw, np.float32))
    d["usl"] = _lhsT_slabs(np.asarray(uw, np.float32))
    d["dsl"] = np.ascontiguousarray(
        np.asarray(dw, np.float32).reshape(8, 4, 128, HB, 128).transpose(0, 4, 3, 1, 2))
    a2 = np.concatenate([inp["L0_ffn_gate_a"], inp["L0_ffn_up_a"]], 0).astype(np.float32)
    d["a2t"] = np.ascontiguousarray(a2.T.reshape(KC, 128, 32).transpose(1, 0, 2))
    d["gbt"] = np.ascontiguousarray(np.asarray(inp["L0_ffn_gate_b"], np.float32).T.reshape(16, HB, 128))
    d["ubt"] = np.ascontiguousarray(np.asarray(inp["L0_ffn_up_b"], np.float32).T.reshape(16, HB, 128))
    d["dat"] = np.ascontiguousarray(
        np.asarray(inp["L0_ffn_down_a"], np.float32).T.reshape(HB, 128, 16).transpose(1, 0, 2))
    d["dbt"] = np.ascontiguousarray(np.asarray(inp["L0_ffn_down_b"], np.float32).T.reshape(16, KC, 128))
    return d



def emit_layernorm(p, xv, xn, xf, sq, t32, rows, ones_bf, ones_f, eps1,
                   psA, psB, psC, psD, s_xs, s_pe, s_act, s_dve):
    sq_tok = [0] * KC
    x_tok = [0] * KC
    peS_tok = [0] * KC

    for kc in range(KC):
        sl = kc % 2

        def _(e, kc=kc, sl=sl):
            if kc >= 2:
                s_act.wait(e, sq_tok[kc - 2])
            x_tok[kc] = s_xs[sl].inc(e.dma_start(xf[:, sl, :], xv[:, kc, :]), 16)
        p.sync(_)

        def _(e, kc=kc, sl=sl):
            s_xs[sl].wait(e, x_tok[kc])
            if kc >= 2:
                s_pe.wait(e, peS_tok[kc - 2])
            e.copy(xn[:, kc * T:(kc + 1) * T], xf[:, sl, :])
            sq_tok[kc] = s_act.inc(e.square(sq[:, sl, :], xf[:, sl, :]))
        p.act(_)

        def _(e, kc=kc, sl=sl):
            s_act.wait(e, sq_tok[kc])
            if kc == 0:
                s_dve.wait(e)
            for tc in range(2):
                e.matmul(psA[0:1, tc * 512:(tc + 1) * 512], ones_bf[:],
                         xn[:, kc * T + tc * 512:kc * T + (tc + 1) * 512],
                         start=(kc == 0), stop=(kc == KC - 1))
                mm = e.matmul(psB[0:1, tc * 512:(tc + 1) * 512], ones_bf[:],
                              sq[:, sl, tc * 512:(tc + 1) * 512],
                              start=(kc == 0), stop=(kc == KC - 1))
            peS_tok[kc] = s_pe.inc(mm)
        p.te(_)

    MU, MSQ, VAR, SD = 0, T, 2 * T, 3 * T

    def _(e):
        s_pe.wait(e)
        e.tensor_scalar(rows[:, MU:MU + T], psA[0:1, :], 1.0 / DM, None, ALU.mult)
        e.tensor_scalar(rows[:, MSQ:MSQ + T], psB[0:1, :], 1.0 / DM, None, ALU.mult)
        e.tensor_tensor(rows[:, VAR:VAR + T], rows[:, MU:MU + T],
                        rows[:, MU:MU + T], ALU.mult)
        s_dve.inc(e.tensor_tensor(rows[:, VAR:VAR + T], rows[:, MSQ:MSQ + T],
                                  rows[:, VAR:VAR + T], ALU.subtract))
    p.dve(_)

    def _(e):
        s_dve.wait(e)
        s_act.inc(e.activation(rows[:, SD:SD + T], rows[:, VAR:VAR + T],
                               AF.Sqrt, bias=eps1[0:1, :], scale=1.0))
    p.act(_)

    def _(e):
        s_act.wait(e)
        s_dve.inc(e.reciprocal(rows[:, SD:SD + T], rows[:, SD:SD + T]))
    p.dve(_)

    def _(e):
        s_dve.wait(e)
        for tc in range(2):
            e.matmul(psC[:, tc * 512:(tc + 1) * 512], ones_f[:],
                     rows[0:1, MU + tc * 512:MU + (tc + 1) * 512],
                     start=True, stop=True)
            mm = e.matmul(psD[:, tc * 512:(tc + 1) * 512], ones_f[:],
                          rows[0:1, SD + tc * 512:SD + (tc + 1) * 512],
                          start=True, stop=True)
        s_pe.inc(mm)
    p.te(_)
    bc_tok = s_pe.val

    ap_tok = [0] * KC
    act_guard = s_act.val
    for kc in range(KC):
        sl = kc % 2

        def _(e, kc=kc, sl=sl):
            if kc >= 2:
                s_dve.wait(e, ap_tok[kc - 2])
            else:
                s_act.wait(e, act_guard)
            x_tok[kc] = s_xs[sl].inc(e.dma_start(xf[:, sl, :], xv[:, kc, :]), 16)
        p.sync(_)

        def _(e, kc=kc, sl=sl):
            s_xs[sl].wait(e, x_tok[kc])
            if kc == 0:
                s_pe.wait(e, bc_tok)
            e.tensor_tensor(t32[:], xf[:, sl, :], psC[:], ALU.subtract)
            ap_tok[kc] = s_dve.inc(e.tensor_tensor(
                xn[:, kc * T:(kc + 1) * T], t32[:], psD[:], ALU.mult))
        p.dve(_)
    return s_dve.val



def build_l0ffn_program():
    p = Prog()
    nc = p.nc
    x_in = nc.dram_tensor("x_in", [DM, T], F32, kind="ExternalInput")
    gsl = nc.dram_tensor("gsl", [HB, 128, KC, 128], F32, kind="ExternalInput")
    usl = nc.dram_tensor("usl", [HB, 128, KC, 128], F32, kind="ExternalInput")
    dsl = nc.dram_tensor("dsl", [8, 128, HB, 4, 128], F32, kind="ExternalInput")
    a2t = nc.dram_tensor("a2t", [128, KC, 32], F32, kind="ExternalInput")
    gbt = nc.dram_tensor("gbt", [16, HB, 128], F32, kind="ExternalInput")
    ubt = nc.dram_tensor("ubt", [16, HB, 128], F32, kind="ExternalInput")
    dat = nc.dram_tensor("dat", [128, HB, 16], F32, kind="ExternalInput")
    dbt = nc.dram_tensor("dbt", [16, KC, 128], F32, kind="ExternalInput")
    x_out = nc.dram_tensor("x_out", [DM, T], F32, kind="ExternalOutput")
    hid_dram = nc.dram_tensor("hid_scr", [HB, 128, T], BF16)

    xv = x_in.ap().rearrange("(c p) t -> p c t", p=128)
    ov = x_out.ap().rearrange("(c p) t -> p c t", p=128)

    xf = p.sbuf("xf", [128, 2, T], F32)
    outsb = p.sbuf("outsb", [128, 2, 512], F32)
    td = p.sbuf("td", [16, T], BF16)
    dbs = p.sbuf("dbs", [16, KC * 128], BF16)
    ones_bf = p.sbuf("ones_bf", [128, 1], BF16)
    ones_f = p.sbuf("ones_f", [1, 128], F32)
    eps1 = p.sbuf("eps1", [1, 1], F32)

    sa = []
    xn = p.sbuf("xn", [128, KC * T], BF16, sa)
    gslb = p.sbuf("gslb", [128, 2, KC * 128], BF16, sa)
    uslb = p.sbuf("uslb", [128, 2, KC * 128], BF16, sa)
    gbs = p.sbuf("gbs", [16, HB * 128], BF16, sa)
    ubs = p.sbuf("ubs", [16, HB * 128], BF16, sa)
    das = p.sbuf("das", [128, HB * 16], BF16, sa)
    a2s = p.sbuf("a2s", [128, KC, 32], BF16, sa)
    tg = p.sbuf("tg", [16, T], BF16, sa)
    tu = p.sbuf("tu", [16, T], BF16, sa)
    silu = p.sbuf("silu", [128, T], BF16, sa)
    ucast = p.sbuf("ucast", [128, T], BF16, sa)
    hid = p.sbuf("hid", [128, 2, T], BF16, sa)
    sq = p.sbuf("sq", [128, 2, T], BF16, sa)
    t32 = p.sbuf("t32", [128, T], F32, sa)
    rows = p.sbuf("rows", [1, 4 * T], F32, sa)

    psA = p.psum("psA", [128, T], F32)
    psB = p.psum("psB", [128, T], F32)
    psC = p.psum("psC", [128, T], F32)
    psD = p.psum("psD", [128, T], F32)
    psbank = [psA, psB, psC, psD]

    s_xs = [p.sem("s_x0"), p.sem("s_x1")]
    s_gs = [p.sem("s_g0"), p.sem("s_g1")]
    s_us = [p.sem("s_u0"), p.sem("s_u1")]
    s_ds = [p.sem("s_d0"), p.sem("s_d1")]
    s_hos = [p.sem("s_ho0"), p.sem("s_ho1")]
    s_hi = p.sem("s_hi")
    s_os = [p.sem("s_o0"), p.sem("s_o1")]
    s_sm = p.sem("s_sm")
    s_pe = p.sem("s_pe")
    s_act = p.sem("s_act")
    s_dve = p.sem("s_dve")

    p.open_block()

    def _(e):
        s_sm.inc(e.dma_start(a2s[:], a2t.ap()), 16)
        s_sm.inc(e.dma_start(gbs[:], gbt.ap().rearrange("p a b -> p (a b)")), 16)
        s_sm.inc(e.dma_start(ubs[:], ubt.ap().rearrange("p a b -> p (a b)")), 16)
        s_sm.inc(e.dma_start(das[:], dat.ap().rearrange("p a b -> p (a b)")), 16)
        s_sm.inc(e.dma_start(dbs[:], dbt.ap().rearrange("p a b -> p (a b)")), 16)
    p.pool(_)

    def _(e):
        e.memset(ones_bf[:], 1.0)
        e.memset(eps1[:], EPS_LN)
        s_dve.inc(e.memset(ones_f[:], 1.0))
    p.dve(_)

    emit_layernorm(p, xv, xn, xf, sq, t32, rows, ones_bf, ones_f, eps1,
                   psA, psB, psC, psD, s_xs, s_pe, s_act, s_dve)

    def _(e):
        s_dve.wait(e)
        s_sm.wait(e)
        for tc in range(2):
            for kc in range(KC):
                e.matmul(psA[0:16, tc * 512:(tc + 1) * 512],
                         a2s[:, kc, 0:16],
                         xn[:, kc * T + tc * 512:kc * T + (tc + 1) * 512],
                         start=(kc == 0), stop=(kc == KC - 1))
            for kc in range(KC):
                mm = e.matmul(psB[0:16, tc * 512:(tc + 1) * 512],
                              a2s[:, kc, 16:32],
                              xn[:, kc * T + tc * 512:kc * T + (tc + 1) * 512],
                              start=(kc == 0), stop=(kc == KC - 1))
        s_pe.inc(mm)
    p.te(_)

    def _(e):
        s_pe.wait(e)
        e.copy(tg[:], psA[0:16, :])
        s_act.inc(e.copy(tu[:], psB[0:16, :]))
    p.act(_)
    tgtu_tok = s_act.val

    pe_hb_tok = [0] * HB
    dve_hb_tok = [0] * HB
    ho_tok = [0] * HB
    silu_tok = [0] * HB
    ucast_tok = [0] * HB
    g_tok = [0] * HB
    u_tok = [0] * HB

    def emit_slab_load(hb):
        sl = hb % 2

        def _(e, hb=hb, sl=sl):
            if hb >= 2:
                s_pe.wait(e, pe_hb_tok[hb - 2])
            g_tok[hb] = s_gs[sl].inc(e.dma_start(
                gslb[:, sl, :], gsl.ap()[hb].rearrange("p a b -> p (a b)"),
                max_dma_last_dim=8192), 16)
            u_tok[hb] = s_us[sl].inc(e.dma_start(
                uslb[:, sl, :], usl.ap()[hb].rearrange("p a b -> p (a b)"),
                max_dma_last_dim=8192), 16)
        p.pool(_)

    emit_slab_load(0)
    emit_slab_load(1)

    for hb in range(HB):
        sl = hb % 2
        psG = psA if sl == 0 else psB

        def _(e, hb=hb, sl=sl, psG=psG):
            s_gs[sl].wait(e, g_tok[hb])
            s_us[sl].wait(e, u_tok[hb])
            if hb == 0:
                s_act.wait(e, tgtu_tok)
            if hb >= 2:
                s_act.wait(e, silu_tok[hb - 2])
            for tc in range(2):
                for kc in range(KC):
                    e.matmul(psG[:, tc * 512:(tc + 1) * 512],
                             gslb[:, sl, kc * 128:(kc + 1) * 128],
                             xn[:, kc * T + tc * 512:kc * T + (tc + 1) * 512],
                             start=(kc == 0), stop=False)
                e.matmul(psG[:, tc * 512:(tc + 1) * 512],
                         gbs[0:16, hb * 128:(hb + 1) * 128],
                         tg[0:16, tc * 512:(tc + 1) * 512],
                         start=False, stop=True)
            if hb > 0:
                s_dve.wait(e, dve_hb_tok[hb - 1])
                for tc in range(2):
                    e.matmul(psD[0:16, tc * 512:(tc + 1) * 512],
                             das[:, (hb - 1) * 16:hb * 16],
                             hid[:, (hb - 1) % 2, tc * 512:(tc + 1) * 512],
                             start=(hb == 1), stop=False)
            if hb >= 1:
                s_act.wait(e, ucast_tok[hb - 1])
            for tc in range(2):
                for kc in range(KC):
                    e.matmul(psC[:, tc * 512:(tc + 1) * 512],
                             uslb[:, sl, kc * 128:(kc + 1) * 128],
                             xn[:, kc * T + tc * 512:kc * T + (tc + 1) * 512],
                             start=(kc == 0), stop=False)
                mm = e.matmul(psC[:, tc * 512:(tc + 1) * 512],
                              ubs[0:16, hb * 128:(hb + 1) * 128],
                              tu[0:16, tc * 512:(tc + 1) * 512],
                              start=False, stop=True)
            pe_hb_tok[hb] = s_pe.inc(mm)
        p.te(_)

        def _(e, hb=hb, psG=psG):
            s_pe.wait(e, pe_hb_tok[hb])
            if hb >= 1:
                s_dve.wait(e, dve_hb_tok[hb - 1])
            silu_tok[hb] = s_act.inc(e.activation(silu[:], psG[:], AF.Silu))
            ucast_tok[hb] = s_act.inc(e.copy(ucast[:], psC[:]))
        p.act(_)

        def _(e, hb=hb, sl=sl):
            s_act.wait(e, ucast_tok[hb])
            if hb >= 2:
                s_hos[sl].wait(e, ho_tok[hb - 2])
            dve_hb_tok[hb] = s_dve.inc(
                e.tensor_tensor(hid[:, sl, :], silu[:], ucast[:], ALU.mult))
        p.dve(_)

        def _(e, hb=hb, sl=sl):
            s_dve.wait(e, dve_hb_tok[hb])
            ho_tok[hb] = s_hos[sl].inc(
                e.dma_start(hid_dram.ap()[hb], hid[:, sl, :]), 16)
        p.sync(_)

        if hb + 2 < HB:
            emit_slab_load(hb + 2)

    def _(e):
        s_dve.wait(e, dve_hb_tok[HB - 1])
        for tc in range(2):
            mm = e.matmul(psD[0:16, tc * 512:(tc + 1) * 512],
                          das[:, (HB - 1) * 16:HB * 16],
                          hid[:, (HB - 1) % 2, tc * 512:(tc + 1) * 512],
                          start=False, stop=True)
        s_pe.inc(mm)
    p.te(_)
    pe_A_end = s_pe.val

    def _(e):
        s_pe.wait(e, pe_A_end)
        s_act.inc(e.copy(td[:], psD[0:16, :]))
    p.act(_)
    act_A_end = s_act.val
    dve_A_end = s_dve.val

    p.close_scope(sa)
    sb = []
    hidall = p.sbuf("hidall", [128, HB * T], BF16, sb)
    dslb = p.sbuf("dslb", [128, 2, 4 * 4 * 128], BF16, sb)

    NG = 8
    groups = [list(range(g * 11, min(HB, (g + 1) * 11))) for g in range(NG)]
    grp_tok = [0] * NG

    def _(e):
        s_pe.wait(e, pe_A_end)
        s_act.wait(e, act_A_end)
        s_dve.wait(e, dve_A_end)
        for g, hbs in enumerate(groups):
            if g >= 2:
                s_hi.wait(e, grp_tok[g - 2])
            for hb in hbs:
                s_hos[hb % 2].wait(e, ho_tok[hb])
                mm = e.dma_start(hidall[:, hb * T:(hb + 1) * T], hid_dram.ap()[hb])
                s_hi.inc(mm, 16)
            grp_tok[g] = s_hi.val
    p.act(_)

    NB = 22
    pe_batch_tok = [0] * (8 * NB)
    dve_ev_tok = [0] * 8
    d_tok = [0] * (8 * NB)
    ev_tok = [0] * (8 * 4 * 2)
    out_tok = [0] * (8 * 4 * 2)
    xb_tok = [0] * (8 * 4)

    for dg in range(8):
        for ib in range(NB):
            h0 = ib * 4
            hn = 4 if ib < 21 else 2
            bi = dg * NB + ib

            def _(e, dg=dg, ib=ib, h0=h0, hn=hn, bi=bi):
                if bi >= 2:
                    s_pe.wait(e, pe_batch_tok[bi - 2])
                elif dg == 0 and ib == 0:
                    s_act.wait(e, act_A_end)
                src = dsl.ap()[dg, :, h0:h0 + hn].rearrange("p h d c -> p (h d c)")
                d_tok[bi] = s_ds[ib % 2].inc(
                    e.dma_start(dslb[:, ib % 2, 0:hn * 512], src,
                                max_dma_last_dim=8192), 16)
            p.pool(_)

            def _(e, dg=dg, ib=ib, h0=h0, hn=hn, bi=bi):
                s_ds[ib % 2].wait(e, d_tok[bi])
                gneed = (h0 + hn - 1) // 11
                s_hi.wait(e, grp_tok[gneed])
                if dg == 0 and ib == 0:
                    s_act.wait(e, act_A_end)
                if dg > 0 and ib == 0:
                    s_dve.wait(e, dve_ev_tok[dg - 1])
                for h in range(hn):
                    hb = h0 + h
                    for d in range(4):
                        for tc in range(2):
                            mm = e.matmul(
                                psbank[d][:, tc * 512:(tc + 1) * 512],
                                dslb[:, ib % 2, (h * 4 + d) * 128:(h * 4 + d + 1) * 128],
                                hidall[:, hb * T + tc * 512:hb * T + (tc + 1) * 512],
                                start=(hb == 0), stop=False)
                pe_batch_tok[bi] = s_pe.inc(mm)
            p.te(_)

        def _(e, dg=dg):
            for d in range(4):
                for tc in range(2):
                    mm = e.matmul(psbank[d][:, tc * 512:(tc + 1) * 512],
                                  dbs[0:16, (dg * 4 + d) * 128:(dg * 4 + d + 1) * 128],
                                  td[0:16, tc * 512:(tc + 1) * 512],
                                  start=False, stop=True)
            s_pe.inc(mm)
        p.te(_)
        dg_done = s_pe.val

        for d in range(4):
            di = dg * 4 + d

            def _(e, dg=dg, d=d, di=di):
                if di >= 2:
                    s_dve.wait(e, xb_tok[di - 2])
                x_t = s_xs[d % 2].inc(e.dma_start(xf[:, d % 2, :], xv[:, di, :]), 16)
                xb_tok[di] = x_t
            p.sync(_)
            xdma_tok = s_xs[d % 2].val

            def _(e, dg=dg, d=d, di=di, dg_done=dg_done, xdma_tok=xdma_tok):
                s_xs[d % 2].wait(e, xdma_tok)
                s_pe.wait(e, dg_done)
                for tc in range(2):
                    oi = di * 2 + tc
                    if oi >= 2:
                        s_os[tc].wait(e, out_tok[oi - 2])
                    ev_tok[oi] = s_dve.inc(e.scalar_tensor_tensor(
                        outsb[:, tc, :],
                        psbank[d][:, tc * 512:(tc + 1) * 512], 0.5,
                        xf[:, d % 2, tc * 512:(tc + 1) * 512],
                        ALU.mult, ALU.add))
                xb_tok[di] = s_dve.val
            p.dve(_)

            def _(e, dg=dg, d=d, di=di):
                for tc in range(2):
                    oi = di * 2 + tc
                    s_dve.wait(e, ev_tok[oi])
                    out_tok[oi] = s_os[tc].inc(
                        e.dma_start(ov[:, di, tc * 512:(tc + 1) * 512],
                                    outsb[:, tc, :]), 16)
            p.sync(_)
        dve_ev_tok[dg] = s_dve.val

    def _(e):
        for tc in range(2):
            s_os[tc].wait(e)
    p.sync(_)

    p.close_scope(sb)
    p.close()
    return p



_prog_cache = {}


def _get_prog(name, builder):
    if name not in _prog_cache:
        _prog_cache[name] = builder()
    return _prog_cache[name]


def _x_to_shards(x):
    out = []
    for c in range(NCORES):
        b, h = c // 2, c % 2
        out.append(np.ascontiguousarray(x[b, h * T:(h + 1) * T, :].T))
    return out


def _shards_to_x(shards):
    x = np.empty((4, 2048, DM), np.float32)
    for c in range(NCORES):
        b, h = c // 2, c % 2
        x[b, h * T:(h + 1) * T, :] = shards[c].T
    return x


def run_l0ffn(x, w):
    p = _get_prog("l0ffn", build_l0ffn_program)
    shards = _x_to_shards(x)
    in_maps = [{"x_in": shards[c], **w} for c in range(NCORES)]
    res = run_bass_kernel_spmd(p.nc, in_maps, core_ids=list(range(NCORES)))
    return _shards_to_x([res.results[c]["x_out"] for c in range(NCORES)])


def kernel(**inputs):
    x = np.asarray(inputs["L0_attn_x"], np.float32)
    w_f0 = prep_l0ffn(inputs)
    w_a0 = prep_l0attn(inputs)
    w_f1 = prep_l1ffn(inputs)
    w_a1 = prep_l1attn(inputs)
    x = run_l0ffn(x, w_f0)
    x = run_l0attn(x, w_a0)
    x = run_l0ffn(x, w_f0)
    x = run_l1ffn(x, w_f1)
    x = run_l1attn(x, w_a1)
    x = run_l1ffn(x, w_f1)
    return x



TF = 2048
NH = 32
NKV = 8
HD = 128
EPS_RMS = 1e-6


def _alibi_slopes(n):
    import math
    start = 2 ** (-2 ** (-(math.log2(n) - 3)))
    return [start * start ** i for i in range(n)]


def prep_l0attn(inp):
    d = {}
    d["wqT"] = _lhsT_slabs(np.asarray(inp["L0_attn_Wq"], np.float32))
    d["wkT"] = _lhsT_slabs(np.asarray(inp["L0_attn_Wk"], np.float32))
    d["woT"] = _lhsT_slabs(np.asarray(inp["L0_attn_Wo"], np.float32))
    d["wvt"] = np.ascontiguousarray(
        np.asarray(inp["L0_attn_Wv"], np.float32).T.reshape(KC, 128, 1024))
    d["qnw"] = np.ascontiguousarray(np.asarray(inp["L0_attn_q_norm_w"], np.float32).reshape(128, 1))
    d["knw"] = np.ascontiguousarray(np.asarray(inp["L0_attn_k_norm_w"], np.float32).reshape(128, 1))
    return d






DEBUG_ATTN = False


def build_l0attn_program():
    import math
    p = Prog()
    nc = p.nc
    x_own = nc.dram_tensor("x_own", [DM, T], F32, kind="ExternalInput")
    x_full = nc.dram_tensor("x_full", [DM, TF], F32, kind="ExternalInput")
    wqT = nc.dram_tensor("wqT", [32, 128, KC, 128], F32, kind="ExternalInput")
    wkT = nc.dram_tensor("wkT", [8, 128, KC, 128], F32, kind="ExternalInput")
    woT = nc.dram_tensor("woT", [32, 128, KC, 128], F32, kind="ExternalInput")
    wvt = nc.dram_tensor("wvt", [KC, 128, 1024], F32, kind="ExternalInput")
    qnw = nc.dram_tensor("qnw", [128, 1], F32, kind="ExternalInput")
    knw = nc.dram_tensor("knw", [128, 1], F32, kind="ExternalInput")
    ownoff = nc.dram_tensor("ownoff", [128, 1], F32, kind="ExternalInput")
    x_out = nc.dram_tensor("x_out", [DM, T], F32, kind="ExternalOutput")
    if DEBUG_ATTN:
        dbg_q = nc.dram_tensor("dbg_q", [128, NH * T], F32, kind="ExternalOutput")
        dbg_k = nc.dram_tensor("dbg_k", [128, NKV * TF], F32, kind="ExternalOutput")
        dbg_v = nc.dram_tensor("dbg_v", [128, 16 * 1024], F32, kind="ExternalOutput")
        dbg_o = nc.dram_tensor("dbg_o", [128, NH * T], F32, kind="ExternalOutput")
        dbg_p = nc.dram_tensor("dbg_p", [128, TF], F32, kind="ExternalOutput")

    xov = x_own.ap().rearrange("(c p) t -> p c t", p=128)
    xfv = x_full.ap().rearrange("(c p) t -> p c t", p=128)
    ov = x_out.ap().rearrange("(c p) t -> p c t", p=128)

    slopes = _alibi_slopes(NH)
    SC = 1.0 / math.sqrt(HD)

    xf = p.sbuf("xf", [128, 2, T], F32)
    ones_bf = p.sbuf("ones_bf", [128, 1], BF16)
    ones_f = p.sbuf("ones_f", [1, 128], F32)
    eps1 = p.sbuf("eps1", [1, 1], F32)
    epsr = p.sbuf("epsr", [1, 1], F32)
    wcol = p.sbuf("wcol", [128, 1], F32)
    knw_c = p.sbuf("knw_c", [128, 1], F32)
    ofcol = p.sbuf("ofcol", [128, 1], F32)
    wsl = p.sbuf("wsl", [128, 2, KC * 128], BF16)
    QO = p.sbuf("QO", [128, NH * T], BF16)

    s_xs = [p.sem("s_x0"), p.sem("s_x1")]
    s_ws = [p.sem("s_w0"), p.sem("s_w1")]
    s_vs = [p.sem("s_v0"), p.sem("s_v1")]
    s_sm = p.sem("s_sm")
    s_os = [p.sem("s_o0"), p.sem("s_o1")]
    s_pe = p.sem("s_pe")
    s_act = p.sem("s_act")
    s_dve = p.sem("s_dve")
    s_po = p.sem("s_po")

    p.open_block()

    def _(e):
        s_sm.inc(e.dma_start(wcol[:], qnw.ap()), 16)
        s_sm.inc(e.dma_start(knw_c[:], knw.ap()), 16)
        s_sm.inc(e.dma_start(ofcol[:], ownoff.ap()), 16)
    p.pool(_)

    def _(e):
        e.memset(ones_bf[:], 1.0)
        e.memset(eps1[:], EPS_LN)
        e.memset(epsr[:], EPS_RMS)
        s_dve.inc(e.memset(ones_f[:], 1.0))
        s_sm.wait(e)
        s_dve.inc(e.tensor_tensor(wcol[:], wcol[:], knw_c[:], ALU.mult))
        s_dve.wait(e)
        s_dve.inc(e.tensor_scalar(wcol[:], wcol[:], SC, None, ALU.mult))
    p.dve(_)

    sa = []
    xn_own = p.sbuf("xn_own", [128, KC * T], BF16, sa)
    sq = p.sbuf("sq", [128, 2, T], BF16, sa)
    t32 = p.sbuf("t32", [128, T], F32, sa)
    rows = p.sbuf("rows", [1, 4 * T], F32, sa)
    sqq = p.sbuf("sqq", [128, T], BF16, sa)
    bcq = p.sbuf("bcq", [128, T], F32, sa)

    pscope_a = []
    for _nm in ("psA", "psB", "psC", "psD"):
        g = nc.psum_tensor(_nm, [128, T], F32)
        pscope_a.append((g, g.__enter__()))
    psA, psB, psC, psD = [t for (_g, t) in pscope_a]

    emit_layernorm(p, xov, xn_own, xf, sq, t32, rows, ones_bf, ones_f,
                   eps1, psA, psB, psC, psD, s_xs, s_pe, s_act, s_dve)
    ln_own_tok = s_dve.val

    w_tok = [0] * 64
    pe_q_tok = [0] * NH
    dve_q_tok = [0] * NH

    def emit_wslab(idx, src_ap, guard_tok=None):
        sl = idx % 2

        def _(e, idx=idx, sl=sl, guard_tok=guard_tok):
            if guard_tok is not None:
                s_pe.wait(e, guard_tok)
            w_tok[idx] = s_ws[sl].inc(e.dma_start(
                wsl[:, sl, :], src_ap.rearrange("p a b -> p (a b)"),
                max_dma_last_dim=8192), 16)
        p.pool(_)

    emit_wslab(0, wqT.ap()[0])
    emit_wslab(1, wqT.ap()[1])

    for h in range(NH):
        sl = h % 2
        psQ = psA if sl == 0 else psB

        def _(e, h=h, sl=sl, psQ=psQ):
            s_ws[sl].wait(e, w_tok[h])
            if h == 0:
                s_dve.wait(e, ln_own_tok)
            if h >= 2:
                s_dve.wait(e, dve_q_tok[h - 2])
            for tc in range(2):
                for kc in range(KC):
                    mm = e.matmul(psQ[:, tc * 512:(tc + 1) * 512],
                                  wsl[:, sl, kc * 128:(kc + 1) * 128],
                                  xn_own[:, kc * T + tc * 512:kc * T + (tc + 1) * 512],
                                  start=(kc == 0), stop=(kc == KC - 1))
            s_pe.inc(mm)
        p.te(_)
        pe_q1 = s_pe.val

        def _(e, pe_q1=pe_q1, psQ=psQ):
            s_pe.wait(e, pe_q1)
            s_act.inc(e.square(sqq[:], psQ[:]))
        p.act(_)
        act_sq = s_act.val

        def _(e, act_sq=act_sq):
            s_act.wait(e, act_sq)
            for tc in range(2):
                mm = e.matmul(psD[0:1, tc * 512:(tc + 1) * 512], ones_bf[:],
                              sqq[:, tc * 512:(tc + 1) * 512], start=True, stop=True)
            s_pe.inc(mm)
        p.te(_)
        pe_q2 = s_pe.val

        def _(e, pe_q2=pe_q2):
            s_pe.wait(e, pe_q2)
            s_dve.inc(e.tensor_scalar(rows[0:1, 0:T], psD[0:1, 0:T], 1.0 / HD,
                                      None, ALU.mult))
        p.dve(_)
        dve_r = s_dve.val

        def _(e, dve_r=dve_r):
            s_dve.wait(e, dve_r)
            s_act.inc(e.activation(rows[0:1, 0:T], rows[0:1, 0:T], AF.Sqrt,
                                   bias=epsr[0:1, :], scale=1.0))
        p.act(_)
        act_r = s_act.val

        def _(e, act_r=act_r):
            s_act.wait(e, act_r)
            s_dve.inc(e.reciprocal(rows[0:1, 0:T], rows[0:1, 0:T]))
        p.dve(_)
        dve_r2 = s_dve.val

        def _(e, h=h, dve_r2=dve_r2):
            s_dve.wait(e, dve_r2)
            for tc in range(2):
                mm = e.matmul(psC[:, tc * 512:(tc + 1) * 512], ones_f[:],
                              rows[0:1, tc * 512:(tc + 1) * 512], start=True, stop=True)
            pe_q_tok[h] = s_pe.inc(mm)
        p.te(_)
        pe_q3 = s_pe.val

        def _(e, pe_q3=pe_q3):
            s_pe.wait(e, pe_q3)
            s_act.inc(e.copy(bcq[:], psC[:]))
        p.act(_)
        act_bc = s_act.val

        def _(e, h=h, psQ=psQ, act_bc=act_bc):
            s_act.wait(e, act_bc)
            dve_q_tok[h] = s_dve.inc(e.scalar_tensor_tensor(
                QO[:, h * T:(h + 1) * T], psQ[:], wcol[:], bcq[:],
                ALU.mult, ALU.mult))
        p.dve(_)

        if h + 2 < NH:
            emit_wslab(h + 2, wqT.ap()[h + 2], pe_q_tok[h])

    q_done_dve = s_dve.val

    if DEBUG_ATTN:
        def _(e):
            s_dve.wait(e, q_done_dve)
            s_sm.inc(e.dma_start(dbg_q.ap(), QO[:], max_dma_last_dim=4096), 16)
        p.pool(_)
    p.close_scope(sa)

    sbp = []
    Ksb = p.sbuf("Ksb", [128, NKV * TF], BF16, sbp)
    Vsb = p.sbuf("Vsb", [128, 16 * 1024], BF16, sbp)

    sbt = []
    rowsF = p.sbuf("rowsF", [1, 2 * TF], F32, sbt)
    t32b = p.sbuf("t32b", [128, 512], F32, sbt)
    bck = p.sbuf("bck", [128, 512], F32, sbt)
    rows2 = bck[0:1, 0:512]

    sb1 = []
    xcF = p.sbuf("xcF", [128, 2, T], BF16, sb1)
    sqF = p.sbuf("sqF", [128, 2, T], BF16, sb1)

    NSC = 2 * KC
    actF_tok = [0] * NSC
    peF_tok = [0] * NSC
    xF_tok = [0] * NSC

    for i in range(NSC):
        kc, half = i // 2, i % 2
        sl = i % 2

        def _(e, i=i, kc=kc, half=half, sl=sl, dvg=s_dve.val, acg=s_act.val):
            if i >= 2:
                s_act.wait(e, actF_tok[i - 2])
            else:
                s_dve.wait(e, dvg)
                s_act.wait(e, acg)
            xF_tok[i] = s_xs[sl].inc(e.dma_start(
                xf[:, sl, :], xfv[:, kc, half * T:(half + 1) * T]), 16)
        p.sync(_)

        def _(e, i=i, sl=sl):
            s_xs[sl].wait(e, xF_tok[i])
            if i >= 2:
                s_pe.wait(e, peF_tok[i - 2])
            e.copy(xcF[:, sl, :], xf[:, sl, :])
            actF_tok[i] = s_act.inc(e.square(sqF[:, sl, :], xf[:, sl, :]))
        p.act(_)

        def _(e, i=i, half=half, sl=sl):
            s_act.wait(e, actF_tok[i])
            if i == 0:
                s_dve.wait(e, q_done_dve)
            psX = psA if half == 0 else psB
            psQ2 = psC if half == 0 else psD
            for tc in range(2):
                e.matmul(psX[0:1, tc * 512:(tc + 1) * 512], ones_bf[:],
                         xcF[:, sl, tc * 512:(tc + 1) * 512],
                         start=(i < 2), stop=(i >= NSC - 2))
                mm = e.matmul(psQ2[0:1, tc * 512:(tc + 1) * 512], ones_bf[:],
                              sqF[:, sl, tc * 512:(tc + 1) * 512],
                              start=(i < 2), stop=(i >= NSC - 2))
            peF_tok[i] = s_pe.inc(mm)
        p.te(_)
    pe_Fstats = s_pe.val

    FMU, FSD = 0, TF

    def _(e):
        s_pe.wait(e, pe_Fstats)
        e.tensor_scalar(rowsF[:, FMU:FMU + T], psA[0:1, 0:T], 1.0 / DM, None, ALU.mult)
        e.tensor_scalar(rowsF[:, FMU + T:FMU + TF], psB[0:1, 0:T], 1.0 / DM, None, ALU.mult)
        e.tensor_scalar(rowsF[:, FSD:FSD + T], psC[0:1, 0:T], 1.0 / DM, None, ALU.mult)
        e.tensor_scalar(rowsF[:, FSD + T:FSD + TF], psD[0:1, 0:T], 1.0 / DM, None, ALU.mult)
        for seg in range(4):
            a, b = seg * 512, (seg + 1) * 512
            e.tensor_tensor(rows2, rowsF[:, FMU + a:FMU + b],
                            rowsF[:, FMU + a:FMU + b], ALU.mult)
            mmv = e.tensor_tensor(rowsF[:, FSD + a:FSD + b], rowsF[:, FSD + a:FSD + b],
                                  rows2, ALU.subtract)
        s_dve.inc(mmv)
    p.dve(_)

    def _(e):
        s_dve.wait(e)
        s_act.inc(e.activation(rowsF[:, FSD:FSD + TF], rowsF[:, FSD:FSD + TF],
                               AF.Sqrt, bias=eps1[0:1, :], scale=1.0))
    p.act(_)

    def _(e):
        s_act.wait(e)
        s_dve.inc(e.reciprocal(rowsF[:, FSD:FSD + TF], rowsF[:, FSD:FSD + TF]))
    p.dve(_)
    fstats_tok = s_dve.val

    p.close_scope(sb1)
    xnb = p.sbuf("xnb", [128, KC * 512], BF16, sbt)
    sqk = p.sbuf("sqk", [128, 512], BF16, sbt)
    vsl = p.sbuf("vsl", [128, 2, 512], BF16, sbt)

    wk_tok = [0] * 32
    pe_k1_tok = [0] * 32
    pe_k2_tok = [0] * 32
    pe_k3_tok = [0] * 32
    act_k_tok = [0] * 32
    act_k2_tok = [0] * 32
    act_k3_tok = [0] * 32
    dve_k1_tok = [0] * 32
    dve_k2_tok = [0] * 32
    dve_k_tok = [0] * 32
    v_tok = [0] * 2
    pe_v_tok = [0] * (2 * KC)
    kv_dve_done = fstats_tok
    psV = [psA, psB, psC, psD]

    for tcol in range(4):
        def _(e, tcol=tcol, kv_dve_done=kv_dve_done):
            if tcol == 0:
                s_dve.wait(e, fstats_tok)
            else:
                s_dve.wait(e, kv_dve_done)
            mm = e.matmul(psC[:, 0:512], ones_f[:],
                          rowsF[0:1, FMU + tcol * 512:FMU + (tcol + 1) * 512],
                          start=True, stop=True)
            mm = e.matmul(psD[:, 0:512], ones_f[:],
                          rowsF[0:1, FSD + tcol * 512:FSD + (tcol + 1) * 512],
                          start=True, stop=True)
            s_pe.inc(mm)
        p.te(_)
        pe_bc = s_pe.val

        xnb_tok = [0] * KC
        for kc in range(KC):
            sl = kc % 2

            def _(e, tcol=tcol, kc=kc, sl=sl, kv_dve_done=kv_dve_done, acg=s_act.val):
                if kc >= 2:
                    s_dve.wait(e, xnb_tok[kc - 2])
                elif tcol > 0:
                    s_dve.wait(e, kv_dve_done)
                else:
                    s_act.wait(e, acg)
                xF_tok[kc % 2] = s_xs[sl].inc(e.dma_start(
                    xf[:, sl, 0:512], xfv[:, kc, tcol * 512:(tcol + 1) * 512]), 16)
            p.sync(_)
            xk_tok = s_xs[sl].val

            def _(e, kc=kc, sl=sl, pe_bc=pe_bc, xk_tok=xk_tok):
                s_xs[sl].wait(e, xk_tok)
                if kc == 0:
                    s_pe.wait(e, pe_bc)
                e.tensor_tensor(t32b[:], xf[:, sl, 0:512], psC[:, 0:512], ALU.subtract)
                xnb_tok[kc] = s_dve.inc(e.tensor_tensor(
                    xnb[:, kc * 512:(kc + 1) * 512], t32b[:], psD[:, 0:512], ALU.mult))
            p.dve(_)
        xnb_done = s_dve.val

        for kvh in range(NKV):
            kk = tcol * NKV + kvh
            sl = kk % 2
            psK = psA if sl == 0 else psB

            def _(e, kk=kk, kvh=kvh, sl=sl):
                if kk >= 2:
                    s_pe.wait(e, pe_k1_tok[kk - 2])
                wk_tok[kk] = s_ws[sl].inc(e.dma_start(
                    wsl[:, sl, :], wkT.ap()[kvh].rearrange("p a b -> p (a b)"),
                    max_dma_last_dim=8192), 16)
            p.pool(_)

            def _(e, kk=kk, sl=sl, psK=psK, xnb_done=xnb_done):
                s_ws[sl].wait(e, wk_tok[kk])
                s_dve.wait(e, xnb_done)
                if kk >= 2:
                    s_dve.wait(e, dve_k_tok[kk - 2])
                for kc in range(KC):
                    mm = e.matmul(psK[:, 0:512],
                                  wsl[:, sl, kc * 128:(kc + 1) * 128],
                                  xnb[:, kc * 512:(kc + 1) * 512],
                                  start=(kc == 0), stop=(kc == KC - 1))
                pe_k1_tok[kk] = s_pe.inc(mm)
            p.te(_)

            def _(e, kk=kk, psK=psK):
                s_pe.wait(e, pe_k1_tok[kk])
                if kk >= 1:
                    s_pe.wait(e, pe_k2_tok[kk - 1])
                act_k_tok[kk] = s_act.inc(e.square(sqk[:], psK[:, 0:512]))
            p.act(_)

            def _(e, kk=kk):
                s_act.wait(e, act_k_tok[kk])
                if kk >= 1:
                    s_dve.wait(e, dve_k1_tok[kk - 1])
                mm = e.matmul(psC[0:1, 0:512], ones_bf[:], sqk[:], start=True, stop=True)
                pe_k2_tok[kk] = s_pe.inc(mm)
            p.te(_)

            def _(e, kk=kk):
                s_pe.wait(e, pe_k2_tok[kk])
                dve_k1_tok[kk] = s_dve.inc(e.tensor_scalar(
                    rows2, psC[0:1, 0:512], 1.0 / HD, None, ALU.mult))
            p.dve(_)

            def _(e, kk=kk):
                s_dve.wait(e, dve_k1_tok[kk])
                act_k2_tok[kk] = s_act.inc(e.activation(
                    rows2, rows2, AF.Sqrt, bias=epsr[0:1, :], scale=1.0))
            p.act(_)

            def _(e, kk=kk):
                s_act.wait(e, act_k2_tok[kk])
                dve_k2_tok[kk] = s_dve.inc(e.reciprocal(rows2, rows2))
            p.dve(_)

            def _(e, kk=kk):
                s_dve.wait(e, dve_k2_tok[kk])
                if kk >= 1:
                    s_act.wait(e, act_k3_tok[kk - 1])
                mm = e.matmul(psD[:, 0:512], ones_f[:], rows2,
                              start=True, stop=True)
                pe_k3_tok[kk] = s_pe.inc(mm)
            p.te(_)

            def _(e, kk=kk):
                s_pe.wait(e, pe_k3_tok[kk])
                act_k3_tok[kk] = s_act.inc(e.copy(bck[:], psD[:, 0:512]))
            p.act(_)

            def _(e, kk=kk, kvh=kvh, tcol=tcol, psK=psK):
                s_act.wait(e, act_k3_tok[kk])
                dve_k_tok[kk] = s_dve.inc(e.tensor_tensor(
                    Ksb[:, kvh * TF + tcol * 512:kvh * TF + (tcol + 1) * 512],
                    psK[:, 0:512], bck[:], ALU.mult))
            p.dve(_)

        k_done_tok = dve_k_tok[tcol * NKV + NKV - 1]
        k_done_act = act_k3_tok[tcol * NKV + NKV - 1]

        for vidx in range(2 * KC):
            kc, nn = vidx // 2, vidx % 2
            sl = vidx % 2

            def _(e, vidx=vidx, kc=kc, nn=nn, sl=sl):
                if vidx >= 2:
                    s_pe.wait(e, pe_v_tok[vidx - 2])
                v_tok[vidx % 2] = s_vs[sl].inc(e.dma_start(
                    vsl[:, sl, :], wvt.ap()[kc][:, nn * 512:(nn + 1) * 512],
                    max_dma_last_dim=8192), 16)
            p.pool(_)
            vdma = s_vs[sl].val

            def _(e, vidx=vidx, kc=kc, nn=nn, sl=sl, vdma=vdma,
                  k_done_tok=k_done_tok, k_done_act=k_done_act):
                s_vs[sl].wait(e, vdma)
                if vidx == 0:
                    s_dve.wait(e, k_done_tok)
                    s_act.wait(e, k_done_act)
                for mt in range(4):
                    mm = e.matmul(
                        psV[mt][:, nn * 512:(nn + 1) * 512],
                        xnb[:, kc * 512 + mt * 128:kc * 512 + (mt + 1) * 128],
                        vsl[:, sl, :],
                        start=(kc == 0), stop=(kc == KC - 1))
                pe_v_tok[vidx] = s_pe.inc(mm)
            p.te(_)
        pe_v_done = s_pe.val

        def _(e, tcol=tcol, pe_v_done=pe_v_done):
            s_pe.wait(e, pe_v_done)
            for mt in range(4):
                s_dve.inc(e.tensor_copy(
                    Vsb[:, (tcol * 4 + mt) * 1024:(tcol * 4 + mt + 1) * 1024],
                    psV[mt][:]))
        p.dve(_)
        kv_dve_done = s_dve.val

    if DEBUG_ATTN:
        def _(e, kv_dve_done=kv_dve_done):
            s_dve.wait(e, kv_dve_done)
            s_sm.inc(e.dma_start(dbg_k.ap(), Ksb[:], max_dma_last_dim=4096), 16)
            s_sm.inc(e.dma_start(dbg_v.ap(), Vsb[:], max_dma_last_dim=4096), 16)
        p.pool(_)

    p.close_scope(sbt)
    for (g, _t) in reversed(pscope_a):
        g.__exit__(None, None, None)

    sc3 = []
    iotaS = p.sbuf("iotaS", [128, TF], F32, sc3)
    idn = p.sbuf("idn", [128, 128], BF16, sc3)
    Rbuf = p.sbuf("Rbuf", [128, TF], F32, sc3)
    c0col = p.sbuf("c0col", [128, 1], F32, sc3)
    tb = p.sbuf("tb", [128, 2, TF], F32, sc3)
    pbf = p.sbuf("pbf", [128, 2, TF], BF16, sc3)
    pTsb = p.sbuf("pTsb", [128, 2, TF], BF16, sc3)
    mrow = p.sbuf("mrow", [128, 2, 4], F32, sc3)
    negm = p.sbuf("negm", [128, 2, 1], F32, sc3)
    lrow = p.sbuf("lrow", [128, 2, 1], F32, sc3)
    linv = p.sbuf("linv", [128, 2, 1], F32, sc3)
    outsb = p.sbuf("outsb", [128, 2, 512], F32, sc3)

    pscope_b = []
    for _nm, _sh, _dt in (("psS", [128, 2, 512], F32), ("psAV", [128, 512], F32),
                          ("psTT", [128, 512], BF16), ("psO", [128, 2, 1024], F32)):
        g = nc.psum_tensor(_nm, _sh, _dt)
        pscope_b.append((g, g.__enter__()))
    psS = pscope_b[0][1]; psAV = pscope_b[1][1]
    psTT = pscope_b[2][1]; psO = pscope_b[3][1]

    def _(e):
        s_po.inc(e.iota(iotaS[:], pattern=[[1, TF]], base=0, channel_multiplier=-1,
                        allow_small_or_imprecise_dtypes=True))
    p.pool(_)
    iota_tok = s_po.val

    def _(e):
        s_po.wait(e, iota_tok)
        s_dve.inc(e.tensor_scalar(idn[:], iotaS[:, 0:128], 0.0, None, ALU.is_equal))
    p.dve(_)
    qresc_tok = s_dve.val

    NIT = 8 * NH
    sc_gc_tok = {}
    ttr_gc_tok = {-2: 0, -1: 0}
    tr_tok = {}
    trevac_tok = {}
    exp_tok = [0] * NIT
    recip_tok = [0] * NIT
    norm_tok = [0] * NIT
    av_tok = [0] * NIT
    oevac_tok = [0] * NIT
    rbuf_tok = [0] * 8
    GC = [0]

    def it_params(it):
        qt, head = it // NH, it % NH
        n128 = 9 + qt
        sspan = n128 * 128
        nch = (sspan + 511) // 512
        return qt, head, n128, sspan, nch

    for it in range(NIT + 1):
        if it < NIT:
            qt, head, n128, sspan, nch = it_params(it)
            sl = it % 2
            kvh = head // 4

            if head == 0:
                def _(e, qt=qt, it=it):
                    if qt == 0:
                        s_dve.wait(e, qresc_tok)
                    if it >= 2:
                        s_act.wait(e, exp_tok[it - 2])
                    s_dve.inc(e.tensor_scalar(c0col[:], ofcol[:], float(128 * qt),
                                              None, ALU.add))
                    s_dve.wait(e)
                    e.tensor_scalar(Rbuf[:], iotaS[:], c0col[:], None, ALU.subtract)
                    e.tensor_scalar(tb[:, 0, :], Rbuf[:], 0.0, -1.3e33,
                                    ALU.is_gt, ALU.mult)
                    s_dve.inc(e.tensor_tensor(Rbuf[:], Rbuf[:], tb[:, 0, :], ALU.add))
                    rbuf_tok[qt] = s_dve.val
                p.dve(_)

            for ch in range(nch):
                gc = GC[0]
                n = min(512, sspan - ch * 512)

                def _(e, it=it, qt=qt, head=head, kvh=kvh, ch=ch, gc=gc, n=n):
                    if it == 0 and ch == 0:
                        s_dve.wait(e, kv_dve_done)
                    s_dve.wait(e, ttr_gc_tok[gc - 2])
                    mm = e.matmul(psS[:, gc % 2, 0:n],
                                  QO[:, head * T + qt * 128:head * T + (qt + 1) * 128],
                                  Ksb[:, kvh * TF + ch * 512:kvh * TF + ch * 512 + n],
                                  start=True, stop=True)
                    sc_gc_tok[gc] = s_pe.inc(mm)
                p.te(_)

                def _(e, it=it, qt=qt, head=head, ch=ch, gc=gc, n=n, sl=sl):
                    s_pe.wait(e, sc_gc_tok[gc])
                    if ch == 0:
                        s_dve.wait(e, rbuf_tok[qt])
                        if it >= 2:
                            s_act.wait(e, exp_tok[it - 2])
                    e.scalar_tensor_tensor(
                        tb[:, sl, ch * 512:ch * 512 + n],
                        Rbuf[:, ch * 512:ch * 512 + n], slopes[head],
                        psS[:, gc % 2, 0:n], ALU.mult, ALU.add)
                    ttr_gc_tok[gc] = s_dve.inc(e.tensor_reduce(
                        mrow[:, sl, ch:ch + 1], tb[:, sl, ch * 512:ch * 512 + n],
                        AX.X, ALU.max))
                p.dve(_)
                GC[0] += 1

            def _(e, it=it, sl=sl, nch=nch):
                s_dve.wait(e)
                s_dve.inc(e.tensor_reduce(negm[:, sl, :], mrow[:, sl, 0:nch],
                                          AX.X, ALU.max))
                s_dve.wait(e)
                s_dve.inc(e.tensor_scalar(negm[:, sl, :], negm[:, sl, :], -1.0,
                                          None, ALU.mult))
            p.dve(_)
            negm_tok = s_dve.val

            def _(e, it=it, sl=sl, sspan=sspan, negm_tok=negm_tok):
                s_dve.wait(e, negm_tok)
                if it >= 2:
                    s_pe.wait(e, av_tok[it - 2])
                exp_tok[it] = s_act.inc(e.activation(
                    pbf[:, sl, 0:sspan], tb[:, sl, 0:sspan], AF.Exp,
                    bias=negm[:, sl, :], scale=1.0, accum_out=lrow[:, sl, :]))
            p.act(_)

            def _(e, it=it, sl=sl):
                s_act.wait(e, exp_tok[it])
                recip_tok[it] = s_dve.inc(e.reciprocal(linv[:, sl, :], lrow[:, sl, :]))
            p.dve(_)

            def _(e, it=it, sl=sl, sspan=sspan):
                s_dve.wait(e, recip_tok[it])
                norm_tok[it] = s_po.inc(e.tensor_scalar(
                    pbf[:, sl, 0:sspan], pbf[:, sl, 0:sspan], linv[:, sl, :],
                    None, ALU.mult))
                if DEBUG_ATTN and it == 33:
                    s_sm.inc(e.dma_start(dbg_p.ap()[:, 0:sspan], pbf[:, sl, 0:sspan],
                                         max_dma_last_dim=4096), 16)
            p.pool(_)

        if it >= 1:
            jt = it - 1
            qt1, head1, n128_1, sspan1, nch1 = it_params(jt)
            sl1 = jt % 2
            kvh1 = head1 // 4

            for j in range(n128_1):
                def _(e, jt=jt, j=j, sl1=sl1, kvh1=kvh1, n128_1=n128_1):
                    if j == 0:
                        s_po.wait(e, norm_tok[jt])
                        if jt >= 1:
                            s_act.wait(e, oevac_tok[jt - 1])
                    if (jt, j - 4) in trevac_tok:
                        s_act.wait(e, trevac_tok[(jt, j - 4)])
                    tr = e.transpose(psTT[:, (j % 4) * 128:(j % 4 + 1) * 128],
                                     pbf[:, sl1, j * 128:(j + 1) * 128], idn[:])
                    tr_tok[(jt, j)] = s_pe.inc(tr)
                    if j >= 1:
                        s_act.wait(e, trevac_tok[(jt, j - 1)])
                        mm = e.matmul(
                            psAV[:, 0:128],
                            Vsb[:, (j - 1) * 1024 + kvh1 * 128:(j - 1) * 1024 + (kvh1 + 1) * 128],
                            pTsb[:, sl1, (j - 1) * 128:j * 128],
                            start=(j == 1), stop=False)
                        s_pe.inc(mm)
                p.te(_)

                def _(e, jt=jt, j=j, sl1=sl1):
                    s_pe.wait(e, tr_tok[(jt, j)])
                    trevac_tok[(jt, j)] = s_act.inc(e.copy(
                        pTsb[:, sl1, j * 128:(j + 1) * 128],
                        psTT[:, (j % 4) * 128:(j % 4 + 1) * 128]))
                p.act(_)

            def _(e, jt=jt, sl1=sl1, kvh1=kvh1, n128_1=n128_1):
                j = n128_1 - 1
                s_act.wait(e, trevac_tok[(jt, j)])
                mm = e.matmul(
                    psAV[:, 0:128],
                    Vsb[:, j * 1024 + kvh1 * 128:j * 1024 + (kvh1 + 1) * 128],
                    pTsb[:, sl1, j * 128:(j + 1) * 128],
                    start=False, stop=True)
                av_tok[jt] = s_pe.inc(mm)
            p.te(_)

            def _(e, jt=jt, qt1=qt1, head1=head1):
                s_pe.wait(e, av_tok[jt])
                oevac_tok[jt] = s_act.inc(e.copy(
                    QO[:, head1 * T + qt1 * 128:head1 * T + (qt1 + 1) * 128],
                    psAV[:, 0:128]))
            p.act(_)

    scores_done_act = s_act.val
    if DEBUG_ATTN:
        def _(e):
            s_act.wait(e, scores_done_act)
            s_sm.inc(e.dma_start(dbg_o.ap(), QO[:], max_dma_last_dim=4096), 16)
        p.pool(_)

    wo_tok = [0] * KC
    pe_wo_tok = [0] * KC
    dve_wo_tok = [0] * KC
    out_tok = [0] * (KC * 2)
    xwo_tok = [0] * KC

    for mb in range(KC):
        sl = mb % 2
        psW = psO[:, sl, :]

        def _(e, mb=mb, sl=sl):
            if mb >= 2:
                s_pe.wait(e, pe_wo_tok[mb - 2])
            wo_tok[mb] = s_ws[sl].inc(e.dma_start(
                wsl[:, sl, :], woT.ap()[mb].rearrange("p a b -> p (a b)"),
                max_dma_last_dim=8192), 16)
        p.pool(_)

        def _(e, mb=mb, sl=sl, psW=psW):
            s_ws[sl].wait(e, wo_tok[mb])
            if mb == 0:
                s_act.wait(e, scores_done_act)
            if mb >= 2:
                s_dve.wait(e, dve_wo_tok[mb - 2])
            for tc in range(2):
                for kc in range(KC):
                    mm = e.matmul(psW[:, tc * 512:(tc + 1) * 512],
                                  wsl[:, sl, kc * 128:(kc + 1) * 128],
                                  QO[:, kc * T + tc * 512:kc * T + (tc + 1) * 512],
                                  start=(kc == 0), stop=(kc == KC - 1))
            pe_wo_tok[mb] = s_pe.inc(mm)
        p.te(_)

        def _(e, mb=mb, sl=sl, dvg=s_dve.val):
            if mb >= 2:
                s_dve.wait(e, dve_wo_tok[mb - 2])
            else:
                s_dve.wait(e, dvg)
            xwo_tok[mb] = s_xs[sl].inc(e.dma_start(xf[:, sl, :], xov[:, mb, :]), 16)
        p.sync(_)

        def _(e, mb=mb, sl=sl, psW=psW):
            s_xs[sl].wait(e, xwo_tok[mb])
            s_pe.wait(e, pe_wo_tok[mb])
            for tc in range(2):
                oi = mb * 2 + tc
                if oi >= 2:
                    s_os[tc].wait(e, out_tok[oi - 2])
                s_dve.inc(e.tensor_tensor(
                    outsb[:, tc, :], psW[:, tc * 512:(tc + 1) * 512],
                    xf[:, sl, tc * 512:(tc + 1) * 512], ALU.add))
            dve_wo_tok[mb] = s_dve.val
        p.dve(_)

        def _(e, mb=mb):
            s_dve.wait(e, dve_wo_tok[mb])
            for tc in range(2):
                oi = mb * 2 + tc
                out_tok[oi] = s_os[tc].inc(e.dma_start(
                    ov[:, mb, tc * 512:(tc + 1) * 512], outsb[:, tc, :]), 16)
        p.sync(_)

    def _(e):
        for tc in range(2):
            s_os[tc].wait(e)
    p.sync(_)

    p.close_scope(sc3)
    for (g, _t) in reversed(pscope_b):
        g.__exit__(None, None, None)
    p.close_scope(sbp)
    p.close()
    return p


def run_l0attn(x, w):
    p = _get_prog("l0attn", build_l0attn_program)
    shards = _x_to_shards(x)
    in_maps = []
    for c in range(NCORES):
        b, h = c // 2, c % 2
        xfull = np.ascontiguousarray(x[b].T)
        off = np.full((128, 1), float(h * T), np.float32)
        in_maps.append({"x_own": shards[c], "x_full": xfull, "ownoff": off, **w})
    res = run_bass_kernel_spmd(p.nc, in_maps, core_ids=list(range(NCORES)))
    return _shards_to_x([res.results[c]["x_out"] for c in range(NCORES)])




def prep_l1ffn(inp):
    d = {}
    d["wbT"] = _lhsT_slabs(np.asarray(inp["L1_ffn_W_base"], np.float32))
    d["wgt"] = np.ascontiguousarray(
        np.asarray(inp["L1_ffn_W_gate"], np.float32).T.reshape(KC, 128, 8).transpose(1, 0, 2))
    d["aT"] = np.ascontiguousarray(
        _lhsT_slabs(np.asarray(inp["L1_ffn_A"], np.float32).reshape(128, DM))[0])
    d["bT"] = np.ascontiguousarray(
        np.asarray(inp["L1_ffn_B"], np.float32).transpose(0, 2, 1).reshape(128, DM)
        .reshape(128, KC, 128))
    d["bbs"] = np.ascontiguousarray(np.asarray(inp["L1_ffn_b_base"], np.float32).reshape(KC, 128).T)
    return d


def build_l1ffn_program():
    p = Prog()
    nc = p.nc
    x_in = nc.dram_tensor("x_in", [DM, T], F32, kind="ExternalInput")
    wbT = nc.dram_tensor("wbT", [KC, 128, KC, 128], F32, kind="ExternalInput")
    wgt = nc.dram_tensor("wgt", [128, KC, 8], F32, kind="ExternalInput")
    aT = nc.dram_tensor("aT", [128, KC, 128], F32, kind="ExternalInput")
    bT = nc.dram_tensor("bT", [128, KC, 128], F32, kind="ExternalInput")
    bbs = nc.dram_tensor("bbs", [128, KC], F32, kind="ExternalInput")
    x_out = nc.dram_tensor("x_out", [DM, T], F32, kind="ExternalOutput")

    xv = x_in.ap().rearrange("(c p) t -> p c t", p=128)
    ov = x_out.ap().rearrange("(c p) t -> p c t", p=128)

    xf = p.sbuf("xf", [128, 2, T], F32)
    outsb = p.sbuf("outsb", [128, 2, 512], F32)
    ones_bf = p.sbuf("ones_bf", [128, 1], BF16)
    ones_f = p.sbuf("ones_f", [1, 128], F32)
    eps1 = p.sbuf("eps1", [1, 1], F32)
    wsl = p.sbuf("wsl", [128, 2, KC * 128], BF16)
    xn = p.sbuf("xn", [128, KC * T], BF16)
    sq = p.sbuf("sq", [128, 2, T], BF16)
    t32 = p.sbuf("t32", [128, T], F32)
    rows = p.sbuf("rows", [1, 4 * T], F32)
    aTs = p.sbuf("aTs", [128, KC * 128], BF16)
    bTs = p.sbuf("bTs", [128, KC * 128], BF16)
    wgs = p.sbuf("wgs", [128, KC * 8], BF16)
    bbc = p.sbuf("bbc", [128, KC], F32)
    Lsb = p.sbuf("Lsb", [128, 64], F32)
    Ls1 = p.sbuf("Ls1", [128, 64], F32)
    Ls2 = p.sbuf("Ls2", [128, 64], F32)
    mx1 = p.sbuf("mx1", [128, 8], F32)
    mx2 = p.sbuf("mx2", [128, 8], F32)
    den = p.sbuf("den", [128, 8], F32)
    reb = p.sbuf("reb", [128, 64], BF16)
    res8 = p.sbuf("res8", [8, T], BF16)
    P8 = p.sbuf("P8", [8, 128], BF16)
    P8b = p.sbuf("P8b", [8, 128], BF16)
    p8i = p.sbuf("p8i", [8, 128], F32)
    idn = p.sbuf("idn", [128, 128], BF16)
    io128 = p.sbuf("io128", [128, 128], F32)
    rb = p.sbuf("rb", [128, T], BF16)
    hb = p.sbuf("hb", [128, T], BF16)
    hw = p.sbuf("hw", [128, T], BF16)

    s_xs = [p.sem("s_x0"), p.sem("s_x1")]
    s_ws = [p.sem("s_w0"), p.sem("s_w1")]
    s_sm = p.sem("s_sm")
    s_os = [p.sem("s_o0"), p.sem("s_o1")]
    s_pe = p.sem("s_pe")
    s_act = p.sem("s_act")
    s_dve = p.sem("s_dve")
    s_po = p.sem("s_po")

    p.open_block()

    def _(e):
        s_sm.inc(e.dma_start(aTs[:], aT.ap().rearrange("p a b -> p (a b)"),
                             max_dma_last_dim=8192), 16)
        s_sm.inc(e.dma_start(bTs[:], bT.ap().rearrange("p a b -> p (a b)"),
                             max_dma_last_dim=8192), 16)
        s_sm.inc(e.dma_start(wgs[:], wgt.ap().rearrange("p a b -> p (a b)")), 16)
        s_sm.inc(e.dma_start(bbc[:], bbs.ap()), 16)
        e.iota(p8i[:], pattern=[[1, 128]], base=0, channel_multiplier=-16,
               allow_small_or_imprecise_dtypes=True)
        s_po.inc(e.iota(io128[:], pattern=[[1, 128]], base=0, channel_multiplier=-1,
                        allow_small_or_imprecise_dtypes=True))
    p.pool(_)

    def _(e):
        e.memset(ones_bf[:], 1.0)
        e.memset(eps1[:], EPS_LN)
        s_dve.inc(e.memset(ones_f[:], 1.0))
        s_po.wait(e)
        e.tensor_scalar(P8[:], p8i[:], 0.0, None, ALU.is_ge)
        e.tensor_scalar(P8b[:], p8i[:], 15.0, None, ALU.is_le)
        s_dve.inc(e.tensor_tensor(P8[:], P8[:], P8b[:], ALU.mult))
        s_dve.inc(e.tensor_scalar(idn[:], io128[:], 0.0, None, ALU.is_equal))
    p.dve(_)
    const_tok = s_dve.val

    pscope_a = []
    for _nm in ("psA", "psB", "psC", "psD"):
        g = nc.psum_tensor(_nm, [128, T], F32)
        pscope_a.append((g, g.__enter__()))
    psA, psB, psC, psD = [t for (_g, t) in pscope_a]

    emit_layernorm(p, xv, xn, xf, sq, t32, rows, ones_bf, ones_f,
                   eps1, psA, psB, psC, psD, s_xs, s_pe, s_act, s_dve)
    ln_tok = s_dve.val

    def _(e):
        s_dve.wait(e, ln_tok)
        s_sm.wait(e)
        for mt in range(8):
            for kc in range(KC):
                mm = e.matmul(psD[:, mt * 8:(mt + 1) * 8],
                              xn[:, kc * T + mt * 128:kc * T + (mt + 1) * 128],
                              wgs[:, kc * 8:(kc + 1) * 8],
                              start=(kc == 0), stop=(kc == KC - 1))
        s_pe.inc(mm)
    p.te(_)
    pe_lg = s_pe.val

    def _(e):
        s_pe.wait(e, pe_lg)
        s_act.inc(e.copy(Lsb[:], psD[:, 0:64]))
    p.act(_)
    act_lg = s_act.val

    gv = lambda t: t.rearrange("p (g i) -> p g i", i=8)
    mx1b = bass.AP(mx1, 0, [[8, 128], [1, 8], [0, 8]])
    mx2b = bass.AP(mx2, 0, [[8, 128], [1, 8], [0, 8]])
    denb = bass.AP(den, 0, [[8, 128], [1, 8], [0, 8]])

    def _(e):
        s_act.wait(e, act_lg)
        s_dve.inc(e.tensor_reduce(mx1[:], gv(Lsb[:]), AX.X, ALU.max))
        s_dve.wait(e)
        e.tensor_tensor(gv(Ls1[:]), gv(Lsb[:]), mx1b, ALU.is_ge)
        e.tensor_scalar(Ls1[:], Ls1[:], -1.0e30, None, ALU.mult)
        e.tensor_tensor(Ls1[:], Ls1[:], Lsb[:], ALU.add)
        s_dve.inc(e.tensor_reduce(mx2[:], gv(Ls1[:]), AX.X, ALU.max))
        s_dve.wait(e)
        s_dve.inc(e.tensor_tensor(gv(Ls2[:]), gv(Lsb[:]), mx1b, ALU.subtract))
    p.dve(_)
    dve_g1 = s_dve.val

    def _(e):
        s_dve.wait(e, dve_g1)
        s_act.inc(e.activation(Ls2[:], Ls2[:], AF.Exp))
    p.act(_)
    act_e = s_act.val

    def _(e):
        s_act.wait(e, act_e)
        e.tensor_tensor(gv(Ls1[:]), gv(Lsb[:]), mx2b, ALU.is_ge)
        e.tensor_tensor(Ls2[:], Ls2[:], Ls1[:], ALU.mult)
        s_dve.inc(e.tensor_reduce(den[:], gv(Ls2[:]), AX.X, ALU.add))
        s_dve.wait(e)
        s_dve.inc(e.reciprocal(den[:], den[:]))
        s_dve.wait(e)
        s_dve.inc(e.tensor_tensor(gv(reb[:]), gv(Ls2[:]), denb, ALU.mult))
    p.dve(_)
    dve_route = s_dve.val

    psTr = psD[0:8, 128:192].bitcast(BF16)
    tr_tok2 = [0] * 8
    ev_tok2 = [0] * 8

    for mt in range(8):
        def _(e, mt=mt):
            if mt == 0:
                s_dve.wait(e, dve_route)
                s_dve.wait(e, const_tok)
            if mt >= 1:
                s_act.wait(e, ev_tok2[mt - 1])
            tr = e.transpose(psTr, reb[:, mt * 8:(mt + 1) * 8], idn[:])
            tr_tok2[mt] = s_pe.inc(tr)
        p.te(_)

        def _(e, mt=mt):
            s_pe.wait(e, tr_tok2[mt])
            ev_tok2[mt] = s_act.inc(e.copy(res8[:, mt * 128:(mt + 1) * 128], psTr))
        p.act(_)
    route8_tok = ev_tok2[7]

    def _(e):
        s_act.wait(e, route8_tok)
        for tc in range(2):
            mm = e.matmul(psC[:, tc * 512:(tc + 1) * 512], P8[:],
                          res8[:, tc * 512:(tc + 1) * 512], start=True, stop=True)
        s_pe.inc(mm)
    p.te(_)
    pe_rfm = s_pe.val

    def _(e):
        s_pe.wait(e, pe_rfm)
        s_act.inc(e.copy(rb[:], psC[:]))
    p.act(_)
    act_rb = s_act.val

    def _(e):
        s_dve.wait(e, ln_tok)
        for tc in range(2):
            for kc in range(KC):
                mm = e.matmul(psB[:, tc * 512:(tc + 1) * 512],
                              aTs[:, kc * 128:(kc + 1) * 128],
                              xn[:, kc * T + tc * 512:kc * T + (tc + 1) * 512],
                              start=(kc == 0), stop=(kc == KC - 1))
        s_pe.inc(mm)
    p.te(_)
    pe_h = s_pe.val

    def _(e):
        s_pe.wait(e, pe_h)
        s_act.inc(e.copy(hb[:], psB[:]))
    p.act(_)
    act_hb = s_act.val

    def _(e):
        s_act.wait(e, act_hb)
        s_act.wait(e, act_rb)
        s_dve.inc(e.tensor_tensor(hw[:], hb[:], rb[:], ALU.mult))
    p.dve(_)
    hw_tok = s_dve.val

    wb_tok = [0] * KC
    pe_mb_tok = [0] * KC
    dve_mb_tok = [0] * KC
    out_tok = [0] * (KC * 2)
    xm_tok = [0] * KC

    for mb in range(KC):
        sl = mb % 2
        psW = psA if sl == 0 else psB

        def _(e, mb=mb, sl=sl):
            if mb >= 2:
                s_pe.wait(e, pe_mb_tok[mb - 2])
            wb_tok[mb] = s_ws[sl].inc(e.dma_start(
                wsl[:, sl, :], wbT.ap()[mb].rearrange("p a b -> p (a b)"),
                max_dma_last_dim=8192), 16)
        p.pool(_)

        def _(e, mb=mb, sl=sl, psW=psW):
            s_ws[sl].wait(e, wb_tok[mb])
            if mb == 0:
                s_dve.wait(e, hw_tok)
            if mb == 1:
                s_act.wait(e, act_hb)
            if mb >= 2:
                s_dve.wait(e, dve_mb_tok[mb - 2])
            for tc in range(2):
                for kc in range(KC):
                    e.matmul(psW[:, tc * 512:(tc + 1) * 512],
                             wsl[:, sl, kc * 128:(kc + 1) * 128],
                             xn[:, kc * T + tc * 512:kc * T + (tc + 1) * 512],
                             start=(kc == 0), stop=False)
                mm = e.matmul(psW[:, tc * 512:(tc + 1) * 512],
                              bTs[:, mb * 128:(mb + 1) * 128],
                              hw[:, tc * 512:(tc + 1) * 512],
                              start=False, stop=True)
            pe_mb_tok[mb] = s_pe.inc(mm)
        p.te(_)

        def _(e, mb=mb, sl=sl, dvg=s_dve.val):
            if mb >= 2:
                s_dve.wait(e, dve_mb_tok[mb - 2])
            else:
                s_dve.wait(e, dvg)
            xm_tok[mb] = s_xs[sl].inc(e.dma_start(xf[:, sl, :], xv[:, mb, :]), 16)
        p.sync(_)

        def _(e, mb=mb, sl=sl, psW=psW):
            s_xs[sl].wait(e, xm_tok[mb])
            s_pe.wait(e, pe_mb_tok[mb])
            for tc in range(2):
                oi = mb * 2 + tc
                if oi >= 2:
                    s_os[tc].wait(e, out_tok[oi - 2])
                e.tensor_scalar(outsb[:, tc, :], psW[:, tc * 512:(tc + 1) * 512],
                                bbc[:, mb:mb + 1], 0.5, ALU.add, ALU.mult)
                s_dve.inc(e.tensor_tensor(outsb[:, tc, :], outsb[:, tc, :],
                                          xf[:, sl, tc * 512:(tc + 1) * 512], ALU.add))
            dve_mb_tok[mb] = s_dve.val
        p.dve(_)

        def _(e, mb=mb):
            s_dve.wait(e, dve_mb_tok[mb])
            for tc in range(2):
                oi = mb * 2 + tc
                out_tok[oi] = s_os[tc].inc(e.dma_start(
                    ov[:, mb, tc * 512:(tc + 1) * 512], outsb[:, tc, :]), 16)
        p.sync(_)

    def _(e):
        for tc in range(2):
            s_os[tc].wait(e)
    p.sync(_)

    for (g2, _t) in reversed(pscope_a):
        g2.__exit__(None, None, None)
    p.close()
    return p


def run_l1ffn(x, w):
    p = _get_prog("l1ffn", build_l1ffn_program)
    shards = _x_to_shards(x)
    in_maps = [{"x_in": shards[c], **w} for c in range(NCORES)]
    res = run_bass_kernel_spmd(p.nc, in_maps, core_ids=list(range(NCORES)))
    return _shards_to_x([res.results[c]["x_out"] for c in range(NCORES)])



DEBUG_L1 = False
D1 = 512
ND0 = 4
DFF = 2048
NSUB = 4
EPS_L1 = 1e-5


def prep_l1attn(inp):
    d = {}
    d["winT"] = _lhsT_slabs(np.asarray(inp["L1_attn_Win"], np.float32))
    d["binc"] = np.ascontiguousarray(np.asarray(inp["L1_attn_bin"], np.float32).reshape(ND0, 128).T)
    for nm in ("ln1_w", "ln1_b", "phz_re", "phz_im", "lc_re", "lc_im",
               "sc_lin", "ln2_w", "ln2_b", "b2"):
        a = np.asarray(inp["L1_attn_" + nm], np.float32)
        d[nm] = np.ascontiguousarray(a.reshape(NSUB, ND0, 128).transpose(0, 2, 1))
    d["w1T"] = np.stack([_lhsT_slabs(np.asarray(inp["L1_attn_W1"], np.float32)[s])
                         for s in range(NSUB)])
    d["w2T"] = np.stack([_lhsT_slabs(np.asarray(inp["L1_attn_W2"], np.float32)[s])
                         for s in range(NSUB)])
    d["b1c"] = np.ascontiguousarray(np.asarray(inp["L1_attn_b1"], np.float32)
                                    .reshape(NSUB, 16, 128).transpose(0, 2, 1))
    d["lnlw"] = np.ascontiguousarray(np.asarray(inp["L1_attn_lnl_w"], np.float32).reshape(ND0, 128).T)
    d["lnlb"] = np.ascontiguousarray(np.asarray(inp["L1_attn_lnl_b"], np.float32).reshape(ND0, 128).T)
    d["woutT"] = _lhsT_slabs(np.asarray(inp["L1_attn_Wout"], np.float32))
    d["boutc"] = np.ascontiguousarray(np.asarray(inp["L1_attn_bout"], np.float32).reshape(KC, 128).T)
    return d


def build_l1attn_program():
    p = Prog()
    nc = p.nc
    x_full = nc.dram_tensor("x_full", [DM, TF], F32, kind="ExternalInput")
    winT = nc.dram_tensor("winT", [ND0, 128, KC, 128], F32, kind="ExternalInput")
    binc = nc.dram_tensor("binc", [128, ND0], F32, kind="ExternalInput")
    cols_in = {}
    for nm in ("ln1_w", "ln1_b", "phz_re", "phz_im", "lc_re", "lc_im",
               "sc_lin", "ln2_w", "ln2_b", "b2"):
        cols_in[nm] = nc.dram_tensor(nm, [NSUB, 128, ND0], F32, kind="ExternalInput")
    w1T = nc.dram_tensor("w1T", [NSUB, 16, 128, ND0, 128], F32, kind="ExternalInput")
    w2T = nc.dram_tensor("w2T", [NSUB, ND0, 128, 16, 128], F32, kind="ExternalInput")
    b1c = nc.dram_tensor("b1c", [NSUB, 128, 16], F32, kind="ExternalInput")
    lnlw = nc.dram_tensor("lnlw", [128, ND0], F32, kind="ExternalInput")
    lnlb = nc.dram_tensor("lnlb", [128, ND0], F32, kind="ExternalInput")
    woutT = nc.dram_tensor("woutT", [KC, 128, ND0, 128], F32, kind="ExternalInput")
    boutc = nc.dram_tensor("boutc", [128, KC], F32, kind="ExternalInput")
    x_out = nc.dram_tensor("x_out", [DM, TF], F32, kind="ExternalOutput")
    if DEBUG_L1:
        dbg_h = nc.dram_tensor("dbg_h", [128, ND0 * TF], F32, kind="ExternalOutput")
        dbg_y = nc.dram_tensor("dbg_y", [128, ND0 * TF], F32, kind="ExternalOutput")
        dbg_yc = nc.dram_tensor("dbg_yc", [128, ND0 * TF], F32, kind="ExternalOutput")
        dbg_h2 = nc.dram_tensor("dbg_h2", [128, ND0 * TF], F32, kind="ExternalOutput")
        dbg_hs = nc.dram_tensor("dbg_hs", [NSUB, 128, ND0 * TF], F32, kind="ExternalOutput")
        dbg_lnl = nc.dram_tensor("dbg_lnl", [128, ND0 * TF], F32, kind="ExternalOutput")

    xfv = x_full.ap().rearrange("(c p) t -> p c t", p=128)
    ov = x_out.ap().rearrange("(c p) t -> p c t", p=128)

    xf = p.sbuf("xf", [128, 2, T], F32)
    outsb = p.sbuf("outsb", [128, 2, 512], F32)
    ones_bf = p.sbuf("ones_bf", [128, 1], BF16)
    ones_f = p.sbuf("ones_f", [1, 128], F32)
    eps1 = p.sbuf("eps1", [1, 1], F32)
    rows5 = p.sbuf("rows5", [1, 2 * TF], F32)
    rowsF = rows5
    xnk = p.sbuf("xnk", [128, 2, T], BF16)
    h = p.sbuf("h", [128, ND0, TF], F32)
    y = p.sbuf("y", [128, ND0, TF], BF16)
    big1 = p.sbuf("big1", [128, ND0 * KC * 128], BF16)
    yc = big1[:].bitcast(F32).rearrange("p (d t) -> p d t", d=ND0)
    lnq = p.sbuf("lnq", [128, ND0, 512], BF16)
    fq = p.sbuf("fq", [128, 16, 512], BF16)
    w1s = p.sbuf("w1s", [128, 16 * ND0 * 128], BF16)
    w2s = p.sbuf("w2s", [128, ND0 * 16 * 128], BF16)
    wsl = p.sbuf("wsl", [128, 2, ND0 * 128], BF16)
    cr = p.sbuf("cr", [128, 128], F32)
    ci = p.sbuf("ci", [128, 128], F32)
    Xs = p.sbuf("Xs", [128, 128], F32)
    tm1 = p.sbuf("tm1", [128, 128], F32)
    tm2 = p.sbuf("tm2", [128, 128], F32)
    tm3 = p.sbuf("tm3", [128, 128], F32)
    tm4 = p.sbuf("tm4", [128, 128], F32)
    shr = p.sbuf("shr", [128, 128], F32)
    shi = p.sbuf("shi", [128, 128], F32)
    sneg = p.sbuf("sneg", [128, 128], F32)
    prt = p.sbuf("prt", [128, ND0, 32], F32)
    pit = p.sbuf("pit", [128, ND0, 32], F32)
    P1r = p.sbuf("P1r", [128, ND0, 64], F32)
    P1i = p.sbuf("P1i", [128, ND0, 64], F32)
    p64k = p.sbuf("p64k", [128, ND0, 12], F32)
    ccols = p.sbuf("ccols", [128, ND0, 14], F32)
    subc = p.sbuf("subc", [128, ND0, 10], F32)
    b1s = p.sbuf("b1s", [128, 16], F32)
    binb = p.sbuf("binb", [128, ND0], F32)
    lnlwc = p.sbuf("lnlwc", [128, ND0], F32)
    lnlbc = p.sbuf("lnlbc", [128, ND0], F32)
    boutb = p.sbuf("boutb", [128, KC], F32)

    s_xs = [p.sem("s_x0"), p.sem("s_x1")]
    s_ws = [p.sem("s_w0"), p.sem("s_w1")]
    s_sm = p.sem("s_sm")
    s_os = [p.sem("s_o0"), p.sem("s_o1")]
    s_pe = p.sem("s_pe")
    s_act = p.sem("s_act")
    s_dve = p.sem("s_dve")

    p.open_block()

    def _(e):
        s_sm.inc(e.dma_start(binb[:], binc.ap()), 16)
        s_sm.inc(e.dma_start(lnlwc[:], lnlw.ap()), 16)
        s_sm.inc(e.dma_start(lnlbc[:], lnlb.ap()), 16)
        s_sm.inc(e.dma_start(boutb[:], boutc.ap()), 16)
    p.pool(_)

    def _(e):
        e.memset(ones_bf[:], 1.0)
        e.memset(eps1[:], EPS_LN)
        s_dve.inc(e.memset(ones_f[:], 1.0))
    p.dve(_)

    pscope = []
    for _nm in ("psA", "psB", "psC", "psD"):
        g = nc.psum_tensor(_nm, [128, T], F32)
        pscope.append((g, g.__enter__()))
    psA, psB, psC, psD = [t for (_g, t) in pscope]

    NSC = 2 * KC
    actF_tok = [0] * NSC
    peF_tok = [0] * NSC
    xF_tok = [0] * NSC
    sqF = p.sbuf("sqF", [128, 2, T], BF16)
    xcF = p.sbuf("xcF", [128, 2, T], BF16)

    for i in range(NSC):
        kc, half = i // 2, i % 2
        sl = i % 2

        def _(e, i=i, kc=kc, half=half, sl=sl, dvg=s_dve.val, acg=s_act.val):
            if i >= 2:
                s_act.wait(e, actF_tok[i - 2])
            else:
                s_dve.wait(e, dvg)
                s_act.wait(e, acg)
            xF_tok[i] = s_xs[sl].inc(e.dma_start(
                xf[:, sl, :], xfv[:, kc, half * T:(half + 1) * T]), 16)
        p.sync(_)

        def _(e, i=i, sl=sl):
            s_xs[sl].wait(e, xF_tok[i])
            if i >= 2:
                s_pe.wait(e, peF_tok[i - 2])
            e.copy(xcF[:, sl, :], xf[:, sl, :])
            actF_tok[i] = s_act.inc(e.square(sqF[:, sl, :], xf[:, sl, :]))
        p.act(_)

        def _(e, i=i, half=half, sl=sl):
            s_act.wait(e, actF_tok[i])
            if i == 0:
                s_dve.wait(e)
            psX = psA if half == 0 else psB
            psQ2 = psC if half == 0 else psD
            for tc in range(2):
                e.matmul(psX[0:1, tc * 512:(tc + 1) * 512], ones_bf[:],
                         xcF[:, sl, tc * 512:(tc + 1) * 512],
                         start=(i < 2), stop=(i >= NSC - 2))
                mm = e.matmul(psQ2[0:1, tc * 512:(tc + 1) * 512], ones_bf[:],
                              sqF[:, sl, tc * 512:(tc + 1) * 512],
                              start=(i < 2), stop=(i >= NSC - 2))
            peF_tok[i] = s_pe.inc(mm)
        p.te(_)
    pe_Fstats = s_pe.val

    FMU, FSD = 0, TF

    def _(e):
        s_pe.wait(e, pe_Fstats)
        e.tensor_scalar(rowsF[:, FMU:FMU + T], psA[0:1, 0:T], 1.0 / DM, None, ALU.mult)
        e.tensor_scalar(rowsF[:, FMU + T:FMU + TF], psB[0:1, 0:T], 1.0 / DM, None, ALU.mult)
        e.tensor_scalar(rowsF[:, FSD:FSD + T], psC[0:1, 0:T], 1.0 / DM, None, ALU.mult)
        e.tensor_scalar(rowsF[:, FSD + T:FSD + TF], psD[0:1, 0:T], 1.0 / DM, None, ALU.mult)
        for seg in range(4):
            a, b = seg * 512, (seg + 1) * 512
            e.tensor_tensor(rows5[0:1, 0:512], rowsF[:, FMU + a:FMU + b],
                            rowsF[:, FMU + a:FMU + b], ALU.mult)
            mmv = e.tensor_tensor(rowsF[:, FSD + a:FSD + b], rowsF[:, FSD + a:FSD + b],
                                  rows5[0:1, 0:512], ALU.subtract)
        s_dve.inc(mmv)
    p.dve(_)

    def _(e):
        s_dve.wait(e)
        s_act.inc(e.activation(rowsF[:, FSD:FSD + TF], rowsF[:, FSD:FSD + TF],
                               AF.Sqrt, bias=eps1[0:1, :], scale=1.0))
    p.act(_)

    def _(e):
        s_act.wait(e)
        s_dve.inc(e.reciprocal(rowsF[:, FSD:FSD + TF], rowsF[:, FSD:FSD + TF]))
    p.dve(_)
    fstats_tok = s_dve.val

    psWin = [psA, psB, psC, psD]
    win_sl = big1
    mub = p.sbuf("mub", [128, T], F32)
    sdb = p.sbuf("sdb", [128, T], F32)

    def _(e):
        for a in range(ND0):
            s_sm.inc(e.dma_start(win_sl[:, a * KC * 128:(a + 1) * KC * 128],
                                 winT.ap()[a].rearrange("p b c -> p (b c)"),
                                 max_dma_last_dim=8192), 16)
    p.pool(_)
    win_ld = s_sm.val

    for half in range(2):
        def _(e, half=half):
            if half == 0:
                s_dve.wait(e, fstats_tok)
            else:
                s_dve.wait(e, win_apply_done)
            for tc in range(2):
                e.matmul(psA[:, tc * 512:(tc + 1) * 512], ones_f[:],
                         rowsF[0:1, FMU + half * T + tc * 512:FMU + half * T + (tc + 1) * 512],
                         start=True, stop=True)
                mm = e.matmul(psB[:, tc * 512:(tc + 1) * 512], ones_f[:],
                              rowsF[0:1, FSD + half * T + tc * 512:FSD + half * T + (tc + 1) * 512],
                              start=True, stop=True)
            s_pe.inc(mm)
        p.te(_)
        pe_hbc = s_pe.val

        def _(e, pe_hbc=pe_hbc):
            s_pe.wait(e, pe_hbc)
            e.copy(mub[:], psA[:])
            s_act.inc(e.copy(sdb[:], psB[:]))
        p.act(_)
        act_hbc = s_act.val

        xk_tok = [0] * KC
        xd_tok = [0] * KC
        win_pe_tok = [0] * KC
        for kc in range(KC):
            sl = kc % 2

            def _(e, half=half, kc=kc, sl=sl, actg=s_act.val):
                if kc >= 2:
                    s_dve.wait(e, xd_tok[kc - 2])
                elif half > 0:
                    s_dve.wait(e, win_apply_done)
                else:
                    s_act.wait(e, actg)
                xk_tok[kc] = s_xs[sl].inc(e.dma_start(
                    xf[:, sl, :], xfv[:, kc, half * T:(half + 1) * T]), 16)
            p.sync(_)

            def _(e, half=half, kc=kc, sl=sl):
                s_xs[sl].wait(e, xk_tok[kc])
                if kc == 0:
                    s_act.wait(e, act_hbc)
                if kc >= 2:
                    s_pe.wait(e, win_pe_tok[kc - 2])
                elif half == 1:
                    s_pe.wait(e, win_pe_tok[KC - 2 + kc])
                e.tensor_tensor(outsb[:, 0, :], xf[:, sl, 0:512], mub[:, 0:512], ALU.subtract)
                e.tensor_tensor(xnk[:, sl, 0:512], outsb[:, 0, :], sdb[:, 0:512], ALU.mult)
                e.tensor_tensor(outsb[:, 0, :], xf[:, sl, 512:1024], mub[:, 512:1024], ALU.subtract)
                xd_tok[kc] = s_dve.inc(e.tensor_tensor(
                    xnk[:, sl, 512:1024], outsb[:, 0, :], sdb[:, 512:1024], ALU.mult))
            p.dve(_)

            def _(e, half=half, kc=kc, sl=sl, act_hbc=act_hbc):
                s_dve.wait(e, xd_tok[kc])
                if kc == 0:
                    s_sm.wait(e, win_ld)
                    s_act.wait(e, act_hbc)
                for mb in range(ND0):
                    for tc in range(2):
                        mm = e.matmul(
                            psWin[mb][:, tc * 512:(tc + 1) * 512],
                            win_sl[:, (mb * KC + kc) * 128:(mb * KC + kc + 1) * 128],
                            xnk[:, sl, tc * 512:(tc + 1) * 512],
                            start=(kc == 0), stop=(kc == KC - 1))
                win_pe_tok[kc] = s_pe.inc(mm)
            p.te(_)
        pe_win = s_pe.val

        def _(e, half=half, pe_win=pe_win):
            s_pe.wait(e, pe_win)
            for mb in range(ND0):
                s_dve.inc(e.tensor_scalar(h[:, mb, half * T:(half + 1) * T],
                                          psWin[mb][:], binb[:, mb:mb + 1],
                                          None, ALU.add))
        p.dve(_)
        win_apply_done = s_dve.val
    h_ready = s_dve.val
    if DEBUG_L1:
        def _(e):
            s_dve.wait(e, h_ready)
            s_sm.inc(e.dma_start(dbg_h.ap(), h[:].rearrange("p a b -> p (a b)"),
                                 max_dma_last_dim=8192), 16)
        p.pool(_)

    def col_bc(buf, n):
        return bass.AP(buf.tensor, buf.offset, [buf.ap[0], buf.ap[1], [0, n]])

    CPR, CPI, CLCR, CLCI, CSCL, CL1W, CL1B, CL2W, CL2B, CB2 = range(10)

    def emit_sub_consts(s):
        def _(e, s=s, dvg=s_dve.val, peg=s_pe.val, acg=s_act.val):
            s_dve.wait(e, dvg)
            s_pe.wait(e, peg)
            s_act.wait(e, acg)
            s_sm.inc(e.dma_start(subc[:, :, CL1W:CL1W + 1], cols_in["ln1_w"].ap()[s].unsqueeze(2), allow_slow_non_contiguous=True), 16)
            s_sm.inc(e.dma_start(subc[:, :, CL1B:CL1B + 1], cols_in["ln1_b"].ap()[s].unsqueeze(2), allow_slow_non_contiguous=True), 16)
            s_sm.inc(e.dma_start(subc[:, :, CL2W:CL2W + 1], cols_in["ln2_w"].ap()[s].unsqueeze(2), allow_slow_non_contiguous=True), 16)
            s_sm.inc(e.dma_start(subc[:, :, CL2B:CL2B + 1], cols_in["ln2_b"].ap()[s].unsqueeze(2), allow_slow_non_contiguous=True), 16)
            s_sm.inc(e.dma_start(subc[:, :, CSCL:CSCL + 1], cols_in["sc_lin"].ap()[s].unsqueeze(2), allow_slow_non_contiguous=True), 16)
            s_sm.inc(e.dma_start(ccols[:, :, 0:1], cols_in["phz_re"].ap()[s].unsqueeze(2), allow_slow_non_contiguous=True), 16)
            s_sm.inc(e.dma_start(ccols[:, :, 1:2], cols_in["phz_im"].ap()[s].unsqueeze(2), allow_slow_non_contiguous=True), 16)
            s_sm.inc(e.dma_start(ccols[:, :, 2:3], cols_in["lc_re"].ap()[s].unsqueeze(2), allow_slow_non_contiguous=True), 16)
            s_sm.inc(e.dma_start(ccols[:, :, 3:4], cols_in["lc_im"].ap()[s].unsqueeze(2), allow_slow_non_contiguous=True), 16)
            s_sm.inc(e.dma_start(subc[:, :, CB2:CB2 + 1], cols_in["b2"].ap()[s].unsqueeze(2), allow_slow_non_contiguous=True), 16)
            s_sm.inc(e.dma_start(b1s[:], b1c.ap()[s]), 16)
            for a in range(16):
                s_sm.inc(e.dma_start(w1s[:, a * ND0 * 128:(a + 1) * ND0 * 128],
                                     w1T.ap()[s][a].rearrange("p b c -> p (b c)"),
                                     max_dma_last_dim=8192), 16)
            for a in range(ND0):
                s_sm.inc(e.dma_start(w2s[:, a * 16 * 128:(a + 1) * 16 * 128],
                                     w2T.ap()[s][a].rearrange("p b c -> p (b c)"),
                                     max_dma_last_dim=8192), 16)
        p.pool(_)
        lds = s_sm.val

        def _(e, s=s, lds=lds):
            s_sm.wait(e, lds)
            W = lambda: s_dve.wait(e)
            I = lambda ins: s_dve.inc(ins)
            phzr = ccols[:, :, 0:1]; phzi = ccols[:, :, 1:2]
            lcr = ccols[:, :, 2:3]; lci = ccols[:, :, 3:4]
            r2 = ccols[:, :, 4:5]; tmp = ccols[:, :, 5:6]
            fac = ccols[:, :, 6:7]
            pr = subc[:, :, CPR:CPR + 1]; pi = subc[:, :, CPI:CPI + 1]
            lcpr = ccols[:, :, 7:8]; lcpi = ccols[:, :, 8:9]
            I(e.tensor_tensor(r2, phzr, phzr, ALU.mult)); W()
            I(e.tensor_tensor(tmp, phzi, phzi, ALU.mult)); W()
            I(e.tensor_tensor(r2, r2, tmp, ALU.add)); W()
        p.dve(_)

        def _(e):
            s_dve.wait(e)
            s_act.inc(e.activation(ccols[:, :, 4:5], ccols[:, :, 4:5], AF.Sqrt,
                                   bias=0.0, scale=1.0))
        p.act(_)

        def _(e):
            s_act.wait(e)
            s_dve.inc(e.reciprocal(ccols[:, :, 5:6], ccols[:, :, 4:5]))
        p.dve(_)

        def _(e):
            s_dve.wait(e)
            s_act.inc(e.activation(ccols[:, :, 4:5], ccols[:, :, 4:5], AF.Tanh,
                                   bias=0.0, scale=1.0))
        p.act(_)

        def _(e, s=s):
            s_act.wait(e)
            W = lambda: s_dve.wait(e)
            I = lambda ins: s_dve.inc(ins)
            phzr = ccols[:, :, 0:1]; phzi = ccols[:, :, 1:2]
            lcr = ccols[:, :, 2:3]; lci = ccols[:, :, 3:4]
            th = ccols[:, :, 4:5]; rinv = ccols[:, :, 5:6]
            fac = ccols[:, :, 6:7]
            pr = subc[:, :, CPR:CPR + 1]; pi = subc[:, :, CPI:CPI + 1]
            lcpr = ccols[:, :, 7:8]; lcpi = ccols[:, :, 8:9]
            t2 = ccols[:, :, 9:10]
            I(e.tensor_tensor(fac, th, rinv, ALU.mult)); W()
            I(e.tensor_tensor(pr, phzr, fac, ALU.mult)); W()
            I(e.tensor_tensor(pi, phzi, fac, ALU.mult)); W()
            I(e.tensor_copy(prt[:], col_bc(pr, 32))); W()
            I(e.tensor_copy(pit[:], col_bc(pi, 32))); W()
            I(e.tensor_copy(P1r[:, :, 0:1], pr)); W()
            I(e.tensor_copy(P1i[:, :, 0:1], pi)); W()
            L = 1
            while L < 64:
                ar = P1r[:, :, 0:L]; ai = P1i[:, :, 0:L]
                br = col_bc(P1r[:, :, L - 1:L], L); bi = col_bc(P1i[:, :, L - 1:L], L)
                s1 = tm1[:, 0:ND0 * L].rearrange("p (d l) -> p d l", d=ND0)
                s2 = tm2[:, 0:ND0 * L].rearrange("p (d l) -> p d l", d=ND0)
                I(e.tensor_tensor(s1, ar, br, ALU.mult)); W()
                I(e.tensor_tensor(s2, ai, bi, ALU.mult)); W()
                I(e.tensor_tensor(P1i[:, :, L:2 * L], ar, bi, ALU.mult)); W()
                I(e.tensor_tensor(s2, s1, s2, ALU.subtract)); W()
                I(e.tensor_tensor(s1, ai, br, ALU.mult)); W()
                I(e.tensor_tensor(P1i[:, :, L:2 * L], P1i[:, :, L:2 * L], s1, ALU.add)); W()
                I(e.tensor_copy(P1r[:, :, L:2 * L], s2)); W()
                L *= 2
            I(e.tensor_copy(p64k[:, :, 0:1], P1r[:, :, 63:64])); W()
            I(e.tensor_copy(p64k[:, :, 1:2], P1i[:, :, 63:64])); W()
            for k in range(1, 5):
                ar = p64k[:, :, 2 * k - 2:2 * k - 1]; ai = p64k[:, :, 2 * k - 1:2 * k]
                I(e.tensor_tensor(ccols[:, :, 10:11], ar, ar, ALU.mult)); W()
                I(e.tensor_tensor(ccols[:, :, 11:12], ai, ai, ALU.mult)); W()
                I(e.tensor_tensor(p64k[:, :, 2 * k:2 * k + 1], ccols[:, :, 10:11],
                                  ccols[:, :, 11:12], ALU.subtract)); W()
                I(e.tensor_tensor(ccols[:, :, 10:11], ar, ai, ALU.mult)); W()
                I(e.tensor_scalar(p64k[:, :, 2 * k + 1:2 * k + 2], ccols[:, :, 10:11],
                                  2.0, None, ALU.mult)); W()
            p64r = p64k[:, :, 0:1]; p64i = p64k[:, :, 1:2]
            I(e.tensor_tensor(ccols[:, :, 10:11], lcr, p64r, ALU.mult)); W()
            I(e.tensor_tensor(ccols[:, :, 11:12], lci, p64i, ALU.mult)); W()
            I(e.tensor_tensor(lcpr, ccols[:, :, 10:11], ccols[:, :, 11:12], ALU.subtract)); W()
            I(e.tensor_tensor(ccols[:, :, 10:11], lcr, p64i, ALU.mult)); W()
            I(e.tensor_tensor(ccols[:, :, 11:12], lci, p64r, ALU.mult)); W()
            I(e.tensor_tensor(lcpi, ccols[:, :, 10:11], ccols[:, :, 11:12], ALU.add)); W()
            I(e.tensor_scalar(ccols[:, :, 12:13], lci, -1.0, None, ALU.mult)); W()
        p.dve(_)
        return s_dve.val

    def emit_inner_stats():
        tok_in = [0] * (ND0 * 2)
        for i in range(ND0 * 2):
            d0, half = i // 2, i % 2
            sl = i % 2

            def _(e, i=i, d0=d0, half=half, sl=sl, hg=s_dve.val):
                if i == 0:
                    s_dve.wait(e, hg)
                if i >= 2:
                    s_pe.wait(e, tok_in[i - 2])
                e.copy(xcF[:, sl, :], h[:, d0, half * T:(half + 1) * T])
                s_act.inc(e.square(sqF[:, sl, :], h[:, d0, half * T:(half + 1) * T]))
            p.act(_)
            a_tok = s_act.val

            def _(e, i=i, d0=d0, half=half, sl=sl, a_tok=a_tok):
                s_act.wait(e, a_tok)
                psX = psA if half == 0 else psB
                psQ2 = psC if half == 0 else psD
                for tc in range(2):
                    e.matmul(psX[0:1, tc * 512:(tc + 1) * 512], ones_bf[:],
                             xcF[:, sl, tc * 512:(tc + 1) * 512],
                             start=(i < 2), stop=(i >= ND0 * 2 - 2))
                    mm = e.matmul(psQ2[0:1, tc * 512:(tc + 1) * 512], ones_bf[:],
                                  sqF[:, sl, tc * 512:(tc + 1) * 512],
                                  start=(i < 2), stop=(i >= ND0 * 2 - 2))
                tok_in[i] = s_pe.inc(mm)
            p.te(_)
        pe_st = s_pe.val

        def _(e, pe_st=pe_st):
            s_pe.wait(e, pe_st)
            e.tensor_scalar(rows5[:, 0:T], psA[0:1, 0:T], 1.0 / D1, None, ALU.mult)
            e.tensor_scalar(rows5[:, T:TF], psB[0:1, 0:T], 1.0 / D1, None, ALU.mult)
            e.tensor_scalar(rows5[:, TF:TF + T], psC[0:1, 0:T], 1.0 / D1, None, ALU.mult)
            e.tensor_scalar(rows5[:, TF + T:2 * TF], psD[0:1, 0:T], 1.0 / D1, None, ALU.mult)
            for seg in range(4):
                a, b = seg * 512, (seg + 1) * 512
                e.tensor_tensor(mub[0:1, 0:512], rows5[:, a:b], rows5[:, a:b], ALU.mult)
                mmv = e.tensor_tensor(rows5[:, TF + a:TF + b], rows5[:, TF + a:TF + b],
                                      mub[0:1, 0:512], ALU.subtract)
            s_dve.inc(mmv)
        p.dve(_)

        def _(e):
            s_dve.wait(e)
            s_act.inc(e.activation(rows5[:, TF:2 * TF], rows5[:, TF:2 * TF],
                                   AF.Sqrt, bias=eps1[0:1, :], scale=1.0))
        p.act(_)

        def _(e):
            s_act.wait(e)
            s_dve.inc(e.reciprocal(rows5[:, TF:2 * TF], rows5[:, TF:2 * TF]))
        p.dve(_)
        return s_dve.val

    def emit_quarter_bcast(q, guard_dve):
        def _(e, q=q, guard_dve=guard_dve, allg=None):
            s_dve.wait(e, guard_dve)
            s_dve.wait(e)
            e.matmul(psA[:, 512:1024], ones_f[:], rows5[0:1, q * 512:(q + 1) * 512],
                     start=True, stop=True)
            mm = e.matmul(psB[:, 512:1024], ones_f[:],
                          rows5[0:1, TF + q * 512:TF + (q + 1) * 512],
                          start=True, stop=True)
            s_pe.inc(mm)
        p.te(_)
        bq = s_pe.val

        def _(e, bq=bq):
            s_pe.wait(e, bq)
            e.copy(mub[:, 0:512], psA[:, 512:1024])
            s_act.inc(e.copy(sdb[:, 0:512], psB[:, 512:1024]))
        p.act(_)
        return s_act.val

    last_dve = h_ready
    for s in range(NSUB):
        cst_tok = emit_sub_consts(s)
        st_tok = emit_inner_stats()

        silu_tok = [0] * 4
        for q in range(4):
            bc_tok = emit_quarter_bcast(q, st_tok)
            dv_toks = [0] * ND0
            for d0 in range(ND0):
                def _(e, q=q, d0=d0, bc_tok=bc_tok, cst_tok=cst_tok):
                    s_act.wait(e)
                    if q == 0 and d0 == 0:
                        s_dve.wait(e, cst_tok)
                    sl5 = h[:, d0, q * 512:(q + 1) * 512]
                    e.tensor_tensor(sdb[:, 512:1024], sl5, mub[:, 0:512], ALU.subtract)
                    e.tensor_tensor(sdb[:, 512:1024], sdb[:, 512:1024], sdb[:, 0:512], ALU.mult)
                    dv_toks[d0] = s_dve.inc(e.tensor_scalar(
                        sdb[:, 512:1024], sdb[:, 512:1024],
                        subc[:, d0][:, CL1W:CL1W + 1],
                        subc[:, d0][:, CL1B:CL1B + 1], ALU.mult, ALU.add))
                p.dve(_)

                def _(e, q=q, d0=d0):
                    s_dve.wait(e, dv_toks[d0])
                    silu_tok[q] = s_act.inc(e.activation(
                        y[:, d0, q * 512:(q + 1) * 512], sdb[:, 512:1024], AF.Silu))
                p.act(_)
        y_ready = silu_tok[3]
        if DEBUG_L1 and s == 0:
            def _(e, y_ready=y_ready):
                s_act.wait(e, y_ready)
                s_sm.inc(e.dma_start(dbg_y.ap(), y[:].rearrange("p a b -> p (a b)"),
                                     max_dma_last_dim=8192), 16)
            p.pool(_)

        def _(e, s=s, y_ready=y_ready, cst_tok=cst_tok):
            s_act.wait(e, y_ready)
            s_dve.wait(e, cst_tok)
            e.memset(cr[:], 0.0)
            e.memset(ci[:], 0.0)
            yv = y.rearrange("p d (c u) -> p d c u", u=64)
            ycv = yc.rearrange("p d (c u) -> p d c u", u=64)
            for u in range(64):
                e.tensor_copy(Xs[:].rearrange("p (d c) -> p d c", d=ND0), yv[:, :, :, u])
                e.tensor_tensor(tm1[:], ci[:], pit[:].rearrange("p d c -> p (d c)"), ALU.mult)
                e.tensor_tensor(tm2[:], cr[:], prt[:].rearrange("p d c -> p (d c)"), ALU.mult)
                e.tensor_tensor(tm3[:], ci[:], prt[:].rearrange("p d c -> p (d c)"), ALU.mult)
                e.tensor_tensor(tm4[:], cr[:], pit[:].rearrange("p d c -> p (d c)"), ALU.mult)
                e.tensor_tensor(cr[:], tm2[:], tm1[:], ALU.subtract)
                e.tensor_tensor(cr[:], cr[:], Xs[:], ALU.add)
                e.tensor_tensor(ci[:], tm3[:], tm4[:], ALU.add)
                e.tensor_copy(ycv[:, :, :, u], cr[:].rearrange("p (d c) -> p d c", d=ND0))
            lcpr = ccols[:, :, 7:8]; lcpi = ccols[:, :, 8:9]
            for d0 in range(ND0):
                s_dve.inc(e.tensor_tensor(cr[:, d0 * 32:d0 * 32 + 1], cr[:, d0 * 32:d0 * 32 + 1],
                                          lcpr[:, d0, :], ALU.add))
                s_dve.wait(e)
                s_dve.inc(e.tensor_tensor(ci[:, d0 * 32:d0 * 32 + 1], ci[:, d0 * 32:d0 * 32 + 1],
                                          lcpi[:, d0, :], ALU.add))
                s_dve.wait(e)
            for ki, k in enumerate([1, 2, 4, 8, 16]):
                pkr = col_bc(p64k[:, :, 2 * ki:2 * ki + 1], 32 - k)
                pki = col_bc(p64k[:, :, 2 * ki + 1:2 * ki + 2], 32 - k)
                e.memset(shr[:], 0.0)
                e.memset(shi[:], 0.0)
                for d0 in range(ND0):
                    e.tensor_copy(shr[:, d0 * 32 + k:(d0 + 1) * 32],
                                  cr[:, d0 * 32:(d0 + 1) * 32 - k])
                    e.tensor_copy(shi[:, d0 * 32 + k:(d0 + 1) * 32],
                                  ci[:, d0 * 32:(d0 + 1) * 32 - k])
                shrv = shr.rearrange("p (d c) -> p d c", d=ND0)[:, :, k:32]
                shiv = shi.rearrange("p (d c) -> p d c", d=ND0)[:, :, k:32]
                t1v = tm1.rearrange("p (d c) -> p d c", d=ND0)[:, :, k:32]
                t2v = tm2.rearrange("p (d c) -> p d c", d=ND0)[:, :, k:32]
                crv = cr.rearrange("p (d c) -> p d c", d=ND0)[:, :, k:32]
                civ = ci.rearrange("p (d c) -> p d c", d=ND0)[:, :, k:32]
                e.tensor_tensor(t1v, shrv, pkr, ALU.mult)
                e.tensor_tensor(t2v, shiv, pki, ALU.mult)
                e.tensor_tensor(t1v, t1v, t2v, ALU.subtract)
                e.tensor_tensor(crv, crv, t1v, ALU.add)
                e.tensor_tensor(t1v, shrv, pki, ALU.mult)
                e.tensor_tensor(t2v, shiv, pkr, ALU.mult)
                e.tensor_tensor(t1v, t1v, t2v, ALU.add)
                e.tensor_tensor(civ, civ, t1v, ALU.add)
            s_dve.inc(e.tensor_scalar(sneg[:], ci[:], -1.0, None, ALU.mult))
            s_dve.wait(e)
            for d0 in range(ND0):
                for c in range(32):
                    if c == 0:
                        sr = ccols[:, d0][:, 2:3]
                        si_n = ccols[:, d0][:, 12:13]
                    else:
                        sr = cr[:, d0 * 32 + c - 1:d0 * 32 + c]
                        si_n = sneg[:, d0 * 32 + c - 1:d0 * 32 + c]
                    tgt = yc[:, d0, c * 64:(c + 1) * 64]
                    e.scalar_tensor_tensor(tgt, P1r[:, d0, :], sr, tgt, ALU.mult, ALU.add)
                    e.scalar_tensor_tensor(tgt, P1i[:, d0, :], si_n, tgt, ALU.mult, ALU.add)
            for d0 in range(ND0):
                mmv = e.scalar_tensor_tensor(
                    h[:, d0, :], yc[:, d0, :], subc[:, d0][:, CSCL:CSCL + 1],
                    h[:, d0, :], ALU.mult, ALU.add)
            s_dve.inc(mmv)
        p.dve(_)
        scan_tok = s_dve.val
        if DEBUG_L1 and s == 0:
            def _(e, scan_tok=scan_tok):
                s_dve.wait(e, scan_tok)
                s_sm.inc(e.dma_start(dbg_yc.ap(), yc[:, :, :].rearrange("p a b -> p (a b)"),
                                     max_dma_last_dim=8192), 16)
                s_sm.inc(e.dma_start(dbg_h2.ap(), h[:].rearrange("p a b -> p (a b)"),
                                     max_dma_last_dim=8192), 16)
            p.pool(_)

        st2_tok = emit_inner_stats()

        for q in range(4):
            bc2 = emit_quarter_bcast(q, st2_tok)
            dvq = [0] * ND0
            for d0 in range(ND0):
                def _(e, q=q, d0=d0, bc2=bc2, scan_tok=scan_tok, peg=s_pe.val):
                    s_act.wait(e, bc2)
                    if d0 == 0:
                        s_pe.wait(e, peg)
                    if q == 0 and d0 == 0:
                        s_dve.wait(e, scan_tok)
                    sl5 = h[:, d0, q * 512:(q + 1) * 512]
                    e.tensor_tensor(sdb[:, 512:1024], sl5, mub[:, 0:512], ALU.subtract)
                    e.tensor_tensor(sdb[:, 512:1024], sdb[:, 512:1024], sdb[:, 0:512], ALU.mult)
                    dvq[d0] = s_dve.inc(e.tensor_scalar(
                        lnq[:, d0, :], sdb[:, 512:1024],
                        subc[:, d0][:, CL2W:CL2W + 1],
                        subc[:, d0][:, CL2B:CL2B + 1], ALU.mult, ALU.add))
                p.dve(_)
            lnq_tok = dvq[ND0 - 1]

            fa_tok = [0] * 16
            sl_tok = [0] * 16
            for mb in range(16):
                def _(e, q=q, mb=mb, lnq_tok=lnq_tok):
                    s_dve.wait(e, lnq_tok)
                    if mb >= 2:
                        s_act.wait(e, sl_tok[mb - 2])
                    for kt in range(ND0):
                        mm = e.matmul(psD[:, (mb % 2) * 512:(mb % 2) * 512 + 512],
                                      w1s[:, (mb * ND0 + kt) * 128:(mb * ND0 + kt + 1) * 128],
                                      lnq[:, kt, :],
                                      start=(kt == 0), stop=(kt == ND0 - 1))
                    fa_tok[mb] = s_pe.inc(mm)
                p.te(_)

                def _(e, q=q, mb=mb):
                    s_pe.wait(e, fa_tok[mb])
                    sl_tok[mb] = s_act.inc(e.activation(
                        fq[:, mb, :], psD[:, (mb % 2) * 512:(mb % 2) * 512 + 512],
                        AF.Silu, bias=b1s[:, mb:mb + 1], scale=1.0))
                p.act(_)

            psY = [psA[:, 0:512], psA[:, 512:1024], psB[:, 0:512], psB[:, 512:1024]]
            def _(e, q=q):
                s_act.wait(e, sl_tok[15])
                for d0 in range(ND0):
                    for kt in range(16):
                        mm = e.matmul(psY[d0],
                                      w2s[:, (d0 * 16 + kt) * 128:(d0 * 16 + kt + 1) * 128],
                                      fq[:, kt, :],
                                      start=(kt == 0), stop=(kt == 15))
                s_pe.inc(mm)
            p.te(_)
            fb_tok = s_pe.val

            def _(e, q=q, fb_tok=fb_tok):
                s_pe.wait(e, fb_tok)
                for d0 in range(ND0):
                    mmv = e.scalar_tensor_tensor(
                        h[:, d0, q * 512:(q + 1) * 512], psY[d0],
                        subc[:, d0][:, CB2:CB2 + 1],
                        h[:, d0, q * 512:(q + 1) * 512], ALU.add, ALU.add)
                s_dve.inc(mmv)
            p.dve(_)
        last_dve = s_dve.val
        if DEBUG_L1:
            def _(e, s=s, last_dve=last_dve):
                s_dve.wait(e, last_dve)
                s_sm.inc(e.dma_start(dbg_hs.ap()[s], h[:].rearrange("p a b -> p (a b)"),
                                     max_dma_last_dim=8192), 16)
            p.pool(_)

    stl_tok = emit_inner_stats()
    lnl_tok = [0] * 4
    for q in range(4):
        bcl = emit_quarter_bcast(q, stl_tok)
        for d0 in range(ND0):
            def _(e, q=q, d0=d0, bcl=bcl):
                s_act.wait(e, bcl)
                sl5 = h[:, d0, q * 512:(q + 1) * 512]
                e.tensor_tensor(sdb[:, 512:1024], sl5, mub[:, 0:512], ALU.subtract)
                e.tensor_tensor(sdb[:, 512:1024], sdb[:, 512:1024], sdb[:, 0:512], ALU.mult)
                lnl_tok[q] = s_dve.inc(e.tensor_scalar(
                    y[:, d0, q * 512:(q + 1) * 512], sdb[:, 512:1024],
                    lnlwc[:, d0:d0 + 1], lnlbc[:, d0:d0 + 1], ALU.mult, ALU.add))
            p.dve(_)
    lnl_done = lnl_tok[3]
    if DEBUG_L1:
        def _(e):
            s_dve.wait(e, lnl_done)
            s_sm.inc(e.dma_start(dbg_lnl.ap(), y[:].rearrange("p a b -> p (a b)"),
                                 max_dma_last_dim=8192), 16)
        p.pool(_)

    wo_tok = [0] * KC
    pe_wo = [0] * KC
    dve_wo = [0] * KC
    out_tok = [0] * (KC * 4)
    ev_q = [0] * (KC * 4)
    xw_tok = [0] * (KC * 2)

    for mb in range(KC):
        sl = mb % 2
        psW = [psA, psB] if sl == 0 else [psC, psD]

        def _(e, mb=mb, sl=sl):
            if mb >= 2:
                s_pe.wait(e, pe_wo[mb - 2])
            wo_tok[mb] = s_ws[sl].inc(e.dma_start(
                wsl[:, sl, :], woutT.ap()[mb].rearrange("p a b -> p (a b)"),
                max_dma_last_dim=8192), 16)
        p.pool(_)

        def _(e, mb=mb, sl=sl, psW=psW):
            s_ws[sl].wait(e, wo_tok[mb])
            if mb == 0:
                s_dve.wait(e, lnl_done)
            if mb >= 2:
                s_dve.wait(e, dve_wo[mb - 2])
            for q in range(4):
                tgt = psW[q // 2][:, (q % 2) * 512:(q % 2) * 512 + 512]
                for kt in range(ND0):
                    mm = e.matmul(tgt,
                                  wsl[:, sl, kt * 128:(kt + 1) * 128],
                                  y[:, kt, q * 512:(q + 1) * 512],
                                  start=(kt == 0), stop=(kt == ND0 - 1))
            pe_wo[mb] = s_pe.inc(mm)
        p.te(_)

        for half in range(2):
            def _(e, mb=mb, half=half, dvg=s_dve.val):
                if mb >= 1:
                    s_dve.wait(e, dve_wo[mb - 1])
                else:
                    s_dve.wait(e, dvg)
                xw_tok[mb * 2 + half] = s_xs[half].inc(e.dma_start(
                    xf[:, half, :], xfv[:, mb, half * T:(half + 1) * T]), 16)
            p.sync(_)

        for q in range(4):
            oi = mb * 4 + q

            def _(e, mb=mb, sl=sl, psW=psW, q=q, oi=oi):
                if q < 2:
                    s_xs[q].wait(e, xw_tok[mb * 2 + q])
                s_pe.wait(e, pe_wo[mb])
                srcp = psW[q // 2][:, (q % 2) * 512:(q % 2) * 512 + 512]
                if oi >= 2:
                    s_os[q % 2].wait(e, out_tok[oi - 2])
                e.tensor_scalar(outsb[:, q % 2, :], srcp, boutb[:, mb:mb + 1],
                                None, ALU.add)
                s_dve.inc(e.tensor_tensor(
                    outsb[:, q % 2, :], outsb[:, q % 2, :],
                    xf[:, q // 2, (q % 2) * 512:(q % 2) * 512 + 512], ALU.add))
                ev_q[oi] = s_dve.val
            p.dve(_)

            def _(e, mb=mb, q=q, oi=oi):
                s_dve.wait(e, ev_q[oi])
                out_tok[oi] = s_os[q % 2].inc(e.dma_start(
                    ov[:, mb, q * 512:(q + 1) * 512], outsb[:, q % 2, :]), 16)
            p.sync(_)
        dve_wo[mb] = s_dve.val

    def _(e):
        for tc in range(2):
            s_os[tc].wait(e)
    p.sync(_)

    for (g, _t) in reversed(pscope):
        g.__exit__(None, None, None)
    p.close()
    return p


def run_l1attn(x, w):
    p = _get_prog("l1attn", build_l1attn_program)
    in_maps = []
    for c in range(NCORES):
        b = c // 2
        xfull = np.ascontiguousarray(x[b].T)
        in_maps.append({"x_full": xfull, **w})
    res = run_bass_kernel_spmd(p.nc, in_maps, core_ids=list(range(NCORES)))
    out = np.empty((4, 2048, DM), np.float32)
    for c in range(NCORES):
        b, h = c // 2, c % 2
        out[b, h * T:(h + 1) * T, :] = res.results[c]["x_out"].T[h * T:(h + 1) * T, :]
    return out


# version 04
# speedup vs baseline: 1.0026x; 1.0026x over previous
import sys
import numpy as np

sys.path.insert(0, "/opt/trn_rl_repo")

import concourse.bass as bass
import concourse.mybir as mybir
from concourse.bass_utils import run_bass_kernel_spmd

F32 = mybir.dt.float32
BF16 = mybir.dt.bfloat16
AF = mybir.ActivationFunctionType
ALU = mybir.AluOpType
AX = mybir.AxisListType

NCORES = 8
T = 1024
DM = 4096
DH = 11008
HB = DH // 128
KC = DM // 128
EPS_LN = 1e-5


class CSem:
    def __init__(self, handle):
        self.h = handle
        self.val = 0

    def inc(self, instr, n=1):
        instr.then_inc(self.h, n)
        self.val += n
        return self.val

    def wait(self, eng, val=None):
        eng.wait_ge(self.h, self.val if val is None else val)


class Prog:
    def __init__(self):
        self.nc = bass.Bass("TRN2", target_bir_lowering=False, debug=False)
        self._ctxs = []
        self.sems = {}

    def sem(self, name):
        if name not in self.sems:
            g = self.nc.semaphore(name)
            self.sems[name] = CSem(g.__enter__())
            self._ctxs.append(g)
        return self.sems[name]

    def sbuf(self, name, shape, dtype, scope=None):
        g = self.nc.sbuf_tensor(name, shape, dtype)
        t = g.__enter__()
        (self._ctxs if scope is None else scope).append(g)
        return t

    def psum(self, name, shape, dtype=F32):
        g = self.nc.psum_tensor(name, shape, dtype)
        t = g.__enter__()
        self._ctxs.append(g)
        return t

    def open_block(self):
        self._blk = self.nc.Block()
        self.block = self._blk.__enter__()

    def close_scope(self, scope):
        for g in reversed(scope):
            g.__exit__(None, None, None)

    def close(self):
        self._blk.__exit__(None, None, None)
        for g in reversed(self._ctxs):
            g.__exit__(None, None, None)

    def te(self, f):
        self.block.tensor(f)

    def dve(self, f):
        self.block.vector(f)

    def act(self, f):
        self.block.scalar(f)

    def pool(self, f):
        self.block.gpsimd(f)

    def sync(self, f):
        self.block.sync(f)



def _lhsT_slabs(W):
    o, i = W.shape
    return np.ascontiguousarray(
        W.reshape(o // 128, 128, i // 128, 128).transpose(0, 3, 2, 1))


def prep_l0ffn(inp):
    gw = inp["L0_ffn_gate_w"]; uw = inp["L0_ffn_up_w"]; dw = inp["L0_ffn_down_w"]
    d = {}
    d["gsl"] = _lhsT_slabs(np.asarray(gw, np.float32))
    d["usl"] = _lhsT_slabs(np.asarray(uw, np.float32))
    d["dsl"] = np.ascontiguousarray(
        np.asarray(dw, np.float32).reshape(8, 4, 128, HB, 128).transpose(0, 4, 3, 1, 2))
    a2 = np.concatenate([inp["L0_ffn_gate_a"], inp["L0_ffn_up_a"]], 0).astype(np.float32)
    d["a2t"] = np.ascontiguousarray(a2.T.reshape(KC, 128, 32).transpose(1, 0, 2))
    d["gbt"] = np.ascontiguousarray(np.asarray(inp["L0_ffn_gate_b"], np.float32).T.reshape(16, HB, 128))
    d["ubt"] = np.ascontiguousarray(np.asarray(inp["L0_ffn_up_b"], np.float32).T.reshape(16, HB, 128))
    d["dat"] = np.ascontiguousarray(
        np.asarray(inp["L0_ffn_down_a"], np.float32).T.reshape(HB, 128, 16).transpose(1, 0, 2))
    d["dbt"] = np.ascontiguousarray(np.asarray(inp["L0_ffn_down_b"], np.float32).T.reshape(16, KC, 128))
    return d



def emit_layernorm(p, xv, xn, xf, sq, t32, rows, ones_bf, ones_f, eps1,
                   psA, psB, psC, psD, s_xs, s_pe, s_act, s_dve):
    sq_tok = [0] * KC
    x_tok = [0] * KC
    peS_tok = [0] * KC

    for kc in range(KC):
        sl = kc % 2

        def _(e, kc=kc, sl=sl):
            if kc >= 2:
                s_act.wait(e, sq_tok[kc - 2])
            x_tok[kc] = s_xs[sl].inc(e.dma_start(xf[:, sl, :], xv[:, kc, :]), 16)
        p.sync(_)

        def _(e, kc=kc, sl=sl):
            s_xs[sl].wait(e, x_tok[kc])
            if kc >= 2:
                s_pe.wait(e, peS_tok[kc - 2])
            e.copy(xn[:, kc * T:(kc + 1) * T], xf[:, sl, :])
            sq_tok[kc] = s_act.inc(e.square(sq[:, sl, :], xf[:, sl, :]))
        p.act(_)

        def _(e, kc=kc, sl=sl):
            s_act.wait(e, sq_tok[kc])
            if kc == 0:
                s_dve.wait(e)
            for tc in range(2):
                e.matmul(psA[0:1, tc * 512:(tc + 1) * 512], ones_bf[:],
                         xn[:, kc * T + tc * 512:kc * T + (tc + 1) * 512],
                         start=(kc == 0), stop=(kc == KC - 1))
                mm = e.matmul(psB[0:1, tc * 512:(tc + 1) * 512], ones_bf[:],
                              sq[:, sl, tc * 512:(tc + 1) * 512],
                              start=(kc == 0), stop=(kc == KC - 1))
            peS_tok[kc] = s_pe.inc(mm)
        p.te(_)

    MU, MSQ, VAR, SD = 0, T, 2 * T, 3 * T

    def _(e):
        s_pe.wait(e)
        e.tensor_scalar(rows[:, MU:MU + T], psA[0:1, :], 1.0 / DM, None, ALU.mult)
        s_dve.inc(e.tensor_scalar(rows[:, MSQ:MSQ + T], psB[0:1, :], 1.0 / DM, None, ALU.mult))
        s_dve.wait(e)
        s_dve.inc(e.tensor_tensor(rows[:, VAR:VAR + T], rows[:, MU:MU + T],
                                  rows[:, MU:MU + T], ALU.mult))
        s_dve.wait(e)
        s_dve.inc(e.tensor_tensor(rows[:, VAR:VAR + T], rows[:, MSQ:MSQ + T],
                                  rows[:, VAR:VAR + T], ALU.subtract))
    p.dve(_)

    def _(e):
        s_dve.wait(e)
        s_act.inc(e.activation(rows[:, SD:SD + T], rows[:, VAR:VAR + T],
                               AF.Sqrt, bias=eps1[0:1, :], scale=1.0))
    p.act(_)

    def _(e):
        s_act.wait(e)
        s_dve.inc(e.reciprocal(rows[:, SD:SD + T], rows[:, SD:SD + T]))
    p.dve(_)

    def _(e):
        s_dve.wait(e)
        for tc in range(2):
            e.matmul(psC[:, tc * 512:(tc + 1) * 512], ones_f[:],
                     rows[0:1, MU + tc * 512:MU + (tc + 1) * 512],
                     start=True, stop=True)
            mm = e.matmul(psD[:, tc * 512:(tc + 1) * 512], ones_f[:],
                          rows[0:1, SD + tc * 512:SD + (tc + 1) * 512],
                          start=True, stop=True)
        s_pe.inc(mm)
    p.te(_)
    bc_tok = s_pe.val

    ap_tok = [0] * KC
    act_guard = s_act.val
    for kc in range(KC):
        sl = kc % 2

        def _(e, kc=kc, sl=sl):
            if kc >= 2:
                s_dve.wait(e, ap_tok[kc - 2])
            else:
                s_act.wait(e, act_guard)
            x_tok[kc] = s_xs[sl].inc(e.dma_start(xf[:, sl, :], xv[:, kc, :]), 16)
        p.sync(_)

        def _(e, kc=kc, sl=sl):
            s_xs[sl].wait(e, x_tok[kc])
            if kc == 0:
                s_pe.wait(e, bc_tok)
            e.tensor_tensor(t32[:], xf[:, sl, :], psC[:], ALU.subtract)
            ap_tok[kc] = s_dve.inc(e.tensor_tensor(
                xn[:, kc * T:(kc + 1) * T], t32[:], psD[:], ALU.mult))
        p.dve(_)
    return s_dve.val



def build_l0ffn_program():
    p = Prog()
    nc = p.nc
    x_in = nc.dram_tensor("x_in", [DM, T], F32, kind="ExternalInput")
    gsl = nc.dram_tensor("gsl", [HB, 128, KC, 128], F32, kind="ExternalInput")
    usl = nc.dram_tensor("usl", [HB, 128, KC, 128], F32, kind="ExternalInput")
    dsl = nc.dram_tensor("dsl", [8, 128, HB, 4, 128], F32, kind="ExternalInput")
    a2t = nc.dram_tensor("a2t", [128, KC, 32], F32, kind="ExternalInput")
    gbt = nc.dram_tensor("gbt", [16, HB, 128], F32, kind="ExternalInput")
    ubt = nc.dram_tensor("ubt", [16, HB, 128], F32, kind="ExternalInput")
    dat = nc.dram_tensor("dat", [128, HB, 16], F32, kind="ExternalInput")
    dbt = nc.dram_tensor("dbt", [16, KC, 128], F32, kind="ExternalInput")
    x_out = nc.dram_tensor("x_out", [DM, T], F32, kind="ExternalOutput")
    hid_dram = nc.dram_tensor("hid_scr", [HB, 128, T], BF16)

    xv = x_in.ap().rearrange("(c p) t -> p c t", p=128)
    ov = x_out.ap().rearrange("(c p) t -> p c t", p=128)

    xf = p.sbuf("xf", [128, 2, T], F32)
    outsb = p.sbuf("outsb", [128, 2, 512], F32)
    td = p.sbuf("td", [16, T], BF16)
    dbs = p.sbuf("dbs", [16, KC * 128], BF16)
    ones_bf = p.sbuf("ones_bf", [128, 1], BF16)
    ones_f = p.sbuf("ones_f", [1, 128], F32)
    eps1 = p.sbuf("eps1", [1, 1], F32)

    sa = []
    xn = p.sbuf("xn", [128, KC * T], BF16, sa)
    gslb = p.sbuf("gslb", [128, 2, KC * 128], BF16, sa)
    uslb = p.sbuf("uslb", [128, 2, KC * 128], BF16, sa)
    gbs = p.sbuf("gbs", [16, HB * 128], BF16, sa)
    ubs = p.sbuf("ubs", [16, HB * 128], BF16, sa)
    das = p.sbuf("das", [128, HB * 16], BF16, sa)
    a2s = p.sbuf("a2s", [128, KC, 32], BF16, sa)
    tg = p.sbuf("tg", [16, T], BF16, sa)
    tu = p.sbuf("tu", [16, T], BF16, sa)
    silu = p.sbuf("silu", [128, T], BF16, sa)
    ucast = p.sbuf("ucast", [128, T], BF16, sa)
    hid = p.sbuf("hid", [128, 2, T], BF16, sa)
    sq = p.sbuf("sq", [128, 2, T], BF16, sa)
    t32 = p.sbuf("t32", [128, T], F32, sa)
    rows = p.sbuf("rows", [1, 4 * T], F32, sa)

    psA = p.psum("psA", [128, T], F32)
    psB = p.psum("psB", [128, T], F32)
    psC = p.psum("psC", [128, T], F32)
    psD = p.psum("psD", [128, T], F32)
    psbank = [psA, psB, psC, psD]

    s_xs = [p.sem("s_x0"), p.sem("s_x1")]
    s_gs = [p.sem("s_g0"), p.sem("s_g1")]
    s_us = [p.sem("s_u0"), p.sem("s_u1")]
    s_ds = [p.sem("s_d0"), p.sem("s_d1")]
    s_hos = [p.sem("s_ho0"), p.sem("s_ho1")]
    s_hi = p.sem("s_hi")
    s_os = [p.sem("s_o0"), p.sem("s_o1")]
    s_sm = p.sem("s_sm")
    s_pe = p.sem("s_pe")
    s_act = p.sem("s_act")
    s_dve = p.sem("s_dve")

    p.open_block()

    def _(e):
        s_sm.inc(e.dma_start(a2s[:], a2t.ap()), 16)
        s_sm.inc(e.dma_start(gbs[:], gbt.ap().rearrange("p a b -> p (a b)")), 16)
        s_sm.inc(e.dma_start(ubs[:], ubt.ap().rearrange("p a b -> p (a b)")), 16)
        s_sm.inc(e.dma_start(das[:], dat.ap().rearrange("p a b -> p (a b)")), 16)
        s_sm.inc(e.dma_start(dbs[:], dbt.ap().rearrange("p a b -> p (a b)")), 16)
    p.pool(_)

    def _(e):
        e.memset(ones_bf[:], 1.0)
        e.memset(eps1[:], EPS_LN)
        s_dve.inc(e.memset(ones_f[:], 1.0))
    p.dve(_)

    emit_layernorm(p, xv, xn, xf, sq, t32, rows, ones_bf, ones_f, eps1,
                   psA, psB, psC, psD, s_xs, s_pe, s_act, s_dve)

    def _(e):
        s_dve.wait(e)
        s_sm.wait(e)
        for tc in range(2):
            for kc in range(KC):
                e.matmul(psA[0:16, tc * 512:(tc + 1) * 512],
                         a2s[:, kc, 0:16],
                         xn[:, kc * T + tc * 512:kc * T + (tc + 1) * 512],
                         start=(kc == 0), stop=(kc == KC - 1))
            for kc in range(KC):
                mm = e.matmul(psB[0:16, tc * 512:(tc + 1) * 512],
                              a2s[:, kc, 16:32],
                              xn[:, kc * T + tc * 512:kc * T + (tc + 1) * 512],
                              start=(kc == 0), stop=(kc == KC - 1))
        s_pe.inc(mm)
    p.te(_)

    def _(e):
        s_pe.wait(e)
        e.copy(tg[:], psA[0:16, :])
        s_act.inc(e.copy(tu[:], psB[0:16, :]))
    p.act(_)
    tgtu_tok = s_act.val

    pe_hb_tok = [0] * HB
    dve_hb_tok = [0] * HB
    ho_tok = [0] * HB
    silu_tok = [0] * HB
    ucast_tok = [0] * HB
    g_tok = [0] * HB
    u_tok = [0] * HB

    def emit_slab_load(hb):
        sl = hb % 2

        def _(e, hb=hb, sl=sl):
            if hb >= 2:
                s_pe.wait(e, pe_hb_tok[hb - 2])
            g_tok[hb] = s_gs[sl].inc(e.dma_start(
                gslb[:, sl, :], gsl.ap()[hb].rearrange("p a b -> p (a b)"),
                max_dma_last_dim=8192), 16)
            u_tok[hb] = s_us[sl].inc(e.dma_start(
                uslb[:, sl, :], usl.ap()[hb].rearrange("p a b -> p (a b)"),
                max_dma_last_dim=8192), 16)
        p.pool(_)

    emit_slab_load(0)
    emit_slab_load(1)

    for hb in range(HB):
        sl = hb % 2
        psG = psA if sl == 0 else psB

        def _(e, hb=hb, sl=sl, psG=psG):
            s_gs[sl].wait(e, g_tok[hb])
            s_us[sl].wait(e, u_tok[hb])
            if hb == 0:
                s_act.wait(e, tgtu_tok)
            if hb >= 2:
                s_act.wait(e, silu_tok[hb - 2])
            for tc in range(2):
                for kc in range(KC):
                    e.matmul(psG[:, tc * 512:(tc + 1) * 512],
                             gslb[:, sl, kc * 128:(kc + 1) * 128],
                             xn[:, kc * T + tc * 512:kc * T + (tc + 1) * 512],
                             start=(kc == 0), stop=False)
                e.matmul(psG[:, tc * 512:(tc + 1) * 512],
                         gbs[0:16, hb * 128:(hb + 1) * 128],
                         tg[0:16, tc * 512:(tc + 1) * 512],
                         start=False, stop=True)
            if hb > 0:
                s_dve.wait(e, dve_hb_tok[hb - 1])
                for tc in range(2):
                    e.matmul(psD[0:16, tc * 512:(tc + 1) * 512],
                             das[:, (hb - 1) * 16:hb * 16],
                             hid[:, (hb - 1) % 2, tc * 512:(tc + 1) * 512],
                             start=(hb == 1), stop=False)
            if hb >= 1:
                s_act.wait(e, ucast_tok[hb - 1])
            for tc in range(2):
                for kc in range(KC):
                    e.matmul(psC[:, tc * 512:(tc + 1) * 512],
                             uslb[:, sl, kc * 128:(kc + 1) * 128],
                             xn[:, kc * T + tc * 512:kc * T + (tc + 1) * 512],
                             start=(kc == 0), stop=False)
                mm = e.matmul(psC[:, tc * 512:(tc + 1) * 512],
                              ubs[0:16, hb * 128:(hb + 1) * 128],
                              tu[0:16, tc * 512:(tc + 1) * 512],
                              start=False, stop=True)
            pe_hb_tok[hb] = s_pe.inc(mm)
        p.te(_)

        def _(e, hb=hb, psG=psG):
            s_pe.wait(e, pe_hb_tok[hb])
            if hb >= 1:
                s_dve.wait(e, dve_hb_tok[hb - 1])
            silu_tok[hb] = s_act.inc(e.activation(silu[:], psG[:], AF.Silu))
            ucast_tok[hb] = s_act.inc(e.copy(ucast[:], psC[:]))
        p.act(_)

        def _(e, hb=hb, sl=sl):
            s_act.wait(e, ucast_tok[hb])
            if hb >= 2:
                s_hos[sl].wait(e, ho_tok[hb - 2])
            dve_hb_tok[hb] = s_dve.inc(
                e.tensor_tensor(hid[:, sl, :], silu[:], ucast[:], ALU.mult))
        p.dve(_)

        def _(e, hb=hb, sl=sl):
            s_dve.wait(e, dve_hb_tok[hb])
            ho_tok[hb] = s_hos[sl].inc(
                e.dma_start(hid_dram.ap()[hb], hid[:, sl, :]), 16)
        p.sync(_)

        if hb + 2 < HB:
            emit_slab_load(hb + 2)

    def _(e):
        s_dve.wait(e, dve_hb_tok[HB - 1])
        for tc in range(2):
            mm = e.matmul(psD[0:16, tc * 512:(tc + 1) * 512],
                          das[:, (HB - 1) * 16:HB * 16],
                          hid[:, (HB - 1) % 2, tc * 512:(tc + 1) * 512],
                          start=False, stop=True)
        s_pe.inc(mm)
    p.te(_)
    pe_A_end = s_pe.val

    def _(e):
        s_pe.wait(e, pe_A_end)
        s_act.inc(e.copy(td[:], psD[0:16, :]))
    p.act(_)
    act_A_end = s_act.val
    dve_A_end = s_dve.val

    p.close_scope(sa)
    sb = []
    hidall = p.sbuf("hidall", [128, HB * T], BF16, sb)
    dslb = p.sbuf("dslb", [128, 2, 4 * 4 * 128], BF16, sb)

    NG = 8
    groups = [list(range(g * 11, min(HB, (g + 1) * 11))) for g in range(NG)]
    grp_tok = [0] * NG

    def _(e):
        s_pe.wait(e, pe_A_end)
        s_act.wait(e, act_A_end)
        s_dve.wait(e, dve_A_end)
        for g, hbs in enumerate(groups):
            if g >= 2:
                s_hi.wait(e, grp_tok[g - 2])
            for hb in hbs:
                s_hos[hb % 2].wait(e, ho_tok[hb])
                mm = e.dma_start(hidall[:, hb * T:(hb + 1) * T], hid_dram.ap()[hb])
                s_hi.inc(mm, 16)
            grp_tok[g] = s_hi.val
    p.act(_)

    NB = 22
    pe_batch_tok = [0] * (8 * NB)
    dve_ev_tok = [0] * 8
    d_tok = [0] * (8 * NB)
    ev_tok = [0] * (8 * 4 * 2)
    out_tok = [0] * (8 * 4 * 2)
    xb_tok = [0] * (8 * 4)

    for dg in range(8):
        for ib in range(NB):
            h0 = ib * 4
            hn = 4 if ib < 21 else 2
            bi = dg * NB + ib

            def _(e, dg=dg, ib=ib, h0=h0, hn=hn, bi=bi):
                if bi >= 2:
                    s_pe.wait(e, pe_batch_tok[bi - 2])
                elif dg == 0 and ib == 0:
                    s_act.wait(e, act_A_end)
                src = dsl.ap()[dg, :, h0:h0 + hn].rearrange("p h d c -> p (h d c)")
                d_tok[bi] = s_ds[ib % 2].inc(
                    e.dma_start(dslb[:, ib % 2, 0:hn * 512], src,
                                max_dma_last_dim=8192), 16)
            p.pool(_)

            def _(e, dg=dg, ib=ib, h0=h0, hn=hn, bi=bi):
                s_ds[ib % 2].wait(e, d_tok[bi])
                gneed = (h0 + hn - 1) // 11
                s_hi.wait(e, grp_tok[gneed])
                if dg == 0 and ib == 0:
                    s_act.wait(e, act_A_end)
                if dg > 0 and ib == 0:
                    s_dve.wait(e, dve_ev_tok[dg - 1])
                for h in range(hn):
                    hb = h0 + h
                    for d in range(4):
                        for tc in range(2):
                            mm = e.matmul(
                                psbank[d][:, tc * 512:(tc + 1) * 512],
                                dslb[:, ib % 2, (h * 4 + d) * 128:(h * 4 + d + 1) * 128],
                                hidall[:, hb * T + tc * 512:hb * T + (tc + 1) * 512],
                                start=(hb == 0), stop=False)
                pe_batch_tok[bi] = s_pe.inc(mm)
            p.te(_)

        def _(e, dg=dg):
            for d in range(4):
                for tc in range(2):
                    mm = e.matmul(psbank[d][:, tc * 512:(tc + 1) * 512],
                                  dbs[0:16, (dg * 4 + d) * 128:(dg * 4 + d + 1) * 128],
                                  td[0:16, tc * 512:(tc + 1) * 512],
                                  start=False, stop=True)
            s_pe.inc(mm)
        p.te(_)
        dg_done = s_pe.val

        for d in range(4):
            di = dg * 4 + d

            def _(e, dg=dg, d=d, di=di):
                if di >= 2:
                    s_dve.wait(e, xb_tok[di - 2])
                x_t = s_xs[d % 2].inc(e.dma_start(xf[:, d % 2, :], xv[:, di, :]), 16)
                xb_tok[di] = x_t
            p.sync(_)
            xdma_tok = s_xs[d % 2].val

            def _(e, dg=dg, d=d, di=di, dg_done=dg_done, xdma_tok=xdma_tok):
                s_xs[d % 2].wait(e, xdma_tok)
                s_pe.wait(e, dg_done)
                for tc in range(2):
                    oi = di * 2 + tc
                    if oi >= 2:
                        s_os[tc].wait(e, out_tok[oi - 2])
                    ev_tok[oi] = s_dve.inc(e.scalar_tensor_tensor(
                        outsb[:, tc, :],
                        psbank[d][:, tc * 512:(tc + 1) * 512], 0.5,
                        xf[:, d % 2, tc * 512:(tc + 1) * 512],
                        ALU.mult, ALU.add))
                xb_tok[di] = s_dve.val
            p.dve(_)

            def _(e, dg=dg, d=d, di=di):
                for tc in range(2):
                    oi = di * 2 + tc
                    s_dve.wait(e, ev_tok[oi])
                    out_tok[oi] = s_os[tc].inc(
                        e.dma_start(ov[:, di, tc * 512:(tc + 1) * 512],
                                    outsb[:, tc, :]), 16)
            p.sync(_)
        dve_ev_tok[dg] = s_dve.val

    def _(e):
        for tc in range(2):
            s_os[tc].wait(e)
    p.sync(_)

    p.close_scope(sb)
    p.close()
    return p



_prog_cache = {}


def _get_prog(name, builder):
    if name not in _prog_cache:
        _prog_cache[name] = builder()
    return _prog_cache[name]


def _x_to_shards(x):
    out = []
    for c in range(NCORES):
        b, h = c // 2, c % 2
        out.append(np.ascontiguousarray(x[b, h * T:(h + 1) * T, :].T))
    return out


def _shards_to_x(shards):
    x = np.empty((4, 2048, DM), np.float32)
    for c in range(NCORES):
        b, h = c // 2, c % 2
        x[b, h * T:(h + 1) * T, :] = shards[c].T
    return x


def run_l0ffn(x, w):
    p = _get_prog("l0ffn", build_l0ffn_program)
    shards = _x_to_shards(x)
    in_maps = [{"x_in": shards[c], **w} for c in range(NCORES)]
    res = run_bass_kernel_spmd(p.nc, in_maps, core_ids=list(range(NCORES)))
    return _shards_to_x([res.results[c]["x_out"] for c in range(NCORES)])


def kernel(**inputs):
    x = np.asarray(inputs["L0_attn_x"], np.float32)
    w_f0 = prep_l0ffn(inputs)
    w_a0 = prep_l0attn(inputs)
    w_f1 = prep_l1ffn(inputs)
    w_a1 = prep_l1attn(inputs)
    x = run_l0ffn(x, w_f0)
    x = run_l0attn(x, w_a0)
    x = run_l0ffn(x, w_f0)
    x = run_l1ffn(x, w_f1)
    x = run_l1attn(x, w_a1)
    x = run_l1ffn(x, w_f1)
    return x



TF = 2048
NH = 32
NKV = 8
HD = 128
EPS_RMS = 1e-6


def _alibi_slopes(n):
    import math
    start = 2 ** (-2 ** (-(math.log2(n) - 3)))
    return [start * start ** i for i in range(n)]


def prep_l0attn(inp):
    d = {}
    d["wqT"] = _lhsT_slabs(np.asarray(inp["L0_attn_Wq"], np.float32))
    d["wkT"] = _lhsT_slabs(np.asarray(inp["L0_attn_Wk"], np.float32))
    d["woT"] = _lhsT_slabs(np.asarray(inp["L0_attn_Wo"], np.float32))
    d["wvt"] = np.ascontiguousarray(
        np.asarray(inp["L0_attn_Wv"], np.float32).T.reshape(KC, 128, 1024))
    d["qnw"] = np.ascontiguousarray(np.asarray(inp["L0_attn_q_norm_w"], np.float32).reshape(128, 1))
    d["knw"] = np.ascontiguousarray(np.asarray(inp["L0_attn_k_norm_w"], np.float32).reshape(128, 1))
    return d






DEBUG_ATTN = False


def build_l0attn_program():
    import math
    p = Prog()
    nc = p.nc
    x_own = nc.dram_tensor("x_own", [DM, T], F32, kind="ExternalInput")
    x_full = nc.dram_tensor("x_full", [DM, TF], F32, kind="ExternalInput")
    wqT = nc.dram_tensor("wqT", [32, 128, KC, 128], F32, kind="ExternalInput")
    wkT = nc.dram_tensor("wkT", [8, 128, KC, 128], F32, kind="ExternalInput")
    woT = nc.dram_tensor("woT", [32, 128, KC, 128], F32, kind="ExternalInput")
    wvt = nc.dram_tensor("wvt", [KC, 128, 1024], F32, kind="ExternalInput")
    qnw = nc.dram_tensor("qnw", [128, 1], F32, kind="ExternalInput")
    knw = nc.dram_tensor("knw", [128, 1], F32, kind="ExternalInput")
    ownoff = nc.dram_tensor("ownoff", [128, 1], F32, kind="ExternalInput")
    x_out = nc.dram_tensor("x_out", [DM, T], F32, kind="ExternalOutput")
    if DEBUG_ATTN:
        dbg_q = nc.dram_tensor("dbg_q", [128, NH * T], F32, kind="ExternalOutput")
        dbg_k = nc.dram_tensor("dbg_k", [128, NKV * TF], F32, kind="ExternalOutput")
        dbg_v = nc.dram_tensor("dbg_v", [128, 16 * 1024], F32, kind="ExternalOutput")
        dbg_o = nc.dram_tensor("dbg_o", [128, NH * T], F32, kind="ExternalOutput")
        dbg_p = nc.dram_tensor("dbg_p", [128, TF], F32, kind="ExternalOutput")

    xov = x_own.ap().rearrange("(c p) t -> p c t", p=128)
    xfv = x_full.ap().rearrange("(c p) t -> p c t", p=128)
    ov = x_out.ap().rearrange("(c p) t -> p c t", p=128)

    slopes = _alibi_slopes(NH)
    SC = 1.0 / math.sqrt(HD)

    xf = p.sbuf("xf", [128, 2, T], F32)
    ones_bf = p.sbuf("ones_bf", [128, 1], BF16)
    ones_f = p.sbuf("ones_f", [1, 128], F32)
    eps1 = p.sbuf("eps1", [1, 1], F32)
    epsr = p.sbuf("epsr", [1, 1], F32)
    wcol = p.sbuf("wcol", [128, 1], F32)
    knw_c = p.sbuf("knw_c", [128, 1], F32)
    ofcol = p.sbuf("ofcol", [128, 1], F32)
    wsl = p.sbuf("wsl", [128, 2, KC * 128], BF16)
    QO = p.sbuf("QO", [128, NH * T], BF16)

    s_xs = [p.sem("s_x0"), p.sem("s_x1")]
    s_ws = [p.sem("s_w0"), p.sem("s_w1")]
    s_vs = [p.sem("s_v0"), p.sem("s_v1")]
    s_sm = p.sem("s_sm")
    s_os = [p.sem("s_o0"), p.sem("s_o1")]
    s_pe = p.sem("s_pe")
    s_act = p.sem("s_act")
    s_dve = p.sem("s_dve")
    s_po = p.sem("s_po")

    p.open_block()

    def _(e):
        s_sm.inc(e.dma_start(wcol[:], qnw.ap()), 16)
        s_sm.inc(e.dma_start(knw_c[:], knw.ap()), 16)
        s_sm.inc(e.dma_start(ofcol[:], ownoff.ap()), 16)
    p.pool(_)

    def _(e):
        e.memset(ones_bf[:], 1.0)
        e.memset(eps1[:], EPS_LN)
        e.memset(epsr[:], EPS_RMS)
        s_dve.inc(e.memset(ones_f[:], 1.0))
        s_sm.wait(e)
        s_dve.inc(e.tensor_tensor(wcol[:], wcol[:], knw_c[:], ALU.mult))
        s_dve.wait(e)
        s_dve.inc(e.tensor_scalar(wcol[:], wcol[:], SC, None, ALU.mult))
    p.dve(_)

    sa = []
    xn_own = p.sbuf("xn_own", [128, KC * T], BF16, sa)
    sq = p.sbuf("sq", [128, 2, T], BF16, sa)
    t32 = p.sbuf("t32", [128, T], F32, sa)
    rows = p.sbuf("rows", [1, 4 * T], F32, sa)
    sqq = p.sbuf("sqq", [128, T], BF16, sa)
    bcq = p.sbuf("bcq", [128, T], F32, sa)

    pscope_a = []
    for _nm in ("psA", "psB", "psC", "psD"):
        g = nc.psum_tensor(_nm, [128, T], F32)
        pscope_a.append((g, g.__enter__()))
    psA, psB, psC, psD = [t for (_g, t) in pscope_a]

    emit_layernorm(p, xov, xn_own, xf, sq, t32, rows, ones_bf, ones_f,
                   eps1, psA, psB, psC, psD, s_xs, s_pe, s_act, s_dve)
    ln_own_tok = s_dve.val

    w_tok = [0] * 64
    pe_q_tok = [0] * NH
    dve_q_tok = [0] * NH

    def emit_wslab(idx, src_ap, guard_tok=None):
        sl = idx % 2

        def _(e, idx=idx, sl=sl, guard_tok=guard_tok):
            if guard_tok is not None:
                s_pe.wait(e, guard_tok)
            w_tok[idx] = s_ws[sl].inc(e.dma_start(
                wsl[:, sl, :], src_ap.rearrange("p a b -> p (a b)"),
                max_dma_last_dim=8192), 16)
        p.pool(_)

    emit_wslab(0, wqT.ap()[0])
    emit_wslab(1, wqT.ap()[1])

    for h in range(NH):
        sl = h % 2
        psQ = psA if sl == 0 else psB

        def _(e, h=h, sl=sl, psQ=psQ):
            s_ws[sl].wait(e, w_tok[h])
            if h == 0:
                s_dve.wait(e, ln_own_tok)
            if h >= 2:
                s_dve.wait(e, dve_q_tok[h - 2])
            for tc in range(2):
                for kc in range(KC):
                    mm = e.matmul(psQ[:, tc * 512:(tc + 1) * 512],
                                  wsl[:, sl, kc * 128:(kc + 1) * 128],
                                  xn_own[:, kc * T + tc * 512:kc * T + (tc + 1) * 512],
                                  start=(kc == 0), stop=(kc == KC - 1))
            s_pe.inc(mm)
        p.te(_)
        pe_q1 = s_pe.val

        def _(e, pe_q1=pe_q1, psQ=psQ):
            s_pe.wait(e, pe_q1)
            s_act.inc(e.square(sqq[:], psQ[:]))
        p.act(_)
        act_sq = s_act.val

        def _(e, act_sq=act_sq):
            s_act.wait(e, act_sq)
            for tc in range(2):
                mm = e.matmul(psD[0:1, tc * 512:(tc + 1) * 512], ones_bf[:],
                              sqq[:, tc * 512:(tc + 1) * 512], start=True, stop=True)
            s_pe.inc(mm)
        p.te(_)
        pe_q2 = s_pe.val

        def _(e, pe_q2=pe_q2):
            s_pe.wait(e, pe_q2)
            s_dve.inc(e.tensor_scalar(rows[0:1, 0:T], psD[0:1, 0:T], 1.0 / HD,
                                      None, ALU.mult))
        p.dve(_)
        dve_r = s_dve.val

        def _(e, dve_r=dve_r):
            s_dve.wait(e, dve_r)
            s_act.inc(e.activation(rows[0:1, 0:T], rows[0:1, 0:T], AF.Sqrt,
                                   bias=epsr[0:1, :], scale=1.0))
        p.act(_)
        act_r = s_act.val

        def _(e, act_r=act_r):
            s_act.wait(e, act_r)
            s_dve.inc(e.reciprocal(rows[0:1, 0:T], rows[0:1, 0:T]))
        p.dve(_)
        dve_r2 = s_dve.val

        def _(e, h=h, dve_r2=dve_r2):
            s_dve.wait(e, dve_r2)
            for tc in range(2):
                mm = e.matmul(psC[:, tc * 512:(tc + 1) * 512], ones_f[:],
                              rows[0:1, tc * 512:(tc + 1) * 512], start=True, stop=True)
            pe_q_tok[h] = s_pe.inc(mm)
        p.te(_)
        pe_q3 = s_pe.val

        def _(e, pe_q3=pe_q3):
            s_pe.wait(e, pe_q3)
            s_act.inc(e.copy(bcq[:], psC[:]))
        p.act(_)
        act_bc = s_act.val

        def _(e, h=h, psQ=psQ, act_bc=act_bc):
            s_act.wait(e, act_bc)
            dve_q_tok[h] = s_dve.inc(e.scalar_tensor_tensor(
                QO[:, h * T:(h + 1) * T], psQ[:], wcol[:], bcq[:],
                ALU.mult, ALU.mult))
        p.dve(_)

        if h + 2 < NH:
            emit_wslab(h + 2, wqT.ap()[h + 2], pe_q_tok[h])

    q_done_dve = s_dve.val

    if DEBUG_ATTN:
        def _(e):
            s_dve.wait(e, q_done_dve)
            s_sm.inc(e.dma_start(dbg_q.ap(), QO[:], max_dma_last_dim=4096), 16)
        p.pool(_)
    p.close_scope(sa)

    sbp = []
    Ksb = p.sbuf("Ksb", [128, NKV * TF], BF16, sbp)
    Vsb = p.sbuf("Vsb", [128, 16 * 1024], BF16, sbp)

    sbt = []
    rowsF = p.sbuf("rowsF", [1, 2 * TF], F32, sbt)
    t32b = p.sbuf("t32b", [128, 512], F32, sbt)
    bck = p.sbuf("bck", [128, 512], F32, sbt)
    rows2 = bck[0:1, 0:512]

    sb1 = []
    xcF = p.sbuf("xcF", [128, 2, T], BF16, sb1)
    sqF = p.sbuf("sqF", [128, 2, T], BF16, sb1)

    NSC = 2 * KC
    actF_tok = [0] * NSC
    peF_tok = [0] * NSC
    xF_tok = [0] * NSC

    for i in range(NSC):
        kc, half = i // 2, i % 2
        sl = i % 2

        def _(e, i=i, kc=kc, half=half, sl=sl, dvg=s_dve.val, acg=s_act.val):
            if i >= 2:
                s_act.wait(e, actF_tok[i - 2])
            else:
                s_dve.wait(e, dvg)
                s_act.wait(e, acg)
            xF_tok[i] = s_xs[sl].inc(e.dma_start(
                xf[:, sl, :], xfv[:, kc, half * T:(half + 1) * T]), 16)
        p.sync(_)

        def _(e, i=i, sl=sl):
            s_xs[sl].wait(e, xF_tok[i])
            if i >= 2:
                s_pe.wait(e, peF_tok[i - 2])
            e.copy(xcF[:, sl, :], xf[:, sl, :])
            actF_tok[i] = s_act.inc(e.square(sqF[:, sl, :], xf[:, sl, :]))
        p.act(_)

        def _(e, i=i, half=half, sl=sl):
            s_act.wait(e, actF_tok[i])
            if i == 0:
                s_dve.wait(e, q_done_dve)
            psX = psA if half == 0 else psB
            psQ2 = psC if half == 0 else psD
            for tc in range(2):
                e.matmul(psX[0:1, tc * 512:(tc + 1) * 512], ones_bf[:],
                         xcF[:, sl, tc * 512:(tc + 1) * 512],
                         start=(i < 2), stop=(i >= NSC - 2))
                mm = e.matmul(psQ2[0:1, tc * 512:(tc + 1) * 512], ones_bf[:],
                              sqF[:, sl, tc * 512:(tc + 1) * 512],
                              start=(i < 2), stop=(i >= NSC - 2))
            peF_tok[i] = s_pe.inc(mm)
        p.te(_)
    pe_Fstats = s_pe.val

    FMU, FSD = 0, TF

    def _(e):
        s_pe.wait(e, pe_Fstats)
        e.tensor_scalar(rowsF[:, FMU:FMU + T], psA[0:1, 0:T], 1.0 / DM, None, ALU.mult)
        e.tensor_scalar(rowsF[:, FMU + T:FMU + TF], psB[0:1, 0:T], 1.0 / DM, None, ALU.mult)
        e.tensor_scalar(rowsF[:, FSD:FSD + T], psC[0:1, 0:T], 1.0 / DM, None, ALU.mult)
        e.tensor_scalar(rowsF[:, FSD + T:FSD + TF], psD[0:1, 0:T], 1.0 / DM, None, ALU.mult)
        s_dve.inc(e.engine_nop())
        for seg in range(4):
            a, b = seg * 512, (seg + 1) * 512
            s_dve.wait(e)
            s_dve.inc(e.tensor_tensor(rows2, rowsF[:, FMU + a:FMU + b],
                                      rowsF[:, FMU + a:FMU + b], ALU.mult))
            s_dve.wait(e)
            s_dve.inc(e.tensor_tensor(rowsF[:, FSD + a:FSD + b],
                                      rowsF[:, FSD + a:FSD + b],
                                      rows2, ALU.subtract))
    p.dve(_)

    def _(e):
        s_dve.wait(e)
        s_act.inc(e.activation(rowsF[:, FSD:FSD + TF], rowsF[:, FSD:FSD + TF],
                               AF.Sqrt, bias=eps1[0:1, :], scale=1.0))
    p.act(_)

    def _(e):
        s_act.wait(e)
        s_dve.inc(e.reciprocal(rowsF[:, FSD:FSD + TF], rowsF[:, FSD:FSD + TF]))
    p.dve(_)
    fstats_tok = s_dve.val

    p.close_scope(sb1)
    xnb = p.sbuf("xnb", [128, KC * 512], BF16, sbt)
    sqk = p.sbuf("sqk", [128, 512], BF16, sbt)
    vsl = p.sbuf("vsl", [128, 2, 512], BF16, sbt)

    wk_tok = [0] * 32
    pe_k1_tok = [0] * 32
    pe_k2_tok = [0] * 32
    pe_k3_tok = [0] * 32
    act_k_tok = [0] * 32
    act_k2_tok = [0] * 32
    act_k3_tok = [0] * 32
    dve_k1_tok = [0] * 32
    dve_k2_tok = [0] * 32
    dve_k_tok = [0] * 32
    v_tok = [0] * 2
    pe_v_tok = [0] * (2 * KC)
    kv_dve_done = fstats_tok
    psV = [psA, psB, psC, psD]

    for tcol in range(4):
        def _(e, tcol=tcol, kv_dve_done=kv_dve_done):
            if tcol == 0:
                s_dve.wait(e, fstats_tok)
            else:
                s_dve.wait(e, kv_dve_done)
            mm = e.matmul(psC[:, 0:512], ones_f[:],
                          rowsF[0:1, FMU + tcol * 512:FMU + (tcol + 1) * 512],
                          start=True, stop=True)
            mm = e.matmul(psD[:, 0:512], ones_f[:],
                          rowsF[0:1, FSD + tcol * 512:FSD + (tcol + 1) * 512],
                          start=True, stop=True)
            s_pe.inc(mm)
        p.te(_)
        pe_bc = s_pe.val

        xnb_tok = [0] * KC
        for kc in range(KC):
            sl = kc % 2

            def _(e, tcol=tcol, kc=kc, sl=sl, kv_dve_done=kv_dve_done, acg=s_act.val):
                if kc >= 2:
                    s_dve.wait(e, xnb_tok[kc - 2])
                elif tcol > 0:
                    s_dve.wait(e, kv_dve_done)
                else:
                    s_act.wait(e, acg)
                xF_tok[kc % 2] = s_xs[sl].inc(e.dma_start(
                    xf[:, sl, 0:512], xfv[:, kc, tcol * 512:(tcol + 1) * 512]), 16)
            p.sync(_)
            xk_tok = s_xs[sl].val

            def _(e, kc=kc, sl=sl, pe_bc=pe_bc, xk_tok=xk_tok):
                s_xs[sl].wait(e, xk_tok)
                if kc == 0:
                    s_pe.wait(e, pe_bc)
                e.tensor_tensor(t32b[:], xf[:, sl, 0:512], psC[:, 0:512], ALU.subtract)
                xnb_tok[kc] = s_dve.inc(e.tensor_tensor(
                    xnb[:, kc * 512:(kc + 1) * 512], t32b[:], psD[:, 0:512], ALU.mult))
            p.dve(_)
        xnb_done = s_dve.val

        for kvh in range(NKV):
            kk = tcol * NKV + kvh
            sl = kk % 2
            psK = psA if sl == 0 else psB

            def _(e, kk=kk, kvh=kvh, sl=sl):
                if kk >= 2:
                    s_pe.wait(e, pe_k1_tok[kk - 2])
                wk_tok[kk] = s_ws[sl].inc(e.dma_start(
                    wsl[:, sl, :], wkT.ap()[kvh].rearrange("p a b -> p (a b)"),
                    max_dma_last_dim=8192), 16)
            p.pool(_)

            def _(e, kk=kk, sl=sl, psK=psK, xnb_done=xnb_done):
                s_ws[sl].wait(e, wk_tok[kk])
                s_dve.wait(e, xnb_done)
                if kk >= 2:
                    s_dve.wait(e, dve_k_tok[kk - 2])
                for kc in range(KC):
                    mm = e.matmul(psK[:, 0:512],
                                  wsl[:, sl, kc * 128:(kc + 1) * 128],
                                  xnb[:, kc * 512:(kc + 1) * 512],
                                  start=(kc == 0), stop=(kc == KC - 1))
                pe_k1_tok[kk] = s_pe.inc(mm)
            p.te(_)

            def _(e, kk=kk, psK=psK):
                s_pe.wait(e, pe_k1_tok[kk])
                if kk >= 1:
                    s_pe.wait(e, pe_k2_tok[kk - 1])
                act_k_tok[kk] = s_act.inc(e.square(sqk[:], psK[:, 0:512]))
            p.act(_)

            def _(e, kk=kk):
                s_act.wait(e, act_k_tok[kk])
                if kk >= 1:
                    s_dve.wait(e, dve_k1_tok[kk - 1])
                mm = e.matmul(psC[0:1, 0:512], ones_bf[:], sqk[:], start=True, stop=True)
                pe_k2_tok[kk] = s_pe.inc(mm)
            p.te(_)

            def _(e, kk=kk):
                s_pe.wait(e, pe_k2_tok[kk])
                dve_k1_tok[kk] = s_dve.inc(e.tensor_scalar(
                    rows2, psC[0:1, 0:512], 1.0 / HD, None, ALU.mult))
            p.dve(_)

            def _(e, kk=kk):
                s_dve.wait(e, dve_k1_tok[kk])
                act_k2_tok[kk] = s_act.inc(e.activation(
                    rows2, rows2, AF.Sqrt, bias=epsr[0:1, :], scale=1.0))
            p.act(_)

            def _(e, kk=kk):
                s_act.wait(e, act_k2_tok[kk])
                dve_k2_tok[kk] = s_dve.inc(e.reciprocal(rows2, rows2))
            p.dve(_)

            def _(e, kk=kk):
                s_dve.wait(e, dve_k2_tok[kk])
                if kk >= 1:
                    s_act.wait(e, act_k3_tok[kk - 1])
                mm = e.matmul(psD[:, 0:512], ones_f[:], rows2,
                              start=True, stop=True)
                pe_k3_tok[kk] = s_pe.inc(mm)
            p.te(_)

            def _(e, kk=kk):
                s_pe.wait(e, pe_k3_tok[kk])
                act_k3_tok[kk] = s_act.inc(e.copy(bck[:], psD[:, 0:512]))
            p.act(_)

            def _(e, kk=kk, kvh=kvh, tcol=tcol, psK=psK):
                s_act.wait(e, act_k3_tok[kk])
                dve_k_tok[kk] = s_dve.inc(e.tensor_tensor(
                    Ksb[:, kvh * TF + tcol * 512:kvh * TF + (tcol + 1) * 512],
                    psK[:, 0:512], bck[:], ALU.mult))
            p.dve(_)

        k_done_tok = dve_k_tok[tcol * NKV + NKV - 1]
        k_done_act = act_k3_tok[tcol * NKV + NKV - 1]

        for vidx in range(2 * KC):
            kc, nn = vidx // 2, vidx % 2
            sl = vidx % 2

            def _(e, vidx=vidx, kc=kc, nn=nn, sl=sl):
                if vidx >= 2:
                    s_pe.wait(e, pe_v_tok[vidx - 2])
                v_tok[vidx % 2] = s_vs[sl].inc(e.dma_start(
                    vsl[:, sl, :], wvt.ap()[kc][:, nn * 512:(nn + 1) * 512],
                    max_dma_last_dim=8192), 16)
            p.pool(_)
            vdma = s_vs[sl].val

            def _(e, vidx=vidx, kc=kc, nn=nn, sl=sl, vdma=vdma,
                  k_done_tok=k_done_tok, k_done_act=k_done_act):
                s_vs[sl].wait(e, vdma)
                if vidx == 0:
                    s_dve.wait(e, k_done_tok)
                    s_act.wait(e, k_done_act)
                for mt in range(4):
                    mm = e.matmul(
                        psV[mt][:, nn * 512:(nn + 1) * 512],
                        xnb[:, kc * 512 + mt * 128:kc * 512 + (mt + 1) * 128],
                        vsl[:, sl, :],
                        start=(kc == 0), stop=(kc == KC - 1))
                pe_v_tok[vidx] = s_pe.inc(mm)
            p.te(_)
        pe_v_done = s_pe.val

        def _(e, tcol=tcol, pe_v_done=pe_v_done):
            s_pe.wait(e, pe_v_done)
            for mt in range(4):
                s_dve.inc(e.tensor_copy(
                    Vsb[:, (tcol * 4 + mt) * 1024:(tcol * 4 + mt + 1) * 1024],
                    psV[mt][:]))
        p.dve(_)
        kv_dve_done = s_dve.val

    if DEBUG_ATTN:
        def _(e, kv_dve_done=kv_dve_done):
            s_dve.wait(e, kv_dve_done)
            s_sm.inc(e.dma_start(dbg_k.ap(), Ksb[:], max_dma_last_dim=4096), 16)
            s_sm.inc(e.dma_start(dbg_v.ap(), Vsb[:], max_dma_last_dim=4096), 16)
        p.pool(_)

    p.close_scope(sbt)
    for (g, _t) in reversed(pscope_a):
        g.__exit__(None, None, None)

    sc3 = []
    iotaS = p.sbuf("iotaS", [128, TF], F32, sc3)
    idn = p.sbuf("idn", [128, 128], BF16, sc3)
    Rbuf = p.sbuf("Rbuf", [128, TF], F32, sc3)
    c0col = p.sbuf("c0col", [128, 1], F32, sc3)
    tb = p.sbuf("tb", [128, 2, TF], F32, sc3)
    pbf = p.sbuf("pbf", [128, 2, TF], BF16, sc3)
    pTsb = p.sbuf("pTsb", [128, 2, TF], BF16, sc3)
    mrow = p.sbuf("mrow", [128, 2, 4], F32, sc3)
    negm = p.sbuf("negm", [128, 2, 1], F32, sc3)
    lrow = p.sbuf("lrow", [128, 2, 1], F32, sc3)
    linv = p.sbuf("linv", [128, 2, 1], F32, sc3)
    outsb = p.sbuf("outsb", [128, 2, 512], F32, sc3)

    pscope_b = []
    for _nm, _sh, _dt in (("psS", [128, 2, 512], F32), ("psAV", [128, 512], F32),
                          ("psTT", [128, 512], BF16), ("psO", [128, 2, 1024], F32)):
        g = nc.psum_tensor(_nm, _sh, _dt)
        pscope_b.append((g, g.__enter__()))
    psS = pscope_b[0][1]; psAV = pscope_b[1][1]
    psTT = pscope_b[2][1]; psO = pscope_b[3][1]

    def _(e):
        s_po.inc(e.iota(iotaS[:], pattern=[[1, TF]], base=0, channel_multiplier=-1,
                        allow_small_or_imprecise_dtypes=True))
    p.pool(_)
    iota_tok = s_po.val

    def _(e):
        s_po.wait(e, iota_tok)
        s_dve.inc(e.tensor_scalar(idn[:], iotaS[:, 0:128], 0.0, None, ALU.is_equal))
    p.dve(_)
    qresc_tok = s_dve.val

    NIT = 8 * NH
    sc_gc_tok = {}
    ttr_gc_tok = {-2: 0, -1: 0}
    tr_tok = {}
    trevac_tok = {}
    exp_tok = [0] * NIT
    recip_tok = [0] * NIT
    norm_tok = [0] * NIT
    av_tok = [0] * NIT
    oevac_tok = [0] * NIT
    rbuf_tok = [0] * 8
    GC = [0]

    def it_params(it):
        qt, head = it // NH, it % NH
        n128 = 9 + qt
        sspan = n128 * 128
        nch = (sspan + 511) // 512
        return qt, head, n128, sspan, nch

    for it in range(NIT + 1):
        if it < NIT:
            qt, head, n128, sspan, nch = it_params(it)
            sl = it % 2
            kvh = head // 4

            if head == 0:
                def _(e, qt=qt, it=it):
                    if qt == 0:
                        s_dve.wait(e, qresc_tok)
                    if it >= 2:
                        s_act.wait(e, exp_tok[it - 2])
                    s_dve.inc(e.tensor_scalar(c0col[:], ofcol[:], float(128 * qt),
                                              None, ALU.add))
                    s_dve.wait(e)
                    e.tensor_scalar(Rbuf[:], iotaS[:], c0col[:], None, ALU.subtract)
                    e.tensor_scalar(tb[:, 0, :], Rbuf[:], 0.0, -1.3e33,
                                    ALU.is_gt, ALU.mult)
                    s_dve.inc(e.tensor_tensor(Rbuf[:], Rbuf[:], tb[:, 0, :], ALU.add))
                    rbuf_tok[qt] = s_dve.val
                p.dve(_)

            for ch in range(nch):
                gc = GC[0]
                n = min(512, sspan - ch * 512)

                def _(e, it=it, qt=qt, head=head, kvh=kvh, ch=ch, gc=gc, n=n):
                    if it == 0 and ch == 0:
                        s_dve.wait(e, kv_dve_done)
                    s_dve.wait(e, ttr_gc_tok[gc - 2])
                    mm = e.matmul(psS[:, gc % 2, 0:n],
                                  QO[:, head * T + qt * 128:head * T + (qt + 1) * 128],
                                  Ksb[:, kvh * TF + ch * 512:kvh * TF + ch * 512 + n],
                                  start=True, stop=True)
                    sc_gc_tok[gc] = s_pe.inc(mm)
                p.te(_)

                def _(e, it=it, qt=qt, head=head, ch=ch, gc=gc, n=n, sl=sl):
                    s_pe.wait(e, sc_gc_tok[gc])
                    if ch == 0:
                        s_dve.wait(e, rbuf_tok[qt])
                        if it >= 2:
                            s_act.wait(e, exp_tok[it - 2])
                    e.scalar_tensor_tensor(
                        tb[:, sl, ch * 512:ch * 512 + n],
                        Rbuf[:, ch * 512:ch * 512 + n], slopes[head],
                        psS[:, gc % 2, 0:n], ALU.mult, ALU.add)
                    ttr_gc_tok[gc] = s_dve.inc(e.tensor_reduce(
                        mrow[:, sl, ch:ch + 1], tb[:, sl, ch * 512:ch * 512 + n],
                        AX.X, ALU.max))
                p.dve(_)
                GC[0] += 1

            def _(e, it=it, sl=sl, nch=nch):
                s_dve.wait(e)
                s_dve.inc(e.tensor_reduce(negm[:, sl, :], mrow[:, sl, 0:nch],
                                          AX.X, ALU.max))
                s_dve.wait(e)
                s_dve.inc(e.tensor_scalar(negm[:, sl, :], negm[:, sl, :], -1.0,
                                          None, ALU.mult))
            p.dve(_)
            negm_tok = s_dve.val

            def _(e, it=it, sl=sl, sspan=sspan, negm_tok=negm_tok):
                s_dve.wait(e, negm_tok)
                if it >= 2:
                    s_pe.wait(e, av_tok[it - 2])
                exp_tok[it] = s_act.inc(e.activation(
                    pbf[:, sl, 0:sspan], tb[:, sl, 0:sspan], AF.Exp,
                    bias=negm[:, sl, :], scale=1.0, accum_out=lrow[:, sl, :]))
            p.act(_)

            def _(e, it=it, sl=sl):
                s_act.wait(e, exp_tok[it])
                recip_tok[it] = s_dve.inc(e.reciprocal(linv[:, sl, :], lrow[:, sl, :]))
            p.dve(_)

            def _(e, it=it, sl=sl, sspan=sspan):
                s_dve.wait(e, recip_tok[it])
                norm_tok[it] = s_po.inc(e.tensor_scalar(
                    pbf[:, sl, 0:sspan], pbf[:, sl, 0:sspan], linv[:, sl, :],
                    None, ALU.mult))
                if DEBUG_ATTN and it == 33:
                    s_sm.inc(e.dma_start(dbg_p.ap()[:, 0:sspan], pbf[:, sl, 0:sspan],
                                         max_dma_last_dim=4096), 16)
            p.pool(_)

        if it >= 1:
            jt = it - 1
            qt1, head1, n128_1, sspan1, nch1 = it_params(jt)
            sl1 = jt % 2
            kvh1 = head1 // 4

            for j in range(n128_1):
                def _(e, jt=jt, j=j, sl1=sl1, kvh1=kvh1, n128_1=n128_1):
                    if j == 0:
                        s_po.wait(e, norm_tok[jt])
                        if jt >= 1:
                            s_act.wait(e, oevac_tok[jt - 1])
                    if (jt, j - 4) in trevac_tok:
                        s_act.wait(e, trevac_tok[(jt, j - 4)])
                    tr = e.transpose(psTT[:, (j % 4) * 128:(j % 4 + 1) * 128],
                                     pbf[:, sl1, j * 128:(j + 1) * 128], idn[:])
                    tr_tok[(jt, j)] = s_pe.inc(tr)
                    if j >= 1:
                        s_act.wait(e, trevac_tok[(jt, j - 1)])
                        mm = e.matmul(
                            psAV[:, 0:128],
                            Vsb[:, (j - 1) * 1024 + kvh1 * 128:(j - 1) * 1024 + (kvh1 + 1) * 128],
                            pTsb[:, sl1, (j - 1) * 128:j * 128],
                            start=(j == 1), stop=False)
                        s_pe.inc(mm)
                p.te(_)

                def _(e, jt=jt, j=j, sl1=sl1):
                    s_pe.wait(e, tr_tok[(jt, j)])
                    trevac_tok[(jt, j)] = s_act.inc(e.copy(
                        pTsb[:, sl1, j * 128:(j + 1) * 128],
                        psTT[:, (j % 4) * 128:(j % 4 + 1) * 128]))
                p.act(_)

            def _(e, jt=jt, sl1=sl1, kvh1=kvh1, n128_1=n128_1):
                j = n128_1 - 1
                s_act.wait(e, trevac_tok[(jt, j)])
                mm = e.matmul(
                    psAV[:, 0:128],
                    Vsb[:, j * 1024 + kvh1 * 128:j * 1024 + (kvh1 + 1) * 128],
                    pTsb[:, sl1, j * 128:(j + 1) * 128],
                    start=False, stop=True)
                av_tok[jt] = s_pe.inc(mm)
            p.te(_)

            def _(e, jt=jt, qt1=qt1, head1=head1):
                s_pe.wait(e, av_tok[jt])
                oevac_tok[jt] = s_act.inc(e.copy(
                    QO[:, head1 * T + qt1 * 128:head1 * T + (qt1 + 1) * 128],
                    psAV[:, 0:128]))
            p.act(_)

    scores_done_act = s_act.val
    if DEBUG_ATTN:
        def _(e):
            s_act.wait(e, scores_done_act)
            s_sm.inc(e.dma_start(dbg_o.ap(), QO[:], max_dma_last_dim=4096), 16)
        p.pool(_)

    wo_tok = [0] * KC
    pe_wo_tok = [0] * KC
    dve_wo_tok = [0] * KC
    out_tok = [0] * (KC * 2)
    xwo_tok = [0] * KC

    for mb in range(KC):
        sl = mb % 2
        psW = psO[:, sl, :]

        def _(e, mb=mb, sl=sl):
            if mb >= 2:
                s_pe.wait(e, pe_wo_tok[mb - 2])
            wo_tok[mb] = s_ws[sl].inc(e.dma_start(
                wsl[:, sl, :], woT.ap()[mb].rearrange("p a b -> p (a b)"),
                max_dma_last_dim=8192), 16)
        p.pool(_)

        def _(e, mb=mb, sl=sl, psW=psW):
            s_ws[sl].wait(e, wo_tok[mb])
            if mb == 0:
                s_act.wait(e, scores_done_act)
            if mb >= 2:
                s_dve.wait(e, dve_wo_tok[mb - 2])
            for tc in range(2):
                for kc in range(KC):
                    mm = e.matmul(psW[:, tc * 512:(tc + 1) * 512],
                                  wsl[:, sl, kc * 128:(kc + 1) * 128],
                                  QO[:, kc * T + tc * 512:kc * T + (tc + 1) * 512],
                                  start=(kc == 0), stop=(kc == KC - 1))
            pe_wo_tok[mb] = s_pe.inc(mm)
        p.te(_)

        def _(e, mb=mb, sl=sl, dvg=s_dve.val):
            if mb >= 2:
                s_dve.wait(e, dve_wo_tok[mb - 2])
            else:
                s_dve.wait(e, dvg)
            xwo_tok[mb] = s_xs[sl].inc(e.dma_start(xf[:, sl, :], xov[:, mb, :]), 16)
        p.sync(_)

        def _(e, mb=mb, sl=sl, psW=psW):
            s_xs[sl].wait(e, xwo_tok[mb])
            s_pe.wait(e, pe_wo_tok[mb])
            for tc in range(2):
                oi = mb * 2 + tc
                if oi >= 2:
                    s_os[tc].wait(e, out_tok[oi - 2])
                s_dve.inc(e.tensor_tensor(
                    outsb[:, tc, :], psW[:, tc * 512:(tc + 1) * 512],
                    xf[:, sl, tc * 512:(tc + 1) * 512], ALU.add))
            dve_wo_tok[mb] = s_dve.val
        p.dve(_)

        def _(e, mb=mb):
            s_dve.wait(e, dve_wo_tok[mb])
            for tc in range(2):
                oi = mb * 2 + tc
                out_tok[oi] = s_os[tc].inc(e.dma_start(
                    ov[:, mb, tc * 512:(tc + 1) * 512], outsb[:, tc, :]), 16)
        p.sync(_)

    def _(e):
        for tc in range(2):
            s_os[tc].wait(e)
    p.sync(_)

    p.close_scope(sc3)
    for (g, _t) in reversed(pscope_b):
        g.__exit__(None, None, None)
    p.close_scope(sbp)
    p.close()
    return p


def run_l0attn(x, w):
    p = _get_prog("l0attn", build_l0attn_program)
    shards = _x_to_shards(x)
    in_maps = []
    for c in range(NCORES):
        b, h = c // 2, c % 2
        xfull = np.ascontiguousarray(x[b].T)
        off = np.full((128, 1), float(h * T), np.float32)
        in_maps.append({"x_own": shards[c], "x_full": xfull, "ownoff": off, **w})
    res = run_bass_kernel_spmd(p.nc, in_maps, core_ids=list(range(NCORES)))
    return _shards_to_x([res.results[c]["x_out"] for c in range(NCORES)])




def prep_l1ffn(inp):
    d = {}
    d["wbT"] = _lhsT_slabs(np.asarray(inp["L1_ffn_W_base"], np.float32))
    d["wgt"] = np.ascontiguousarray(
        np.asarray(inp["L1_ffn_W_gate"], np.float32).T.reshape(KC, 128, 8).transpose(1, 0, 2))
    d["aT"] = np.ascontiguousarray(
        _lhsT_slabs(np.asarray(inp["L1_ffn_A"], np.float32).reshape(128, DM))[0])
    d["bT"] = np.ascontiguousarray(
        np.asarray(inp["L1_ffn_B"], np.float32).transpose(0, 2, 1).reshape(128, DM)
        .reshape(128, KC, 128))
    d["bbs"] = np.ascontiguousarray(np.asarray(inp["L1_ffn_b_base"], np.float32).reshape(KC, 128).T)
    return d


def build_l1ffn_program():
    p = Prog()
    nc = p.nc
    x_in = nc.dram_tensor("x_in", [DM, T], F32, kind="ExternalInput")
    wbT = nc.dram_tensor("wbT", [KC, 128, KC, 128], F32, kind="ExternalInput")
    wgt = nc.dram_tensor("wgt", [128, KC, 8], F32, kind="ExternalInput")
    aT = nc.dram_tensor("aT", [128, KC, 128], F32, kind="ExternalInput")
    bT = nc.dram_tensor("bT", [128, KC, 128], F32, kind="ExternalInput")
    bbs = nc.dram_tensor("bbs", [128, KC], F32, kind="ExternalInput")
    x_out = nc.dram_tensor("x_out", [DM, T], F32, kind="ExternalOutput")

    xv = x_in.ap().rearrange("(c p) t -> p c t", p=128)
    ov = x_out.ap().rearrange("(c p) t -> p c t", p=128)

    xf = p.sbuf("xf", [128, 2, T], F32)
    outsb = p.sbuf("outsb", [128, 2, 512], F32)
    ones_bf = p.sbuf("ones_bf", [128, 1], BF16)
    ones_f = p.sbuf("ones_f", [1, 128], F32)
    eps1 = p.sbuf("eps1", [1, 1], F32)
    wsl = p.sbuf("wsl", [128, 2, KC * 128], BF16)
    xn = p.sbuf("xn", [128, KC * T], BF16)
    sq = p.sbuf("sq", [128, 2, T], BF16)
    t32 = p.sbuf("t32", [128, T], F32)
    rows = p.sbuf("rows", [1, 4 * T], F32)
    aTs = p.sbuf("aTs", [128, KC * 128], BF16)
    bTs = p.sbuf("bTs", [128, KC * 128], BF16)
    wgs = p.sbuf("wgs", [128, KC * 8], BF16)
    bbc = p.sbuf("bbc", [128, KC], F32)
    Lsb = p.sbuf("Lsb", [128, 64], F32)
    Ls1 = p.sbuf("Ls1", [128, 64], F32)
    Ls2 = p.sbuf("Ls2", [128, 64], F32)
    mx1 = p.sbuf("mx1", [128, 8], F32)
    mx2 = p.sbuf("mx2", [128, 8], F32)
    den = p.sbuf("den", [128, 8], F32)
    reb = p.sbuf("reb", [128, 64], BF16)
    res8 = p.sbuf("res8", [8, T], BF16)
    P8 = p.sbuf("P8", [8, 128], BF16)
    P8b = p.sbuf("P8b", [8, 128], BF16)
    p8i = p.sbuf("p8i", [8, 128], F32)
    idn = p.sbuf("idn", [128, 128], BF16)
    io128 = p.sbuf("io128", [128, 128], F32)
    rb = p.sbuf("rb", [128, T], BF16)
    hb = p.sbuf("hb", [128, T], BF16)
    hw = p.sbuf("hw", [128, T], BF16)

    s_xs = [p.sem("s_x0"), p.sem("s_x1")]
    s_ws = [p.sem("s_w0"), p.sem("s_w1")]
    s_sm = p.sem("s_sm")
    s_os = [p.sem("s_o0"), p.sem("s_o1")]
    s_pe = p.sem("s_pe")
    s_act = p.sem("s_act")
    s_dve = p.sem("s_dve")
    s_po = p.sem("s_po")

    p.open_block()

    def _(e):
        s_sm.inc(e.dma_start(aTs[:], aT.ap().rearrange("p a b -> p (a b)"),
                             max_dma_last_dim=8192), 16)
        s_sm.inc(e.dma_start(bTs[:], bT.ap().rearrange("p a b -> p (a b)"),
                             max_dma_last_dim=8192), 16)
        s_sm.inc(e.dma_start(wgs[:], wgt.ap().rearrange("p a b -> p (a b)")), 16)
        s_sm.inc(e.dma_start(bbc[:], bbs.ap()), 16)
        e.iota(p8i[:], pattern=[[1, 128]], base=0, channel_multiplier=-16,
               allow_small_or_imprecise_dtypes=True)
        s_po.inc(e.iota(io128[:], pattern=[[1, 128]], base=0, channel_multiplier=-1,
                        allow_small_or_imprecise_dtypes=True))
    p.pool(_)

    def _(e):
        e.memset(ones_bf[:], 1.0)
        e.memset(eps1[:], EPS_LN)
        s_dve.inc(e.memset(ones_f[:], 1.0))
        s_po.wait(e)
        e.tensor_scalar(P8[:], p8i[:], 0.0, None, ALU.is_ge)
        e.tensor_scalar(P8b[:], p8i[:], 15.0, None, ALU.is_le)
        s_dve.inc(e.tensor_tensor(P8[:], P8[:], P8b[:], ALU.mult))
        s_dve.inc(e.tensor_scalar(idn[:], io128[:], 0.0, None, ALU.is_equal))
    p.dve(_)
    const_tok = s_dve.val

    pscope_a = []
    for _nm in ("psA", "psB", "psC", "psD"):
        g = nc.psum_tensor(_nm, [128, T], F32)
        pscope_a.append((g, g.__enter__()))
    psA, psB, psC, psD = [t for (_g, t) in pscope_a]

    emit_layernorm(p, xv, xn, xf, sq, t32, rows, ones_bf, ones_f,
                   eps1, psA, psB, psC, psD, s_xs, s_pe, s_act, s_dve)
    ln_tok = s_dve.val

    def _(e):
        s_dve.wait(e, ln_tok)
        s_sm.wait(e)
        for mt in range(8):
            for kc in range(KC):
                mm = e.matmul(psD[:, mt * 8:(mt + 1) * 8],
                              xn[:, kc * T + mt * 128:kc * T + (mt + 1) * 128],
                              wgs[:, kc * 8:(kc + 1) * 8],
                              start=(kc == 0), stop=(kc == KC - 1))
        s_pe.inc(mm)
    p.te(_)
    pe_lg = s_pe.val

    def _(e):
        s_pe.wait(e, pe_lg)
        s_act.inc(e.copy(Lsb[:], psD[:, 0:64]))
    p.act(_)
    act_lg = s_act.val

    gv = lambda t: t.rearrange("p (g i) -> p g i", i=8)
    mx1b = bass.AP(mx1, 0, [[8, 128], [1, 8], [0, 8]])
    mx2b = bass.AP(mx2, 0, [[8, 128], [1, 8], [0, 8]])
    denb = bass.AP(den, 0, [[8, 128], [1, 8], [0, 8]])

    def _(e):
        s_act.wait(e, act_lg)
        s_dve.inc(e.tensor_reduce(mx1[:], gv(Lsb[:]), AX.X, ALU.max))
        s_dve.wait(e)
        e.tensor_tensor(gv(Ls1[:]), gv(Lsb[:]), mx1b, ALU.is_ge)
        e.tensor_scalar(Ls1[:], Ls1[:], -1.0e30, None, ALU.mult)
        e.tensor_tensor(Ls1[:], Ls1[:], Lsb[:], ALU.add)
        s_dve.inc(e.tensor_reduce(mx2[:], gv(Ls1[:]), AX.X, ALU.max))
        s_dve.wait(e)
        s_dve.inc(e.tensor_tensor(gv(Ls2[:]), gv(Lsb[:]), mx1b, ALU.subtract))
    p.dve(_)
    dve_g1 = s_dve.val

    def _(e):
        s_dve.wait(e, dve_g1)
        s_act.inc(e.activation(Ls2[:], Ls2[:], AF.Exp))
    p.act(_)
    act_e = s_act.val

    def _(e):
        s_act.wait(e, act_e)
        e.tensor_tensor(gv(Ls1[:]), gv(Lsb[:]), mx2b, ALU.is_ge)
        e.tensor_tensor(Ls2[:], Ls2[:], Ls1[:], ALU.mult)
        s_dve.inc(e.tensor_reduce(den[:], gv(Ls2[:]), AX.X, ALU.add))
        s_dve.wait(e)
        s_dve.inc(e.reciprocal(den[:], den[:]))
        s_dve.wait(e)
        s_dve.inc(e.tensor_tensor(gv(reb[:]), gv(Ls2[:]), denb, ALU.mult))
    p.dve(_)
    dve_route = s_dve.val

    psTr = psD[0:8, 128:192].bitcast(BF16)
    tr_tok2 = [0] * 8
    ev_tok2 = [0] * 8

    for mt in range(8):
        def _(e, mt=mt):
            if mt == 0:
                s_dve.wait(e, dve_route)
                s_dve.wait(e, const_tok)
            if mt >= 1:
                s_act.wait(e, ev_tok2[mt - 1])
            tr = e.transpose(psTr, reb[:, mt * 8:(mt + 1) * 8], idn[:])
            tr_tok2[mt] = s_pe.inc(tr)
        p.te(_)

        def _(e, mt=mt):
            s_pe.wait(e, tr_tok2[mt])
            ev_tok2[mt] = s_act.inc(e.copy(res8[:, mt * 128:(mt + 1) * 128], psTr))
        p.act(_)
    route8_tok = ev_tok2[7]

    def _(e):
        s_act.wait(e, route8_tok)
        for tc in range(2):
            mm = e.matmul(psC[:, tc * 512:(tc + 1) * 512], P8[:],
                          res8[:, tc * 512:(tc + 1) * 512], start=True, stop=True)
        s_pe.inc(mm)
    p.te(_)
    pe_rfm = s_pe.val

    def _(e):
        s_pe.wait(e, pe_rfm)
        s_act.inc(e.copy(rb[:], psC[:]))
    p.act(_)
    act_rb = s_act.val

    def _(e):
        s_dve.wait(e, ln_tok)
        for tc in range(2):
            for kc in range(KC):
                mm = e.matmul(psB[:, tc * 512:(tc + 1) * 512],
                              aTs[:, kc * 128:(kc + 1) * 128],
                              xn[:, kc * T + tc * 512:kc * T + (tc + 1) * 512],
                              start=(kc == 0), stop=(kc == KC - 1))
        s_pe.inc(mm)
    p.te(_)
    pe_h = s_pe.val

    def _(e):
        s_pe.wait(e, pe_h)
        s_act.inc(e.copy(hb[:], psB[:]))
    p.act(_)
    act_hb = s_act.val

    def _(e):
        s_act.wait(e, act_hb)
        s_act.wait(e, act_rb)
        s_dve.inc(e.tensor_tensor(hw[:], hb[:], rb[:], ALU.mult))
    p.dve(_)
    hw_tok = s_dve.val

    wb_tok = [0] * KC
    pe_mb_tok = [0] * KC
    dve_mb_tok = [0] * KC
    out_tok = [0] * (KC * 2)
    xm_tok = [0] * KC

    for mb in range(KC):
        sl = mb % 2
        psW = psA if sl == 0 else psB

        def _(e, mb=mb, sl=sl):
            if mb >= 2:
                s_pe.wait(e, pe_mb_tok[mb - 2])
            wb_tok[mb] = s_ws[sl].inc(e.dma_start(
                wsl[:, sl, :], wbT.ap()[mb].rearrange("p a b -> p (a b)"),
                max_dma_last_dim=8192), 16)
        p.pool(_)

        def _(e, mb=mb, sl=sl, psW=psW):
            s_ws[sl].wait(e, wb_tok[mb])
            if mb == 0:
                s_dve.wait(e, hw_tok)
            if mb == 1:
                s_act.wait(e, act_hb)
            if mb >= 2:
                s_dve.wait(e, dve_mb_tok[mb - 2])
            for tc in range(2):
                for kc in range(KC):
                    e.matmul(psW[:, tc * 512:(tc + 1) * 512],
                             wsl[:, sl, kc * 128:(kc + 1) * 128],
                             xn[:, kc * T + tc * 512:kc * T + (tc + 1) * 512],
                             start=(kc == 0), stop=False)
                mm = e.matmul(psW[:, tc * 512:(tc + 1) * 512],
                              bTs[:, mb * 128:(mb + 1) * 128],
                              hw[:, tc * 512:(tc + 1) * 512],
                              start=False, stop=True)
            pe_mb_tok[mb] = s_pe.inc(mm)
        p.te(_)

        def _(e, mb=mb, sl=sl, dvg=s_dve.val):
            if mb >= 2:
                s_dve.wait(e, dve_mb_tok[mb - 2])
            else:
                s_dve.wait(e, dvg)
            xm_tok[mb] = s_xs[sl].inc(e.dma_start(xf[:, sl, :], xv[:, mb, :]), 16)
        p.sync(_)

        def _(e, mb=mb, sl=sl, psW=psW):
            s_xs[sl].wait(e, xm_tok[mb])
            s_pe.wait(e, pe_mb_tok[mb])
            for tc in range(2):
                oi = mb * 2 + tc
                if oi >= 2:
                    s_os[tc].wait(e, out_tok[oi - 2])
                e.tensor_scalar(outsb[:, tc, :], psW[:, tc * 512:(tc + 1) * 512],
                                bbc[:, mb:mb + 1], 0.5, ALU.add, ALU.mult)
                s_dve.inc(e.tensor_tensor(outsb[:, tc, :], outsb[:, tc, :],
                                          xf[:, sl, tc * 512:(tc + 1) * 512], ALU.add))
            dve_mb_tok[mb] = s_dve.val
        p.dve(_)

        def _(e, mb=mb):
            s_dve.wait(e, dve_mb_tok[mb])
            for tc in range(2):
                oi = mb * 2 + tc
                out_tok[oi] = s_os[tc].inc(e.dma_start(
                    ov[:, mb, tc * 512:(tc + 1) * 512], outsb[:, tc, :]), 16)
        p.sync(_)

    def _(e):
        for tc in range(2):
            s_os[tc].wait(e)
    p.sync(_)

    for (g2, _t) in reversed(pscope_a):
        g2.__exit__(None, None, None)
    p.close()
    return p


def run_l1ffn(x, w):
    p = _get_prog("l1ffn", build_l1ffn_program)
    shards = _x_to_shards(x)
    in_maps = [{"x_in": shards[c], **w} for c in range(NCORES)]
    res = run_bass_kernel_spmd(p.nc, in_maps, core_ids=list(range(NCORES)))
    return _shards_to_x([res.results[c]["x_out"] for c in range(NCORES)])



DEBUG_L1 = False
D1 = 512
ND0 = 4
DFF = 2048
NSUB = 4
EPS_L1 = 1e-5


def prep_l1attn(inp):
    d = {}
    d["winT"] = _lhsT_slabs(np.asarray(inp["L1_attn_Win"], np.float32))
    d["binc"] = np.ascontiguousarray(np.asarray(inp["L1_attn_bin"], np.float32).reshape(ND0, 128).T)
    for nm in ("ln1_w", "ln1_b", "phz_re", "phz_im", "lc_re", "lc_im",
               "sc_lin", "ln2_w", "ln2_b", "b2"):
        a = np.asarray(inp["L1_attn_" + nm], np.float32)
        d[nm] = np.ascontiguousarray(a.reshape(NSUB, ND0, 128).transpose(0, 2, 1))
    d["w1T"] = np.stack([_lhsT_slabs(np.asarray(inp["L1_attn_W1"], np.float32)[s])
                         for s in range(NSUB)])
    d["w2T"] = np.stack([_lhsT_slabs(np.asarray(inp["L1_attn_W2"], np.float32)[s])
                         for s in range(NSUB)])
    d["b1c"] = np.ascontiguousarray(np.asarray(inp["L1_attn_b1"], np.float32)
                                    .reshape(NSUB, 16, 128).transpose(0, 2, 1))
    d["lnlw"] = np.ascontiguousarray(np.asarray(inp["L1_attn_lnl_w"], np.float32).reshape(ND0, 128).T)
    d["lnlb"] = np.ascontiguousarray(np.asarray(inp["L1_attn_lnl_b"], np.float32).reshape(ND0, 128).T)
    d["woutT"] = _lhsT_slabs(np.asarray(inp["L1_attn_Wout"], np.float32))
    d["boutc"] = np.ascontiguousarray(np.asarray(inp["L1_attn_bout"], np.float32).reshape(KC, 128).T)
    return d


def build_l1attn_program():
    p = Prog()
    nc = p.nc
    x_full = nc.dram_tensor("x_full", [DM, TF], F32, kind="ExternalInput")
    winT = nc.dram_tensor("winT", [ND0, 128, KC, 128], F32, kind="ExternalInput")
    binc = nc.dram_tensor("binc", [128, ND0], F32, kind="ExternalInput")
    cols_in = {}
    for nm in ("ln1_w", "ln1_b", "phz_re", "phz_im", "lc_re", "lc_im",
               "sc_lin", "ln2_w", "ln2_b", "b2"):
        cols_in[nm] = nc.dram_tensor(nm, [NSUB, 128, ND0], F32, kind="ExternalInput")
    w1T = nc.dram_tensor("w1T", [NSUB, 16, 128, ND0, 128], F32, kind="ExternalInput")
    w2T = nc.dram_tensor("w2T", [NSUB, ND0, 128, 16, 128], F32, kind="ExternalInput")
    b1c = nc.dram_tensor("b1c", [NSUB, 128, 16], F32, kind="ExternalInput")
    lnlw = nc.dram_tensor("lnlw", [128, ND0], F32, kind="ExternalInput")
    lnlb = nc.dram_tensor("lnlb", [128, ND0], F32, kind="ExternalInput")
    woutT = nc.dram_tensor("woutT", [KC, 128, ND0, 128], F32, kind="ExternalInput")
    boutc = nc.dram_tensor("boutc", [128, KC], F32, kind="ExternalInput")
    x_out = nc.dram_tensor("x_out", [DM, TF], F32, kind="ExternalOutput")
    if DEBUG_L1:
        dbg_h = nc.dram_tensor("dbg_h", [128, ND0 * TF], F32, kind="ExternalOutput")
        dbg_y = nc.dram_tensor("dbg_y", [128, ND0 * TF], F32, kind="ExternalOutput")
        dbg_yc = nc.dram_tensor("dbg_yc", [128, ND0 * TF], F32, kind="ExternalOutput")
        dbg_h2 = nc.dram_tensor("dbg_h2", [128, ND0 * TF], F32, kind="ExternalOutput")
        dbg_hs = nc.dram_tensor("dbg_hs", [NSUB, 128, ND0 * TF], F32, kind="ExternalOutput")
        dbg_lnl = nc.dram_tensor("dbg_lnl", [128, ND0 * TF], F32, kind="ExternalOutput")

    xfv = x_full.ap().rearrange("(c p) t -> p c t", p=128)
    ov = x_out.ap().rearrange("(c p) t -> p c t", p=128)

    xf = p.sbuf("xf", [128, 2, T], F32)
    outsb = p.sbuf("outsb", [128, 2, 512], F32)
    ones_bf = p.sbuf("ones_bf", [128, 1], BF16)
    ones_f = p.sbuf("ones_f", [1, 128], F32)
    eps1 = p.sbuf("eps1", [1, 1], F32)
    rows5 = p.sbuf("rows5", [1, 2 * TF], F32)
    rowsF = rows5
    xnk = p.sbuf("xnk", [128, 2, T], BF16)
    h = p.sbuf("h", [128, ND0, TF], F32)
    y = p.sbuf("y", [128, ND0, TF], BF16)
    big1 = p.sbuf("big1", [128, ND0 * KC * 128], BF16)
    yc = big1[:].bitcast(F32).rearrange("p (d t) -> p d t", d=ND0)
    lnq = p.sbuf("lnq", [128, ND0, 512], BF16)
    fq = p.sbuf("fq", [128, 16, 512], BF16)
    w1s = p.sbuf("w1s", [128, 16 * ND0 * 128], BF16)
    w2s = p.sbuf("w2s", [128, ND0 * 16 * 128], BF16)
    wsl = p.sbuf("wsl", [128, 2, ND0 * 128], BF16)
    cr = p.sbuf("cr", [128, 128], F32)
    ci = p.sbuf("ci", [128, 128], F32)
    Xs = p.sbuf("Xs", [128, 128], F32)
    tm1 = p.sbuf("tm1", [128, 128], F32)
    tm2 = p.sbuf("tm2", [128, 128], F32)
    tm3 = p.sbuf("tm3", [128, 128], F32)
    tm4 = p.sbuf("tm4", [128, 128], F32)
    shr = p.sbuf("shr", [128, 128], F32)
    shi = p.sbuf("shi", [128, 128], F32)
    sneg = p.sbuf("sneg", [128, 128], F32)
    prt = p.sbuf("prt", [128, ND0, 32], F32)
    pit = p.sbuf("pit", [128, ND0, 32], F32)
    P1r = p.sbuf("P1r", [128, ND0, 64], F32)
    P1i = p.sbuf("P1i", [128, ND0, 64], F32)
    p64k = p.sbuf("p64k", [128, ND0, 12], F32)
    ccols = p.sbuf("ccols", [128, ND0, 14], F32)
    subc = p.sbuf("subc", [128, ND0, 10], F32)
    b1s = p.sbuf("b1s", [128, 16], F32)
    binb = p.sbuf("binb", [128, ND0], F32)
    lnlwc = p.sbuf("lnlwc", [128, ND0], F32)
    lnlbc = p.sbuf("lnlbc", [128, ND0], F32)
    boutb = p.sbuf("boutb", [128, KC], F32)

    s_xs = [p.sem("s_x0"), p.sem("s_x1")]
    s_ws = [p.sem("s_w0"), p.sem("s_w1")]
    s_sm = p.sem("s_sm")
    s_os = [p.sem("s_o0"), p.sem("s_o1")]
    s_pe = p.sem("s_pe")
    s_act = p.sem("s_act")
    s_dve = p.sem("s_dve")

    p.open_block()

    def _(e):
        s_sm.inc(e.dma_start(binb[:], binc.ap()), 16)
        s_sm.inc(e.dma_start(lnlwc[:], lnlw.ap()), 16)
        s_sm.inc(e.dma_start(lnlbc[:], lnlb.ap()), 16)
        s_sm.inc(e.dma_start(boutb[:], boutc.ap()), 16)
    p.pool(_)

    def _(e):
        e.memset(ones_bf[:], 1.0)
        e.memset(eps1[:], EPS_LN)
        s_dve.inc(e.memset(ones_f[:], 1.0))
    p.dve(_)

    pscope = []
    for _nm in ("psA", "psB", "psC", "psD"):
        g = nc.psum_tensor(_nm, [128, T], F32)
        pscope.append((g, g.__enter__()))
    psA, psB, psC, psD = [t for (_g, t) in pscope]

    NSC = 2 * KC
    actF_tok = [0] * NSC
    peF_tok = [0] * NSC
    xF_tok = [0] * NSC
    sqF = p.sbuf("sqF", [128, 2, T], BF16)
    xcF = p.sbuf("xcF", [128, 2, T], BF16)

    for i in range(NSC):
        kc, half = i // 2, i % 2
        sl = i % 2

        def _(e, i=i, kc=kc, half=half, sl=sl, dvg=s_dve.val, acg=s_act.val):
            if i >= 2:
                s_act.wait(e, actF_tok[i - 2])
            else:
                s_dve.wait(e, dvg)
                s_act.wait(e, acg)
            xF_tok[i] = s_xs[sl].inc(e.dma_start(
                xf[:, sl, :], xfv[:, kc, half * T:(half + 1) * T]), 16)
        p.sync(_)

        def _(e, i=i, sl=sl):
            s_xs[sl].wait(e, xF_tok[i])
            if i >= 2:
                s_pe.wait(e, peF_tok[i - 2])
            e.copy(xcF[:, sl, :], xf[:, sl, :])
            actF_tok[i] = s_act.inc(e.square(sqF[:, sl, :], xf[:, sl, :]))
        p.act(_)

        def _(e, i=i, half=half, sl=sl):
            s_act.wait(e, actF_tok[i])
            if i == 0:
                s_dve.wait(e)
            psX = psA if half == 0 else psB
            psQ2 = psC if half == 0 else psD
            for tc in range(2):
                e.matmul(psX[0:1, tc * 512:(tc + 1) * 512], ones_bf[:],
                         xcF[:, sl, tc * 512:(tc + 1) * 512],
                         start=(i < 2), stop=(i >= NSC - 2))
                mm = e.matmul(psQ2[0:1, tc * 512:(tc + 1) * 512], ones_bf[:],
                              sqF[:, sl, tc * 512:(tc + 1) * 512],
                              start=(i < 2), stop=(i >= NSC - 2))
            peF_tok[i] = s_pe.inc(mm)
        p.te(_)
    pe_Fstats = s_pe.val

    FMU, FSD = 0, TF

    def _(e):
        s_pe.wait(e, pe_Fstats)
        e.tensor_scalar(rowsF[:, FMU:FMU + T], psA[0:1, 0:T], 1.0 / DM, None, ALU.mult)
        e.tensor_scalar(rowsF[:, FMU + T:FMU + TF], psB[0:1, 0:T], 1.0 / DM, None, ALU.mult)
        e.tensor_scalar(rowsF[:, FSD:FSD + T], psC[0:1, 0:T], 1.0 / DM, None, ALU.mult)
        e.tensor_scalar(rowsF[:, FSD + T:FSD + TF], psD[0:1, 0:T], 1.0 / DM, None, ALU.mult)
        s_dve.inc(e.engine_nop())
        for seg in range(4):
            a, b = seg * 512, (seg + 1) * 512
            s_dve.wait(e)
            s_dve.inc(e.tensor_tensor(rows5[0:1, 0:512], rowsF[:, FMU + a:FMU + b],
                                      rowsF[:, FMU + a:FMU + b], ALU.mult))
            s_dve.wait(e)
            s_dve.inc(e.tensor_tensor(rowsF[:, FSD + a:FSD + b],
                                      rowsF[:, FSD + a:FSD + b],
                                      rows5[0:1, 0:512], ALU.subtract))
    p.dve(_)

    def _(e):
        s_dve.wait(e)
        s_act.inc(e.activation(rowsF[:, FSD:FSD + TF], rowsF[:, FSD:FSD + TF],
                               AF.Sqrt, bias=eps1[0:1, :], scale=1.0))
    p.act(_)

    def _(e):
        s_act.wait(e)
        s_dve.inc(e.reciprocal(rowsF[:, FSD:FSD + TF], rowsF[:, FSD:FSD + TF]))
    p.dve(_)
    fstats_tok = s_dve.val

    psWin = [psA, psB, psC, psD]
    win_sl = big1
    mub = p.sbuf("mub", [128, T], F32)
    sdb = p.sbuf("sdb", [128, T], F32)

    def _(e):
        for a in range(ND0):
            s_sm.inc(e.dma_start(win_sl[:, a * KC * 128:(a + 1) * KC * 128],
                                 winT.ap()[a].rearrange("p b c -> p (b c)"),
                                 max_dma_last_dim=8192), 16)
    p.pool(_)
    win_ld = s_sm.val

    for half in range(2):
        def _(e, half=half):
            if half == 0:
                s_dve.wait(e, fstats_tok)
            else:
                s_dve.wait(e, win_apply_done)
            for tc in range(2):
                e.matmul(psA[:, tc * 512:(tc + 1) * 512], ones_f[:],
                         rowsF[0:1, FMU + half * T + tc * 512:FMU + half * T + (tc + 1) * 512],
                         start=True, stop=True)
                mm = e.matmul(psB[:, tc * 512:(tc + 1) * 512], ones_f[:],
                              rowsF[0:1, FSD + half * T + tc * 512:FSD + half * T + (tc + 1) * 512],
                              start=True, stop=True)
            s_pe.inc(mm)
        p.te(_)
        pe_hbc = s_pe.val

        def _(e, pe_hbc=pe_hbc):
            s_pe.wait(e, pe_hbc)
            e.copy(mub[:], psA[:])
            s_act.inc(e.copy(sdb[:], psB[:]))
        p.act(_)
        act_hbc = s_act.val

        xk_tok = [0] * KC
        xd_tok = [0] * KC
        win_pe_tok = [0] * KC
        for kc in range(KC):
            sl = kc % 2

            def _(e, half=half, kc=kc, sl=sl, actg=s_act.val):
                if kc >= 2:
                    s_dve.wait(e, xd_tok[kc - 2])
                elif half > 0:
                    s_dve.wait(e, win_apply_done)
                else:
                    s_act.wait(e, actg)
                xk_tok[kc] = s_xs[sl].inc(e.dma_start(
                    xf[:, sl, :], xfv[:, kc, half * T:(half + 1) * T]), 16)
            p.sync(_)

            def _(e, half=half, kc=kc, sl=sl):
                s_xs[sl].wait(e, xk_tok[kc])
                if kc == 0:
                    s_act.wait(e, act_hbc)
                if kc >= 2:
                    s_pe.wait(e, win_pe_tok[kc - 2])
                elif half == 1:
                    s_pe.wait(e, win_pe_tok[KC - 2 + kc])
                e.tensor_tensor(outsb[:, 0, :], xf[:, sl, 0:512], mub[:, 0:512], ALU.subtract)
                e.tensor_tensor(xnk[:, sl, 0:512], outsb[:, 0, :], sdb[:, 0:512], ALU.mult)
                e.tensor_tensor(outsb[:, 0, :], xf[:, sl, 512:1024], mub[:, 512:1024], ALU.subtract)
                xd_tok[kc] = s_dve.inc(e.tensor_tensor(
                    xnk[:, sl, 512:1024], outsb[:, 0, :], sdb[:, 512:1024], ALU.mult))
            p.dve(_)

            def _(e, half=half, kc=kc, sl=sl, act_hbc=act_hbc):
                s_dve.wait(e, xd_tok[kc])
                if kc == 0:
                    s_sm.wait(e, win_ld)
                    s_act.wait(e, act_hbc)
                for mb in range(ND0):
                    for tc in range(2):
                        mm = e.matmul(
                            psWin[mb][:, tc * 512:(tc + 1) * 512],
                            win_sl[:, (mb * KC + kc) * 128:(mb * KC + kc + 1) * 128],
                            xnk[:, sl, tc * 512:(tc + 1) * 512],
                            start=(kc == 0), stop=(kc == KC - 1))
                win_pe_tok[kc] = s_pe.inc(mm)
            p.te(_)
        pe_win = s_pe.val

        def _(e, half=half, pe_win=pe_win):
            s_pe.wait(e, pe_win)
            for mb in range(ND0):
                s_dve.inc(e.tensor_scalar(h[:, mb, half * T:(half + 1) * T],
                                          psWin[mb][:], binb[:, mb:mb + 1],
                                          None, ALU.add))
        p.dve(_)
        win_apply_done = s_dve.val
    h_ready = s_dve.val
    if DEBUG_L1:
        def _(e):
            s_dve.wait(e, h_ready)
            s_sm.inc(e.dma_start(dbg_h.ap(), h[:].rearrange("p a b -> p (a b)"),
                                 max_dma_last_dim=8192), 16)
        p.pool(_)

    def col_bc(buf, n):
        return bass.AP(buf.tensor, buf.offset, [buf.ap[0], buf.ap[1], [0, n]])

    CPR, CPI, CLCR, CLCI, CSCL, CL1W, CL1B, CL2W, CL2B, CB2 = range(10)

    def emit_sub_consts(s):
        def _(e, s=s, dvg=s_dve.val, peg=s_pe.val, acg=s_act.val):
            s_dve.wait(e, dvg)
            s_pe.wait(e, peg)
            s_act.wait(e, acg)
            s_sm.inc(e.dma_start(subc[:, :, CL1W:CL1W + 1], cols_in["ln1_w"].ap()[s].unsqueeze(2), allow_slow_non_contiguous=True), 16)
            s_sm.inc(e.dma_start(subc[:, :, CL1B:CL1B + 1], cols_in["ln1_b"].ap()[s].unsqueeze(2), allow_slow_non_contiguous=True), 16)
            s_sm.inc(e.dma_start(subc[:, :, CL2W:CL2W + 1], cols_in["ln2_w"].ap()[s].unsqueeze(2), allow_slow_non_contiguous=True), 16)
            s_sm.inc(e.dma_start(subc[:, :, CL2B:CL2B + 1], cols_in["ln2_b"].ap()[s].unsqueeze(2), allow_slow_non_contiguous=True), 16)
            s_sm.inc(e.dma_start(subc[:, :, CSCL:CSCL + 1], cols_in["sc_lin"].ap()[s].unsqueeze(2), allow_slow_non_contiguous=True), 16)
            s_sm.inc(e.dma_start(ccols[:, :, 0:1], cols_in["phz_re"].ap()[s].unsqueeze(2), allow_slow_non_contiguous=True), 16)
            s_sm.inc(e.dma_start(ccols[:, :, 1:2], cols_in["phz_im"].ap()[s].unsqueeze(2), allow_slow_non_contiguous=True), 16)
            s_sm.inc(e.dma_start(ccols[:, :, 2:3], cols_in["lc_re"].ap()[s].unsqueeze(2), allow_slow_non_contiguous=True), 16)
            s_sm.inc(e.dma_start(ccols[:, :, 3:4], cols_in["lc_im"].ap()[s].unsqueeze(2), allow_slow_non_contiguous=True), 16)
            s_sm.inc(e.dma_start(subc[:, :, CB2:CB2 + 1], cols_in["b2"].ap()[s].unsqueeze(2), allow_slow_non_contiguous=True), 16)
            s_sm.inc(e.dma_start(b1s[:], b1c.ap()[s]), 16)
            for a in range(16):
                s_sm.inc(e.dma_start(w1s[:, a * ND0 * 128:(a + 1) * ND0 * 128],
                                     w1T.ap()[s][a].rearrange("p b c -> p (b c)"),
                                     max_dma_last_dim=8192), 16)
            for a in range(ND0):
                s_sm.inc(e.dma_start(w2s[:, a * 16 * 128:(a + 1) * 16 * 128],
                                     w2T.ap()[s][a].rearrange("p b c -> p (b c)"),
                                     max_dma_last_dim=8192), 16)
        p.pool(_)
        lds = s_sm.val

        def _(e, s=s, lds=lds):
            s_sm.wait(e, lds)
            W = lambda: s_dve.wait(e)
            I = lambda ins: s_dve.inc(ins)
            phzr = ccols[:, :, 0:1]; phzi = ccols[:, :, 1:2]
            lcr = ccols[:, :, 2:3]; lci = ccols[:, :, 3:4]
            r2 = ccols[:, :, 4:5]; tmp = ccols[:, :, 5:6]
            fac = ccols[:, :, 6:7]
            pr = subc[:, :, CPR:CPR + 1]; pi = subc[:, :, CPI:CPI + 1]
            lcpr = ccols[:, :, 7:8]; lcpi = ccols[:, :, 8:9]
            I(e.tensor_tensor(r2, phzr, phzr, ALU.mult)); W()
            I(e.tensor_tensor(tmp, phzi, phzi, ALU.mult)); W()
            I(e.tensor_tensor(r2, r2, tmp, ALU.add)); W()
        p.dve(_)

        def _(e):
            s_dve.wait(e)
            s_act.inc(e.activation(ccols[:, :, 4:5], ccols[:, :, 4:5], AF.Sqrt,
                                   bias=0.0, scale=1.0))
        p.act(_)

        def _(e):
            s_act.wait(e)
            s_dve.inc(e.reciprocal(ccols[:, :, 5:6], ccols[:, :, 4:5]))
        p.dve(_)

        def _(e):
            s_dve.wait(e)
            s_act.inc(e.activation(ccols[:, :, 4:5], ccols[:, :, 4:5], AF.Tanh,
                                   bias=0.0, scale=1.0))
        p.act(_)

        def _(e, s=s):
            s_act.wait(e)
            W = lambda: s_dve.wait(e)
            I = lambda ins: s_dve.inc(ins)
            phzr = ccols[:, :, 0:1]; phzi = ccols[:, :, 1:2]
            lcr = ccols[:, :, 2:3]; lci = ccols[:, :, 3:4]
            th = ccols[:, :, 4:5]; rinv = ccols[:, :, 5:6]
            fac = ccols[:, :, 6:7]
            pr = subc[:, :, CPR:CPR + 1]; pi = subc[:, :, CPI:CPI + 1]
            lcpr = ccols[:, :, 7:8]; lcpi = ccols[:, :, 8:9]
            t2 = ccols[:, :, 9:10]
            I(e.tensor_tensor(fac, th, rinv, ALU.mult)); W()
            I(e.tensor_tensor(pr, phzr, fac, ALU.mult)); W()
            I(e.tensor_tensor(pi, phzi, fac, ALU.mult)); W()
            I(e.tensor_copy(prt[:], col_bc(pr, 32))); W()
            I(e.tensor_copy(pit[:], col_bc(pi, 32))); W()
            I(e.tensor_copy(P1r[:, :, 0:1], pr)); W()
            I(e.tensor_copy(P1i[:, :, 0:1], pi)); W()
            L = 1
            while L < 64:
                ar = P1r[:, :, 0:L]; ai = P1i[:, :, 0:L]
                br = col_bc(P1r[:, :, L - 1:L], L); bi = col_bc(P1i[:, :, L - 1:L], L)
                s1 = tm1[:, 0:ND0 * L].rearrange("p (d l) -> p d l", d=ND0)
                s2 = tm2[:, 0:ND0 * L].rearrange("p (d l) -> p d l", d=ND0)
                I(e.tensor_tensor(s1, ar, br, ALU.mult)); W()
                I(e.tensor_tensor(s2, ai, bi, ALU.mult)); W()
                I(e.tensor_tensor(P1i[:, :, L:2 * L], ar, bi, ALU.mult)); W()
                I(e.tensor_tensor(s2, s1, s2, ALU.subtract)); W()
                I(e.tensor_tensor(s1, ai, br, ALU.mult)); W()
                I(e.tensor_tensor(P1i[:, :, L:2 * L], P1i[:, :, L:2 * L], s1, ALU.add)); W()
                I(e.tensor_copy(P1r[:, :, L:2 * L], s2)); W()
                L *= 2
            I(e.tensor_copy(p64k[:, :, 0:1], P1r[:, :, 63:64])); W()
            I(e.tensor_copy(p64k[:, :, 1:2], P1i[:, :, 63:64])); W()
            for k in range(1, 5):
                ar = p64k[:, :, 2 * k - 2:2 * k - 1]; ai = p64k[:, :, 2 * k - 1:2 * k]
                I(e.tensor_tensor(ccols[:, :, 10:11], ar, ar, ALU.mult)); W()
                I(e.tensor_tensor(ccols[:, :, 11:12], ai, ai, ALU.mult)); W()
                I(e.tensor_tensor(p64k[:, :, 2 * k:2 * k + 1], ccols[:, :, 10:11],
                                  ccols[:, :, 11:12], ALU.subtract)); W()
                I(e.tensor_tensor(ccols[:, :, 10:11], ar, ai, ALU.mult)); W()
                I(e.tensor_scalar(p64k[:, :, 2 * k + 1:2 * k + 2], ccols[:, :, 10:11],
                                  2.0, None, ALU.mult)); W()
            p64r = p64k[:, :, 0:1]; p64i = p64k[:, :, 1:2]
            I(e.tensor_tensor(ccols[:, :, 10:11], lcr, p64r, ALU.mult)); W()
            I(e.tensor_tensor(ccols[:, :, 11:12], lci, p64i, ALU.mult)); W()
            I(e.tensor_tensor(lcpr, ccols[:, :, 10:11], ccols[:, :, 11:12], ALU.subtract)); W()
            I(e.tensor_tensor(ccols[:, :, 10:11], lcr, p64i, ALU.mult)); W()
            I(e.tensor_tensor(ccols[:, :, 11:12], lci, p64r, ALU.mult)); W()
            I(e.tensor_tensor(lcpi, ccols[:, :, 10:11], ccols[:, :, 11:12], ALU.add)); W()
            I(e.tensor_scalar(ccols[:, :, 12:13], lci, -1.0, None, ALU.mult)); W()
        p.dve(_)
        return s_dve.val

    def emit_inner_stats():
        tok_in = [0] * (ND0 * 2)
        for i in range(ND0 * 2):
            d0, half = i // 2, i % 2
            sl = i % 2

            def _(e, i=i, d0=d0, half=half, sl=sl, hg=s_dve.val):
                if i == 0:
                    s_dve.wait(e, hg)
                if i >= 2:
                    s_pe.wait(e, tok_in[i - 2])
                e.copy(xcF[:, sl, :], h[:, d0, half * T:(half + 1) * T])
                s_act.inc(e.square(sqF[:, sl, :], h[:, d0, half * T:(half + 1) * T]))
            p.act(_)
            a_tok = s_act.val

            def _(e, i=i, d0=d0, half=half, sl=sl, a_tok=a_tok):
                s_act.wait(e, a_tok)
                psX = psA if half == 0 else psB
                psQ2 = psC if half == 0 else psD
                for tc in range(2):
                    e.matmul(psX[0:1, tc * 512:(tc + 1) * 512], ones_bf[:],
                             xcF[:, sl, tc * 512:(tc + 1) * 512],
                             start=(i < 2), stop=(i >= ND0 * 2 - 2))
                    mm = e.matmul(psQ2[0:1, tc * 512:(tc + 1) * 512], ones_bf[:],
                                  sqF[:, sl, tc * 512:(tc + 1) * 512],
                                  start=(i < 2), stop=(i >= ND0 * 2 - 2))
                tok_in[i] = s_pe.inc(mm)
            p.te(_)
        pe_st = s_pe.val

        def _(e, pe_st=pe_st):
            s_pe.wait(e, pe_st)
            e.tensor_scalar(rows5[:, 0:T], psA[0:1, 0:T], 1.0 / D1, None, ALU.mult)
            e.tensor_scalar(rows5[:, T:TF], psB[0:1, 0:T], 1.0 / D1, None, ALU.mult)
            e.tensor_scalar(rows5[:, TF:TF + T], psC[0:1, 0:T], 1.0 / D1, None, ALU.mult)
            e.tensor_scalar(rows5[:, TF + T:2 * TF], psD[0:1, 0:T], 1.0 / D1, None, ALU.mult)
            s_dve.inc(e.engine_nop())
            for seg in range(4):
                a, b = seg * 512, (seg + 1) * 512
                s_dve.wait(e)
                s_dve.inc(e.tensor_tensor(mub[0:1, 0:512], rows5[:, a:b],
                                          rows5[:, a:b], ALU.mult))
                s_dve.wait(e)
                s_dve.inc(e.tensor_tensor(rows5[:, TF + a:TF + b],
                                          rows5[:, TF + a:TF + b],
                                          mub[0:1, 0:512], ALU.subtract))
        p.dve(_)

        def _(e):
            s_dve.wait(e)
            s_act.inc(e.activation(rows5[:, TF:2 * TF], rows5[:, TF:2 * TF],
                                   AF.Sqrt, bias=eps1[0:1, :], scale=1.0))
        p.act(_)

        def _(e):
            s_act.wait(e)
            s_dve.inc(e.reciprocal(rows5[:, TF:2 * TF], rows5[:, TF:2 * TF]))
        p.dve(_)
        return s_dve.val

    def emit_quarter_bcast(q, guard_dve):
        def _(e, q=q, guard_dve=guard_dve, allg=None):
            s_dve.wait(e, guard_dve)
            s_dve.wait(e)
            e.matmul(psA[:, 512:1024], ones_f[:], rows5[0:1, q * 512:(q + 1) * 512],
                     start=True, stop=True)
            mm = e.matmul(psB[:, 512:1024], ones_f[:],
                          rows5[0:1, TF + q * 512:TF + (q + 1) * 512],
                          start=True, stop=True)
            s_pe.inc(mm)
        p.te(_)
        bq = s_pe.val

        def _(e, bq=bq):
            s_pe.wait(e, bq)
            e.copy(mub[:, 0:512], psA[:, 512:1024])
            s_act.inc(e.copy(sdb[:, 0:512], psB[:, 512:1024]))
        p.act(_)
        return s_act.val

    last_dve = h_ready
    for s in range(NSUB):
        cst_tok = emit_sub_consts(s)
        st_tok = emit_inner_stats()

        silu_tok = [0] * 4
        for q in range(4):
            bc_tok = emit_quarter_bcast(q, st_tok)
            dv_toks = [0] * ND0
            for d0 in range(ND0):
                def _(e, q=q, d0=d0, bc_tok=bc_tok, cst_tok=cst_tok):
                    s_act.wait(e)
                    if q == 0 and d0 == 0:
                        s_dve.wait(e, cst_tok)
                    sl5 = h[:, d0, q * 512:(q + 1) * 512]
                    e.tensor_tensor(sdb[:, 512:1024], sl5, mub[:, 0:512], ALU.subtract)
                    e.tensor_tensor(sdb[:, 512:1024], sdb[:, 512:1024], sdb[:, 0:512], ALU.mult)
                    dv_toks[d0] = s_dve.inc(e.tensor_scalar(
                        sdb[:, 512:1024], sdb[:, 512:1024],
                        subc[:, d0][:, CL1W:CL1W + 1],
                        subc[:, d0][:, CL1B:CL1B + 1], ALU.mult, ALU.add))
                p.dve(_)

                def _(e, q=q, d0=d0):
                    s_dve.wait(e, dv_toks[d0])
                    silu_tok[q] = s_act.inc(e.activation(
                        y[:, d0, q * 512:(q + 1) * 512], sdb[:, 512:1024], AF.Silu))
                p.act(_)
        y_ready = silu_tok[3]
        if DEBUG_L1 and s == 0:
            def _(e, y_ready=y_ready):
                s_act.wait(e, y_ready)
                s_sm.inc(e.dma_start(dbg_y.ap(), y[:].rearrange("p a b -> p (a b)"),
                                     max_dma_last_dim=8192), 16)
            p.pool(_)

        def _(e, s=s, y_ready=y_ready, cst_tok=cst_tok):
            s_act.wait(e, y_ready)
            s_dve.wait(e, cst_tok)
            e.memset(cr[:], 0.0)
            e.memset(ci[:], 0.0)
            yv = y.rearrange("p d (c u) -> p d c u", u=64)
            ycv = yc.rearrange("p d (c u) -> p d c u", u=64)
            for u in range(64):
                e.tensor_copy(Xs[:].rearrange("p (d c) -> p d c", d=ND0), yv[:, :, :, u])
                e.tensor_tensor(tm1[:], ci[:], pit[:].rearrange("p d c -> p (d c)"), ALU.mult)
                e.tensor_tensor(tm2[:], cr[:], prt[:].rearrange("p d c -> p (d c)"), ALU.mult)
                e.tensor_tensor(tm3[:], ci[:], prt[:].rearrange("p d c -> p (d c)"), ALU.mult)
                e.tensor_tensor(tm4[:], cr[:], pit[:].rearrange("p d c -> p (d c)"), ALU.mult)
                e.tensor_tensor(cr[:], tm2[:], tm1[:], ALU.subtract)
                e.tensor_tensor(cr[:], cr[:], Xs[:], ALU.add)
                e.tensor_tensor(ci[:], tm3[:], tm4[:], ALU.add)
                e.tensor_copy(ycv[:, :, :, u], cr[:].rearrange("p (d c) -> p d c", d=ND0))
            lcpr = ccols[:, :, 7:8]; lcpi = ccols[:, :, 8:9]
            for d0 in range(ND0):
                s_dve.inc(e.tensor_tensor(cr[:, d0 * 32:d0 * 32 + 1], cr[:, d0 * 32:d0 * 32 + 1],
                                          lcpr[:, d0, :], ALU.add))
                s_dve.wait(e)
                s_dve.inc(e.tensor_tensor(ci[:, d0 * 32:d0 * 32 + 1], ci[:, d0 * 32:d0 * 32 + 1],
                                          lcpi[:, d0, :], ALU.add))
                s_dve.wait(e)
            for ki, k in enumerate([1, 2, 4, 8, 16]):
                pkr = col_bc(p64k[:, :, 2 * ki:2 * ki + 1], 32 - k)
                pki = col_bc(p64k[:, :, 2 * ki + 1:2 * ki + 2], 32 - k)
                e.memset(shr[:], 0.0)
                e.memset(shi[:], 0.0)
                for d0 in range(ND0):
                    e.tensor_copy(shr[:, d0 * 32 + k:(d0 + 1) * 32],
                                  cr[:, d0 * 32:(d0 + 1) * 32 - k])
                    e.tensor_copy(shi[:, d0 * 32 + k:(d0 + 1) * 32],
                                  ci[:, d0 * 32:(d0 + 1) * 32 - k])
                shrv = shr.rearrange("p (d c) -> p d c", d=ND0)[:, :, k:32]
                shiv = shi.rearrange("p (d c) -> p d c", d=ND0)[:, :, k:32]
                t1v = tm1.rearrange("p (d c) -> p d c", d=ND0)[:, :, k:32]
                t2v = tm2.rearrange("p (d c) -> p d c", d=ND0)[:, :, k:32]
                crv = cr.rearrange("p (d c) -> p d c", d=ND0)[:, :, k:32]
                civ = ci.rearrange("p (d c) -> p d c", d=ND0)[:, :, k:32]
                e.tensor_tensor(t1v, shrv, pkr, ALU.mult)
                e.tensor_tensor(t2v, shiv, pki, ALU.mult)
                e.tensor_tensor(t1v, t1v, t2v, ALU.subtract)
                e.tensor_tensor(crv, crv, t1v, ALU.add)
                e.tensor_tensor(t1v, shrv, pki, ALU.mult)
                e.tensor_tensor(t2v, shiv, pkr, ALU.mult)
                e.tensor_tensor(t1v, t1v, t2v, ALU.add)
                e.tensor_tensor(civ, civ, t1v, ALU.add)
            s_dve.inc(e.tensor_scalar(sneg[:], ci[:], -1.0, None, ALU.mult))
            s_dve.wait(e)
            for d0 in range(ND0):
                for c in range(32):
                    if c == 0:
                        sr = ccols[:, d0][:, 2:3]
                        si_n = ccols[:, d0][:, 12:13]
                    else:
                        sr = cr[:, d0 * 32 + c - 1:d0 * 32 + c]
                        si_n = sneg[:, d0 * 32 + c - 1:d0 * 32 + c]
                    tgt = yc[:, d0, c * 64:(c + 1) * 64]
                    e.scalar_tensor_tensor(tgt, P1r[:, d0, :], sr, tgt, ALU.mult, ALU.add)
                    e.scalar_tensor_tensor(tgt, P1i[:, d0, :], si_n, tgt, ALU.mult, ALU.add)
            for d0 in range(ND0):
                mmv = e.scalar_tensor_tensor(
                    h[:, d0, :], yc[:, d0, :], subc[:, d0][:, CSCL:CSCL + 1],
                    h[:, d0, :], ALU.mult, ALU.add)
            s_dve.inc(mmv)
        p.dve(_)
        scan_tok = s_dve.val
        if DEBUG_L1 and s == 0:
            def _(e, scan_tok=scan_tok):
                s_dve.wait(e, scan_tok)
                s_sm.inc(e.dma_start(dbg_yc.ap(), yc[:, :, :].rearrange("p a b -> p (a b)"),
                                     max_dma_last_dim=8192), 16)
                s_sm.inc(e.dma_start(dbg_h2.ap(), h[:].rearrange("p a b -> p (a b)"),
                                     max_dma_last_dim=8192), 16)
            p.pool(_)

        st2_tok = emit_inner_stats()

        for q in range(4):
            bc2 = emit_quarter_bcast(q, st2_tok)
            dvq = [0] * ND0
            for d0 in range(ND0):
                def _(e, q=q, d0=d0, bc2=bc2, scan_tok=scan_tok, peg=s_pe.val):
                    s_act.wait(e, bc2)
                    if d0 == 0:
                        s_pe.wait(e, peg)
                    if q == 0 and d0 == 0:
                        s_dve.wait(e, scan_tok)
                    sl5 = h[:, d0, q * 512:(q + 1) * 512]
                    e.tensor_tensor(sdb[:, 512:1024], sl5, mub[:, 0:512], ALU.subtract)
                    e.tensor_tensor(sdb[:, 512:1024], sdb[:, 512:1024], sdb[:, 0:512], ALU.mult)
                    dvq[d0] = s_dve.inc(e.tensor_scalar(
                        lnq[:, d0, :], sdb[:, 512:1024],
                        subc[:, d0][:, CL2W:CL2W + 1],
                        subc[:, d0][:, CL2B:CL2B + 1], ALU.mult, ALU.add))
                p.dve(_)
            lnq_tok = dvq[ND0 - 1]

            fa_tok = [0] * 16
            sl_tok = [0] * 16
            for mb in range(16):
                def _(e, q=q, mb=mb, lnq_tok=lnq_tok):
                    s_dve.wait(e, lnq_tok)
                    if mb >= 2:
                        s_act.wait(e, sl_tok[mb - 2])
                    for kt in range(ND0):
                        mm = e.matmul(psD[:, (mb % 2) * 512:(mb % 2) * 512 + 512],
                                      w1s[:, (mb * ND0 + kt) * 128:(mb * ND0 + kt + 1) * 128],
                                      lnq[:, kt, :],
                                      start=(kt == 0), stop=(kt == ND0 - 1))
                    fa_tok[mb] = s_pe.inc(mm)
                p.te(_)

                def _(e, q=q, mb=mb):
                    s_pe.wait(e, fa_tok[mb])
                    sl_tok[mb] = s_act.inc(e.activation(
                        fq[:, mb, :], psD[:, (mb % 2) * 512:(mb % 2) * 512 + 512],
                        AF.Silu, bias=b1s[:, mb:mb + 1], scale=1.0))
                p.act(_)

            psY = [psA[:, 0:512], psA[:, 512:1024], psB[:, 0:512], psB[:, 512:1024]]
            def _(e, q=q):
                s_act.wait(e, sl_tok[15])
                for d0 in range(ND0):
                    for kt in range(16):
                        mm = e.matmul(psY[d0],
                                      w2s[:, (d0 * 16 + kt) * 128:(d0 * 16 + kt + 1) * 128],
                                      fq[:, kt, :],
                                      start=(kt == 0), stop=(kt == 15))
                s_pe.inc(mm)
            p.te(_)
            fb_tok = s_pe.val

            def _(e, q=q, fb_tok=fb_tok):
                s_pe.wait(e, fb_tok)
                for d0 in range(ND0):
                    mmv = e.scalar_tensor_tensor(
                        h[:, d0, q * 512:(q + 1) * 512], psY[d0],
                        subc[:, d0][:, CB2:CB2 + 1],
                        h[:, d0, q * 512:(q + 1) * 512], ALU.add, ALU.add)
                s_dve.inc(mmv)
            p.dve(_)
        last_dve = s_dve.val
        if DEBUG_L1:
            def _(e, s=s, last_dve=last_dve):
                s_dve.wait(e, last_dve)
                s_sm.inc(e.dma_start(dbg_hs.ap()[s], h[:].rearrange("p a b -> p (a b)"),
                                     max_dma_last_dim=8192), 16)
            p.pool(_)

    stl_tok = emit_inner_stats()
    lnl_tok = [0] * 4
    for q in range(4):
        bcl = emit_quarter_bcast(q, stl_tok)
        for d0 in range(ND0):
            def _(e, q=q, d0=d0, bcl=bcl):
                s_act.wait(e, bcl)
                sl5 = h[:, d0, q * 512:(q + 1) * 512]
                e.tensor_tensor(sdb[:, 512:1024], sl5, mub[:, 0:512], ALU.subtract)
                e.tensor_tensor(sdb[:, 512:1024], sdb[:, 512:1024], sdb[:, 0:512], ALU.mult)
                lnl_tok[q] = s_dve.inc(e.tensor_scalar(
                    y[:, d0, q * 512:(q + 1) * 512], sdb[:, 512:1024],
                    lnlwc[:, d0:d0 + 1], lnlbc[:, d0:d0 + 1], ALU.mult, ALU.add))
            p.dve(_)
    lnl_done = lnl_tok[3]
    if DEBUG_L1:
        def _(e):
            s_dve.wait(e, lnl_done)
            s_sm.inc(e.dma_start(dbg_lnl.ap(), y[:].rearrange("p a b -> p (a b)"),
                                 max_dma_last_dim=8192), 16)
        p.pool(_)

    wo_tok = [0] * KC
    pe_wo = [0] * KC
    dve_wo = [0] * KC
    out_tok = [0] * (KC * 4)
    ev_q = [0] * (KC * 4)
    xw_tok = [0] * (KC * 2)

    for mb in range(KC):
        sl = mb % 2
        psW = [psA, psB] if sl == 0 else [psC, psD]

        def _(e, mb=mb, sl=sl):
            if mb >= 2:
                s_pe.wait(e, pe_wo[mb - 2])
            wo_tok[mb] = s_ws[sl].inc(e.dma_start(
                wsl[:, sl, :], woutT.ap()[mb].rearrange("p a b -> p (a b)"),
                max_dma_last_dim=8192), 16)
        p.pool(_)

        def _(e, mb=mb, sl=sl, psW=psW):
            s_ws[sl].wait(e, wo_tok[mb])
            if mb == 0:
                s_dve.wait(e, lnl_done)
            if mb >= 2:
                s_dve.wait(e, dve_wo[mb - 2])
            for q in range(4):
                tgt = psW[q // 2][:, (q % 2) * 512:(q % 2) * 512 + 512]
                for kt in range(ND0):
                    mm = e.matmul(tgt,
                                  wsl[:, sl, kt * 128:(kt + 1) * 128],
                                  y[:, kt, q * 512:(q + 1) * 512],
                                  start=(kt == 0), stop=(kt == ND0 - 1))
            pe_wo[mb] = s_pe.inc(mm)
        p.te(_)

        for half in range(2):
            def _(e, mb=mb, half=half, dvg=s_dve.val):
                if mb >= 1:
                    s_dve.wait(e, dve_wo[mb - 1])
                else:
                    s_dve.wait(e, dvg)
                xw_tok[mb * 2 + half] = s_xs[half].inc(e.dma_start(
                    xf[:, half, :], xfv[:, mb, half * T:(half + 1) * T]), 16)
            p.sync(_)

        for q in range(4):
            oi = mb * 4 + q

            def _(e, mb=mb, sl=sl, psW=psW, q=q, oi=oi):
                if q < 2:
                    s_xs[q].wait(e, xw_tok[mb * 2 + q])
                s_pe.wait(e, pe_wo[mb])
                srcp = psW[q // 2][:, (q % 2) * 512:(q % 2) * 512 + 512]
                if oi >= 2:
                    s_os[q % 2].wait(e, out_tok[oi - 2])
                e.tensor_scalar(outsb[:, q % 2, :], srcp, boutb[:, mb:mb + 1],
                                None, ALU.add)
                s_dve.inc(e.tensor_tensor(
                    outsb[:, q % 2, :], outsb[:, q % 2, :],
                    xf[:, q // 2, (q % 2) * 512:(q % 2) * 512 + 512], ALU.add))
                ev_q[oi] = s_dve.val
            p.dve(_)

            def _(e, mb=mb, q=q, oi=oi):
                s_dve.wait(e, ev_q[oi])
                out_tok[oi] = s_os[q % 2].inc(e.dma_start(
                    ov[:, mb, q * 512:(q + 1) * 512], outsb[:, q % 2, :]), 16)
            p.sync(_)
        dve_wo[mb] = s_dve.val

    def _(e):
        for tc in range(2):
            s_os[tc].wait(e)
    p.sync(_)

    for (g, _t) in reversed(pscope):
        g.__exit__(None, None, None)
    p.close()
    return p


def run_l1attn(x, w):
    p = _get_prog("l1attn", build_l1attn_program)
    in_maps = []
    for c in range(NCORES):
        b = c // 2
        xfull = np.ascontiguousarray(x[b].T)
        in_maps.append({"x_full": xfull, **w})
    res = run_bass_kernel_spmd(p.nc, in_maps, core_ids=list(range(NCORES)))
    out = np.empty((4, 2048, DM), np.float32)
    for c in range(NCORES):
        b, h = c // 2, c % 2
        out[b, h * T:(h + 1) * T, :] = res.results[c]["x_out"].T[h * T:(h + 1) * T, :]
    return out


# version 05
# speedup vs baseline: 1.0335x; 1.0308x over previous
import sys
import numpy as np

sys.path.insert(0, "/opt/trn_rl_repo")

import concourse.bass as bass
import concourse.mybir as mybir
from concourse.bass_utils import run_bass_kernel_spmd

F32 = mybir.dt.float32
BF16 = mybir.dt.bfloat16
AF = mybir.ActivationFunctionType
ALU = mybir.AluOpType
AX = mybir.AxisListType

NCORES = 8
T = 1024
DM = 4096
DH = 11008
HB = DH // 128
KC = DM // 128
EPS_LN = 1e-5


class CSem:
    def __init__(self, handle):
        self.h = handle
        self.val = 0

    def inc(self, instr, n=1):
        instr.then_inc(self.h, n)
        self.val += n
        return self.val

    def wait(self, eng, val=None):
        eng.wait_ge(self.h, self.val if val is None else val)


class Prog:
    def __init__(self):
        self.nc = bass.Bass("TRN2", target_bir_lowering=False, debug=False)
        self._ctxs = []
        self.sems = {}
        self._scope_stack = []
        self._uid = 0

    def sem(self, name):
        if name not in self.sems:
            g = self.nc.semaphore(name)
            self.sems[name] = CSem(g.__enter__())
            self._ctxs.append(g)
        return self.sems[name]

    def push_scope(self):
        s = []
        self._scope_stack.append(s)
        return s

    def pop_scope(self):
        s = self._scope_stack.pop()
        self.close_scope(s)

    def _default_scope(self):
        return self._scope_stack[-1] if self._scope_stack else self._ctxs

    def sbuf(self, name, shape, dtype, scope=None):
        self._uid += 1
        g = self.nc.sbuf_tensor(f"{name}_u{self._uid}", shape, dtype)
        t = g.__enter__()
        (self._default_scope() if scope is None else scope).append(g)
        return t

    def psum(self, name, shape, dtype=F32):
        self._uid += 1
        g = self.nc.psum_tensor(f"{name}_u{self._uid}", shape, dtype)
        t = g.__enter__()
        self._default_scope().append(g)
        return t

    def open_block(self):
        self._blk = self.nc.Block()
        self.block = self._blk.__enter__()

    def close_scope(self, scope):
        for g in reversed(scope):
            g.__exit__(None, None, None)

    def close(self):
        self._blk.__exit__(None, None, None)
        for g in reversed(self._ctxs):
            g.__exit__(None, None, None)

    def te(self, f):
        self.block.tensor(f)

    def dve(self, f):
        self.block.vector(f)

    def act(self, f):
        self.block.scalar(f)

    def pool(self, f):
        self.block.gpsimd(f)

    def sync(self, f):
        self.block.sync(f)


def emit_barrier(p):
    toks = [(s, s.val) for s in p.sems.values()]

    def body(e):
        for s, v in toks:
            e.wait_ge(s.h, v)
    p.te(body)
    p.act(body)
    p.dve(body)
    p.pool(body)
    p.sync(body)



def _lhsT_slabs(W):
    o, i = W.shape
    return np.ascontiguousarray(
        W.reshape(o // 128, 128, i // 128, 128).transpose(0, 3, 2, 1))


def prep_l0ffn(inp):
    gw = inp["L0_ffn_gate_w"]; uw = inp["L0_ffn_up_w"]; dw = inp["L0_ffn_down_w"]
    d = {}
    d["gsl"] = _lhsT_slabs(np.asarray(gw, np.float32))
    d["usl"] = _lhsT_slabs(np.asarray(uw, np.float32))
    d["dsl"] = np.ascontiguousarray(
        np.asarray(dw, np.float32).reshape(8, 4, 128, HB, 128).transpose(0, 4, 3, 1, 2))
    a2 = np.concatenate([inp["L0_ffn_gate_a"], inp["L0_ffn_up_a"]], 0).astype(np.float32)
    d["a2t"] = np.ascontiguousarray(a2.T.reshape(KC, 128, 32).transpose(1, 0, 2))
    d["gbt"] = np.ascontiguousarray(np.asarray(inp["L0_ffn_gate_b"], np.float32).T.reshape(16, HB, 128))
    d["ubt"] = np.ascontiguousarray(np.asarray(inp["L0_ffn_up_b"], np.float32).T.reshape(16, HB, 128))
    d["dat"] = np.ascontiguousarray(
        np.asarray(inp["L0_ffn_down_a"], np.float32).T.reshape(HB, 128, 16).transpose(1, 0, 2))
    d["dbt"] = np.ascontiguousarray(np.asarray(inp["L0_ffn_down_b"], np.float32).T.reshape(16, KC, 128))
    return d



def emit_layernorm(p, xv, xn, xf, sq, t32, rows, ones_bf, ones_f, eps1,
                   psA, psB, psC, psD, s_xs, s_pe, s_act, s_dve):
    sq_tok = [0] * KC
    x_tok = [0] * KC
    peS_tok = [0] * KC

    for kc in range(KC):
        sl = kc % 2

        def _(e, kc=kc, sl=sl):
            if kc >= 2:
                s_act.wait(e, sq_tok[kc - 2])
            x_tok[kc] = s_xs[sl].inc(e.dma_start(xf[:, sl, :], xv[:, kc, :]), 16)
        p.sync(_)

        def _(e, kc=kc, sl=sl):
            s_xs[sl].wait(e, x_tok[kc])
            if kc >= 2:
                s_pe.wait(e, peS_tok[kc - 2])
            e.copy(xn[:, kc * T:(kc + 1) * T], xf[:, sl, :])
            sq_tok[kc] = s_act.inc(e.square(sq[:, sl, :], xf[:, sl, :]))
        p.act(_)

        def _(e, kc=kc, sl=sl):
            s_act.wait(e, sq_tok[kc])
            if kc == 0:
                s_dve.wait(e)
            for tc in range(2):
                e.matmul(psA[0:1, tc * 512:(tc + 1) * 512], ones_bf[:],
                         xn[:, kc * T + tc * 512:kc * T + (tc + 1) * 512],
                         start=(kc == 0), stop=(kc == KC - 1))
                mm = e.matmul(psB[0:1, tc * 512:(tc + 1) * 512], ones_bf[:],
                              sq[:, sl, tc * 512:(tc + 1) * 512],
                              start=(kc == 0), stop=(kc == KC - 1))
            peS_tok[kc] = s_pe.inc(mm)
        p.te(_)

    MU, MSQ, VAR, SD = 0, T, 2 * T, 3 * T

    def _(e):
        s_pe.wait(e)
        e.tensor_scalar(rows[:, MU:MU + T], psA[0:1, :], 1.0 / DM, None, ALU.mult)
        s_dve.inc(e.tensor_scalar(rows[:, MSQ:MSQ + T], psB[0:1, :], 1.0 / DM, None, ALU.mult))
        s_dve.wait(e)
        s_dve.inc(e.tensor_tensor(rows[:, VAR:VAR + T], rows[:, MU:MU + T],
                                  rows[:, MU:MU + T], ALU.mult))
        s_dve.wait(e)
        s_dve.inc(e.tensor_tensor(rows[:, VAR:VAR + T], rows[:, MSQ:MSQ + T],
                                  rows[:, VAR:VAR + T], ALU.subtract))
    p.dve(_)

    def _(e):
        s_dve.wait(e)
        s_act.inc(e.activation(rows[:, SD:SD + T], rows[:, VAR:VAR + T],
                               AF.Sqrt, bias=eps1[0:1, :], scale=1.0))
    p.act(_)

    def _(e):
        s_act.wait(e)
        s_dve.inc(e.reciprocal(rows[:, SD:SD + T], rows[:, SD:SD + T]))
    p.dve(_)

    def _(e):
        s_dve.wait(e)
        for tc in range(2):
            e.matmul(psC[:, tc * 512:(tc + 1) * 512], ones_f[:],
                     rows[0:1, MU + tc * 512:MU + (tc + 1) * 512],
                     start=True, stop=True)
            mm = e.matmul(psD[:, tc * 512:(tc + 1) * 512], ones_f[:],
                          rows[0:1, SD + tc * 512:SD + (tc + 1) * 512],
                          start=True, stop=True)
        s_pe.inc(mm)
    p.te(_)
    bc_tok = s_pe.val

    ap_tok = [0] * KC
    act_guard = s_act.val
    for kc in range(KC):
        sl = kc % 2

        def _(e, kc=kc, sl=sl):
            if kc >= 2:
                s_dve.wait(e, ap_tok[kc - 2])
            else:
                s_act.wait(e, act_guard)
            x_tok[kc] = s_xs[sl].inc(e.dma_start(xf[:, sl, :], xv[:, kc, :]), 16)
        p.sync(_)

        def _(e, kc=kc, sl=sl):
            s_xs[sl].wait(e, x_tok[kc])
            if kc == 0:
                s_pe.wait(e, bc_tok)
            e.tensor_tensor(t32[:], xf[:, sl, :], psC[:], ALU.subtract)
            ap_tok[kc] = s_dve.inc(e.tensor_tensor(
                xn[:, kc * T:(kc + 1) * T], t32[:], psD[:], ALU.mult))
        p.dve(_)
    return s_dve.val



def declare_l0ffn_io(nc):
    D = {}
    D["gsl"] = nc.dram_tensor("gsl", [HB, 128, KC, 128], F32, kind="ExternalInput")
    D["usl"] = nc.dram_tensor("usl", [HB, 128, KC, 128], F32, kind="ExternalInput")
    D["dsl"] = nc.dram_tensor("dsl", [8, 128, HB, 4, 128], F32, kind="ExternalInput")
    D["a2t"] = nc.dram_tensor("a2t", [128, KC, 32], F32, kind="ExternalInput")
    D["gbt"] = nc.dram_tensor("gbt", [16, HB, 128], F32, kind="ExternalInput")
    D["ubt"] = nc.dram_tensor("ubt", [16, HB, 128], F32, kind="ExternalInput")
    D["dat"] = nc.dram_tensor("dat", [128, HB, 16], F32, kind="ExternalInput")
    D["dbt"] = nc.dram_tensor("dbt", [16, KC, 128], F32, kind="ExternalInput")
    D["hid_scr"] = nc.dram_tensor("hid_scr", [HB, 128, T], BF16)
    return D


def emit_l0ffn(p, D, x_in_h, x_out_h):
    nc = p.nc
    gsl = D["gsl"]; usl = D["usl"]; dsl = D["dsl"]; a2t = D["a2t"]
    gbt = D["gbt"]; ubt = D["ubt"]; dat = D["dat"]; dbt = D["dbt"]
    hid_dram = D["hid_scr"]
    _scope = p.push_scope()

    xv = x_in_h.ap().rearrange("(c p) t -> p c t", p=128)
    ov = x_out_h.ap().rearrange("(c p) t -> p c t", p=128)

    xf = p.sbuf("xf", [128, 2, T], F32)
    outsb = p.sbuf("outsb", [128, 2, 512], F32)
    td = p.sbuf("td", [16, T], BF16)
    dbs = p.sbuf("dbs", [16, KC * 128], BF16)
    ones_bf = p.sbuf("ones_bf", [128, 1], BF16)
    ones_f = p.sbuf("ones_f", [1, 128], F32)
    eps1 = p.sbuf("eps1", [1, 1], F32)

    sa = []
    xn = p.sbuf("xn", [128, KC * T], BF16, sa)
    gslb = p.sbuf("gslb", [128, 2, KC * 128], BF16, sa)
    uslb = p.sbuf("uslb", [128, 2, KC * 128], BF16, sa)
    gbs = p.sbuf("gbs", [16, HB * 128], BF16, sa)
    ubs = p.sbuf("ubs", [16, HB * 128], BF16, sa)
    das = p.sbuf("das", [128, HB * 16], BF16, sa)
    a2s = p.sbuf("a2s", [128, KC, 32], BF16, sa)
    tg = p.sbuf("tg", [16, T], BF16, sa)
    tu = p.sbuf("tu", [16, T], BF16, sa)
    silu = p.sbuf("silu", [128, T], BF16, sa)
    ucast = p.sbuf("ucast", [128, T], BF16, sa)
    hid = p.sbuf("hid", [128, 2, T], BF16, sa)
    sq = p.sbuf("sq", [128, 2, T], BF16, sa)
    t32 = p.sbuf("t32", [128, T], F32, sa)
    rows = p.sbuf("rows", [1, 4 * T], F32, sa)

    psA = p.psum("psA", [128, T], F32)
    psB = p.psum("psB", [128, T], F32)
    psC = p.psum("psC", [128, T], F32)
    psD = p.psum("psD", [128, T], F32)
    psbank = [psA, psB, psC, psD]

    s_xs = [p.sem("s_x0"), p.sem("s_x1")]
    s_gs = [p.sem("s_g0"), p.sem("s_g1")]
    s_us = [p.sem("s_u0"), p.sem("s_u1")]
    s_ds = [p.sem("s_d0"), p.sem("s_d1")]
    s_hos = [p.sem("s_ho0"), p.sem("s_ho1")]
    s_hi = p.sem("s_hi")
    s_os = [p.sem("s_o0"), p.sem("s_o1")]
    s_sm = p.sem("s_sm")
    s_pe = p.sem("s_pe")
    s_act = p.sem("s_act")
    s_dve = p.sem("s_dve")

    def _(e):
        s_sm.inc(e.dma_start(a2s[:], a2t.ap()), 16)
        s_sm.inc(e.dma_start(gbs[:], gbt.ap().rearrange("p a b -> p (a b)")), 16)
        s_sm.inc(e.dma_start(ubs[:], ubt.ap().rearrange("p a b -> p (a b)")), 16)
        s_sm.inc(e.dma_start(das[:], dat.ap().rearrange("p a b -> p (a b)")), 16)
        s_sm.inc(e.dma_start(dbs[:], dbt.ap().rearrange("p a b -> p (a b)")), 16)
    p.pool(_)

    def _(e):
        e.memset(ones_bf[:], 1.0)
        e.memset(eps1[:], EPS_LN)
        s_dve.inc(e.memset(ones_f[:], 1.0))
    p.dve(_)

    emit_layernorm(p, xv, xn, xf, sq, t32, rows, ones_bf, ones_f, eps1,
                   psA, psB, psC, psD, s_xs, s_pe, s_act, s_dve)

    def _(e):
        s_dve.wait(e)
        s_sm.wait(e)
        for tc in range(2):
            for kc in range(KC):
                e.matmul(psA[0:16, tc * 512:(tc + 1) * 512],
                         a2s[:, kc, 0:16],
                         xn[:, kc * T + tc * 512:kc * T + (tc + 1) * 512],
                         start=(kc == 0), stop=(kc == KC - 1))
            for kc in range(KC):
                mm = e.matmul(psB[0:16, tc * 512:(tc + 1) * 512],
                              a2s[:, kc, 16:32],
                              xn[:, kc * T + tc * 512:kc * T + (tc + 1) * 512],
                              start=(kc == 0), stop=(kc == KC - 1))
        s_pe.inc(mm)
    p.te(_)

    def _(e):
        s_pe.wait(e)
        e.copy(tg[:], psA[0:16, :])
        s_act.inc(e.copy(tu[:], psB[0:16, :]))
    p.act(_)
    tgtu_tok = s_act.val

    pe_hb_tok = [0] * HB
    dve_hb_tok = [0] * HB
    ho_tok = [0] * HB
    silu_tok = [0] * HB
    ucast_tok = [0] * HB
    g_tok = [0] * HB
    u_tok = [0] * HB

    def emit_slab_load(hb):
        sl = hb % 2

        def _(e, hb=hb, sl=sl):
            if hb >= 2:
                s_pe.wait(e, pe_hb_tok[hb - 2])
            g_tok[hb] = s_gs[sl].inc(e.dma_start(
                gslb[:, sl, :], gsl.ap()[hb].rearrange("p a b -> p (a b)"),
                max_dma_last_dim=8192), 16)
            u_tok[hb] = s_us[sl].inc(e.dma_start(
                uslb[:, sl, :], usl.ap()[hb].rearrange("p a b -> p (a b)"),
                max_dma_last_dim=8192), 16)
        p.pool(_)

    emit_slab_load(0)
    emit_slab_load(1)

    for hb in range(HB):
        sl = hb % 2
        psG = psA if sl == 0 else psB

        def _(e, hb=hb, sl=sl, psG=psG):
            s_gs[sl].wait(e, g_tok[hb])
            s_us[sl].wait(e, u_tok[hb])
            if hb == 0:
                s_act.wait(e, tgtu_tok)
            if hb >= 2:
                s_act.wait(e, silu_tok[hb - 2])
            for tc in range(2):
                for kc in range(KC):
                    e.matmul(psG[:, tc * 512:(tc + 1) * 512],
                             gslb[:, sl, kc * 128:(kc + 1) * 128],
                             xn[:, kc * T + tc * 512:kc * T + (tc + 1) * 512],
                             start=(kc == 0), stop=False)
                e.matmul(psG[:, tc * 512:(tc + 1) * 512],
                         gbs[0:16, hb * 128:(hb + 1) * 128],
                         tg[0:16, tc * 512:(tc + 1) * 512],
                         start=False, stop=True)
            if hb > 0:
                s_dve.wait(e, dve_hb_tok[hb - 1])
                for tc in range(2):
                    e.matmul(psD[0:16, tc * 512:(tc + 1) * 512],
                             das[:, (hb - 1) * 16:hb * 16],
                             hid[:, (hb - 1) % 2, tc * 512:(tc + 1) * 512],
                             start=(hb == 1), stop=False)
            if hb >= 1:
                s_act.wait(e, ucast_tok[hb - 1])
            for tc in range(2):
                for kc in range(KC):
                    e.matmul(psC[:, tc * 512:(tc + 1) * 512],
                             uslb[:, sl, kc * 128:(kc + 1) * 128],
                             xn[:, kc * T + tc * 512:kc * T + (tc + 1) * 512],
                             start=(kc == 0), stop=False)
                mm = e.matmul(psC[:, tc * 512:(tc + 1) * 512],
                              ubs[0:16, hb * 128:(hb + 1) * 128],
                              tu[0:16, tc * 512:(tc + 1) * 512],
                              start=False, stop=True)
            pe_hb_tok[hb] = s_pe.inc(mm)
        p.te(_)

        def _(e, hb=hb, psG=psG):
            s_pe.wait(e, pe_hb_tok[hb])
            if hb >= 1:
                s_dve.wait(e, dve_hb_tok[hb - 1])
            silu_tok[hb] = s_act.inc(e.activation(silu[:], psG[:], AF.Silu))
            ucast_tok[hb] = s_act.inc(e.copy(ucast[:], psC[:]))
        p.act(_)

        def _(e, hb=hb, sl=sl):
            s_act.wait(e, ucast_tok[hb])
            if hb >= 2:
                s_hos[sl].wait(e, ho_tok[hb - 2])
            dve_hb_tok[hb] = s_dve.inc(
                e.tensor_tensor(hid[:, sl, :], silu[:], ucast[:], ALU.mult))
        p.dve(_)

        def _(e, hb=hb, sl=sl):
            s_dve.wait(e, dve_hb_tok[hb])
            ho_tok[hb] = s_hos[sl].inc(
                e.dma_start(hid_dram.ap()[hb], hid[:, sl, :]), 16)
        p.sync(_)

        if hb + 2 < HB:
            emit_slab_load(hb + 2)

    def _(e):
        s_dve.wait(e, dve_hb_tok[HB - 1])
        for tc in range(2):
            mm = e.matmul(psD[0:16, tc * 512:(tc + 1) * 512],
                          das[:, (HB - 1) * 16:HB * 16],
                          hid[:, (HB - 1) % 2, tc * 512:(tc + 1) * 512],
                          start=False, stop=True)
        s_pe.inc(mm)
    p.te(_)
    pe_A_end = s_pe.val

    def _(e):
        s_pe.wait(e, pe_A_end)
        s_act.inc(e.copy(td[:], psD[0:16, :]))
    p.act(_)
    act_A_end = s_act.val
    dve_A_end = s_dve.val

    p.close_scope(sa)
    sb = []
    hidall = p.sbuf("hidall", [128, HB * T], BF16, sb)
    dslb = p.sbuf("dslb", [128, 2, 4 * 4 * 128], BF16, sb)

    NG = 8
    groups = [list(range(g * 11, min(HB, (g + 1) * 11))) for g in range(NG)]
    grp_tok = [0] * NG

    def _(e):
        s_pe.wait(e, pe_A_end)
        s_act.wait(e, act_A_end)
        s_dve.wait(e, dve_A_end)
        for g, hbs in enumerate(groups):
            if g >= 2:
                s_hi.wait(e, grp_tok[g - 2])
            for hb in hbs:
                s_hos[hb % 2].wait(e, ho_tok[hb])
                mm = e.dma_start(hidall[:, hb * T:(hb + 1) * T], hid_dram.ap()[hb])
                s_hi.inc(mm, 16)
            grp_tok[g] = s_hi.val
    p.act(_)

    NB = 22
    pe_batch_tok = [0] * (8 * NB)
    dve_ev_tok = [0] * 8
    d_tok = [0] * (8 * NB)
    ev_tok = [0] * (8 * 4 * 2)
    out_tok = [0] * (8 * 4 * 2)
    xb_tok = [0] * (8 * 4)

    for dg in range(8):
        for ib in range(NB):
            h0 = ib * 4
            hn = 4 if ib < 21 else 2
            bi = dg * NB + ib

            def _(e, dg=dg, ib=ib, h0=h0, hn=hn, bi=bi):
                if bi >= 2:
                    s_pe.wait(e, pe_batch_tok[bi - 2])
                elif dg == 0 and ib == 0:
                    s_act.wait(e, act_A_end)
                src = dsl.ap()[dg, :, h0:h0 + hn].rearrange("p h d c -> p (h d c)")
                d_tok[bi] = s_ds[ib % 2].inc(
                    e.dma_start(dslb[:, ib % 2, 0:hn * 512], src,
                                max_dma_last_dim=8192), 16)
            p.pool(_)

            def _(e, dg=dg, ib=ib, h0=h0, hn=hn, bi=bi):
                s_ds[ib % 2].wait(e, d_tok[bi])
                gneed = (h0 + hn - 1) // 11
                s_hi.wait(e, grp_tok[gneed])
                if dg == 0 and ib == 0:
                    s_act.wait(e, act_A_end)
                if dg > 0 and ib == 0:
                    s_dve.wait(e, dve_ev_tok[dg - 1])
                for h in range(hn):
                    hb = h0 + h
                    for d in range(4):
                        for tc in range(2):
                            mm = e.matmul(
                                psbank[d][:, tc * 512:(tc + 1) * 512],
                                dslb[:, ib % 2, (h * 4 + d) * 128:(h * 4 + d + 1) * 128],
                                hidall[:, hb * T + tc * 512:hb * T + (tc + 1) * 512],
                                start=(hb == 0), stop=False)
                pe_batch_tok[bi] = s_pe.inc(mm)
            p.te(_)

        def _(e, dg=dg):
            for d in range(4):
                for tc in range(2):
                    mm = e.matmul(psbank[d][:, tc * 512:(tc + 1) * 512],
                                  dbs[0:16, (dg * 4 + d) * 128:(dg * 4 + d + 1) * 128],
                                  td[0:16, tc * 512:(tc + 1) * 512],
                                  start=False, stop=True)
            s_pe.inc(mm)
        p.te(_)
        dg_done = s_pe.val

        for d in range(4):
            di = dg * 4 + d

            def _(e, dg=dg, d=d, di=di):
                if di >= 2:
                    s_dve.wait(e, xb_tok[di - 2])
                x_t = s_xs[d % 2].inc(e.dma_start(xf[:, d % 2, :], xv[:, di, :]), 16)
                xb_tok[di] = x_t
            p.sync(_)
            xdma_tok = s_xs[d % 2].val

            def _(e, dg=dg, d=d, di=di, dg_done=dg_done, xdma_tok=xdma_tok):
                s_xs[d % 2].wait(e, xdma_tok)
                s_pe.wait(e, dg_done)
                for tc in range(2):
                    oi = di * 2 + tc
                    if oi >= 2:
                        s_os[tc].wait(e, out_tok[oi - 2])
                    ev_tok[oi] = s_dve.inc(e.scalar_tensor_tensor(
                        outsb[:, tc, :],
                        psbank[d][:, tc * 512:(tc + 1) * 512], 0.5,
                        xf[:, d % 2, tc * 512:(tc + 1) * 512],
                        ALU.mult, ALU.add))
                xb_tok[di] = s_dve.val
            p.dve(_)

            def _(e, dg=dg, d=d, di=di):
                for tc in range(2):
                    oi = di * 2 + tc
                    s_dve.wait(e, ev_tok[oi])
                    out_tok[oi] = s_os[tc].inc(
                        e.dma_start(ov[:, di, tc * 512:(tc + 1) * 512],
                                    outsb[:, tc, :]), 16)
            p.sync(_)
        dve_ev_tok[dg] = s_dve.val

    def _(e):
        for tc in range(2):
            s_os[tc].wait(e)
    p.sync(_)

    p.close_scope(sb)
    emit_barrier(p)
    p.pop_scope()


def build_l0ffn_program():
    p = Prog()
    D = declare_l0ffn_io(p.nc)
    D["x_in"] = p.nc.dram_tensor("x_in", [DM, T], F32, kind="ExternalInput")
    D["x_out"] = p.nc.dram_tensor("x_out", [DM, T], F32, kind="ExternalOutput")
    p.open_block()
    emit_l0ffn(p, D, D["x_in"], D["x_out"])
    p.close()
    return p



_prog_cache = {}


def _get_prog(name, builder):
    if name not in _prog_cache:
        _prog_cache[name] = builder()
    return _prog_cache[name]


def _x_to_shards(x):
    out = []
    for c in range(NCORES):
        b, h = c // 2, c % 2
        out.append(np.ascontiguousarray(x[b, h * T:(h + 1) * T, :].T))
    return out


def _shards_to_x(shards):
    x = np.empty((4, 2048, DM), np.float32)
    for c in range(NCORES):
        b, h = c // 2, c % 2
        x[b, h * T:(h + 1) * T, :] = shards[c].T
    return x


def run_l0ffn(x, w):
    p = _get_prog("l0ffn", build_l0ffn_program)
    shards = _x_to_shards(x)
    in_maps = [{"x_in": shards[c], **w} for c in range(NCORES)]
    res = run_bass_kernel_spmd(p.nc, in_maps, core_ids=list(range(NCORES)))
    return _shards_to_x([res.results[c]["x_out"] for c in range(NCORES)])


def build_fused_program():
    p = Prog()
    nc = p.nc
    Df0 = declare_l0ffn_io(nc)
    Da0 = declare_l0attn_io(nc)
    Df1 = declare_l1ffn_io(nc)
    Da1 = declare_l1attn_io(nc)
    x_in = nc.dram_tensor("x_in", [DM, T], F32, kind="ExternalInput")
    x_out = nc.dram_tensor("x_out", [DM, T], F32, kind="ExternalOutput")
    bufs = [nc.dram_tensor(f"xbuf{i}", [DM, T], F32) for i in range(5)]
    p.open_block()
    emit_l0ffn(p, Df0, x_in, bufs[0])
    emit_l0attn(p, Da0, bufs[0], bufs[1])
    emit_l0ffn(p, Df0, bufs[1], bufs[2])
    emit_l1ffn(p, Df1, bufs[2], bufs[3])
    emit_l1attn(p, Da1, bufs[3], bufs[4])
    emit_l1ffn(p, Df1, bufs[4], x_out)
    p.close()
    return p


def prep_all(inputs):
    return {**prep_l0ffn(inputs), **prep_l0attn(inputs),
            **prep_l1ffn(inputs), **prep_l1attn(inputs)}


def kernel(**inputs):
    x = np.asarray(inputs["L0_attn_x"], np.float32)
    w = prep_all(inputs)
    p = _get_prog("fused", build_fused_program)
    shards = _x_to_shards(x)
    in_maps = [{"x_in": shards[c], **attn_meta(c), **l1attn_meta(c), **w}
               for c in range(NCORES)]
    res = run_bass_kernel_spmd(p.nc, in_maps, core_ids=list(range(NCORES)))
    return _shards_to_x([res.results[c]["x_out"] for c in range(NCORES)])


def kernel_unfused(**inputs):
    x = np.asarray(inputs["L0_attn_x"], np.float32)
    w_f0 = prep_l0ffn(inputs)
    w_a0 = prep_l0attn(inputs)
    w_f1 = prep_l1ffn(inputs)
    w_a1 = prep_l1attn(inputs)
    x = run_l0ffn(x, w_f0)
    x = run_l0attn(x, w_a0)
    x = run_l0ffn(x, w_f0)
    x = run_l1ffn(x, w_f1)
    x = run_l1attn(x, w_a1)
    x = run_l1ffn(x, w_f1)
    return x



TF = 2048
NH = 32
NKV = 8
HD = 128
EPS_RMS = 1e-6


def _alibi_slopes(n):
    import math
    start = 2 ** (-2 ** (-(math.log2(n) - 3)))
    return [start * start ** i for i in range(n)]


def prep_l0attn(inp):
    d = {}
    d["wqT"] = _lhsT_slabs(np.asarray(inp["L0_attn_Wq"], np.float32))
    d["wkT"] = _lhsT_slabs(np.asarray(inp["L0_attn_Wk"], np.float32))
    d["woT"] = _lhsT_slabs(np.asarray(inp["L0_attn_Wo"], np.float32))
    d["wvt"] = np.ascontiguousarray(
        np.asarray(inp["L0_attn_Wv"], np.float32).T.reshape(KC, 128, 1024))
    d["qnw"] = np.ascontiguousarray(np.asarray(inp["L0_attn_q_norm_w"], np.float32).reshape(128, 1))
    d["knw"] = np.ascontiguousarray(np.asarray(inp["L0_attn_k_norm_w"], np.float32).reshape(128, 1))
    return d






DEBUG_ATTN = False


def declare_l0attn_io(nc):
    D = {}
    D["wqT"] = nc.dram_tensor("wqT", [32, 128, KC, 128], F32, kind="ExternalInput")
    D["wkT"] = nc.dram_tensor("wkT", [8, 128, KC, 128], F32, kind="ExternalInput")
    D["woT"] = nc.dram_tensor("woT", [32, 128, KC, 128], F32, kind="ExternalInput")
    D["wvt"] = nc.dram_tensor("wvt", [KC, 128, 1024], F32, kind="ExternalInput")
    D["qnw"] = nc.dram_tensor("qnw", [128, 1], F32, kind="ExternalInput")
    D["knw"] = nc.dram_tensor("knw", [128, 1], F32, kind="ExternalInput")
    D["ownoff"] = nc.dram_tensor("ownoff", [128, 1], F32, kind="ExternalInput")
    D["pairidx"] = nc.dram_tensor("pairidx", [1, 1], mybir.dt.int32, kind="ExternalInput")
    D["cc_kv_in"] = nc.dram_tensor("cc_kv_in", [128, 16384], BF16)
    D["cc_kv_out"] = nc.dram_tensor("cc_kv_out", [8 * 128, 16384], BF16)
    return D


def emit_l0attn(p, D, x_in_h, x_out_h):
    import math
    nc = p.nc
    wqT = D["wqT"]; wkT = D["wkT"]; woT = D["woT"]; wvt = D["wvt"]
    qnw = D["qnw"]; knw = D["knw"]; ownoff = D["ownoff"]
    pairidx = D["pairidx"]; cc_kv_in = D["cc_kv_in"]; cc_kv_out = D["cc_kv_out"]
    x_own = x_in_h
    x_out = x_out_h
    _scope = p.push_scope()
    xov = x_own.ap().rearrange("(c p) t -> p c t", p=128)
    ov = x_out.ap().rearrange("(c p) t -> p c t", p=128)

    slopes = _alibi_slopes(NH)
    SC = 1.0 / math.sqrt(HD)

    xf = p.sbuf("xf", [128, 2, T], F32)
    ones_bf = p.sbuf("ones_bf", [128, 1], BF16)
    ones_f = p.sbuf("ones_f", [1, 128], F32)
    eps1 = p.sbuf("eps1", [1, 1], F32)
    epsr = p.sbuf("epsr", [1, 1], F32)
    wcol = p.sbuf("wcol", [128, 1], F32)
    knw_c = p.sbuf("knw_c", [128, 1], F32)
    ofcol = p.sbuf("ofcol", [128, 1], F32)
    wsl = p.sbuf("wsl", [128, 2, KC * 128], BF16)
    QO = p.sbuf("QO", [128, NH * T], BF16)

    s_xs = [p.sem("s_x0"), p.sem("s_x1")]
    s_ws = [p.sem("s_w0"), p.sem("s_w1")]
    s_vs = [p.sem("s_v0"), p.sem("s_v1")]
    s_sm = p.sem("s_sm")
    s_os = [p.sem("s_o0"), p.sem("s_o1")]
    s_pe = p.sem("s_pe")
    s_act = p.sem("s_act")
    s_dve = p.sem("s_dve")
    s_po = p.sem("s_po")
    s_cc = p.sem("s_cc")

    def _(e):
        s_sm.inc(e.dma_start(wcol[:], qnw.ap()), 16)
        s_sm.inc(e.dma_start(knw_c[:], knw.ap()), 16)
        s_sm.inc(e.dma_start(ofcol[:], ownoff.ap()), 16)
    p.pool(_)

    def _(e):
        e.memset(ones_bf[:], 1.0)
        e.memset(eps1[:], EPS_LN)
        e.memset(epsr[:], EPS_RMS)
        s_dve.inc(e.memset(ones_f[:], 1.0))
        s_sm.wait(e)
        s_dve.inc(e.tensor_tensor(wcol[:], wcol[:], knw_c[:], ALU.mult))
        s_dve.wait(e)
        s_dve.inc(e.tensor_scalar(wcol[:], wcol[:], SC, None, ALU.mult))
    p.dve(_)

    sa = []
    sa_q = []
    xn_own = p.sbuf("xn_own", [128, KC * T], BF16, sa)
    sq = p.sbuf("sq", [128, 2, T], BF16, sa_q)
    t32 = p.sbuf("t32", [128, T], F32, sa_q)
    rows = p.sbuf("rows", [1, 4 * T], F32, sa_q)
    sqq = p.sbuf("sqq", [128, T], BF16, sa_q)
    bcq = p.sbuf("bcq", [128, T], F32, sa_q)

    pscope_a = []
    for _nm in ("psA", "psB", "psC", "psD"):
        p._uid += 1
        g = nc.psum_tensor(f"{_nm}_u{p._uid}", [128, T], F32)
        pscope_a.append((g, g.__enter__()))
    psA, psB, psC, psD = [t for (_g, t) in pscope_a]

    emit_layernorm(p, xov, xn_own, xf, sq, t32, rows, ones_bf, ones_f,
                   eps1, psA, psB, psC, psD, s_xs, s_pe, s_act, s_dve)
    ln_own_tok = s_dve.val

    w_tok = [0] * 64
    pe_q_tok = [0] * NH
    dve_q_tok = [0] * NH

    def emit_wslab(idx, src_ap, guard_tok=None):
        sl = idx % 2

        def _(e, idx=idx, sl=sl, guard_tok=guard_tok):
            if guard_tok is not None:
                s_pe.wait(e, guard_tok)
            w_tok[idx] = s_ws[sl].inc(e.dma_start(
                wsl[:, sl, :], src_ap.rearrange("p a b -> p (a b)"),
                max_dma_last_dim=8192), 16)
        p.pool(_)

    emit_wslab(0, wqT.ap()[0])
    emit_wslab(1, wqT.ap()[1])

    for h in range(NH):
        sl = h % 2
        psQ = psA if sl == 0 else psB

        def _(e, h=h, sl=sl, psQ=psQ):
            s_ws[sl].wait(e, w_tok[h])
            if h == 0:
                s_dve.wait(e, ln_own_tok)
            if h >= 2:
                s_dve.wait(e, dve_q_tok[h - 2])
            for tc in range(2):
                for kc in range(KC):
                    mm = e.matmul(psQ[:, tc * 512:(tc + 1) * 512],
                                  wsl[:, sl, kc * 128:(kc + 1) * 128],
                                  xn_own[:, kc * T + tc * 512:kc * T + (tc + 1) * 512],
                                  start=(kc == 0), stop=(kc == KC - 1))
            s_pe.inc(mm)
        p.te(_)
        pe_q1 = s_pe.val

        def _(e, pe_q1=pe_q1, psQ=psQ):
            s_pe.wait(e, pe_q1)
            s_act.inc(e.square(sqq[:], psQ[:]))
        p.act(_)
        act_sq = s_act.val

        def _(e, act_sq=act_sq):
            s_act.wait(e, act_sq)
            for tc in range(2):
                mm = e.matmul(psD[0:1, tc * 512:(tc + 1) * 512], ones_bf[:],
                              sqq[:, tc * 512:(tc + 1) * 512], start=True, stop=True)
            s_pe.inc(mm)
        p.te(_)
        pe_q2 = s_pe.val

        def _(e, pe_q2=pe_q2):
            s_pe.wait(e, pe_q2)
            s_dve.inc(e.tensor_scalar(rows[0:1, 0:T], psD[0:1, 0:T], 1.0 / HD,
                                      None, ALU.mult))
        p.dve(_)
        dve_r = s_dve.val

        def _(e, dve_r=dve_r):
            s_dve.wait(e, dve_r)
            s_act.inc(e.activation(rows[0:1, 0:T], rows[0:1, 0:T], AF.Sqrt,
                                   bias=epsr[0:1, :], scale=1.0))
        p.act(_)
        act_r = s_act.val

        def _(e, act_r=act_r):
            s_act.wait(e, act_r)
            s_dve.inc(e.reciprocal(rows[0:1, 0:T], rows[0:1, 0:T]))
        p.dve(_)
        dve_r2 = s_dve.val

        def _(e, h=h, dve_r2=dve_r2):
            s_dve.wait(e, dve_r2)
            for tc in range(2):
                mm = e.matmul(psC[:, tc * 512:(tc + 1) * 512], ones_f[:],
                              rows[0:1, tc * 512:(tc + 1) * 512], start=True, stop=True)
            pe_q_tok[h] = s_pe.inc(mm)
        p.te(_)
        pe_q3 = s_pe.val

        def _(e, pe_q3=pe_q3):
            s_pe.wait(e, pe_q3)
            s_act.inc(e.copy(bcq[:], psC[:]))
        p.act(_)
        act_bc = s_act.val

        def _(e, h=h, psQ=psQ, act_bc=act_bc):
            s_act.wait(e, act_bc)
            dve_q_tok[h] = s_dve.inc(e.scalar_tensor_tensor(
                QO[:, h * T:(h + 1) * T], psQ[:], wcol[:], bcq[:],
                ALU.mult, ALU.mult))
        p.dve(_)

        if h + 2 < NH:
            emit_wslab(h + 2, wqT.ap()[h + 2], pe_q_tok[h])

    q_done_dve = s_dve.val

    p.close_scope(sa_q)
    kvb = p.sbuf("kvb", [128, 16384], BF16, sa)
    sqk = p.sbuf("sqk", [128, 512], BF16, sa)
    bck = p.sbuf("bck", [128, 512], F32, sa)
    rows2 = bck[0:1, 0:512]
    vsl = p.sbuf("vsl", [128, 2, 512], BF16, sa)
    pidx = p.sbuf("pidx", [1, 1], mybir.dt.int32, sa)

    wk_tok = [0] * 16
    pe_k1_tok = [0] * 16
    pe_k2_tok = [0] * 16
    pe_k3_tok = [0] * 16
    act_k_tok = [0] * 16
    act_k2_tok = [0] * 16
    act_k3_tok = [0] * 16
    dve_k1_tok = [0] * 16
    dve_k_tok = [0] * 16
    v_tok = [0] * 2
    pe_v_tok = [0] * (2 * KC)
    psV = [psA, psB, psC, psD]

    for tcol in range(2):
        for kvh in range(NKV):
            kk = tcol * NKV + kvh
            sl = kk % 2
            psK = psA if sl == 0 else psB

            def _(e, kk=kk, kvh=kvh, sl=sl):
                if kk >= 2:
                    s_pe.wait(e, pe_k1_tok[kk - 2])
                wk_tok[kk] = s_ws[sl].inc(e.dma_start(
                    wsl[:, sl, :], wkT.ap()[kvh].rearrange("p a b -> p (a b)"),
                    max_dma_last_dim=8192), 16)
            p.pool(_)

            def _(e, kk=kk, tcol=tcol, sl=sl, psK=psK,
                  q_done_dve=q_done_dve, actg=s_act.val):
                s_ws[sl].wait(e, wk_tok[kk])
                if kk == 0:
                    s_dve.wait(e, q_done_dve)
                    s_act.wait(e, actg)
                if kk >= 2:
                    s_dve.wait(e, dve_k_tok[kk - 2])
                for kc in range(KC):
                    mm = e.matmul(psK[:, 0:512],
                                  wsl[:, sl, kc * 128:(kc + 1) * 128],
                                  xn_own[:, kc * T + tcol * 512:kc * T + (tcol + 1) * 512],
                                  start=(kc == 0), stop=(kc == KC - 1))
                pe_k1_tok[kk] = s_pe.inc(mm)
            p.te(_)

            def _(e, kk=kk, psK=psK):
                s_pe.wait(e, pe_k1_tok[kk])
                if kk >= 1:
                    s_pe.wait(e, pe_k2_tok[kk - 1])
                act_k_tok[kk] = s_act.inc(e.square(sqk[:], psK[:, 0:512]))
            p.act(_)

            def _(e, kk=kk):
                s_act.wait(e, act_k_tok[kk])
                if kk >= 1:
                    s_dve.wait(e, dve_k1_tok[kk - 1])
                mm = e.matmul(psC[0:1, 0:512], ones_bf[:], sqk[:], start=True, stop=True)
                pe_k2_tok[kk] = s_pe.inc(mm)
            p.te(_)

            def _(e, kk=kk):
                s_pe.wait(e, pe_k2_tok[kk])
                dve_k1_tok[kk] = s_dve.inc(e.tensor_scalar(
                    rows2, psC[0:1, 0:512], 1.0 / HD, None, ALU.mult))
            p.dve(_)

            def _(e, kk=kk):
                s_dve.wait(e, dve_k1_tok[kk])
                act_k2_tok[kk] = s_act.inc(e.activation(
                    rows2, rows2, AF.Sqrt, bias=epsr[0:1, :], scale=1.0))
            p.act(_)

            def _(e, kk=kk):
                s_act.wait(e, act_k2_tok[kk])
                s_dve.inc(e.reciprocal(rows2, rows2))
            p.dve(_)
            dve_k2 = s_dve.val

            def _(e, kk=kk, dve_k2=dve_k2):
                s_dve.wait(e, dve_k2)
                if kk >= 1:
                    s_act.wait(e, act_k3_tok[kk - 1])
                mm = e.matmul(psD[:, 0:512], ones_f[:], rows2,
                              start=True, stop=True)
                pe_k3_tok[kk] = s_pe.inc(mm)
            p.te(_)

            def _(e, kk=kk):
                s_pe.wait(e, pe_k3_tok[kk])
                act_k3_tok[kk] = s_act.inc(e.copy(bck[:], psD[:, 0:512]))
            p.act(_)

            def _(e, kk=kk, kvh=kvh, tcol=tcol, psK=psK):
                s_act.wait(e, act_k3_tok[kk])
                dve_k_tok[kk] = s_dve.inc(e.tensor_tensor(
                    kvb[:, kvh * 1024 + tcol * 512:kvh * 1024 + (tcol + 1) * 512],
                    psK[:, 0:512], bck[:], ALU.mult))
            p.dve(_)

        k_done_tok = dve_k_tok[tcol * NKV + NKV - 1]
        k_done_act = act_k3_tok[tcol * NKV + NKV - 1]

        for vidx in range(2 * KC):
            kc, nn = vidx // 2, vidx % 2
            sl = vidx % 2

            def _(e, vidx=vidx, kc=kc, nn=nn, sl=sl):
                if vidx >= 2:
                    s_pe.wait(e, pe_v_tok[vidx - 2])
                v_tok[vidx % 2] = s_vs[sl].inc(e.dma_start(
                    vsl[:, sl, :], wvt.ap()[kc][:, nn * 512:(nn + 1) * 512],
                    max_dma_last_dim=8192), 16)
            p.pool(_)
            vdma = s_vs[sl].val

            def _(e, vidx=vidx, kc=kc, nn=nn, sl=sl, tcol=tcol, vdma=vdma,
                  k_done_tok=k_done_tok, k_done_act=k_done_act):
                s_vs[sl].wait(e, vdma)
                if vidx == 0:
                    s_dve.wait(e, k_done_tok)
                    s_act.wait(e, k_done_act)
                for mt in range(4):
                    mm = e.matmul(
                        psV[mt][:, nn * 512:(nn + 1) * 512],
                        xn_own[:, kc * T + tcol * 512 + mt * 128:kc * T + tcol * 512 + (mt + 1) * 128],
                        vsl[:, sl, :],
                        start=(kc == 0), stop=(kc == KC - 1))
                pe_v_tok[vidx] = s_pe.inc(mm)
            p.te(_)
        pe_v_done = s_pe.val

        def _(e, tcol=tcol, pe_v_done=pe_v_done):
            s_pe.wait(e, pe_v_done)
            for mt in range(4):
                s_dve.inc(e.tensor_copy(
                    kvb[:, 8192 + (tcol * 4 + mt) * 1024:8192 + (tcol * 4 + mt + 1) * 1024],
                    psV[mt][:]))
        p.dve(_)
    kv_own_done = s_dve.val

    def _(e, kv_own_done=kv_own_done):
        s_sm.inc(e.dma_start(pidx[:], pairidx.ap()), 16)
        s_dve.wait(e, kv_own_done)
        s_sm.inc(e.dma_start(cc_kv_in.ap(), kvb[:], max_dma_last_dim=8192), 16)
        s_sm.wait(e)
        e.collective_compute(
            "AllGather", ALU.bypass,
            replica_groups=[[0, 1, 2, 3, 4, 5, 6, 7]],
            ins=[cc_kv_in.ap().opt()],
            outs=[cc_kv_out.ap().opt()],
        ).then_inc(s_cc.h, 1)
        s_cc.val += 1
    p.pool(_)
    cc_tok = s_cc.val

    p.close_scope(sa)
    sbp = []
    Ksb = p.sbuf("Ksb", [128, NKV * TF], BF16, sbp)
    Vsb = p.sbuf("Vsb", [128, 16 * 1024], BF16, sbp)

    r0 = nc.gpsimd.register("r_kv")
    r0h = r0.__enter__()
    _scope.append(r0)
    r1 = nc.gpsimd.register("r_kv2")
    r1h = r1.__enter__()
    _scope.append(r1)

    def _(e, cc_tok=cc_tok):
        e.wait_ge(s_cc.h, cc_tok)
        e.reg_load(r0h, pidx[:1, :1])
        e.reg_mul(r0h, r0h, 128 * 16384)
        for j in range(2):
            e.reg_add(r1h, r0h, 8192)
            s_sm.inc(e.dma_start(
                bass.AP(Ksb, j * 1024, [[NKV * TF, 128], [2048, NKV], [1, 1024]]),
                bass.AP(cc_kv_out, r0h, [[16384, 128], [1024, NKV], [1, 1024]])), 16)
            s_sm.inc(e.dma_start(
                bass.AP(Vsb, j * 8 * 1024, [[16 * 1024, 128], [1024, 8], [1, 1024]]),
                bass.AP(cc_kv_out, r1h, [[16384, 128], [1024, 8], [1, 1024]])), 16)
            if j == 0:
                e.reg_add(r0h, r0h, 128 * 16384)
    p.pool(_)
    kv_ready_tok = s_sm.val

    for (g, _t) in reversed(pscope_a):
        g.__exit__(None, None, None)

    sc3 = []
    iotaS = p.sbuf("iotaS", [128, TF], F32, sc3)
    idn = p.sbuf("idn", [128, 128], BF16, sc3)
    Rbuf = p.sbuf("Rbuf", [128, TF], F32, sc3)
    c0col = p.sbuf("c0col", [128, 1], F32, sc3)
    tb = p.sbuf("tb", [128, 2, TF], F32, sc3)
    pbf = p.sbuf("pbf", [128, 2, TF], BF16, sc3)
    pTsb = p.sbuf("pTsb", [128, 2, TF], BF16, sc3)
    mrow = p.sbuf("mrow", [128, 2, 4], F32, sc3)
    negm = p.sbuf("negm", [128, 2, 1], F32, sc3)
    lrow = p.sbuf("lrow", [128, 2, 1], F32, sc3)
    linv = p.sbuf("linv", [128, 2, 1], F32, sc3)
    outsb = p.sbuf("outsb", [128, 2, 512], F32, sc3)

    pscope_b = []
    for _nm, _sh, _dt in (("psS", [128, 2, 512], F32), ("psAV", [128, 512], F32),
                          ("psTT", [128, 512], BF16), ("psO", [128, 2, 1024], F32)):
        p._uid += 1
        g = nc.psum_tensor(f"{_nm}_u{p._uid}", _sh, _dt)
        pscope_b.append((g, g.__enter__()))
    psS = pscope_b[0][1]; psAV = pscope_b[1][1]
    psTT = pscope_b[2][1]; psO = pscope_b[3][1]

    def _(e):
        s_po.inc(e.iota(iotaS[:], pattern=[[1, TF]], base=0, channel_multiplier=-1,
                        allow_small_or_imprecise_dtypes=True))
    p.pool(_)
    iota_tok = s_po.val

    def _(e):
        s_po.wait(e, iota_tok)
        s_dve.inc(e.tensor_scalar(idn[:], iotaS[:, 0:128], 0.0, None, ALU.is_equal))
    p.dve(_)
    qresc_tok = s_dve.val

    NIT = 8 * NH
    sc_gc_tok = {}
    ttr_gc_tok = {-2: 0, -1: 0}
    tr_tok = {}
    trevac_tok = {}
    exp_tok = [0] * NIT
    recip_tok = [0] * NIT
    norm_tok = [0] * NIT
    av_tok = [0] * NIT
    oevac_tok = [0] * NIT
    rbuf_tok = [0] * 8
    GC = [0]

    def it_params(it):
        qt, head = it // NH, it % NH
        n128 = 9 + qt
        sspan = n128 * 128
        nch = (sspan + 511) // 512
        return qt, head, n128, sspan, nch

    for it in range(NIT + 1):
        if it < NIT:
            qt, head, n128, sspan, nch = it_params(it)
            sl = it % 2
            kvh = head // 4

            if head == 0:
                def _(e, qt=qt, it=it):
                    if qt == 0:
                        s_dve.wait(e, qresc_tok)
                    if it >= 2:
                        s_act.wait(e, exp_tok[it - 2])
                    s_dve.inc(e.tensor_scalar(c0col[:], ofcol[:], float(128 * qt),
                                              None, ALU.add))
                    s_dve.wait(e)
                    e.tensor_scalar(Rbuf[:], iotaS[:], c0col[:], None, ALU.subtract)
                    e.tensor_scalar(tb[:, 0, :], Rbuf[:], 0.0, -1.3e33,
                                    ALU.is_gt, ALU.mult)
                    s_dve.inc(e.tensor_tensor(Rbuf[:], Rbuf[:], tb[:, 0, :], ALU.add))
                    rbuf_tok[qt] = s_dve.val
                p.dve(_)

            for ch in range(nch):
                gc = GC[0]
                n = min(512, sspan - ch * 512)

                def _(e, it=it, qt=qt, head=head, kvh=kvh, ch=ch, gc=gc, n=n):
                    if it == 0 and ch == 0:
                        s_sm.wait(e, kv_ready_tok)
                    s_dve.wait(e, ttr_gc_tok[gc - 2])
                    mm = e.matmul(psS[:, gc % 2, 0:n],
                                  QO[:, head * T + qt * 128:head * T + (qt + 1) * 128],
                                  Ksb[:, kvh * TF + ch * 512:kvh * TF + ch * 512 + n],
                                  start=True, stop=True)
                    sc_gc_tok[gc] = s_pe.inc(mm)
                p.te(_)

                def _(e, it=it, qt=qt, head=head, ch=ch, gc=gc, n=n, sl=sl):
                    s_pe.wait(e, sc_gc_tok[gc])
                    if ch == 0:
                        s_dve.wait(e, rbuf_tok[qt])
                        if it >= 2:
                            s_act.wait(e, exp_tok[it - 2])
                    e.scalar_tensor_tensor(
                        tb[:, sl, ch * 512:ch * 512 + n],
                        Rbuf[:, ch * 512:ch * 512 + n], slopes[head],
                        psS[:, gc % 2, 0:n], ALU.mult, ALU.add)
                    ttr_gc_tok[gc] = s_dve.inc(e.tensor_reduce(
                        mrow[:, sl, ch:ch + 1], tb[:, sl, ch * 512:ch * 512 + n],
                        AX.X, ALU.max))
                p.dve(_)
                GC[0] += 1

            def _(e, it=it, sl=sl, nch=nch):
                s_dve.wait(e)
                s_dve.inc(e.tensor_reduce(negm[:, sl, :], mrow[:, sl, 0:nch],
                                          AX.X, ALU.max))
                s_dve.wait(e)
                s_dve.inc(e.tensor_scalar(negm[:, sl, :], negm[:, sl, :], -1.0,
                                          None, ALU.mult))
            p.dve(_)
            negm_tok = s_dve.val

            def _(e, it=it, sl=sl, sspan=sspan, negm_tok=negm_tok):
                s_dve.wait(e, negm_tok)
                if it >= 2:
                    s_pe.wait(e, av_tok[it - 2])
                exp_tok[it] = s_act.inc(e.activation(
                    pbf[:, sl, 0:sspan], tb[:, sl, 0:sspan], AF.Exp,
                    bias=negm[:, sl, :], scale=1.0, accum_out=lrow[:, sl, :]))
            p.act(_)

            def _(e, it=it, sl=sl):
                s_act.wait(e, exp_tok[it])
                recip_tok[it] = s_dve.inc(e.reciprocal(linv[:, sl, :], lrow[:, sl, :]))
            p.dve(_)

            def _(e, it=it, sl=sl, sspan=sspan):
                s_dve.wait(e, recip_tok[it])
                norm_tok[it] = s_po.inc(e.tensor_scalar(
                    pbf[:, sl, 0:sspan], pbf[:, sl, 0:sspan], linv[:, sl, :],
                    None, ALU.mult))
                if DEBUG_ATTN and it == 33:
                    s_sm.inc(e.dma_start(dbg_p.ap()[:, 0:sspan], pbf[:, sl, 0:sspan],
                                         max_dma_last_dim=4096), 16)
            p.pool(_)

        if it >= 1:
            jt = it - 1
            qt1, head1, n128_1, sspan1, nch1 = it_params(jt)
            sl1 = jt % 2
            kvh1 = head1 // 4

            for j in range(n128_1):
                def _(e, jt=jt, j=j, sl1=sl1, kvh1=kvh1, n128_1=n128_1):
                    if j == 0:
                        s_po.wait(e, norm_tok[jt])
                        if jt >= 1:
                            s_act.wait(e, oevac_tok[jt - 1])
                    if (jt, j - 4) in trevac_tok:
                        s_act.wait(e, trevac_tok[(jt, j - 4)])
                    tr = e.transpose(psTT[:, (j % 4) * 128:(j % 4 + 1) * 128],
                                     pbf[:, sl1, j * 128:(j + 1) * 128], idn[:])
                    tr_tok[(jt, j)] = s_pe.inc(tr)
                    if j >= 1:
                        s_act.wait(e, trevac_tok[(jt, j - 1)])
                        mm = e.matmul(
                            psAV[:, 0:128],
                            Vsb[:, (j - 1) * 1024 + kvh1 * 128:(j - 1) * 1024 + (kvh1 + 1) * 128],
                            pTsb[:, sl1, (j - 1) * 128:j * 128],
                            start=(j == 1), stop=False)
                        s_pe.inc(mm)
                p.te(_)

                def _(e, jt=jt, j=j, sl1=sl1):
                    s_pe.wait(e, tr_tok[(jt, j)])
                    trevac_tok[(jt, j)] = s_act.inc(e.copy(
                        pTsb[:, sl1, j * 128:(j + 1) * 128],
                        psTT[:, (j % 4) * 128:(j % 4 + 1) * 128]))
                p.act(_)

            def _(e, jt=jt, sl1=sl1, kvh1=kvh1, n128_1=n128_1):
                j = n128_1 - 1
                s_act.wait(e, trevac_tok[(jt, j)])
                mm = e.matmul(
                    psAV[:, 0:128],
                    Vsb[:, j * 1024 + kvh1 * 128:j * 1024 + (kvh1 + 1) * 128],
                    pTsb[:, sl1, j * 128:(j + 1) * 128],
                    start=False, stop=True)
                av_tok[jt] = s_pe.inc(mm)
            p.te(_)

            def _(e, jt=jt, qt1=qt1, head1=head1):
                s_pe.wait(e, av_tok[jt])
                oevac_tok[jt] = s_act.inc(e.copy(
                    QO[:, head1 * T + qt1 * 128:head1 * T + (qt1 + 1) * 128],
                    psAV[:, 0:128]))
            p.act(_)

    scores_done_act = s_act.val
    if DEBUG_ATTN:
        def _(e):
            s_act.wait(e, scores_done_act)
            s_sm.inc(e.dma_start(dbg_o.ap(), QO[:], max_dma_last_dim=4096), 16)
        p.pool(_)

    wo_tok = [0] * KC
    pe_wo_tok = [0] * KC
    dve_wo_tok = [0] * KC
    out_tok = [0] * (KC * 2)
    xwo_tok = [0] * KC

    for mb in range(KC):
        sl = mb % 2
        psW = psO[:, sl, :]

        def _(e, mb=mb, sl=sl):
            if mb >= 2:
                s_pe.wait(e, pe_wo_tok[mb - 2])
            wo_tok[mb] = s_ws[sl].inc(e.dma_start(
                wsl[:, sl, :], woT.ap()[mb].rearrange("p a b -> p (a b)"),
                max_dma_last_dim=8192), 16)
        p.pool(_)

        def _(e, mb=mb, sl=sl, psW=psW):
            s_ws[sl].wait(e, wo_tok[mb])
            if mb == 0:
                s_act.wait(e, scores_done_act)
            if mb >= 2:
                s_dve.wait(e, dve_wo_tok[mb - 2])
            for tc in range(2):
                for kc in range(KC):
                    mm = e.matmul(psW[:, tc * 512:(tc + 1) * 512],
                                  wsl[:, sl, kc * 128:(kc + 1) * 128],
                                  QO[:, kc * T + tc * 512:kc * T + (tc + 1) * 512],
                                  start=(kc == 0), stop=(kc == KC - 1))
            pe_wo_tok[mb] = s_pe.inc(mm)
        p.te(_)

        def _(e, mb=mb, sl=sl, dvg=s_dve.val):
            if mb >= 2:
                s_dve.wait(e, dve_wo_tok[mb - 2])
            else:
                s_dve.wait(e, dvg)
            xwo_tok[mb] = s_xs[sl].inc(e.dma_start(xf[:, sl, :], xov[:, mb, :]), 16)
        p.sync(_)

        def _(e, mb=mb, sl=sl, psW=psW):
            s_xs[sl].wait(e, xwo_tok[mb])
            s_pe.wait(e, pe_wo_tok[mb])
            for tc in range(2):
                oi = mb * 2 + tc
                if oi >= 2:
                    s_os[tc].wait(e, out_tok[oi - 2])
                s_dve.inc(e.tensor_tensor(
                    outsb[:, tc, :], psW[:, tc * 512:(tc + 1) * 512],
                    xf[:, sl, tc * 512:(tc + 1) * 512], ALU.add))
            dve_wo_tok[mb] = s_dve.val
        p.dve(_)

        def _(e, mb=mb):
            s_dve.wait(e, dve_wo_tok[mb])
            for tc in range(2):
                oi = mb * 2 + tc
                out_tok[oi] = s_os[tc].inc(e.dma_start(
                    ov[:, mb, tc * 512:(tc + 1) * 512], outsb[:, tc, :]), 16)
        p.sync(_)

    def _(e):
        for tc in range(2):
            s_os[tc].wait(e)
    p.sync(_)

    p.close_scope(sc3)
    for (g, _t) in reversed(pscope_b):
        g.__exit__(None, None, None)
    p.close_scope(sbp)
    emit_barrier(p)
    p.pop_scope()


def build_l0attn_program():
    p = Prog()
    nc = p.nc
    D = declare_l0attn_io(nc)
    D["x_in"] = nc.dram_tensor("x_in", [DM, T], F32, kind="ExternalInput")
    D["x_out"] = nc.dram_tensor("x_out", [DM, T], F32, kind="ExternalOutput")
    p.open_block()
    emit_l0attn(p, D, D["x_in"], D["x_out"])
    p.close()
    return p


def attn_meta(c):
    b, h = c // 2, c % 2
    return {
        "ownoff": np.full((128, 1), float(h * T), np.float32),
        "pairidx": np.array([[2 * b]], np.int32),
    }


def run_l0attn(x, w):
    p = _get_prog("l0attn", build_l0attn_program)
    shards = _x_to_shards(x)
    in_maps = []
    for c in range(NCORES):
        in_maps.append({"x_in": shards[c], **attn_meta(c), **w})
    res = run_bass_kernel_spmd(p.nc, in_maps, core_ids=list(range(NCORES)))
    return _shards_to_x([res.results[c]["x_out"] for c in range(NCORES)])




def prep_l1ffn(inp):
    d = {}
    d["wbT"] = _lhsT_slabs(np.asarray(inp["L1_ffn_W_base"], np.float32))
    d["wgt"] = np.ascontiguousarray(
        np.asarray(inp["L1_ffn_W_gate"], np.float32).T.reshape(KC, 128, 8).transpose(1, 0, 2))
    d["aT"] = np.ascontiguousarray(
        _lhsT_slabs(np.asarray(inp["L1_ffn_A"], np.float32).reshape(128, DM))[0])
    d["bT"] = np.ascontiguousarray(
        np.asarray(inp["L1_ffn_B"], np.float32).transpose(0, 2, 1).reshape(128, DM)
        .reshape(128, KC, 128))
    d["bbs"] = np.ascontiguousarray(np.asarray(inp["L1_ffn_b_base"], np.float32).reshape(KC, 128).T)
    return d


def declare_l1ffn_io(nc):
    D = {}
    D["wbT"] = nc.dram_tensor("wbT", [KC, 128, KC, 128], F32, kind="ExternalInput")
    D["wgt"] = nc.dram_tensor("wgt", [128, KC, 8], F32, kind="ExternalInput")
    D["aT"] = nc.dram_tensor("aT", [128, KC, 128], F32, kind="ExternalInput")
    D["bT"] = nc.dram_tensor("bT", [128, KC, 128], F32, kind="ExternalInput")
    D["bbs"] = nc.dram_tensor("bbs", [128, KC], F32, kind="ExternalInput")
    return D


def emit_l1ffn(p, D, x_in_h, x_out_h):
    nc = p.nc
    wbT = D["wbT"]; wgt = D["wgt"]; aT = D["aT"]; bT = D["bT"]; bbs = D["bbs"]
    _scope = p.push_scope()

    xv = x_in_h.ap().rearrange("(c p) t -> p c t", p=128)
    ov = x_out_h.ap().rearrange("(c p) t -> p c t", p=128)

    xf = p.sbuf("xf", [128, 2, T], F32)
    outsb = p.sbuf("outsb", [128, 2, 512], F32)
    ones_bf = p.sbuf("ones_bf", [128, 1], BF16)
    ones_f = p.sbuf("ones_f", [1, 128], F32)
    eps1 = p.sbuf("eps1", [1, 1], F32)
    wsl = p.sbuf("wsl", [128, 2, KC * 128], BF16)
    xn = p.sbuf("xn", [128, KC * T], BF16)
    sq = p.sbuf("sq", [128, 2, T], BF16)
    t32 = p.sbuf("t32", [128, T], F32)
    rows = p.sbuf("rows", [1, 4 * T], F32)
    aTs = p.sbuf("aTs", [128, KC * 128], BF16)
    bTs = p.sbuf("bTs", [128, KC * 128], BF16)
    wgs = p.sbuf("wgs", [128, KC * 8], BF16)
    bbc = p.sbuf("bbc", [128, KC], F32)
    Lsb = p.sbuf("Lsb", [128, 64], F32)
    Ls1 = p.sbuf("Ls1", [128, 64], F32)
    Ls2 = p.sbuf("Ls2", [128, 64], F32)
    mx1 = p.sbuf("mx1", [128, 8], F32)
    mx2 = p.sbuf("mx2", [128, 8], F32)
    den = p.sbuf("den", [128, 8], F32)
    reb = p.sbuf("reb", [128, 64], BF16)
    res8 = p.sbuf("res8", [8, T], BF16)
    P8 = p.sbuf("P8", [8, 128], BF16)
    P8b = p.sbuf("P8b", [8, 128], BF16)
    p8i = p.sbuf("p8i", [8, 128], F32)
    idn = p.sbuf("idn", [128, 128], BF16)
    io128 = p.sbuf("io128", [128, 128], F32)
    rb = p.sbuf("rb", [128, T], BF16)
    hb = p.sbuf("hb", [128, T], BF16)
    hw = p.sbuf("hw", [128, T], BF16)

    s_xs = [p.sem("s_x0"), p.sem("s_x1")]
    s_ws = [p.sem("s_w0"), p.sem("s_w1")]
    s_sm = p.sem("s_sm")
    s_os = [p.sem("s_o0"), p.sem("s_o1")]
    s_pe = p.sem("s_pe")
    s_act = p.sem("s_act")
    s_dve = p.sem("s_dve")
    s_po = p.sem("s_po")

    def _(e):
        s_sm.inc(e.dma_start(aTs[:], aT.ap().rearrange("p a b -> p (a b)"),
                             max_dma_last_dim=8192), 16)
        s_sm.inc(e.dma_start(bTs[:], bT.ap().rearrange("p a b -> p (a b)"),
                             max_dma_last_dim=8192), 16)
        s_sm.inc(e.dma_start(wgs[:], wgt.ap().rearrange("p a b -> p (a b)")), 16)
        s_sm.inc(e.dma_start(bbc[:], bbs.ap()), 16)
        e.iota(p8i[:], pattern=[[1, 128]], base=0, channel_multiplier=-16,
               allow_small_or_imprecise_dtypes=True)
        s_po.inc(e.iota(io128[:], pattern=[[1, 128]], base=0, channel_multiplier=-1,
                        allow_small_or_imprecise_dtypes=True))
    p.pool(_)

    def _(e):
        e.memset(ones_bf[:], 1.0)
        e.memset(eps1[:], EPS_LN)
        s_dve.inc(e.memset(ones_f[:], 1.0))
        s_po.wait(e)
        e.tensor_scalar(P8[:], p8i[:], 0.0, None, ALU.is_ge)
        e.tensor_scalar(P8b[:], p8i[:], 15.0, None, ALU.is_le)
        s_dve.inc(e.tensor_tensor(P8[:], P8[:], P8b[:], ALU.mult))
        s_dve.inc(e.tensor_scalar(idn[:], io128[:], 0.0, None, ALU.is_equal))
    p.dve(_)
    const_tok = s_dve.val

    pscope_a = []
    for _nm in ("psA", "psB", "psC", "psD"):
        p._uid += 1
        g = nc.psum_tensor(f"{_nm}_u{p._uid}", [128, T], F32)
        pscope_a.append((g, g.__enter__()))
    psA, psB, psC, psD = [t for (_g, t) in pscope_a]

    emit_layernorm(p, xv, xn, xf, sq, t32, rows, ones_bf, ones_f,
                   eps1, psA, psB, psC, psD, s_xs, s_pe, s_act, s_dve)
    ln_tok = s_dve.val

    def _(e):
        s_dve.wait(e, ln_tok)
        s_sm.wait(e)
        for mt in range(8):
            for kc in range(KC):
                mm = e.matmul(psD[:, mt * 8:(mt + 1) * 8],
                              xn[:, kc * T + mt * 128:kc * T + (mt + 1) * 128],
                              wgs[:, kc * 8:(kc + 1) * 8],
                              start=(kc == 0), stop=(kc == KC - 1))
        s_pe.inc(mm)
    p.te(_)
    pe_lg = s_pe.val

    def _(e):
        s_pe.wait(e, pe_lg)
        s_act.inc(e.copy(Lsb[:], psD[:, 0:64]))
    p.act(_)
    act_lg = s_act.val

    gv = lambda t: t.rearrange("p (g i) -> p g i", i=8)
    mx1b = bass.AP(mx1, 0, [[8, 128], [1, 8], [0, 8]])
    mx2b = bass.AP(mx2, 0, [[8, 128], [1, 8], [0, 8]])
    denb = bass.AP(den, 0, [[8, 128], [1, 8], [0, 8]])

    def _(e):
        s_act.wait(e, act_lg)
        s_dve.inc(e.tensor_reduce(mx1[:], gv(Lsb[:]), AX.X, ALU.max))
        s_dve.wait(e)
        e.tensor_tensor(gv(Ls1[:]), gv(Lsb[:]), mx1b, ALU.is_ge)
        e.tensor_scalar(Ls1[:], Ls1[:], -1.0e30, None, ALU.mult)
        e.tensor_tensor(Ls1[:], Ls1[:], Lsb[:], ALU.add)
        s_dve.inc(e.tensor_reduce(mx2[:], gv(Ls1[:]), AX.X, ALU.max))
        s_dve.wait(e)
        s_dve.inc(e.tensor_tensor(gv(Ls2[:]), gv(Lsb[:]), mx1b, ALU.subtract))
    p.dve(_)
    dve_g1 = s_dve.val

    def _(e):
        s_dve.wait(e, dve_g1)
        s_act.inc(e.activation(Ls2[:], Ls2[:], AF.Exp))
    p.act(_)
    act_e = s_act.val

    def _(e):
        s_act.wait(e, act_e)
        e.tensor_tensor(gv(Ls1[:]), gv(Lsb[:]), mx2b, ALU.is_ge)
        e.tensor_tensor(Ls2[:], Ls2[:], Ls1[:], ALU.mult)
        s_dve.inc(e.tensor_reduce(den[:], gv(Ls2[:]), AX.X, ALU.add))
        s_dve.wait(e)
        s_dve.inc(e.reciprocal(den[:], den[:]))
        s_dve.wait(e)
        s_dve.inc(e.tensor_tensor(gv(reb[:]), gv(Ls2[:]), denb, ALU.mult))
    p.dve(_)
    dve_route = s_dve.val

    psTr = psD[0:8, 128:192].bitcast(BF16)
    tr_tok2 = [0] * 8
    ev_tok2 = [0] * 8

    for mt in range(8):
        def _(e, mt=mt):
            if mt == 0:
                s_dve.wait(e, dve_route)
                s_dve.wait(e, const_tok)
            if mt >= 1:
                s_act.wait(e, ev_tok2[mt - 1])
            tr = e.transpose(psTr, reb[:, mt * 8:(mt + 1) * 8], idn[:])
            tr_tok2[mt] = s_pe.inc(tr)
        p.te(_)

        def _(e, mt=mt):
            s_pe.wait(e, tr_tok2[mt])
            ev_tok2[mt] = s_act.inc(e.copy(res8[:, mt * 128:(mt + 1) * 128], psTr))
        p.act(_)
    route8_tok = ev_tok2[7]

    def _(e):
        s_act.wait(e, route8_tok)
        for tc in range(2):
            mm = e.matmul(psC[:, tc * 512:(tc + 1) * 512], P8[:],
                          res8[:, tc * 512:(tc + 1) * 512], start=True, stop=True)
        s_pe.inc(mm)
    p.te(_)
    pe_rfm = s_pe.val

    def _(e):
        s_pe.wait(e, pe_rfm)
        s_act.inc(e.copy(rb[:], psC[:]))
    p.act(_)
    act_rb = s_act.val

    def _(e):
        s_dve.wait(e, ln_tok)
        for tc in range(2):
            for kc in range(KC):
                mm = e.matmul(psB[:, tc * 512:(tc + 1) * 512],
                              aTs[:, kc * 128:(kc + 1) * 128],
                              xn[:, kc * T + tc * 512:kc * T + (tc + 1) * 512],
                              start=(kc == 0), stop=(kc == KC - 1))
        s_pe.inc(mm)
    p.te(_)
    pe_h = s_pe.val

    def _(e):
        s_pe.wait(e, pe_h)
        s_act.inc(e.copy(hb[:], psB[:]))
    p.act(_)
    act_hb = s_act.val

    def _(e):
        s_act.wait(e, act_hb)
        s_act.wait(e, act_rb)
        s_dve.inc(e.tensor_tensor(hw[:], hb[:], rb[:], ALU.mult))
    p.dve(_)
    hw_tok = s_dve.val

    wb_tok = [0] * KC
    pe_mb_tok = [0] * KC
    dve_mb_tok = [0] * KC
    out_tok = [0] * (KC * 2)
    xm_tok = [0] * KC

    for mb in range(KC):
        sl = mb % 2
        psW = psA if sl == 0 else psB

        def _(e, mb=mb, sl=sl):
            if mb >= 2:
                s_pe.wait(e, pe_mb_tok[mb - 2])
            wb_tok[mb] = s_ws[sl].inc(e.dma_start(
                wsl[:, sl, :], wbT.ap()[mb].rearrange("p a b -> p (a b)"),
                max_dma_last_dim=8192), 16)
        p.pool(_)

        def _(e, mb=mb, sl=sl, psW=psW):
            s_ws[sl].wait(e, wb_tok[mb])
            if mb == 0:
                s_dve.wait(e, hw_tok)
            if mb == 1:
                s_act.wait(e, act_hb)
            if mb >= 2:
                s_dve.wait(e, dve_mb_tok[mb - 2])
            for tc in range(2):
                for kc in range(KC):
                    e.matmul(psW[:, tc * 512:(tc + 1) * 512],
                             wsl[:, sl, kc * 128:(kc + 1) * 128],
                             xn[:, kc * T + tc * 512:kc * T + (tc + 1) * 512],
                             start=(kc == 0), stop=False)
                mm = e.matmul(psW[:, tc * 512:(tc + 1) * 512],
                              bTs[:, mb * 128:(mb + 1) * 128],
                              hw[:, tc * 512:(tc + 1) * 512],
                              start=False, stop=True)
            pe_mb_tok[mb] = s_pe.inc(mm)
        p.te(_)

        def _(e, mb=mb, sl=sl, dvg=s_dve.val):
            if mb >= 2:
                s_dve.wait(e, dve_mb_tok[mb - 2])
            else:
                s_dve.wait(e, dvg)
            xm_tok[mb] = s_xs[sl].inc(e.dma_start(xf[:, sl, :], xv[:, mb, :]), 16)
        p.sync(_)

        def _(e, mb=mb, sl=sl, psW=psW):
            s_xs[sl].wait(e, xm_tok[mb])
            s_pe.wait(e, pe_mb_tok[mb])
            for tc in range(2):
                oi = mb * 2 + tc
                if oi >= 2:
                    s_os[tc].wait(e, out_tok[oi - 2])
                e.tensor_scalar(outsb[:, tc, :], psW[:, tc * 512:(tc + 1) * 512],
                                bbc[:, mb:mb + 1], 0.5, ALU.add, ALU.mult)
                s_dve.inc(e.tensor_tensor(outsb[:, tc, :], outsb[:, tc, :],
                                          xf[:, sl, tc * 512:(tc + 1) * 512], ALU.add))
            dve_mb_tok[mb] = s_dve.val
        p.dve(_)

        def _(e, mb=mb):
            s_dve.wait(e, dve_mb_tok[mb])
            for tc in range(2):
                oi = mb * 2 + tc
                out_tok[oi] = s_os[tc].inc(e.dma_start(
                    ov[:, mb, tc * 512:(tc + 1) * 512], outsb[:, tc, :]), 16)
        p.sync(_)

    def _(e):
        for tc in range(2):
            s_os[tc].wait(e)
    p.sync(_)

    for (g2, _t) in reversed(pscope_a):
        g2.__exit__(None, None, None)
    emit_barrier(p)
    p.pop_scope()


def build_l1ffn_program():
    p = Prog()
    nc = p.nc
    D = declare_l1ffn_io(nc)
    D["x_in"] = nc.dram_tensor("x_in", [DM, T], F32, kind="ExternalInput")
    D["x_out"] = nc.dram_tensor("x_out", [DM, T], F32, kind="ExternalOutput")
    p.open_block()
    emit_l1ffn(p, D, D["x_in"], D["x_out"])
    p.close()
    return p


def run_l1ffn(x, w):
    p = _get_prog("l1ffn", build_l1ffn_program)
    shards = _x_to_shards(x)
    in_maps = [{"x_in": shards[c], **w} for c in range(NCORES)]
    res = run_bass_kernel_spmd(p.nc, in_maps, core_ids=list(range(NCORES)))
    return _shards_to_x([res.results[c]["x_out"] for c in range(NCORES)])



DEBUG_L1 = False
D1 = 512
ND0 = 4
DFF = 2048
NSUB = 4
EPS_L1 = 1e-5


def prep_l1attn(inp):
    d = {}
    d["winT"] = _lhsT_slabs(np.asarray(inp["L1_attn_Win"], np.float32))
    d["binc"] = np.ascontiguousarray(np.asarray(inp["L1_attn_bin"], np.float32).reshape(ND0, 128).T)
    for nm in ("ln1_w", "ln1_b", "phz_re", "phz_im", "lc_re", "lc_im",
               "sc_lin", "ln2_w", "ln2_b", "b2"):
        a = np.asarray(inp["L1_attn_" + nm], np.float32)
        d[nm] = np.ascontiguousarray(a.reshape(NSUB, ND0, 128).transpose(0, 2, 1))
    d["w1T"] = np.stack([_lhsT_slabs(np.asarray(inp["L1_attn_W1"], np.float32)[s])
                         for s in range(NSUB)])
    d["w2T"] = np.stack([_lhsT_slabs(np.asarray(inp["L1_attn_W2"], np.float32)[s])
                         for s in range(NSUB)])
    d["b1c"] = np.ascontiguousarray(np.asarray(inp["L1_attn_b1"], np.float32)
                                    .reshape(NSUB, 16, 128).transpose(0, 2, 1))
    d["lnlw"] = np.ascontiguousarray(np.asarray(inp["L1_attn_lnl_w"], np.float32).reshape(ND0, 128).T)
    d["lnlb"] = np.ascontiguousarray(np.asarray(inp["L1_attn_lnl_b"], np.float32).reshape(ND0, 128).T)
    d["woutT"] = _lhsT_slabs(np.asarray(inp["L1_attn_Wout"], np.float32))
    d["boutc"] = np.ascontiguousarray(np.asarray(inp["L1_attn_bout"], np.float32).reshape(KC, 128).T)
    return d


def declare_l1attn_io(nc):
    D = {}
    D["winT"] = nc.dram_tensor("winT", [ND0, 128, KC, 128], F32, kind="ExternalInput")
    D["binc"] = nc.dram_tensor("binc", [128, ND0], F32, kind="ExternalInput")
    for nm in ("ln1_w", "ln1_b", "phz_re", "phz_im", "lc_re", "lc_im",
               "sc_lin", "ln2_w", "ln2_b", "b2"):
        D[nm] = nc.dram_tensor(nm, [NSUB, 128, ND0], F32, kind="ExternalInput")
    D["w1T"] = nc.dram_tensor("w1T", [NSUB, 16, 128, ND0, 128], F32, kind="ExternalInput")
    D["w2T"] = nc.dram_tensor("w2T", [NSUB, ND0, 128, 16, 128], F32, kind="ExternalInput")
    D["b1c"] = nc.dram_tensor("b1c", [NSUB, 128, 16], F32, kind="ExternalInput")
    D["lnlw"] = nc.dram_tensor("lnlw", [128, ND0], F32, kind="ExternalInput")
    D["lnlb"] = nc.dram_tensor("lnlb", [128, ND0], F32, kind="ExternalInput")
    D["woutT"] = nc.dram_tensor("woutT", [KC, 128, ND0, 128], F32, kind="ExternalInput")
    D["boutc"] = nc.dram_tensor("boutc", [128, KC], F32, kind="ExternalInput")
    D["flagc"] = nc.dram_tensor("flagc", [128, 1], F32, kind="ExternalInput")
    D["mcol"] = nc.dram_tensor("mcol", [128, 8], F32, kind="ExternalInput")
    D["cc_st_in"] = nc.dram_tensor("cc_st_in", [128, 8], F32)
    D["cc_st_out"] = nc.dram_tensor("cc_st_out", [8 * 128, 8], F32)
    return D


def emit_l1attn(p, D, x_in_h, x_out_h):
    nc = p.nc
    winT = D["winT"]; binc = D["binc"]; w1T = D["w1T"]; w2T = D["w2T"]
    b1c = D["b1c"]; lnlw = D["lnlw"]; lnlb = D["lnlb"]
    woutT = D["woutT"]; boutc = D["boutc"]
    flagc_d = D["flagc"]; mcol_d = D["mcol"]
    cc_st_in = D["cc_st_in"]; cc_st_out = D["cc_st_out"]
    cols_in = {nm: D[nm] for nm in ("ln1_w", "ln1_b", "phz_re", "phz_im",
                                    "lc_re", "lc_im", "sc_lin", "ln2_w",
                                    "ln2_b", "b2")}
    _scope = p.push_scope()
    NCH = 16

    xv = x_in_h.ap().rearrange("(c p) t -> p c t", p=128)
    ov = x_out_h.ap().rearrange("(c p) t -> p c t", p=128)

    xf = p.sbuf("xf", [128, 2, T], F32)
    outsb = p.sbuf("outsb", [128, 2, 512], F32)
    ones_bf = p.sbuf("ones_bf", [128, 1], BF16)
    ones_f = p.sbuf("ones_f", [1, 128], F32)
    eps1 = p.sbuf("eps1", [1, 1], F32)
    h = p.sbuf("h", [128, ND0, T], F32)
    y = p.sbuf("y", [128, ND0, T], BF16)
    yc = p.sbuf("yc", [128, ND0, T], F32)
    lnq = p.sbuf("lnq", [128, ND0, 512], BF16)
    fq = p.sbuf("fq", [128, 16, 512], BF16)
    w1s = p.sbuf("w1s", [128, 16 * ND0 * 128], BF16)
    w2s = p.sbuf("w2s", [128, ND0 * 16 * 128], BF16)
    wsl = p.sbuf("wsl", [128, 2, ND0 * 128], BF16)
    rows5 = p.sbuf("rows5", [1, 2 * T], F32)
    mub = p.sbuf("mub", [128, 1024], F32)
    sdb = p.sbuf("sdb", [128, 1024], F32)
    xcF = p.sbuf("xcF", [128, 2, T], BF16)
    sqF = p.sbuf("sqF", [128, 2, T], BF16)
    cr = p.sbuf("cr", [128, 64], F32)
    ci = p.sbuf("ci", [128, 64], F32)
    Xs = p.sbuf("Xs", [128, 64], F32)
    tm1 = p.sbuf("tm1", [128, 64], F32)
    tm2 = p.sbuf("tm2", [128, 64], F32)
    tm3 = p.sbuf("tm3", [128, 64], F32)
    tm4 = p.sbuf("tm4", [128, 64], F32)
    shr = p.sbuf("shr", [128, 64], F32)
    shi = p.sbuf("shi", [128, 64], F32)
    sneg = p.sbuf("sneg", [128, 64], F32)
    pd1 = p.sbuf("pd1", [128, 256], F32)
    pd2 = p.sbuf("pd2", [128, 256], F32)
    prt = p.sbuf("prt", [128, ND0, NCH], F32)
    pit = p.sbuf("pit", [128, ND0, NCH], F32)
    P1r = p.sbuf("P1r", [128, ND0, 64], F32)
    P1i = p.sbuf("P1i", [128, ND0, 64], F32)
    P6r = p.sbuf("P6r", [128, ND0, NCH], F32)
    P6i = p.sbuf("P6i", [128, ND0, NCH], F32)
    SSr = p.sbuf("SSr", [128, ND0, NCH], F32)
    SSi = p.sbuf("SSi", [128, ND0, NCH], F32)
    p64k = p.sbuf("p64k", [128, ND0, 12], F32)
    ccols = p.sbuf("ccols", [128, ND0, 14], F32)
    subc = p.sbuf("subc", [128, ND0, 10], F32)
    b1s = p.sbuf("b1s", [128, 16], F32)
    binb = p.sbuf("binb", [128, ND0], F32)
    lnlwc = p.sbuf("lnlwc", [128, ND0], F32)
    lnlbc = p.sbuf("lnlbc", [128, ND0], F32)
    boutb = p.sbuf("boutb", [128, KC], F32)
    flagc = p.sbuf("flagc", [128, 1], F32)
    gflag = p.sbuf("gflag", [128, 1], F32)
    mcol = p.sbuf("mcol", [128, 8], F32)
    stq = p.sbuf("stq", [128, 8], F32)
    Sall = p.sbuf("Sall", [128, 64], F32)
    Srecv = p.sbuf("Srecv", [128, 8], F32)

    s_xs = [p.sem("s_x0"), p.sem("s_x1")]
    s_ws = [p.sem("s_w0"), p.sem("s_w1")]
    s_sm = p.sem("s_sm")
    s_os = [p.sem("s_o0"), p.sem("s_o1")]
    s_pe = p.sem("s_pe")
    s_act = p.sem("s_act")
    s_dve = p.sem("s_dve")
    s_cc = p.sem("s_cc")

    def _(e):
        s_sm.inc(e.dma_start(binb[:], binc.ap()), 16)
        s_sm.inc(e.dma_start(lnlwc[:], lnlw.ap()), 16)
        s_sm.inc(e.dma_start(lnlbc[:], lnlb.ap()), 16)
        s_sm.inc(e.dma_start(boutb[:], boutc.ap()), 16)
        s_sm.inc(e.dma_start(flagc[:], flagc_d.ap()), 16)
        s_sm.inc(e.dma_start(mcol[:], mcol_d.ap()), 16)
    p.pool(_)
    sm_base = s_sm.val

    def _(e):
        e.memset(ones_bf[:], 1.0)
        e.memset(eps1[:], EPS_LN)
        s_dve.inc(e.memset(ones_f[:], 1.0))
        s_sm.wait(e, sm_base)
        s_dve.inc(e.tensor_scalar(gflag[:], flagc[:], -1.0, 1.0, ALU.mult, ALU.add))
    p.dve(_)

    pscope = []
    for _nm in ("psA", "psB", "psC", "psD"):
        p._uid += 1
        g = nc.psum_tensor(f"{_nm}_l1a_u{p._uid}", [128, T], F32)
        pscope.append((g, g.__enter__()))
    psA, psB, psC, psD = [t for (_g, t) in pscope]
    psWin = [psA, psB, psC, psD]

    sa = []
    win_sl = p.sbuf("win_sl", [128, ND0 * KC * 128], BF16, sa)
    xnk = p.sbuf("xnk", [128, 2, T], BF16, sa)

    def _(e):
        for a in range(ND0):
            s_sm.inc(e.dma_start(win_sl[:, a * KC * 128:(a + 1) * KC * 128],
                                 winT.ap()[a].rearrange("p b c -> p (b c)"),
                                 max_dma_last_dim=8192), 16)
    p.pool(_)
    win_ld = s_sm.val

    actT_tok = [0] * KC
    peT_tok = [0] * KC
    xT_tok = [0] * KC
    for kc in range(KC):
        sl = kc % 2

        def _(e, kc=kc, sl=sl):
            if kc >= 2:
                s_act.wait(e, actT_tok[kc - 2])
            xT_tok[kc] = s_xs[sl].inc(e.dma_start(xf[:, sl, :], xv[:, kc, :]), 16)
        p.sync(_)

        def _(e, kc=kc, sl=sl):
            s_xs[sl].wait(e, xT_tok[kc])
            if kc >= 2:
                s_pe.wait(e, peT_tok[kc - 2])
            e.copy(xcF[:, sl, :], xf[:, sl, :])
            actT_tok[kc] = s_act.inc(e.square(sqF[:, sl, :], xf[:, sl, :]))
        p.act(_)

        def _(e, kc=kc, sl=sl):
            s_act.wait(e, actT_tok[kc])
            if kc == 0:
                s_dve.wait(e)
            for tc in range(2):
                e.matmul(psA[0:1, tc * 512:(tc + 1) * 512], ones_bf[:],
                         xcF[:, sl, tc * 512:(tc + 1) * 512],
                         start=(kc == 0), stop=(kc == KC - 1))
                mm = e.matmul(psC[0:1, tc * 512:(tc + 1) * 512], ones_bf[:],
                              sqF[:, sl, tc * 512:(tc + 1) * 512],
                              start=(kc == 0), stop=(kc == KC - 1))
            peT_tok[kc] = s_pe.inc(mm)
        p.te(_)
    pe_Tstats = s_pe.val

    def _(e, pe_Tstats=pe_Tstats):
        s_pe.wait(e, pe_Tstats)
        e.tensor_scalar(rows5[:, 0:T], psA[0:1, 0:T], 1.0 / DM, None, ALU.mult)
        s_dve.inc(e.tensor_scalar(rows5[:, T:2 * T], psC[0:1, 0:T], 1.0 / DM,
                                  None, ALU.mult))
        s_dve.inc(e.engine_nop())
        for seg in range(2):
            a, b = seg * 512, (seg + 1) * 512
            s_dve.wait(e)
            s_dve.inc(e.tensor_tensor(mub[0:1, 0:512], rows5[:, a:b],
                                      rows5[:, a:b], ALU.mult))
            s_dve.wait(e)
            s_dve.inc(e.tensor_tensor(rows5[:, T + a:T + b], rows5[:, T + a:T + b],
                                      mub[0:1, 0:512], ALU.subtract))
    p.dve(_)

    def _(e):
        s_dve.wait(e)
        s_act.inc(e.activation(rows5[:, T:2 * T], rows5[:, T:2 * T],
                               AF.Sqrt, bias=eps1[0:1, :], scale=1.0))
    p.act(_)

    def _(e):
        s_act.wait(e)
        s_dve.inc(e.reciprocal(rows5[:, T:2 * T], rows5[:, T:2 * T]))
    p.dve(_)
    tstats_tok = s_dve.val

    def _(e, tstats_tok=tstats_tok):
        s_dve.wait(e, tstats_tok)
        for tc in range(2):
            e.matmul(psA[:, tc * 512:(tc + 1) * 512], ones_f[:],
                     rows5[0:1, tc * 512:(tc + 1) * 512], start=True, stop=True)
            mm = e.matmul(psB[:, tc * 512:(tc + 1) * 512], ones_f[:],
                          rows5[0:1, T + tc * 512:T + (tc + 1) * 512],
                          start=True, stop=True)
        s_pe.inc(mm)
    p.te(_)
    pe_tbc = s_pe.val

    def _(e, pe_tbc=pe_tbc):
        s_pe.wait(e, pe_tbc)
        e.copy(mub[:], psA[:])
        s_act.inc(e.copy(sdb[:], psB[:]))
    p.act(_)
    act_tbc = s_act.val

    winpe_tok = [0] * KC
    xd_tok = [0] * KC
    for kc in range(KC):
        sl = kc % 2

        def _(e, kc=kc, sl=sl, act_tbc=act_tbc):
            if kc >= 2:
                s_dve.wait(e, xd_tok[kc - 2])
            else:
                s_act.wait(e, act_tbc)
            xT_tok[kc] = s_xs[sl].inc(e.dma_start(xf[:, sl, :], xv[:, kc, :]), 16)
        p.sync(_)

        def _(e, kc=kc, sl=sl, act_tbc=act_tbc):
            s_xs[sl].wait(e, xT_tok[kc])
            if kc == 0:
                s_act.wait(e, act_tbc)
            if kc >= 2:
                s_pe.wait(e, winpe_tok[kc - 2])
            e.tensor_tensor(outsb[:, 0, :], xf[:, sl, 0:512], mub[:, 0:512], ALU.subtract)
            e.tensor_tensor(xnk[:, sl, 0:512], outsb[:, 0, :], sdb[:, 0:512], ALU.mult)
            e.tensor_tensor(outsb[:, 0, :], xf[:, sl, 512:1024], mub[:, 512:1024], ALU.subtract)
            xd_tok[kc] = s_dve.inc(e.tensor_tensor(
                xnk[:, sl, 512:1024], outsb[:, 0, :], sdb[:, 512:1024], ALU.mult))
        p.dve(_)

        def _(e, kc=kc, sl=sl, win_ld=win_ld):
            s_dve.wait(e, xd_tok[kc])
            if kc == 0:
                s_sm.wait(e, win_ld)
            for mb in range(ND0):
                for tc in range(2):
                    mm = e.matmul(
                        psWin[mb][:, tc * 512:(tc + 1) * 512],
                        win_sl[:, (mb * KC + kc) * 128:(mb * KC + kc + 1) * 128],
                        xnk[:, sl, tc * 512:(tc + 1) * 512],
                        start=(kc == 0), stop=(kc == KC - 1))
            winpe_tok[kc] = s_pe.inc(mm)
        p.te(_)
    pe_win = s_pe.val

    def _(e, pe_win=pe_win):
        s_pe.wait(e, pe_win)
        for mb in range(ND0):
            s_dve.inc(e.tensor_scalar(h[:, mb, :], psWin[mb][:],
                                      binb[:, mb:mb + 1], None, ALU.add))
    p.dve(_)
    h_ready = s_dve.val
    p.close_scope(sa)

    def col_bc(buf, n):
        return bass.AP(buf.tensor, buf.offset, [buf.ap[0], buf.ap[1], [0, n]])

    CPR, CPI, CLCR, CLCI, CSCL, CL1W, CL1B, CL2W, CL2B, CB2 = range(10)

    def emit_sub_consts(s):
        def _(e, s=s, dvg=s_dve.val, peg=s_pe.val, acg=s_act.val):
            s_dve.wait(e, dvg)
            s_pe.wait(e, peg)
            s_act.wait(e, acg)
            s_sm.inc(e.dma_start(subc[:, :, CL1W:CL1W + 1], cols_in["ln1_w"].ap()[s].unsqueeze(2), allow_slow_non_contiguous=True), 16)
            s_sm.inc(e.dma_start(subc[:, :, CL1B:CL1B + 1], cols_in["ln1_b"].ap()[s].unsqueeze(2), allow_slow_non_contiguous=True), 16)
            s_sm.inc(e.dma_start(subc[:, :, CL2W:CL2W + 1], cols_in["ln2_w"].ap()[s].unsqueeze(2), allow_slow_non_contiguous=True), 16)
            s_sm.inc(e.dma_start(subc[:, :, CL2B:CL2B + 1], cols_in["ln2_b"].ap()[s].unsqueeze(2), allow_slow_non_contiguous=True), 16)
            s_sm.inc(e.dma_start(subc[:, :, CSCL:CSCL + 1], cols_in["sc_lin"].ap()[s].unsqueeze(2), allow_slow_non_contiguous=True), 16)
            s_sm.inc(e.dma_start(ccols[:, :, 0:1], cols_in["phz_re"].ap()[s].unsqueeze(2), allow_slow_non_contiguous=True), 16)
            s_sm.inc(e.dma_start(ccols[:, :, 1:2], cols_in["phz_im"].ap()[s].unsqueeze(2), allow_slow_non_contiguous=True), 16)
            s_sm.inc(e.dma_start(ccols[:, :, 2:3], cols_in["lc_re"].ap()[s].unsqueeze(2), allow_slow_non_contiguous=True), 16)
            s_sm.inc(e.dma_start(ccols[:, :, 3:4], cols_in["lc_im"].ap()[s].unsqueeze(2), allow_slow_non_contiguous=True), 16)
            s_sm.inc(e.dma_start(subc[:, :, CB2:CB2 + 1], cols_in["b2"].ap()[s].unsqueeze(2), allow_slow_non_contiguous=True), 16)
            s_sm.inc(e.dma_start(b1s[:], b1c.ap()[s]), 16)
            for a in range(16):
                s_sm.inc(e.dma_start(w1s[:, a * ND0 * 128:(a + 1) * ND0 * 128],
                                     w1T.ap()[s][a].rearrange("p b c -> p (b c)"),
                                     max_dma_last_dim=8192), 16)
            for a in range(ND0):
                s_sm.inc(e.dma_start(w2s[:, a * 16 * 128:(a + 1) * 16 * 128],
                                     w2T.ap()[s][a].rearrange("p b c -> p (b c)"),
                                     max_dma_last_dim=8192), 16)
        p.pool(_)
        lds = s_sm.val

        def _(e, s=s, lds=lds):
            s_sm.wait(e, lds)
            W = lambda: s_dve.wait(e)
            I = lambda ins: s_dve.inc(ins)
            phzr = ccols[:, :, 0:1]; phzi = ccols[:, :, 1:2]
            r2 = ccols[:, :, 4:5]; tmp = ccols[:, :, 5:6]
            I(e.tensor_tensor(r2, phzr, phzr, ALU.mult)); W()
            I(e.tensor_tensor(tmp, phzi, phzi, ALU.mult)); W()
            I(e.tensor_tensor(r2, r2, tmp, ALU.add)); W()
        p.dve(_)

        def _(e):
            s_dve.wait(e)
            s_act.inc(e.activation(ccols[:, :, 4:5], ccols[:, :, 4:5], AF.Sqrt,
                                   bias=0.0, scale=1.0))
        p.act(_)

        def _(e):
            s_act.wait(e)
            s_dve.inc(e.reciprocal(ccols[:, :, 5:6], ccols[:, :, 4:5]))
        p.dve(_)

        def _(e):
            s_dve.wait(e)
            s_act.inc(e.activation(ccols[:, :, 4:5], ccols[:, :, 4:5], AF.Tanh,
                                   bias=0.0, scale=1.0))
        p.act(_)

        def _(e, s=s):
            s_act.wait(e)
            W = lambda: s_dve.wait(e)
            I = lambda ins: s_dve.inc(ins)
            phzr = ccols[:, :, 0:1]; phzi = ccols[:, :, 1:2]
            lcr = ccols[:, :, 2:3]; lci = ccols[:, :, 3:4]
            th = ccols[:, :, 4:5]; rinv = ccols[:, :, 5:6]
            fac = ccols[:, :, 6:7]
            pr = subc[:, :, CPR:CPR + 1]; pi = subc[:, :, CPI:CPI + 1]
            lcpr = ccols[:, :, 7:8]; lcpi = ccols[:, :, 8:9]
            I(e.tensor_scalar(lcr, lcr, gflag[:], None, ALU.mult)); W()
            I(e.tensor_scalar(lci, lci, gflag[:], None, ALU.mult)); W()
            I(e.tensor_tensor(fac, th, rinv, ALU.mult)); W()
            I(e.tensor_tensor(pr, phzr, fac, ALU.mult)); W()
            I(e.tensor_tensor(pi, phzi, fac, ALU.mult)); W()
            I(e.tensor_copy(prt[:], col_bc(pr, NCH))); W()
            I(e.tensor_copy(pit[:], col_bc(pi, NCH))); W()
            I(e.tensor_copy(P1r[:, :, 0:1], pr)); W()
            I(e.tensor_copy(P1i[:, :, 0:1], pi)); W()
            L = 1
            while L < 64:
                ar = P1r[:, :, 0:L]; ai = P1i[:, :, 0:L]
                br = col_bc(P1r[:, :, L - 1:L], L); bi = col_bc(P1i[:, :, L - 1:L], L)
                s1 = pd1[:, 0:ND0 * L].rearrange("p (d l) -> p d l", d=ND0)
                s2 = pd2[:, 0:ND0 * L].rearrange("p (d l) -> p d l", d=ND0)
                I(e.tensor_tensor(s1, ar, br, ALU.mult)); W()
                I(e.tensor_tensor(s2, ai, bi, ALU.mult)); W()
                I(e.tensor_tensor(P1i[:, :, L:2 * L], ar, bi, ALU.mult)); W()
                I(e.tensor_tensor(s2, s1, s2, ALU.subtract)); W()
                I(e.tensor_tensor(s1, ai, br, ALU.mult)); W()
                I(e.tensor_tensor(P1i[:, :, L:2 * L], P1i[:, :, L:2 * L], s1, ALU.add)); W()
                I(e.tensor_copy(P1r[:, :, L:2 * L], s2)); W()
                L *= 2
            I(e.tensor_copy(p64k[:, :, 0:1], P1r[:, :, 63:64])); W()
            I(e.tensor_copy(p64k[:, :, 1:2], P1i[:, :, 63:64])); W()
            for k in range(1, 4):
                ar = p64k[:, :, 2 * k - 2:2 * k - 1]; ai = p64k[:, :, 2 * k - 1:2 * k]
                I(e.tensor_tensor(ccols[:, :, 10:11], ar, ar, ALU.mult)); W()
                I(e.tensor_tensor(ccols[:, :, 11:12], ai, ai, ALU.mult)); W()
                I(e.tensor_tensor(p64k[:, :, 2 * k:2 * k + 1], ccols[:, :, 10:11],
                                  ccols[:, :, 11:12], ALU.subtract)); W()
                I(e.tensor_tensor(ccols[:, :, 10:11], ar, ai, ALU.mult)); W()
                I(e.tensor_scalar(p64k[:, :, 2 * k + 1:2 * k + 2], ccols[:, :, 10:11],
                                  2.0, None, ALU.mult)); W()
            p64r = p64k[:, :, 0:1]; p64i = p64k[:, :, 1:2]
            I(e.tensor_tensor(ccols[:, :, 10:11], lcr, p64r, ALU.mult)); W()
            I(e.tensor_tensor(ccols[:, :, 11:12], lci, p64i, ALU.mult)); W()
            I(e.tensor_tensor(lcpr, ccols[:, :, 10:11], ccols[:, :, 11:12], ALU.subtract)); W()
            I(e.tensor_tensor(ccols[:, :, 10:11], lcr, p64i, ALU.mult)); W()
            I(e.tensor_tensor(ccols[:, :, 11:12], lci, p64r, ALU.mult)); W()
            I(e.tensor_tensor(lcpi, ccols[:, :, 10:11], ccols[:, :, 11:12], ALU.add)); W()
            I(e.tensor_scalar(ccols[:, :, 12:13], ccols[:, :, 3:4], -1.0, None, ALU.mult)); W()
            I(e.memset(P6r[:, :, 0:1], 1.0)); W()
            I(e.memset(P6i[:, :, 0:1], 0.0)); W()
            I(e.tensor_copy(P6r[:, :, 1:2], p64r)); W()
            I(e.tensor_copy(P6i[:, :, 1:2], p64i)); W()
            L = 2
            while L < NCH:
                mr = ccols[:, :, 10:11]; mi = ccols[:, :, 11:12]
                t1 = ccols[:, :, 13:14]
                I(e.tensor_tensor(mr, P6r[:, :, L - 1:L], p64r, ALU.mult)); W()
                I(e.tensor_tensor(t1, P6i[:, :, L - 1:L], p64i, ALU.mult)); W()
                I(e.tensor_tensor(mr, mr, t1, ALU.subtract)); W()
                I(e.tensor_tensor(mi, P6r[:, :, L - 1:L], p64i, ALU.mult)); W()
                I(e.tensor_tensor(t1, P6i[:, :, L - 1:L], p64r, ALU.mult)); W()
                I(e.tensor_tensor(mi, mi, t1, ALU.add)); W()
                n2 = min(L, NCH - L)
                ar = P6r[:, :, 0:n2]; ai = P6i[:, :, 0:n2]
                brc = col_bc(mr, n2); bic = col_bc(mi, n2)
                s1 = pd1[:, 0:ND0 * n2].rearrange("p (d l) -> p d l", d=ND0)
                s2 = pd2[:, 0:ND0 * n2].rearrange("p (d l) -> p d l", d=ND0)
                I(e.tensor_tensor(s1, ar, brc, ALU.mult)); W()
                I(e.tensor_tensor(s2, ai, bic, ALU.mult)); W()
                I(e.tensor_tensor(P6i[:, :, L:L + n2], ar, bic, ALU.mult)); W()
                I(e.tensor_tensor(s2, s1, s2, ALU.subtract)); W()
                I(e.tensor_tensor(s1, ai, brc, ALU.mult)); W()
                I(e.tensor_tensor(P6i[:, :, L:L + n2], P6i[:, :, L:L + n2], s1, ALU.add)); W()
                I(e.tensor_copy(P6r[:, :, L:L + n2], s2)); W()
                L *= 2
        p.dve(_)
        return s_dve.val

    def emit_inner_stats():
        tok_in = [0] * (ND0)
        for d0 in range(ND0):
            sl = d0 % 2

            def _(e, d0=d0, sl=sl, hg=s_dve.val):
                if d0 == 0:
                    s_dve.wait(e, hg)
                if d0 >= 2:
                    s_pe.wait(e, tok_in[d0 - 2])
                e.copy(xcF[:, sl, :], h[:, d0, :])
                s_act.inc(e.square(sqF[:, sl, :], h[:, d0, :]))
            p.act(_)
            a_tok = s_act.val

            def _(e, d0=d0, sl=sl, a_tok=a_tok):
                s_act.wait(e, a_tok)
                for tc in range(2):
                    e.matmul(psA[0:1, tc * 512:(tc + 1) * 512], ones_bf[:],
                             xcF[:, sl, tc * 512:(tc + 1) * 512],
                             start=(d0 == 0), stop=(d0 == ND0 - 1))
                    mm = e.matmul(psC[0:1, tc * 512:(tc + 1) * 512], ones_bf[:],
                                  sqF[:, sl, tc * 512:(tc + 1) * 512],
                                  start=(d0 == 0), stop=(d0 == ND0 - 1))
                tok_in[d0] = s_pe.inc(mm)
            p.te(_)
        pe_st = s_pe.val

        def _(e, pe_st=pe_st):
            s_pe.wait(e, pe_st)
            e.tensor_scalar(rows5[:, 0:T], psA[0:1, 0:T], 1.0 / D1, None, ALU.mult)
            s_dve.inc(e.tensor_scalar(rows5[:, T:2 * T], psC[0:1, 0:T], 1.0 / D1,
                                      None, ALU.mult))
            s_dve.inc(e.engine_nop())
            for seg in range(2):
                a, b = seg * 512, (seg + 1) * 512
                s_dve.wait(e)
                s_dve.inc(e.tensor_tensor(mub[0:1, 0:512], rows5[:, a:b],
                                          rows5[:, a:b], ALU.mult))
                s_dve.wait(e)
                s_dve.inc(e.tensor_tensor(rows5[:, T + a:T + b],
                                          rows5[:, T + a:T + b],
                                          mub[0:1, 0:512], ALU.subtract))
        p.dve(_)

        def _(e):
            s_dve.wait(e)
            s_act.inc(e.activation(rows5[:, T:2 * T], rows5[:, T:2 * T],
                                   AF.Sqrt, bias=eps1[0:1, :], scale=1.0))
        p.act(_)

        def _(e):
            s_act.wait(e)
            s_dve.inc(e.reciprocal(rows5[:, T:2 * T], rows5[:, T:2 * T]))
        p.dve(_)
        return s_dve.val

    def emit_quarter_bcast(q, guard_dve):
        def _(e, q=q, guard_dve=guard_dve):
            s_dve.wait(e, guard_dve)
            s_dve.wait(e)
            e.matmul(psA[:, 512:1024], ones_f[:], rows5[0:1, q * 512:(q + 1) * 512],
                     start=True, stop=True)
            mm = e.matmul(psB[:, 512:1024], ones_f[:],
                          rows5[0:1, T + q * 512:T + (q + 1) * 512],
                          start=True, stop=True)
            s_pe.inc(mm)
        p.te(_)
        bq = s_pe.val

        def _(e, bq=bq):
            s_pe.wait(e, bq)
            e.copy(mub[:, 0:512], psA[:, 512:1024])
            s_act.inc(e.copy(sdb[:, 0:512], psB[:, 512:1024]))
        p.act(_)
        return s_act.val

    NQ = 2
    for s in range(NSUB):
        cst_tok = emit_sub_consts(s)
        st_tok = emit_inner_stats()

        silu_tok = [0] * NQ
        for q in range(NQ):
            bc_tok = emit_quarter_bcast(q, st_tok)
            dv_toks = [0] * ND0
            for d0 in range(ND0):
                def _(e, q=q, d0=d0, bc_tok=bc_tok, cst_tok=cst_tok):
                    s_act.wait(e)
                    if q == 0 and d0 == 0:
                        s_dve.wait(e, cst_tok)
                    sl5 = h[:, d0, q * 512:(q + 1) * 512]
                    e.tensor_tensor(sdb[:, 512:1024], sl5, mub[:, 0:512], ALU.subtract)
                    e.tensor_tensor(sdb[:, 512:1024], sdb[:, 512:1024], sdb[:, 0:512], ALU.mult)
                    dv_toks[d0] = s_dve.inc(e.tensor_scalar(
                        sdb[:, 512:1024], sdb[:, 512:1024],
                        subc[:, d0][:, CL1W:CL1W + 1],
                        subc[:, d0][:, CL1B:CL1B + 1], ALU.mult, ALU.add))
                p.dve(_)

                def _(e, q=q, d0=d0):
                    s_dve.wait(e, dv_toks[d0])
                    silu_tok[q] = s_act.inc(e.activation(
                        y[:, d0, q * 512:(q + 1) * 512], sdb[:, 512:1024], AF.Silu))
                p.act(_)
        y_ready = silu_tok[NQ - 1]

        def _(e, s=s, y_ready=y_ready, cst_tok=cst_tok):
            s_act.wait(e, y_ready)
            s_dve.wait(e, cst_tok)
            e.memset(cr[:], 0.0)
            e.memset(ci[:], 0.0)
            yv = y.rearrange("p d (c u) -> p d c u", u=64)
            ycv = yc.rearrange("p d (c u) -> p d c u", u=64)
            for u in range(64):
                e.tensor_copy(Xs[:].rearrange("p (d c) -> p d c", d=ND0), yv[:, :, :, u])
                e.tensor_tensor(tm1[:], ci[:], pit[:].rearrange("p d c -> p (d c)"), ALU.mult)
                e.tensor_tensor(tm2[:], cr[:], prt[:].rearrange("p d c -> p (d c)"), ALU.mult)
                e.tensor_tensor(tm3[:], ci[:], prt[:].rearrange("p d c -> p (d c)"), ALU.mult)
                e.tensor_tensor(tm4[:], cr[:], pit[:].rearrange("p d c -> p (d c)"), ALU.mult)
                e.tensor_tensor(cr[:], tm2[:], tm1[:], ALU.subtract)
                e.tensor_tensor(cr[:], cr[:], Xs[:], ALU.add)
                e.tensor_tensor(ci[:], tm3[:], tm4[:], ALU.add)
                e.tensor_copy(ycv[:, :, :, u], cr[:].rearrange("p (d c) -> p d c", d=ND0))
            lcpr = ccols[:, :, 7:8]; lcpi = ccols[:, :, 8:9]
            for d0 in range(ND0):
                s_dve.inc(e.tensor_tensor(cr[:, d0 * NCH:d0 * NCH + 1],
                                          cr[:, d0 * NCH:d0 * NCH + 1],
                                          lcpr[:, d0, :], ALU.add))
                s_dve.wait(e)
                s_dve.inc(e.tensor_tensor(ci[:, d0 * NCH:d0 * NCH + 1],
                                          ci[:, d0 * NCH:d0 * NCH + 1],
                                          lcpi[:, d0, :], ALU.add))
                s_dve.wait(e)
            for ki, k in enumerate([1, 2, 4, 8]):
                pkr = col_bc(p64k[:, :, 2 * ki:2 * ki + 1], NCH - k)
                pki = col_bc(p64k[:, :, 2 * ki + 1:2 * ki + 2], NCH - k)
                e.memset(shr[:], 0.0)
                e.memset(shi[:], 0.0)
                for d0 in range(ND0):
                    e.tensor_copy(shr[:, d0 * NCH + k:(d0 + 1) * NCH],
                                  cr[:, d0 * NCH:(d0 + 1) * NCH - k])
                    e.tensor_copy(shi[:, d0 * NCH + k:(d0 + 1) * NCH],
                                  ci[:, d0 * NCH:(d0 + 1) * NCH - k])
                shrv = shr.rearrange("p (d c) -> p d c", d=ND0)[:, :, k:NCH]
                shiv = shi.rearrange("p (d c) -> p d c", d=ND0)[:, :, k:NCH]
                t1v = tm1.rearrange("p (d c) -> p d c", d=ND0)[:, :, k:NCH]
                t2v = tm2.rearrange("p (d c) -> p d c", d=ND0)[:, :, k:NCH]
                crv = cr.rearrange("p (d c) -> p d c", d=ND0)[:, :, k:NCH]
                civ = ci.rearrange("p (d c) -> p d c", d=ND0)[:, :, k:NCH]
                e.tensor_tensor(t1v, shrv, pkr, ALU.mult)
                e.tensor_tensor(t2v, shiv, pki, ALU.mult)
                e.tensor_tensor(t1v, t1v, t2v, ALU.subtract)
                e.tensor_tensor(crv, crv, t1v, ALU.add)
                e.tensor_tensor(t1v, shrv, pki, ALU.mult)
                e.tensor_tensor(t2v, shiv, pkr, ALU.mult)
                e.tensor_tensor(t1v, t1v, t2v, ALU.add)
                e.tensor_tensor(civ, civ, t1v, ALU.add)
            s_dve.inc(e.tensor_copy(stq[:, 0:4], cr[:, NCH - 1::NCH]))
            s_dve.wait(e)
            s_dve.inc(e.tensor_copy(stq[:, 4:8], ci[:, NCH - 1::NCH]))
        p.dve(_)
        hillis_tok = s_dve.val

        def _(e, s=s, hillis_tok=hillis_tok):
            s_dve.wait(e, hillis_tok)
            s_sm.inc(e.dma_start(cc_st_in.ap(), stq[:]), 16)
            s_sm.wait(e)
            e.collective_compute(
                "AllGather", ALU.bypass,
                replica_groups=[[0, 1, 2, 3, 4, 5, 6, 7]],
                ins=[cc_st_in.ap().opt()],
                outs=[cc_st_out.ap().opt()],
            ).then_inc(s_cc.h, 1)
            s_cc.val += 1
            e.wait_ge(s_cc.h, s_cc.val)
            for r in range(8):
                s_sm.inc(e.dma_start(Sall[:, r * 8:(r + 1) * 8],
                                     cc_st_out.ap()[r * 128:(r + 1) * 128, :]), 16)
        p.pool(_)
        exg_tok = s_sm.val

        def _(e, s=s, exg_tok=exg_tok):
            s_sm.wait(e, exg_tok)
            s_dve.inc(e.memset(Srecv[:], 0.0))
            for r in range(8):
                s_dve.wait(e)
                s_dve.inc(e.scalar_tensor_tensor(
                    Srecv[:], Sall[:, r * 8:(r + 1) * 8], mcol[:, r:r + 1],
                    Srecv[:], ALU.mult, ALU.add))
            s_dve.wait(e)
            s_dve.inc(e.tensor_scalar(Srecv[:], Srecv[:], flagc[:], None, ALU.mult))
            s_dve.wait(e)
            for d0 in range(ND0):
                sr = Srecv[:, d0:d0 + 1]
                si = Srecv[:, 4 + d0:5 + d0]
                e.tensor_scalar(tm1[:, 0:NCH], P6i[:, d0, :], si, None, ALU.mult)
                e.scalar_tensor_tensor(SSr[:, d0, :], P6r[:, d0, :], sr,
                                       tm1[:, 0:NCH], ALU.mult, ALU.subtract)
                e.tensor_scalar(tm1[:, 0:NCH], P6i[:, d0, :], sr, None, ALU.mult)
                e.scalar_tensor_tensor(SSi[:, d0, :], P6r[:, d0, :], si,
                                       tm1[:, 0:NCH], ALU.mult, ALU.add)
            s_dve.inc(e.engine_nop())
            s_dve.wait(e)
            s_dve.inc(e.tensor_scalar(SSi[:], SSi[:], -1.0, None, ALU.mult))
            s_dve.wait(e)
            s_dve.inc(e.tensor_scalar(sneg[:], ci[:], -1.0, None, ALU.mult))
            s_dve.wait(e)
            for d0 in range(ND0):
                for c in range(NCH):
                    if c == 0:
                        sr1 = ccols[:, d0][:, 2:3]
                        si1n = ccols[:, d0][:, 12:13]
                    else:
                        sr1 = cr[:, d0 * NCH + c - 1:d0 * NCH + c]
                        si1n = sneg[:, d0 * NCH + c - 1:d0 * NCH + c]
                    tgt = yc[:, d0, c * 64:(c + 1) * 64]
                    e.scalar_tensor_tensor(tgt, P1r[:, d0, :], sr1, tgt, ALU.mult, ALU.add)
                    e.scalar_tensor_tensor(tgt, P1i[:, d0, :], si1n, tgt, ALU.mult, ALU.add)
                    e.scalar_tensor_tensor(tgt, P1r[:, d0, :], SSr[:, d0, c:c + 1],
                                           tgt, ALU.mult, ALU.add)
                    e.scalar_tensor_tensor(tgt, P1i[:, d0, :], SSi[:, d0, c:c + 1],
                                           tgt, ALU.mult, ALU.add)
            for d0 in range(ND0):
                mmv = e.scalar_tensor_tensor(
                    h[:, d0, :], yc[:, d0, :], subc[:, d0][:, CSCL:CSCL + 1],
                    h[:, d0, :], ALU.mult, ALU.add)
            s_dve.inc(mmv)
        p.dve(_)
        scan_tok = s_dve.val

        st2_tok = emit_inner_stats()
        for q in range(NQ):
            bc2 = emit_quarter_bcast(q, st2_tok)
            dvq = [0] * ND0
            for d0 in range(ND0):
                def _(e, q=q, d0=d0, bc2=bc2, scan_tok=scan_tok, peg=s_pe.val):
                    s_act.wait(e, bc2)
                    if d0 == 0:
                        s_pe.wait(e, peg)
                    if q == 0 and d0 == 0:
                        s_dve.wait(e, scan_tok)
                    sl5 = h[:, d0, q * 512:(q + 1) * 512]
                    e.tensor_tensor(sdb[:, 512:1024], sl5, mub[:, 0:512], ALU.subtract)
                    e.tensor_tensor(sdb[:, 512:1024], sdb[:, 512:1024], sdb[:, 0:512], ALU.mult)
                    dvq[d0] = s_dve.inc(e.tensor_scalar(
                        lnq[:, d0, :], sdb[:, 512:1024],
                        subc[:, d0][:, CL2W:CL2W + 1],
                        subc[:, d0][:, CL2B:CL2B + 1], ALU.mult, ALU.add))
                p.dve(_)
            lnq_tok = dvq[ND0 - 1]

            fa_tok = [0] * 16
            sl_tok = [0] * 16
            for mb in range(16):
                def _(e, q=q, mb=mb, lnq_tok=lnq_tok):
                    s_dve.wait(e, lnq_tok)
                    if mb >= 2:
                        s_act.wait(e, sl_tok[mb - 2])
                    for kt in range(ND0):
                        mm = e.matmul(psD[:, (mb % 2) * 512:(mb % 2) * 512 + 512],
                                      w1s[:, (mb * ND0 + kt) * 128:(mb * ND0 + kt + 1) * 128],
                                      lnq[:, kt, :],
                                      start=(kt == 0), stop=(kt == ND0 - 1))
                    fa_tok[mb] = s_pe.inc(mm)
                p.te(_)

                def _(e, q=q, mb=mb):
                    s_pe.wait(e, fa_tok[mb])
                    sl_tok[mb] = s_act.inc(e.activation(
                        fq[:, mb, :], psD[:, (mb % 2) * 512:(mb % 2) * 512 + 512],
                        AF.Silu, bias=b1s[:, mb:mb + 1], scale=1.0))
                p.act(_)

            psY = [psA[:, 0:512], psA[:, 512:1024], psB[:, 0:512], psB[:, 512:1024]]

            def _(e, q=q):
                s_act.wait(e, sl_tok[15])
                for d0 in range(ND0):
                    for kt in range(16):
                        mm = e.matmul(psY[d0],
                                      w2s[:, (d0 * 16 + kt) * 128:(d0 * 16 + kt + 1) * 128],
                                      fq[:, kt, :],
                                      start=(kt == 0), stop=(kt == 15))
                s_pe.inc(mm)
            p.te(_)
            fb_tok = s_pe.val

            def _(e, q=q, fb_tok=fb_tok):
                s_pe.wait(e, fb_tok)
                for d0 in range(ND0):
                    mmv = e.scalar_tensor_tensor(
                        h[:, d0, q * 512:(q + 1) * 512], psY[d0],
                        subc[:, d0][:, CB2:CB2 + 1],
                        h[:, d0, q * 512:(q + 1) * 512], ALU.add, ALU.add)
                s_dve.inc(mmv)
            p.dve(_)

    stl_tok = emit_inner_stats()
    lnl_tok = [0] * NQ
    for q in range(NQ):
        bcl = emit_quarter_bcast(q, stl_tok)
        for d0 in range(ND0):
            def _(e, q=q, d0=d0, bcl=bcl):
                s_act.wait(e, bcl)
                sl5 = h[:, d0, q * 512:(q + 1) * 512]
                e.tensor_tensor(sdb[:, 512:1024], sl5, mub[:, 0:512], ALU.subtract)
                e.tensor_tensor(sdb[:, 512:1024], sdb[:, 512:1024], sdb[:, 0:512], ALU.mult)
                lnl_tok[q] = s_dve.inc(e.tensor_scalar(
                    y[:, d0, q * 512:(q + 1) * 512], sdb[:, 512:1024],
                    lnlwc[:, d0:d0 + 1], lnlbc[:, d0:d0 + 1], ALU.mult, ALU.add))
            p.dve(_)
    lnl_done = lnl_tok[NQ - 1]

    wo_tok = [0] * KC
    pe_wo = [0] * KC
    dve_wo = [0] * KC
    out_tok = [0] * (KC * 2)
    ev_q = [0] * (KC * 2)
    xw_tok = [0] * KC

    for mb in range(KC):
        sl = mb % 2
        psW = psA if sl == 0 else psB

        def _(e, mb=mb, sl=sl):
            if mb >= 2:
                s_pe.wait(e, pe_wo[mb - 2])
            wo_tok[mb] = s_ws[sl].inc(e.dma_start(
                wsl[:, sl, :], woutT.ap()[mb].rearrange("p a b -> p (a b)"),
                max_dma_last_dim=8192), 16)
        p.pool(_)

        def _(e, mb=mb, sl=sl, psW=psW):
            s_ws[sl].wait(e, wo_tok[mb])
            if mb == 0:
                s_dve.wait(e, lnl_done)
            if mb >= 2:
                s_dve.wait(e, dve_wo[mb - 2])
            for tc in range(2):
                for kt in range(ND0):
                    mm = e.matmul(psW[:, tc * 512:(tc + 1) * 512],
                                  wsl[:, sl, kt * 128:(kt + 1) * 128],
                                  y[:, kt, tc * 512:(tc + 1) * 512],
                                  start=(kt == 0), stop=(kt == ND0 - 1))
            pe_wo[mb] = s_pe.inc(mm)
        p.te(_)

        def _(e, mb=mb, sl=sl, dvg=s_dve.val):
            if mb >= 1:
                s_dve.wait(e, dve_wo[mb - 1])
            else:
                s_dve.wait(e, dvg)
            xw_tok[mb] = s_xs[sl].inc(e.dma_start(xf[:, sl, :], xv[:, mb, :]), 16)
        p.sync(_)

        for tc in range(2):
            oi = mb * 2 + tc

            def _(e, mb=mb, sl=sl, psW=psW, tc=tc, oi=oi):
                s_xs[sl].wait(e, xw_tok[mb])
                s_pe.wait(e, pe_wo[mb])
                if oi >= 2:
                    s_os[tc].wait(e, out_tok[oi - 2])
                e.tensor_scalar(outsb[:, tc, :], psW[:, tc * 512:(tc + 1) * 512],
                                boutb[:, mb:mb + 1], None, ALU.add)
                s_dve.inc(e.tensor_tensor(
                    outsb[:, tc, :], outsb[:, tc, :],
                    xf[:, sl, tc * 512:(tc + 1) * 512], ALU.add))
                ev_q[oi] = s_dve.val
            p.dve(_)

            def _(e, mb=mb, tc=tc, oi=oi):
                s_dve.wait(e, ev_q[oi])
                out_tok[oi] = s_os[tc].inc(e.dma_start(
                    ov[:, mb, tc * 512:(tc + 1) * 512], outsb[:, tc, :]), 16)
            p.sync(_)
        dve_wo[mb] = s_dve.val

    def _(e):
        for tc in range(2):
            s_os[tc].wait(e)
    p.sync(_)

    for (g, _t) in reversed(pscope):
        g.__exit__(None, None, None)
    emit_barrier(p)
    p.pop_scope()


def build_l1attn_program():
    p = Prog()
    nc = p.nc
    D = declare_l1attn_io(nc)
    D["x_in"] = nc.dram_tensor("x_in", [DM, T], F32, kind="ExternalInput")
    D["x_out"] = nc.dram_tensor("x_out", [DM, T], F32, kind="ExternalOutput")
    p.open_block()
    emit_l1attn(p, D, D["x_in"], D["x_out"])
    p.close()
    return p


def l1attn_meta(c):
    b, hh = c // 2, c % 2
    m = np.zeros((128, 8), np.float32)
    m[:, 2 * b] = 1.0
    return {
        "flagc": np.full((128, 1), float(hh), np.float32),
        "mcol": m,
    }


def run_l1attn(x, w):
    p = _get_prog("l1attn", build_l1attn_program)
    shards = _x_to_shards(x)
    in_maps = []
    for c in range(NCORES):
        in_maps.append({"x_in": shards[c], **l1attn_meta(c), **w})
    res = run_bass_kernel_spmd(p.nc, in_maps, core_ids=list(range(NCORES)))
    return _shards_to_x([res.results[c]["x_out"] for c in range(NCORES)])


# version 08 (final)
# speedup vs baseline: 1.1871x; 1.1487x over previous
import sys
import numpy as np

sys.path.insert(0, "/opt/trn_rl_repo")

import concourse.bass as bass
import concourse.mybir as mybir
from concourse.bass_utils import run_bass_kernel_spmd

F32 = mybir.dt.float32
BF16 = mybir.dt.bfloat16
AF = mybir.ActivationFunctionType
ALU = mybir.AluOpType
AX = mybir.AxisListType

NCORES = 8
T = 1024
DM = 4096
DH = 11008
HB = DH // 128
KC = DM // 128
EPS_LN = 1e-5


class CSem:
    def __init__(self, handle):
        self.h = handle
        self.val = 0

    def inc(self, instr, n=1):
        instr.then_inc(self.h, n)
        self.val += n
        return self.val

    def wait(self, eng, val=None):
        eng.wait_ge(self.h, self.val if val is None else val)


class Prog:
    def __init__(self):
        self.nc = bass.Bass("TRN2", target_bir_lowering=False, debug=False)
        self._ctxs = []
        self.sems = {}
        self._scope_stack = []
        self._uid = 0

    def sem(self, name):
        if name not in self.sems:
            g = self.nc.semaphore(name)
            self.sems[name] = CSem(g.__enter__())
            self._ctxs.append(g)
        return self.sems[name]

    def push_scope(self):
        s = []
        self._scope_stack.append(s)
        return s

    def pop_scope(self):
        s = self._scope_stack.pop()
        self.close_scope(s)

    def _default_scope(self):
        return self._scope_stack[-1] if self._scope_stack else self._ctxs

    def sbuf(self, name, shape, dtype, scope=None):
        self._uid += 1
        g = self.nc.sbuf_tensor(f"{name}_u{self._uid}", shape, dtype)
        t = g.__enter__()
        (self._default_scope() if scope is None else scope).append(g)
        return t

    def psum(self, name, shape, dtype=F32):
        self._uid += 1
        g = self.nc.psum_tensor(f"{name}_u{self._uid}", shape, dtype)
        t = g.__enter__()
        self._default_scope().append(g)
        return t

    def open_block(self):
        self._blk = self.nc.Block()
        self.block = self._blk.__enter__()

    def close_scope(self, scope):
        for g in reversed(scope):
            g.__exit__(None, None, None)

    def close(self):
        self._blk.__exit__(None, None, None)
        for g in reversed(self._ctxs):
            g.__exit__(None, None, None)

    def te(self, f):
        self.block.tensor(f)

    def dve(self, f):
        self.block.vector(f)

    def act(self, f):
        self.block.scalar(f)

    def pool(self, f):
        self.block.gpsimd(f)

    def sync(self, f):
        self.block.sync(f)


def emit_barrier(p):
    toks = [(s, s.val) for s in p.sems.values()]

    def body(e):
        for s, v in toks:
            e.wait_ge(s.h, v)
    p.te(body)
    p.act(body)
    p.dve(body)
    p.pool(body)
    p.sync(body)



def _lhsT_slabs(W):
    o, i = W.shape
    return np.ascontiguousarray(
        W.reshape(o // 128, 128, i // 128, 128).transpose(0, 3, 2, 1))


def prep_l0ffn(inp):
    gw = inp["L0_ffn_gate_w"]; uw = inp["L0_ffn_up_w"]; dw = inp["L0_ffn_down_w"]
    d = {}
    d["gsl"] = _lhsT_slabs(np.asarray(gw, np.float32))
    d["usl"] = _lhsT_slabs(np.asarray(uw, np.float32))
    d["dsl"] = np.ascontiguousarray(
        np.asarray(dw, np.float32).reshape(8, 4, 128, HB, 128).transpose(0, 4, 3, 1, 2))
    a2 = np.concatenate([inp["L0_ffn_gate_a"], inp["L0_ffn_up_a"]], 0).astype(np.float32)
    d["a2t"] = np.ascontiguousarray(a2.T.reshape(KC, 128, 32).transpose(1, 0, 2))
    d["gbt"] = np.ascontiguousarray(np.asarray(inp["L0_ffn_gate_b"], np.float32).T.reshape(16, HB, 128))
    d["ubt"] = np.ascontiguousarray(np.asarray(inp["L0_ffn_up_b"], np.float32).T.reshape(16, HB, 128))
    d["dat"] = np.ascontiguousarray(
        np.asarray(inp["L0_ffn_down_a"], np.float32).T.reshape(HB, 128, 16).transpose(1, 0, 2))
    d["dbt"] = np.ascontiguousarray(np.asarray(inp["L0_ffn_down_b"], np.float32).T.reshape(16, KC, 128))
    return d



def emit_layernorm(p, xv, xn, xf, sq, t32, rows, ones_bf, ones_f, eps1,
                   psA, psB, psC, psD, s_xs, s_pe, s_act, s_dve):
    sq_tok = [0] * KC
    xn_tok = [0] * KC
    x_tok = [0] * KC
    peS_tok = [0] * KC
    dve_guard0 = s_dve.val

    for kc in range(KC):
        sl = kc % 2

        def _(e, kc=kc, sl=sl):
            if kc >= 2:
                s_act.wait(e, sq_tok[kc - 2])
                s_dve.wait(e, xn_tok[kc - 2])
            x_tok[kc] = s_xs[sl].inc(e.dma_start(xf[:, sl, :], xv[:, kc, :]), 16)
        p.sync(_)

        def _(e, kc=kc, sl=sl):
            s_xs[sl].wait(e, x_tok[kc])
            if kc >= 2:
                s_pe.wait(e, peS_tok[kc - 2])
            sq_tok[kc] = s_act.inc(e.square(sq[:, sl, :], xf[:, sl, :]))
        p.act(_)

        def _(e, kc=kc, sl=sl):
            s_xs[sl].wait(e, x_tok[kc])
            if kc >= 2:
                s_pe.wait(e, peS_tok[kc - 2])
            xn_tok[kc] = s_dve.inc(e.tensor_copy(xn[:, kc * T:(kc + 1) * T],
                                                 xf[:, sl, :]))
        p.dve(_)

        def _(e, kc=kc, sl=sl, dve_guard0=dve_guard0):
            s_act.wait(e, sq_tok[kc])
            s_dve.wait(e, xn_tok[kc])
            if kc == 0:
                s_dve.wait(e, dve_guard0)
            for tc in range(2):
                e.matmul(psA[0:1, tc * 512:(tc + 1) * 512], ones_bf[:],
                         xn[:, kc * T + tc * 512:kc * T + (tc + 1) * 512],
                         start=(kc == 0), stop=(kc == KC - 1))
                mm = e.matmul(psB[0:1, tc * 512:(tc + 1) * 512], ones_bf[:],
                              sq[:, sl, tc * 512:(tc + 1) * 512],
                              start=(kc == 0), stop=(kc == KC - 1))
            peS_tok[kc] = s_pe.inc(mm)
        p.te(_)

    MU, MSQ, VAR, SD = 0, T, 2 * T, 3 * T

    def _(e):
        s_pe.wait(e)
        e.tensor_scalar(rows[:, MU:MU + T], psA[0:1, :], 1.0 / DM, None, ALU.mult)
        s_dve.inc(e.tensor_scalar(rows[:, MSQ:MSQ + T], psB[0:1, :], 1.0 / DM, None, ALU.mult))
        s_dve.wait(e)
        s_dve.inc(e.tensor_tensor(rows[:, VAR:VAR + T], rows[:, MU:MU + T],
                                  rows[:, MU:MU + T], ALU.mult))
        s_dve.wait(e)
        s_dve.inc(e.tensor_tensor(rows[:, VAR:VAR + T], rows[:, MSQ:MSQ + T],
                                  rows[:, VAR:VAR + T], ALU.subtract))
    p.dve(_)

    def _(e):
        s_dve.wait(e)
        s_act.inc(e.activation(rows[:, SD:SD + T], rows[:, VAR:VAR + T],
                               AF.Sqrt, bias=eps1[0:1, :], scale=1.0))
    p.act(_)

    def _(e):
        s_act.wait(e)
        s_dve.inc(e.reciprocal(rows[:, SD:SD + T], rows[:, SD:SD + T]))
    p.dve(_)

    def _(e):
        s_dve.wait(e)
        for tc in range(2):
            e.matmul(psC[:, tc * 512:(tc + 1) * 512], ones_f[:],
                     rows[0:1, MU + tc * 512:MU + (tc + 1) * 512],
                     start=True, stop=True)
            mm = e.matmul(psD[:, tc * 512:(tc + 1) * 512], ones_f[:],
                          rows[0:1, SD + tc * 512:SD + (tc + 1) * 512],
                          start=True, stop=True)
        s_pe.inc(mm)
    p.te(_)
    bc_tok = s_pe.val

    ap_tok = [0] * KC
    act_guard = s_act.val
    for kc in range(KC):
        sl = kc % 2

        def _(e, kc=kc, sl=sl):
            if kc >= 2:
                s_dve.wait(e, ap_tok[kc - 2])
            else:
                s_act.wait(e, act_guard)
            x_tok[kc] = s_xs[sl].inc(e.dma_start(xf[:, sl, :], xv[:, kc, :]), 16)
        p.sync(_)

        def _(e, kc=kc, sl=sl):
            s_xs[sl].wait(e, x_tok[kc])
            if kc == 0:
                s_pe.wait(e, bc_tok)
            e.tensor_tensor(t32[:], xf[:, sl, :], psC[:], ALU.subtract)
            ap_tok[kc] = s_dve.inc(e.tensor_tensor(
                xn[:, kc * T:(kc + 1) * T], t32[:], psD[:], ALU.mult))
        p.dve(_)
    return s_dve.val



def declare_l0ffn_io(nc):
    D = {}
    D["gsl"] = nc.dram_tensor("gsl", [HB, 128, KC, 128], F32, kind="ExternalInput")
    D["usl"] = nc.dram_tensor("usl", [HB, 128, KC, 128], F32, kind="ExternalInput")
    D["dsl"] = nc.dram_tensor("dsl", [8, 128, HB, 4, 128], F32, kind="ExternalInput")
    D["a2t"] = nc.dram_tensor("a2t", [128, KC, 32], F32, kind="ExternalInput")
    D["gbt"] = nc.dram_tensor("gbt", [16, HB, 128], F32, kind="ExternalInput")
    D["ubt"] = nc.dram_tensor("ubt", [16, HB, 128], F32, kind="ExternalInput")
    D["dat"] = nc.dram_tensor("dat", [128, HB, 16], F32, kind="ExternalInput")
    D["dbt"] = nc.dram_tensor("dbt", [16, KC, 128], F32, kind="ExternalInput")
    D["hid_scr"] = nc.dram_tensor("hid_scr", [HB, 128, T], BF16)
    return D


def emit_l0ffn(p, D, x_in_h, x_out_h):
    nc = p.nc
    gsl = D["gsl"]; usl = D["usl"]; dsl = D["dsl"]; a2t = D["a2t"]
    gbt = D["gbt"]; ubt = D["ubt"]; dat = D["dat"]; dbt = D["dbt"]
    hid_dram = D["hid_scr"]
    _scope = p.push_scope()

    xv = x_in_h.ap().rearrange("(c p) t -> p c t", p=128)
    ov = x_out_h.ap().rearrange("(c p) t -> p c t", p=128)

    xf = p.sbuf("xf", [128, 2, T], F32)
    outsb = p.sbuf("outsb", [128, 2, 512], F32)
    td = p.sbuf("td", [16, T], BF16)
    dbs = p.sbuf("dbs", [16, KC * 128], BF16)
    ones_bf = p.sbuf("ones_bf", [128, 1], BF16)
    ones_f = p.sbuf("ones_f", [1, 128], F32)
    eps1 = p.sbuf("eps1", [1, 1], F32)

    sa = []
    xn = p.sbuf("xn", [128, KC * T], BF16, sa)
    gslb = p.sbuf("gslb", [128, 2, KC * 128], BF16, sa)
    uslb = p.sbuf("uslb", [128, 2, KC * 128], BF16, sa)
    gbs = p.sbuf("gbs", [16, HB * 128], BF16, sa)
    ubs = p.sbuf("ubs", [16, HB * 128], BF16, sa)
    das = p.sbuf("das", [128, HB * 16], BF16, sa)
    a2s = p.sbuf("a2s", [128, KC, 32], BF16, sa)
    tg = p.sbuf("tg", [16, T], BF16, sa)
    tu = p.sbuf("tu", [16, T], BF16, sa)
    silu = p.sbuf("silu", [128, T], BF16, sa)
    ucast = p.sbuf("ucast", [128, T], BF16, sa)
    hid = p.sbuf("hid", [128, 2, T], BF16, sa)
    sq = p.sbuf("sq", [128, 2, T], BF16, sa)
    t32 = p.sbuf("t32", [128, T], F32, sa)
    rows = p.sbuf("rows", [1, 4 * T], F32, sa)

    psA = p.psum("psA", [128, T], F32)
    psB = p.psum("psB", [128, T], F32)
    psC = p.psum("psC", [128, T], F32)
    psD = p.psum("psD", [128, T], F32)
    psbank = [psA, psB, psC, psD]

    s_xs = [p.sem("s_x0"), p.sem("s_x1")]
    s_gs = [p.sem("s_g0"), p.sem("s_g1")]
    s_us = [p.sem("s_u0"), p.sem("s_u1")]
    s_ds = [p.sem("s_d0"), p.sem("s_d1")]
    s_hos = [p.sem("s_ho0"), p.sem("s_ho1")]
    s_hi = p.sem("s_hi")
    s_os = [p.sem("s_o0"), p.sem("s_o1")]
    s_sm = p.sem("s_sm")
    s_pe = p.sem("s_pe")
    s_act = p.sem("s_act")
    s_dve = p.sem("s_dve")

    def _(e):
        s_sm.inc(e.dma_start(a2s[:], a2t.ap()), 16)
        s_sm.inc(e.dma_start(gbs[:], gbt.ap().rearrange("p a b -> p (a b)")), 16)
        s_sm.inc(e.dma_start(ubs[:], ubt.ap().rearrange("p a b -> p (a b)")), 16)
        s_sm.inc(e.dma_start(das[:], dat.ap().rearrange("p a b -> p (a b)")), 16)
        s_sm.inc(e.dma_start(dbs[:], dbt.ap().rearrange("p a b -> p (a b)")), 16)
    p.pool(_)

    def _(e):
        e.memset(ones_bf[:], 1.0)
        e.memset(eps1[:], EPS_LN)
        s_dve.inc(e.memset(ones_f[:], 1.0))
    p.dve(_)

    emit_layernorm(p, xv, xn, xf, sq, t32, rows, ones_bf, ones_f, eps1,
                   psA, psB, psC, psD, s_xs, s_pe, s_act, s_dve)

    def _(e):
        s_dve.wait(e)
        s_sm.wait(e)
        for tc in range(2):
            for kc in range(KC):
                e.matmul(psA[0:16, tc * 512:(tc + 1) * 512],
                         a2s[:, kc, 0:16],
                         xn[:, kc * T + tc * 512:kc * T + (tc + 1) * 512],
                         start=(kc == 0), stop=(kc == KC - 1))
            for kc in range(KC):
                mm = e.matmul(psB[0:16, tc * 512:(tc + 1) * 512],
                              a2s[:, kc, 16:32],
                              xn[:, kc * T + tc * 512:kc * T + (tc + 1) * 512],
                              start=(kc == 0), stop=(kc == KC - 1))
        s_pe.inc(mm)
    p.te(_)

    def _(e):
        s_pe.wait(e)
        e.copy(tg[:], psA[0:16, :])
        s_act.inc(e.copy(tu[:], psB[0:16, :]))
    p.act(_)
    tgtu_tok = s_act.val

    pe_hb_tok = [0] * HB
    dve_hb_tok = [0] * HB
    ho_tok = [0] * HB
    silu_tok = [0] * HB
    ucast_tok = [0] * HB
    g_tok = [0] * HB
    u_tok = [0] * HB

    def emit_slab_load(hb):
        sl = hb % 2

        def _(e, hb=hb, sl=sl):
            if hb >= 2:
                s_pe.wait(e, pe_hb_tok[hb - 2])
            g_tok[hb] = s_gs[sl].inc(e.dma_start(
                gslb[:, sl, :], gsl.ap()[hb].rearrange("p a b -> p (a b)"),
                max_dma_last_dim=8192), 16)
            u_tok[hb] = s_us[sl].inc(e.dma_start(
                uslb[:, sl, :], usl.ap()[hb].rearrange("p a b -> p (a b)"),
                max_dma_last_dim=8192), 16)
        p.pool(_)

    emit_slab_load(0)
    emit_slab_load(1)

    for hb in range(HB):
        sl = hb % 2
        psG = psA if sl == 0 else psB

        def _(e, hb=hb, sl=sl, psG=psG):
            s_gs[sl].wait(e, g_tok[hb])
            s_us[sl].wait(e, u_tok[hb])
            if hb == 0:
                s_act.wait(e, tgtu_tok)
            if hb >= 2:
                s_act.wait(e, silu_tok[hb - 2])
            for tc in range(2):
                for kc in range(KC):
                    e.matmul(psG[:, tc * 512:(tc + 1) * 512],
                             gslb[:, sl, kc * 128:(kc + 1) * 128],
                             xn[:, kc * T + tc * 512:kc * T + (tc + 1) * 512],
                             start=(kc == 0), stop=False)
                e.matmul(psG[:, tc * 512:(tc + 1) * 512],
                         gbs[0:16, hb * 128:(hb + 1) * 128],
                         tg[0:16, tc * 512:(tc + 1) * 512],
                         start=False, stop=True)
            if hb > 0:
                s_dve.wait(e, dve_hb_tok[hb - 1])
                for tc in range(2):
                    e.matmul(psD[0:16, tc * 512:(tc + 1) * 512],
                             das[:, (hb - 1) * 16:hb * 16],
                             hid[:, (hb - 1) % 2, tc * 512:(tc + 1) * 512],
                             start=(hb == 1), stop=False)
            for tc in range(2):
                for kc in range(KC):
                    e.matmul(psC[:, tc * 512:(tc + 1) * 512],
                             uslb[:, sl, kc * 128:(kc + 1) * 128],
                             xn[:, kc * T + tc * 512:kc * T + (tc + 1) * 512],
                             start=(kc == 0), stop=False)
                mm = e.matmul(psC[:, tc * 512:(tc + 1) * 512],
                              ubs[0:16, hb * 128:(hb + 1) * 128],
                              tu[0:16, tc * 512:(tc + 1) * 512],
                              start=False, stop=True)
            pe_hb_tok[hb] = s_pe.inc(mm)
        p.te(_)

        def _(e, hb=hb, psG=psG):
            s_pe.wait(e, pe_hb_tok[hb])
            if hb >= 1:
                s_dve.wait(e, dve_hb_tok[hb - 1])
            silu_tok[hb] = s_act.inc(e.activation(silu[:], psG[:], AF.Silu))
        p.act(_)

        def _(e, hb=hb, sl=sl):
            s_act.wait(e, silu_tok[hb])
            if hb >= 2:
                s_hos[sl].wait(e, ho_tok[hb - 2])
            dve_hb_tok[hb] = s_dve.inc(
                e.tensor_tensor(hid[:, sl, :], silu[:], psC[:], ALU.mult))
        p.dve(_)

        def _(e, hb=hb, sl=sl):
            s_dve.wait(e, dve_hb_tok[hb])
            ho_tok[hb] = s_hos[sl].inc(
                e.dma_start(hid_dram.ap()[hb], hid[:, sl, :]), 16)
        p.sync(_)

        if hb + 2 < HB:
            emit_slab_load(hb + 2)

    def _(e):
        s_dve.wait(e, dve_hb_tok[HB - 1])
        for tc in range(2):
            mm = e.matmul(psD[0:16, tc * 512:(tc + 1) * 512],
                          das[:, (HB - 1) * 16:HB * 16],
                          hid[:, (HB - 1) % 2, tc * 512:(tc + 1) * 512],
                          start=False, stop=True)
        s_pe.inc(mm)
    p.te(_)
    pe_A_end = s_pe.val

    def _(e):
        s_pe.wait(e, pe_A_end)
        s_act.inc(e.copy(td[:], psD[0:16, :]))
    p.act(_)
    act_A_end = s_act.val
    dve_A_end = s_dve.val

    p.close_scope(sa)
    sb = []
    hidall = p.sbuf("hidall", [128, HB * T], BF16, sb)
    dslb = p.sbuf("dslb", [128, 2, 4 * 4 * 128], BF16, sb)

    NG = 8
    groups = [list(range(g * 11, min(HB, (g + 1) * 11))) for g in range(NG)]
    grp_tok = [0] * NG

    def _(e):
        s_pe.wait(e, pe_A_end)
        s_act.wait(e, act_A_end)
        s_dve.wait(e, dve_A_end)
        for g, hbs in enumerate(groups):
            if g >= 2:
                s_hi.wait(e, grp_tok[g - 2])
            for hb in hbs:
                s_hos[hb % 2].wait(e, ho_tok[hb])
                mm = e.dma_start(hidall[:, hb * T:(hb + 1) * T], hid_dram.ap()[hb])
                s_hi.inc(mm, 16)
            grp_tok[g] = s_hi.val
    p.act(_)

    NB = 22
    pe_batch_tok = [0] * (8 * NB)
    dve_ev_tok = [0] * 8
    d_tok = [0] * (8 * NB)
    ev_tok = [0] * (8 * 4 * 2)
    out_tok = [0] * (8 * 4 * 2)
    xb_tok = [0] * (8 * 4)

    for dg in range(8):
        for ib in range(NB):
            h0 = ib * 4
            hn = 4 if ib < 21 else 2
            bi = dg * NB + ib

            def _(e, dg=dg, ib=ib, h0=h0, hn=hn, bi=bi):
                if bi >= 2:
                    s_pe.wait(e, pe_batch_tok[bi - 2])
                elif dg == 0 and ib == 0:
                    s_act.wait(e, act_A_end)
                src = dsl.ap()[dg, :, h0:h0 + hn].rearrange("p h d c -> p (h d c)")
                d_tok[bi] = s_ds[ib % 2].inc(
                    e.dma_start(dslb[:, ib % 2, 0:hn * 512], src,
                                max_dma_last_dim=8192), 16)
            p.pool(_)

            def _(e, dg=dg, ib=ib, h0=h0, hn=hn, bi=bi):
                s_ds[ib % 2].wait(e, d_tok[bi])
                gneed = (h0 + hn - 1) // 11
                s_hi.wait(e, grp_tok[gneed])
                if dg == 0 and ib == 0:
                    s_act.wait(e, act_A_end)
                if dg > 0 and ib == 0:
                    s_dve.wait(e, dve_ev_tok[dg - 1])
                for h in range(hn):
                    hb = h0 + h
                    for d in range(4):
                        for tc in range(2):
                            mm = e.matmul(
                                psbank[d][:, tc * 512:(tc + 1) * 512],
                                dslb[:, ib % 2, (h * 4 + d) * 128:(h * 4 + d + 1) * 128],
                                hidall[:, hb * T + tc * 512:hb * T + (tc + 1) * 512],
                                start=(hb == 0), stop=False)
                pe_batch_tok[bi] = s_pe.inc(mm)
            p.te(_)

        def _(e, dg=dg):
            for d in range(4):
                for tc in range(2):
                    mm = e.matmul(psbank[d][:, tc * 512:(tc + 1) * 512],
                                  dbs[0:16, (dg * 4 + d) * 128:(dg * 4 + d + 1) * 128],
                                  td[0:16, tc * 512:(tc + 1) * 512],
                                  start=False, stop=True)
            s_pe.inc(mm)
        p.te(_)
        dg_done = s_pe.val

        for d in range(4):
            di = dg * 4 + d

            def _(e, dg=dg, d=d, di=di):
                if di >= 2:
                    s_dve.wait(e, xb_tok[di - 2])
                x_t = s_xs[d % 2].inc(e.dma_start(xf[:, d % 2, :], xv[:, di, :]), 16)
                xb_tok[di] = x_t
            p.sync(_)
            xdma_tok = s_xs[d % 2].val

            def _(e, dg=dg, d=d, di=di, dg_done=dg_done, xdma_tok=xdma_tok):
                s_xs[d % 2].wait(e, xdma_tok)
                s_pe.wait(e, dg_done)
                for tc in range(2):
                    oi = di * 2 + tc
                    if oi >= 2:
                        s_os[tc].wait(e, out_tok[oi - 2])
                    ev_tok[oi] = s_dve.inc(e.scalar_tensor_tensor(
                        outsb[:, tc, :],
                        psbank[d][:, tc * 512:(tc + 1) * 512], 0.5,
                        xf[:, d % 2, tc * 512:(tc + 1) * 512],
                        ALU.mult, ALU.add))
                xb_tok[di] = s_dve.val
            p.dve(_)

            def _(e, dg=dg, d=d, di=di):
                for tc in range(2):
                    oi = di * 2 + tc
                    s_dve.wait(e, ev_tok[oi])
                    out_tok[oi] = s_os[tc].inc(
                        e.dma_start(ov[:, di, tc * 512:(tc + 1) * 512],
                                    outsb[:, tc, :]), 16)
            p.sync(_)
        dve_ev_tok[dg] = s_dve.val

    def _(e):
        for tc in range(2):
            s_os[tc].wait(e)
    p.sync(_)

    p.close_scope(sb)
    emit_barrier(p)
    p.pop_scope()


def build_l0ffn_program():
    p = Prog()
    D = declare_l0ffn_io(p.nc)
    D["x_in"] = p.nc.dram_tensor("x_in", [DM, T], F32, kind="ExternalInput")
    D["x_out"] = p.nc.dram_tensor("x_out", [DM, T], F32, kind="ExternalOutput")
    p.open_block()
    emit_l0ffn(p, D, D["x_in"], D["x_out"])
    p.close()
    return p



_prog_cache = {}


def _get_prog(name, builder):
    if name not in _prog_cache:
        _prog_cache[name] = builder()
    return _prog_cache[name]


def _x_to_shards(x):
    out = []
    for c in range(NCORES):
        b, h = c // 2, c % 2
        out.append(np.ascontiguousarray(x[b, h * T:(h + 1) * T, :].T))
    return out


def _shards_to_x(shards):
    x = np.empty((4, 2048, DM), np.float32)
    for c in range(NCORES):
        b, h = c // 2, c % 2
        x[b, h * T:(h + 1) * T, :] = shards[c].T
    return x


def run_l0ffn(x, w):
    p = _get_prog("l0ffn", build_l0ffn_program)
    shards = _x_to_shards(x)
    in_maps = [{"x_in": shards[c], **w} for c in range(NCORES)]
    res = run_bass_kernel_spmd(p.nc, in_maps, core_ids=list(range(NCORES)))
    return _shards_to_x([res.results[c]["x_out"] for c in range(NCORES)])


def build_fused_program():
    p = Prog()
    nc = p.nc
    Df0 = declare_l0ffn_io(nc)
    Da0 = declare_l0attn_io(nc)
    Df1 = declare_l1ffn_io(nc)
    Da1 = declare_l1attn_io(nc)
    x_in = nc.dram_tensor("x_in", [DM, T], F32, kind="ExternalInput")
    x_out = nc.dram_tensor("x_out", [DM, T], F32, kind="ExternalOutput")
    bufs = [nc.dram_tensor(f"xbuf{i}", [DM, T], F32) for i in range(5)]
    p.open_block()
    emit_l0ffn(p, Df0, x_in, bufs[0])
    emit_l0attn(p, Da0, bufs[0], bufs[1])
    emit_l0ffn(p, Df0, bufs[1], bufs[2])
    emit_l1ffn(p, Df1, bufs[2], bufs[3])
    emit_l1attn(p, Da1, bufs[3], bufs[4])
    emit_l1ffn(p, Df1, bufs[4], x_out)
    p.close()
    return p


def prep_all(inputs):
    return {**prep_l0ffn(inputs), **prep_l0attn(inputs),
            **prep_l1ffn(inputs), **prep_l1attn(inputs)}


def kernel(**inputs):
    x = np.asarray(inputs["L0_attn_x"], np.float32)
    w = prep_all(inputs)
    p = _get_prog("fused", build_fused_program)
    shards = _x_to_shards(x)
    in_maps = [{"x_in": shards[c], **attn_meta(c), **l1attn_meta(c), **w}
               for c in range(NCORES)]
    res = run_bass_kernel_spmd(p.nc, in_maps, core_ids=list(range(NCORES)))
    return _shards_to_x([res.results[c]["x_out"] for c in range(NCORES)])


def kernel_unfused(**inputs):
    x = np.asarray(inputs["L0_attn_x"], np.float32)
    w_f0 = prep_l0ffn(inputs)
    w_a0 = prep_l0attn(inputs)
    w_f1 = prep_l1ffn(inputs)
    w_a1 = prep_l1attn(inputs)
    x = run_l0ffn(x, w_f0)
    x = run_l0attn(x, w_a0)
    x = run_l0ffn(x, w_f0)
    x = run_l1ffn(x, w_f1)
    x = run_l1attn(x, w_a1)
    x = run_l1ffn(x, w_f1)
    return x



TF = 2048
NH = 32
NKV = 8
HD = 128
EPS_RMS = 1e-6


def _alibi_slopes(n):
    import math
    start = 2 ** (-2 ** (-(math.log2(n) - 3)))
    return [start * start ** i for i in range(n)]


def prep_l0attn(inp):
    d = {}
    d["wqT"] = _lhsT_slabs(np.asarray(inp["L0_attn_Wq"], np.float32))
    d["wkT"] = _lhsT_slabs(np.asarray(inp["L0_attn_Wk"], np.float32))
    d["woT"] = _lhsT_slabs(np.asarray(inp["L0_attn_Wo"], np.float32))
    d["wvt"] = np.ascontiguousarray(
        np.asarray(inp["L0_attn_Wv"], np.float32).T.reshape(KC, 128, 1024))
    d["qnw"] = np.ascontiguousarray(np.asarray(inp["L0_attn_q_norm_w"], np.float32).reshape(128, 1))
    d["knw"] = np.ascontiguousarray(np.asarray(inp["L0_attn_k_norm_w"], np.float32).reshape(128, 1))
    return d






DEBUG_ATTN = False


def declare_l0attn_io(nc):
    D = {}
    D["wqT"] = nc.dram_tensor("wqT", [32, 128, KC, 128], F32, kind="ExternalInput")
    D["wkT"] = nc.dram_tensor("wkT", [8, 128, KC, 128], F32, kind="ExternalInput")
    D["woT"] = nc.dram_tensor("woT", [32, 128, KC, 128], F32, kind="ExternalInput")
    D["wvt"] = nc.dram_tensor("wvt", [KC, 128, 1024], F32, kind="ExternalInput")
    D["qnw"] = nc.dram_tensor("qnw", [128, 1], F32, kind="ExternalInput")
    D["knw"] = nc.dram_tensor("knw", [128, 1], F32, kind="ExternalInput")
    D["ownoff"] = nc.dram_tensor("ownoff", [128, 1], F32, kind="ExternalInput")
    D["pairidx"] = nc.dram_tensor("pairidx", [1, 1], mybir.dt.int32, kind="ExternalInput")
    D["cc_kv_in"] = nc.dram_tensor("cc_kv_in", [128, 16384], BF16)
    D["cc_kv_out"] = nc.dram_tensor("cc_kv_out", [8 * 128, 16384], BF16)
    return D


def emit_l0attn(p, D, x_in_h, x_out_h):
    import math
    nc = p.nc
    wqT = D["wqT"]; wkT = D["wkT"]; woT = D["woT"]; wvt = D["wvt"]
    qnw = D["qnw"]; knw = D["knw"]; ownoff = D["ownoff"]
    pairidx = D["pairidx"]; cc_kv_in = D["cc_kv_in"]; cc_kv_out = D["cc_kv_out"]
    x_own = x_in_h
    x_out = x_out_h
    _scope = p.push_scope()
    xov = x_own.ap().rearrange("(c p) t -> p c t", p=128)
    ov = x_out.ap().rearrange("(c p) t -> p c t", p=128)

    slopes = _alibi_slopes(NH)
    SC = 1.0 / math.sqrt(HD)

    xf = p.sbuf("xf", [128, 2, T], F32)
    ones_bf = p.sbuf("ones_bf", [128, 1], BF16)
    ones_f = p.sbuf("ones_f", [1, 128], F32)
    eps1 = p.sbuf("eps1", [1, 1], F32)
    epsr = p.sbuf("epsr", [1, 1], F32)
    wcol = p.sbuf("wcol", [128, 1], F32)
    knw_c = p.sbuf("knw_c", [128, 1], F32)
    ofcol = p.sbuf("ofcol", [128, 1], F32)
    wsl = p.sbuf("wsl", [128, 2, KC * 128], BF16)
    QO = p.sbuf("QO", [128, NH * T], BF16)

    s_xs = [p.sem("s_x0"), p.sem("s_x1")]
    s_ws = [p.sem("s_w0"), p.sem("s_w1")]
    s_vs = [p.sem("s_v0"), p.sem("s_v1")]
    s_sm = p.sem("s_sm")
    s_os = [p.sem("s_o0"), p.sem("s_o1")]
    s_pe = p.sem("s_pe")
    s_act = p.sem("s_act")
    s_dve = p.sem("s_dve")
    s_po = p.sem("s_po")
    s_cc = p.sem("s_cc")

    def _(e):
        s_sm.inc(e.dma_start(wcol[:], qnw.ap()), 16)
        s_sm.inc(e.dma_start(knw_c[:], knw.ap()), 16)
        s_sm.inc(e.dma_start(ofcol[:], ownoff.ap()), 16)
    p.pool(_)

    def _(e):
        e.memset(ones_bf[:], 1.0)
        e.memset(eps1[:], EPS_LN)
        e.memset(epsr[:], EPS_RMS)
        s_dve.inc(e.memset(ones_f[:], 1.0))
        s_sm.wait(e)
        s_dve.inc(e.tensor_tensor(wcol[:], wcol[:], knw_c[:], ALU.mult))
        s_dve.wait(e)
        s_dve.inc(e.tensor_scalar(wcol[:], wcol[:], SC, None, ALU.mult))
    p.dve(_)

    sa = []
    sa_q = []
    xn_own = p.sbuf("xn_own", [128, KC * T], BF16, sa)
    sq = p.sbuf("sq", [128, 2, T], BF16, sa_q)
    t32 = p.sbuf("t32", [128, T], F32, sa_q)
    rows = p.sbuf("rows", [1, 4 * T], F32, sa_q)
    sqq = p.sbuf("sqq", [128, T], BF16, sa_q)
    bcq = p.sbuf("bcq", [128, T], F32, sa_q)

    pscope_a = []
    for _nm in ("psA", "psB", "psC", "psD"):
        p._uid += 1
        g = nc.psum_tensor(f"{_nm}_u{p._uid}", [128, T], F32)
        pscope_a.append((g, g.__enter__()))
    psA, psB, psC, psD = [t for (_g, t) in pscope_a]

    emit_layernorm(p, xov, xn_own, xf, sq, t32, rows, ones_bf, ones_f,
                   eps1, psA, psB, psC, psD, s_xs, s_pe, s_act, s_dve)
    ln_own_tok = s_dve.val

    w_tok = [0] * 64
    pe_q_tok = [0] * NH
    dve_q_tok = [0] * NH

    def emit_wslab(idx, src_ap, guard_tok=None):
        sl = idx % 2

        def _(e, idx=idx, sl=sl, guard_tok=guard_tok):
            if guard_tok is not None:
                s_pe.wait(e, guard_tok)
            w_tok[idx] = s_ws[sl].inc(e.dma_start(
                wsl[:, sl, :], src_ap.rearrange("p a b -> p (a b)"),
                max_dma_last_dim=8192), 16)
        p.pool(_)

    emit_wslab(0, wqT.ap()[0])
    emit_wslab(1, wqT.ap()[1])

    for h in range(NH):
        sl = h % 2
        psQ = psA if sl == 0 else psB

        def _(e, h=h, sl=sl, psQ=psQ):
            s_ws[sl].wait(e, w_tok[h])
            if h == 0:
                s_dve.wait(e, ln_own_tok)
            if h >= 2:
                s_dve.wait(e, dve_q_tok[h - 2])
            for tc in range(2):
                for kc in range(KC):
                    mm = e.matmul(psQ[:, tc * 512:(tc + 1) * 512],
                                  wsl[:, sl, kc * 128:(kc + 1) * 128],
                                  xn_own[:, kc * T + tc * 512:kc * T + (tc + 1) * 512],
                                  start=(kc == 0), stop=(kc == KC - 1))
            s_pe.inc(mm)
        p.te(_)
        pe_q1 = s_pe.val

        def _(e, pe_q1=pe_q1, psQ=psQ):
            s_pe.wait(e, pe_q1)
            s_act.inc(e.square(sqq[:], psQ[:]))
        p.act(_)
        act_sq = s_act.val

        def _(e, act_sq=act_sq):
            s_act.wait(e, act_sq)
            for tc in range(2):
                mm = e.matmul(psD[0:1, tc * 512:(tc + 1) * 512], ones_bf[:],
                              sqq[:, tc * 512:(tc + 1) * 512], start=True, stop=True)
            s_pe.inc(mm)
        p.te(_)
        pe_q2 = s_pe.val

        def _(e, pe_q2=pe_q2):
            s_pe.wait(e, pe_q2)
            s_dve.inc(e.tensor_scalar(rows[0:1, 0:T], psD[0:1, 0:T], 1.0 / HD,
                                      None, ALU.mult))
        p.dve(_)
        dve_r = s_dve.val

        def _(e, dve_r=dve_r):
            s_dve.wait(e, dve_r)
            s_act.inc(e.activation(rows[0:1, 0:T], rows[0:1, 0:T], AF.Sqrt,
                                   bias=epsr[0:1, :], scale=1.0))
        p.act(_)
        act_r = s_act.val

        def _(e, act_r=act_r):
            s_act.wait(e, act_r)
            s_dve.inc(e.reciprocal(rows[0:1, 0:T], rows[0:1, 0:T]))
        p.dve(_)
        dve_r2 = s_dve.val

        def _(e, h=h, dve_r2=dve_r2):
            s_dve.wait(e, dve_r2)
            for tc in range(2):
                mm = e.matmul(psC[:, tc * 512:(tc + 1) * 512], ones_f[:],
                              rows[0:1, tc * 512:(tc + 1) * 512], start=True, stop=True)
            pe_q_tok[h] = s_pe.inc(mm)
        p.te(_)
        pe_q3 = s_pe.val

        def _(e, h=h, psQ=psQ, pe_q3=pe_q3):
            s_pe.wait(e, pe_q3)
            tcp = s_dve.inc(e.tensor_copy(bcq[:], psC[:]))
            s_dve.wait(e, tcp)
            dve_q_tok[h] = s_dve.inc(e.scalar_tensor_tensor(
                QO[:, h * T:(h + 1) * T], psQ[:], wcol[:], bcq[:],
                ALU.mult, ALU.mult))
        p.dve(_)

        if h + 2 < NH:
            emit_wslab(h + 2, wqT.ap()[h + 2], pe_q_tok[h])

    q_done_dve = s_dve.val

    p.close_scope(sa_q)
    kvb = p.sbuf("kvb", [128, 16384], BF16, sa)
    sqk = p.sbuf("sqk", [128, 512], BF16, sa)
    bck = p.sbuf("bck", [128, 512], F32, sa)
    rows2 = bck[0:1, 0:512]
    vsl = p.sbuf("vsl", [128, 2, 512], BF16, sa)
    pidx = p.sbuf("pidx", [1, 1], mybir.dt.int32, sa)

    wk_tok = [0] * 16
    pe_k1_tok = [0] * 16
    pe_k2_tok = [0] * 16
    pe_k3_tok = [0] * 16
    act_k_tok = [0] * 16
    act_k2_tok = [0] * 16
    act_k3_tok = [0] * 16
    dve_k1_tok = [0] * 16
    dve_k_tok = [0] * 16
    v_tok = [0] * 2
    pe_v_tok = [0] * (2 * KC)
    psV = [psA, psB, psC, psD]

    for tcol in range(2):
        for kvh in range(NKV):
            kk = tcol * NKV + kvh
            sl = kk % 2
            psK = psA if sl == 0 else psB

            def _(e, kk=kk, kvh=kvh, sl=sl):
                if kk >= 2:
                    s_pe.wait(e, pe_k1_tok[kk - 2])
                wk_tok[kk] = s_ws[sl].inc(e.dma_start(
                    wsl[:, sl, :], wkT.ap()[kvh].rearrange("p a b -> p (a b)"),
                    max_dma_last_dim=8192), 16)
            p.pool(_)

            def _(e, kk=kk, tcol=tcol, sl=sl, psK=psK,
                  q_done_dve=q_done_dve, actg=s_act.val):
                s_ws[sl].wait(e, wk_tok[kk])
                if kk == 0:
                    s_dve.wait(e, q_done_dve)
                    s_act.wait(e, actg)
                if kk >= 2:
                    s_dve.wait(e, dve_k_tok[kk - 2])
                for kc in range(KC):
                    mm = e.matmul(psK[:, 0:512],
                                  wsl[:, sl, kc * 128:(kc + 1) * 128],
                                  xn_own[:, kc * T + tcol * 512:kc * T + (tcol + 1) * 512],
                                  start=(kc == 0), stop=(kc == KC - 1))
                pe_k1_tok[kk] = s_pe.inc(mm)
            p.te(_)

            def _(e, kk=kk, psK=psK):
                s_pe.wait(e, pe_k1_tok[kk])
                if kk >= 1:
                    s_pe.wait(e, pe_k2_tok[kk - 1])
                act_k_tok[kk] = s_act.inc(e.square(sqk[:], psK[:, 0:512]))
            p.act(_)

            def _(e, kk=kk):
                s_act.wait(e, act_k_tok[kk])
                if kk >= 1:
                    s_dve.wait(e, dve_k1_tok[kk - 1])
                mm = e.matmul(psC[0:1, 0:512], ones_bf[:], sqk[:], start=True, stop=True)
                pe_k2_tok[kk] = s_pe.inc(mm)
            p.te(_)

            def _(e, kk=kk):
                s_pe.wait(e, pe_k2_tok[kk])
                dve_k1_tok[kk] = s_dve.inc(e.tensor_scalar(
                    rows2, psC[0:1, 0:512], 1.0 / HD, None, ALU.mult))
            p.dve(_)

            def _(e, kk=kk):
                s_dve.wait(e, dve_k1_tok[kk])
                act_k2_tok[kk] = s_act.inc(e.activation(
                    rows2, rows2, AF.Sqrt, bias=epsr[0:1, :], scale=1.0))
            p.act(_)

            def _(e, kk=kk):
                s_act.wait(e, act_k2_tok[kk])
                s_dve.inc(e.reciprocal(rows2, rows2))
            p.dve(_)
            dve_k2 = s_dve.val

            def _(e, kk=kk, dve_k2=dve_k2):
                s_dve.wait(e, dve_k2)
                if kk >= 1:
                    s_dve.wait(e, act_k3_tok[kk - 1])
                mm = e.matmul(psD[:, 0:512], ones_f[:], rows2,
                              start=True, stop=True)
                pe_k3_tok[kk] = s_pe.inc(mm)
            p.te(_)

            def _(e, kk=kk, kvh=kvh, tcol=tcol, psK=psK):
                s_pe.wait(e, pe_k3_tok[kk])
                act_k3_tok[kk] = s_dve.inc(e.tensor_copy(bck[:], psD[:, 0:512]))
                s_dve.wait(e, act_k3_tok[kk])
                dve_k_tok[kk] = s_dve.inc(e.tensor_tensor(
                    kvb[:, kvh * 1024 + tcol * 512:kvh * 1024 + (tcol + 1) * 512],
                    psK[:, 0:512], bck[:], ALU.mult))
            p.dve(_)

        k_done_tok = dve_k_tok[tcol * NKV + NKV - 1]
        k_done_act = act_k3_tok[tcol * NKV + NKV - 1]

        for vidx in range(2 * KC):
            kc, nn = vidx // 2, vidx % 2
            sl = vidx % 2

            def _(e, vidx=vidx, kc=kc, nn=nn, sl=sl):
                if vidx >= 2:
                    s_pe.wait(e, pe_v_tok[vidx - 2])
                v_tok[vidx % 2] = s_vs[sl].inc(e.dma_start(
                    vsl[:, sl, :], wvt.ap()[kc][:, nn * 512:(nn + 1) * 512],
                    max_dma_last_dim=8192), 16)
            p.pool(_)
            vdma = s_vs[sl].val

            def _(e, vidx=vidx, kc=kc, nn=nn, sl=sl, tcol=tcol, vdma=vdma,
                  k_done_tok=k_done_tok, k_done_act=k_done_act):
                s_vs[sl].wait(e, vdma)
                if vidx == 0:
                    s_dve.wait(e, k_done_tok)
                    s_dve.wait(e, k_done_act)
                for mt in range(4):
                    mm = e.matmul(
                        psV[mt][:, nn * 512:(nn + 1) * 512],
                        xn_own[:, kc * T + tcol * 512 + mt * 128:kc * T + tcol * 512 + (mt + 1) * 128],
                        vsl[:, sl, :],
                        start=(kc == 0), stop=(kc == KC - 1))
                pe_v_tok[vidx] = s_pe.inc(mm)
            p.te(_)
        pe_v_done = s_pe.val

        def _(e, tcol=tcol, pe_v_done=pe_v_done):
            s_pe.wait(e, pe_v_done)
            for mt in range(4):
                s_dve.inc(e.tensor_copy(
                    kvb[:, 8192 + (tcol * 4 + mt) * 1024:8192 + (tcol * 4 + mt + 1) * 1024],
                    psV[mt][:]))
        p.dve(_)
    kv_own_done = s_dve.val

    def _(e, kv_own_done=kv_own_done):
        s_sm.inc(e.dma_start(pidx[:], pairidx.ap()), 16)
        s_dve.wait(e, kv_own_done)
        s_sm.inc(e.dma_start(cc_kv_in.ap(), kvb[:], max_dma_last_dim=8192), 16)
        s_sm.wait(e)
        e.collective_compute(
            "AllGather", ALU.bypass,
            replica_groups=[[0, 1, 2, 3, 4, 5, 6, 7]],
            ins=[cc_kv_in.ap().opt()],
            outs=[cc_kv_out.ap().opt()],
        ).then_inc(s_cc.h, 1)
        s_cc.val += 1
    p.pool(_)
    cc_tok = s_cc.val

    p.close_scope(sa)
    sbp = []
    Ksb = p.sbuf("Ksb", [128, NKV * TF], BF16, sbp)
    Vsb = p.sbuf("Vsb", [128, 16 * 1024], BF16, sbp)

    r0 = nc.gpsimd.register("r_kv")
    r0h = r0.__enter__()
    _scope.append(r0)
    r1 = nc.gpsimd.register("r_kv2")
    r1h = r1.__enter__()
    _scope.append(r1)

    def _(e, cc_tok=cc_tok):
        e.wait_ge(s_cc.h, cc_tok)
        e.reg_load(r0h, pidx[:1, :1])
        e.reg_mul(r0h, r0h, 128 * 16384)
        for j in range(2):
            e.reg_add(r1h, r0h, 8192)
            s_sm.inc(e.dma_start(
                bass.AP(Ksb, j * 1024, [[NKV * TF, 128], [2048, NKV], [1, 1024]]),
                bass.AP(cc_kv_out, r0h, [[16384, 128], [1024, NKV], [1, 1024]])), 16)
            s_sm.inc(e.dma_start(
                bass.AP(Vsb, j * 8 * 1024, [[16 * 1024, 128], [1024, 8], [1, 1024]]),
                bass.AP(cc_kv_out, r1h, [[16384, 128], [1024, 8], [1, 1024]])), 16)
            if j == 0:
                e.reg_add(r0h, r0h, 128 * 16384)
    p.pool(_)
    kv_ready_tok = s_sm.val

    for (g, _t) in reversed(pscope_a):
        g.__exit__(None, None, None)

    sc3 = []
    iotaS = p.sbuf("iotaS", [128, TF], F32, sc3)
    idn = p.sbuf("idn", [128, 128], BF16, sc3)
    Rbuf = p.sbuf("Rbuf", [128, TF], F32, sc3)
    c0col = p.sbuf("c0col", [128, 1], F32, sc3)
    tb = p.sbuf("tb", [128, 2, TF], F32, sc3)
    pbf = p.sbuf("pbf", [128, 2, TF], BF16, sc3)
    pTsb = p.sbuf("pTsb", [128, 2, TF], BF16, sc3)
    mrow = p.sbuf("mrow", [128, 2, 4], F32, sc3)
    negm = p.sbuf("negm", [128, 2, 1], F32, sc3)
    lrow = p.sbuf("lrow", [128, 2, 1], F32, sc3)
    linv = p.sbuf("linv", [128, 2, 1], F32, sc3)
    outsb = p.sbuf("outsb", [128, 2, 512], F32, sc3)

    pscope_b = []
    for _nm, _sh, _dt in (("psS", [128, 2, 512], F32), ("psAV", [128, 512], F32),
                          ("psTT", [128, 512], BF16), ("psO", [128, 2, 1024], F32)):
        p._uid += 1
        g = nc.psum_tensor(f"{_nm}_u{p._uid}", _sh, _dt)
        pscope_b.append((g, g.__enter__()))
    psS = pscope_b[0][1]; psAV = pscope_b[1][1]
    psTT = pscope_b[2][1]; psO = pscope_b[3][1]

    def _(e):
        s_po.inc(e.iota(iotaS[:], pattern=[[1, TF]], base=0, channel_multiplier=-1,
                        allow_small_or_imprecise_dtypes=True))
    p.pool(_)
    iota_tok = s_po.val

    def _(e):
        s_po.wait(e, iota_tok)
        s_dve.inc(e.tensor_scalar(idn[:], iotaS[:, 0:128], 0.0, None, ALU.is_equal))
    p.dve(_)
    qresc_tok = s_dve.val

    NIT = 8 * NH
    sc_gc_tok = {}
    ttr_gc_tok = {-2: 0, -1: 0}
    tr_tok = {}
    trevac_tok = {}
    exp_tok = [0] * NIT
    recip_tok = [0] * NIT
    norm_tok = [0] * NIT
    av_tok = [0] * NIT
    oevac_tok = [0] * NIT
    rbuf_tok = [0] * 8
    GC = [0]

    def it_params(it):
        qt, head = it // NH, it % NH
        n128 = 9 + qt
        sspan = n128 * 128
        nch = (sspan + 511) // 512
        return qt, head, n128, sspan, nch

    for it in range(NIT + 1):
        if it < NIT:
            qt, head, n128, sspan, nch = it_params(it)
            sl = it % 2
            kvh = head // 4

            if head == 0:
                def _(e, qt=qt, it=it):
                    if qt == 0:
                        s_dve.wait(e, qresc_tok)
                    if it >= 2:
                        s_act.wait(e, exp_tok[it - 2])
                    s_dve.inc(e.tensor_scalar(c0col[:], ofcol[:], float(128 * qt),
                                              None, ALU.add))
                    s_dve.wait(e)
                    e.tensor_scalar(Rbuf[:], iotaS[:], c0col[:], None, ALU.subtract)
                    e.tensor_scalar(tb[:, 0, :], Rbuf[:], 0.0, -1.3e33,
                                    ALU.is_gt, ALU.mult)
                    s_dve.inc(e.tensor_tensor(Rbuf[:], Rbuf[:], tb[:, 0, :], ALU.add))
                    rbuf_tok[qt] = s_dve.val
                p.dve(_)

            for ch in range(nch):
                gc = GC[0]
                n = min(512, sspan - ch * 512)

                def _(e, it=it, qt=qt, head=head, kvh=kvh, ch=ch, gc=gc, n=n):
                    if it == 0 and ch == 0:
                        s_sm.wait(e, kv_ready_tok)
                    s_dve.wait(e, ttr_gc_tok[gc - 2])
                    mm = e.matmul(psS[:, gc % 2, 0:n],
                                  QO[:, head * T + qt * 128:head * T + (qt + 1) * 128],
                                  Ksb[:, kvh * TF + ch * 512:kvh * TF + ch * 512 + n],
                                  start=True, stop=True)
                    sc_gc_tok[gc] = s_pe.inc(mm)
                p.te(_)

                def _(e, it=it, qt=qt, head=head, ch=ch, gc=gc, n=n, sl=sl):
                    s_pe.wait(e, sc_gc_tok[gc])
                    if ch == 0:
                        s_dve.wait(e, rbuf_tok[qt])
                        if it >= 2:
                            s_act.wait(e, exp_tok[it - 2])
                    e.scalar_tensor_tensor(
                        tb[:, sl, ch * 512:ch * 512 + n],
                        Rbuf[:, ch * 512:ch * 512 + n], slopes[head],
                        psS[:, gc % 2, 0:n], ALU.mult, ALU.add)
                    ttr_gc_tok[gc] = s_dve.inc(e.tensor_reduce(
                        mrow[:, sl, ch:ch + 1], tb[:, sl, ch * 512:ch * 512 + n],
                        AX.X, ALU.max))
                p.dve(_)
                GC[0] += 1

            def _(e, it=it, sl=sl, nch=nch):
                s_dve.wait(e)
                s_dve.inc(e.tensor_reduce(negm[:, sl, :], mrow[:, sl, 0:nch],
                                          AX.X, ALU.max))
                s_dve.wait(e)
                s_dve.inc(e.tensor_scalar(negm[:, sl, :], negm[:, sl, :], -1.0,
                                          None, ALU.mult))
            p.dve(_)
            negm_tok = s_dve.val

            def _(e, it=it, sl=sl, sspan=sspan, negm_tok=negm_tok):
                s_dve.wait(e, negm_tok)
                if it >= 2:
                    s_pe.wait(e, av_tok[it - 2])
                exp_tok[it] = s_act.inc(e.activation(
                    pbf[:, sl, 0:sspan], tb[:, sl, 0:sspan], AF.Exp,
                    bias=negm[:, sl, :], scale=1.0, accum_out=lrow[:, sl, :]))
            p.act(_)

            def _(e, it=it, sl=sl):
                s_act.wait(e, exp_tok[it])
                recip_tok[it] = s_dve.inc(e.reciprocal(linv[:, sl, :], lrow[:, sl, :]))
            p.dve(_)

            def _(e, it=it, sl=sl, sspan=sspan):
                s_dve.wait(e, recip_tok[it])
                norm_tok[it] = s_dve.inc(e.tensor_scalar(
                    pbf[:, sl, 0:sspan], pbf[:, sl, 0:sspan], linv[:, sl, :],
                    None, ALU.mult))
            p.dve(_)

        if it >= 1:
            jt = it - 1
            qt1, head1, n128_1, sspan1, nch1 = it_params(jt)
            sl1 = jt % 2
            kvh1 = head1 // 4

            ngrp = (n128_1 + 3) // 4
            grp_js = [list(range(g * 4, min(n128_1, g * 4 + 4)))
                      for g in range(ngrp)]

            for g in range(ngrp):
                js = grp_js[g]

                def _(e, jt=jt, g=g, js=js, sl1=sl1, kvh1=kvh1, grp_js=grp_js):
                    if g == 0:
                        s_dve.wait(e, norm_tok[jt])
                        if jt >= 1:
                            s_act.wait(e, oevac_tok[jt - 1])
                    else:
                        s_act.wait(e, trevac_tok[(jt, g - 1)])
                    for j in js:
                        tr = e.transpose(psTT[:, (j % 4) * 128:(j % 4 + 1) * 128],
                                         pbf[:, sl1, j * 128:(j + 1) * 128], idn[:])
                    tr_tok[(jt, g)] = s_pe.inc(tr)
                    if g >= 1:
                        for j in grp_js[g - 1]:
                            mm = e.matmul(
                                psAV[:, 0:128],
                                Vsb[:, j * 1024 + kvh1 * 128:j * 1024 + (kvh1 + 1) * 128],
                                pTsb[:, sl1, j * 128:(j + 1) * 128],
                                start=(j == 0), stop=False)
                            s_pe.inc(mm)
                p.te(_)

                def _(e, jt=jt, g=g, js=js, sl1=sl1):
                    s_pe.wait(e, tr_tok[(jt, g)])
                    trevac_tok[(jt, g)] = s_act.inc(e.copy(
                        pTsb[:, sl1, js[0] * 128:(js[-1] + 1) * 128],
                        psTT[:, 0:len(js) * 128]))
                p.act(_)

            def _(e, jt=jt, sl1=sl1, kvh1=kvh1, n128_1=n128_1, ngrp=ngrp,
                  grp_js=grp_js):
                s_act.wait(e, trevac_tok[(jt, ngrp - 1)])
                for j in grp_js[ngrp - 1]:
                    mm = e.matmul(
                        psAV[:, 0:128],
                        Vsb[:, j * 1024 + kvh1 * 128:j * 1024 + (kvh1 + 1) * 128],
                        pTsb[:, sl1, j * 128:(j + 1) * 128],
                        start=(j == 0), stop=(j == n128_1 - 1))
                    av_tok[jt] = s_pe.inc(mm)
            p.te(_)

            def _(e, jt=jt, qt1=qt1, head1=head1):
                s_pe.wait(e, av_tok[jt])
                oevac_tok[jt] = s_act.inc(e.copy(
                    QO[:, head1 * T + qt1 * 128:head1 * T + (qt1 + 1) * 128],
                    psAV[:, 0:128]))
            p.act(_)

    scores_done_act = s_act.val
    if DEBUG_ATTN:
        def _(e):
            s_act.wait(e, scores_done_act)
            s_sm.inc(e.dma_start(dbg_o.ap(), QO[:], max_dma_last_dim=4096), 16)
        p.pool(_)

    wo_tok = [0] * KC
    pe_wo_tok = [0] * KC
    dve_wo_tok = [0] * KC
    out_tok = [0] * (KC * 2)
    xwo_tok = [0] * KC

    for mb in range(KC):
        sl = mb % 2
        psW = psO[:, sl, :]

        def _(e, mb=mb, sl=sl):
            if mb >= 2:
                s_pe.wait(e, pe_wo_tok[mb - 2])
            wo_tok[mb] = s_ws[sl].inc(e.dma_start(
                wsl[:, sl, :], woT.ap()[mb].rearrange("p a b -> p (a b)"),
                max_dma_last_dim=8192), 16)
        p.pool(_)

        def _(e, mb=mb, sl=sl, psW=psW):
            s_ws[sl].wait(e, wo_tok[mb])
            if mb == 0:
                s_act.wait(e, scores_done_act)
            if mb >= 2:
                s_dve.wait(e, dve_wo_tok[mb - 2])
            for tc in range(2):
                for kc in range(KC):
                    mm = e.matmul(psW[:, tc * 512:(tc + 1) * 512],
                                  wsl[:, sl, kc * 128:(kc + 1) * 128],
                                  QO[:, kc * T + tc * 512:kc * T + (tc + 1) * 512],
                                  start=(kc == 0), stop=(kc == KC - 1))
            pe_wo_tok[mb] = s_pe.inc(mm)
        p.te(_)

        def _(e, mb=mb, sl=sl, dvg=s_dve.val):
            if mb >= 2:
                s_dve.wait(e, dve_wo_tok[mb - 2])
            else:
                s_dve.wait(e, dvg)
            xwo_tok[mb] = s_xs[sl].inc(e.dma_start(xf[:, sl, :], xov[:, mb, :]), 16)
        p.sync(_)

        def _(e, mb=mb, sl=sl, psW=psW):
            s_xs[sl].wait(e, xwo_tok[mb])
            s_pe.wait(e, pe_wo_tok[mb])
            for tc in range(2):
                oi = mb * 2 + tc
                if oi >= 2:
                    s_os[tc].wait(e, out_tok[oi - 2])
                s_dve.inc(e.tensor_tensor(
                    outsb[:, tc, :], psW[:, tc * 512:(tc + 1) * 512],
                    xf[:, sl, tc * 512:(tc + 1) * 512], ALU.add))
            dve_wo_tok[mb] = s_dve.val
        p.dve(_)

        def _(e, mb=mb):
            s_dve.wait(e, dve_wo_tok[mb])
            for tc in range(2):
                oi = mb * 2 + tc
                out_tok[oi] = s_os[tc].inc(e.dma_start(
                    ov[:, mb, tc * 512:(tc + 1) * 512], outsb[:, tc, :]), 16)
        p.sync(_)

    def _(e):
        for tc in range(2):
            s_os[tc].wait(e)
    p.sync(_)

    p.close_scope(sc3)
    for (g, _t) in reversed(pscope_b):
        g.__exit__(None, None, None)
    p.close_scope(sbp)
    emit_barrier(p)
    p.pop_scope()


def build_l0attn_program():
    p = Prog()
    nc = p.nc
    D = declare_l0attn_io(nc)
    D["x_in"] = nc.dram_tensor("x_in", [DM, T], F32, kind="ExternalInput")
    D["x_out"] = nc.dram_tensor("x_out", [DM, T], F32, kind="ExternalOutput")
    p.open_block()
    emit_l0attn(p, D, D["x_in"], D["x_out"])
    p.close()
    return p


def attn_meta(c):
    b, h = c // 2, c % 2
    return {
        "ownoff": np.full((128, 1), float(h * T), np.float32),
        "pairidx": np.array([[2 * b]], np.int32),
    }


def run_l0attn(x, w):
    p = _get_prog("l0attn", build_l0attn_program)
    shards = _x_to_shards(x)
    in_maps = []
    for c in range(NCORES):
        in_maps.append({"x_in": shards[c], **attn_meta(c), **w})
    res = run_bass_kernel_spmd(p.nc, in_maps, core_ids=list(range(NCORES)))
    return _shards_to_x([res.results[c]["x_out"] for c in range(NCORES)])




def prep_l1ffn(inp):
    d = {}
    d["wbT"] = _lhsT_slabs(np.asarray(inp["L1_ffn_W_base"], np.float32))
    d["wgt"] = np.ascontiguousarray(
        np.asarray(inp["L1_ffn_W_gate"], np.float32).T.reshape(KC, 128, 8).transpose(1, 0, 2))
    d["aT"] = np.ascontiguousarray(
        _lhsT_slabs(np.asarray(inp["L1_ffn_A"], np.float32).reshape(128, DM))[0])
    d["bT"] = np.ascontiguousarray(
        np.asarray(inp["L1_ffn_B"], np.float32).transpose(0, 2, 1).reshape(128, DM)
        .reshape(128, KC, 128))
    d["bbs"] = np.ascontiguousarray(np.asarray(inp["L1_ffn_b_base"], np.float32).reshape(KC, 128).T)
    return d


def declare_l1ffn_io(nc):
    D = {}
    D["wbT"] = nc.dram_tensor("wbT", [KC, 128, KC, 128], F32, kind="ExternalInput")
    D["wgt"] = nc.dram_tensor("wgt", [128, KC, 8], F32, kind="ExternalInput")
    D["aT"] = nc.dram_tensor("aT", [128, KC, 128], F32, kind="ExternalInput")
    D["bT"] = nc.dram_tensor("bT", [128, KC, 128], F32, kind="ExternalInput")
    D["bbs"] = nc.dram_tensor("bbs", [128, KC], F32, kind="ExternalInput")
    return D


def emit_l1ffn(p, D, x_in_h, x_out_h):
    nc = p.nc
    wbT = D["wbT"]; wgt = D["wgt"]; aT = D["aT"]; bT = D["bT"]; bbs = D["bbs"]
    _scope = p.push_scope()

    xv = x_in_h.ap().rearrange("(c p) t -> p c t", p=128)
    ov = x_out_h.ap().rearrange("(c p) t -> p c t", p=128)

    xf = p.sbuf("xf", [128, 2, T], F32)
    outsb = p.sbuf("outsb", [128, 2, 512], F32)
    ones_bf = p.sbuf("ones_bf", [128, 1], BF16)
    ones_f = p.sbuf("ones_f", [1, 128], F32)
    eps1 = p.sbuf("eps1", [1, 1], F32)
    wsl = p.sbuf("wsl", [128, 2, KC * 128], BF16)
    xn = p.sbuf("xn", [128, KC * T], BF16)
    sq = p.sbuf("sq", [128, 2, T], BF16)
    t32 = p.sbuf("t32", [128, T], F32)
    rows = p.sbuf("rows", [1, 4 * T], F32)
    aTs = p.sbuf("aTs", [128, KC * 128], BF16)
    bTs = p.sbuf("bTs", [128, KC * 128], BF16)
    wgs = p.sbuf("wgs", [128, KC * 8], BF16)
    bbc = p.sbuf("bbc", [128, KC], F32)
    Lsb = p.sbuf("Lsb", [128, 64], F32)
    Ls1 = p.sbuf("Ls1", [128, 64], F32)
    Ls2 = p.sbuf("Ls2", [128, 64], F32)
    mx1 = p.sbuf("mx1", [128, 8], F32)
    mx2 = p.sbuf("mx2", [128, 8], F32)
    den = p.sbuf("den", [128, 8], F32)
    reb = p.sbuf("reb", [128, 64], BF16)
    res8 = p.sbuf("res8", [8, T], BF16)
    P8 = p.sbuf("P8", [8, 128], BF16)
    P8b = p.sbuf("P8b", [8, 128], BF16)
    p8i = p.sbuf("p8i", [8, 128], F32)
    idn = p.sbuf("idn", [128, 128], BF16)
    io128 = p.sbuf("io128", [128, 128], F32)
    rb = p.sbuf("rb", [128, T], BF16)
    hb = p.sbuf("hb", [128, T], BF16)
    hw = p.sbuf("hw", [128, T], BF16)

    s_xs = [p.sem("s_x0"), p.sem("s_x1")]
    s_ws = [p.sem("s_w0"), p.sem("s_w1")]
    s_sm = p.sem("s_sm")
    s_os = [p.sem("s_o0"), p.sem("s_o1")]
    s_pe = p.sem("s_pe")
    s_act = p.sem("s_act")
    s_dve = p.sem("s_dve")
    s_po = p.sem("s_po")

    def _(e):
        s_sm.inc(e.dma_start(aTs[:], aT.ap().rearrange("p a b -> p (a b)"),
                             max_dma_last_dim=8192), 16)
        s_sm.inc(e.dma_start(bTs[:], bT.ap().rearrange("p a b -> p (a b)"),
                             max_dma_last_dim=8192), 16)
        s_sm.inc(e.dma_start(wgs[:], wgt.ap().rearrange("p a b -> p (a b)")), 16)
        s_sm.inc(e.dma_start(bbc[:], bbs.ap()), 16)
        e.iota(p8i[:], pattern=[[1, 128]], base=0, channel_multiplier=-16,
               allow_small_or_imprecise_dtypes=True)
        s_po.inc(e.iota(io128[:], pattern=[[1, 128]], base=0, channel_multiplier=-1,
                        allow_small_or_imprecise_dtypes=True))
    p.pool(_)

    def _(e):
        e.memset(ones_bf[:], 1.0)
        e.memset(eps1[:], EPS_LN)
        s_dve.inc(e.memset(ones_f[:], 1.0))
        s_po.wait(e)
        e.tensor_scalar(P8[:], p8i[:], 0.0, None, ALU.is_ge)
        e.tensor_scalar(P8b[:], p8i[:], 15.0, None, ALU.is_le)
        s_dve.inc(e.tensor_tensor(P8[:], P8[:], P8b[:], ALU.mult))
        s_dve.inc(e.tensor_scalar(idn[:], io128[:], 0.0, None, ALU.is_equal))
    p.dve(_)
    const_tok = s_dve.val

    pscope_a = []
    for _nm in ("psA", "psB", "psC", "psD"):
        p._uid += 1
        g = nc.psum_tensor(f"{_nm}_u{p._uid}", [128, T], F32)
        pscope_a.append((g, g.__enter__()))
    psA, psB, psC, psD = [t for (_g, t) in pscope_a]

    emit_layernorm(p, xv, xn, xf, sq, t32, rows, ones_bf, ones_f,
                   eps1, psA, psB, psC, psD, s_xs, s_pe, s_act, s_dve)
    ln_tok = s_dve.val

    def _(e):
        s_dve.wait(e, ln_tok)
        s_sm.wait(e)
        for mt in range(8):
            for kc in range(KC):
                mm = e.matmul(psD[:, mt * 8:(mt + 1) * 8],
                              xn[:, kc * T + mt * 128:kc * T + (mt + 1) * 128],
                              wgs[:, kc * 8:(kc + 1) * 8],
                              start=(kc == 0), stop=(kc == KC - 1))
        s_pe.inc(mm)
    p.te(_)
    pe_lg = s_pe.val

    def _(e):
        s_pe.wait(e, pe_lg)
        s_act.inc(e.copy(Lsb[:], psD[:, 0:64]))
    p.act(_)
    act_lg = s_act.val

    gv = lambda t: t.rearrange("p (g i) -> p g i", i=8)
    mx1b = bass.AP(mx1, 0, [[8, 128], [1, 8], [0, 8]])
    mx2b = bass.AP(mx2, 0, [[8, 128], [1, 8], [0, 8]])
    denb = bass.AP(den, 0, [[8, 128], [1, 8], [0, 8]])

    def _(e):
        s_act.wait(e, act_lg)
        s_dve.inc(e.tensor_reduce(mx1[:], gv(Lsb[:]), AX.X, ALU.max))
        s_dve.wait(e)
        e.tensor_tensor(gv(Ls1[:]), gv(Lsb[:]), mx1b, ALU.is_ge)
        e.tensor_scalar(Ls1[:], Ls1[:], -1.0e30, None, ALU.mult)
        e.tensor_tensor(Ls1[:], Ls1[:], Lsb[:], ALU.add)
        s_dve.inc(e.tensor_reduce(mx2[:], gv(Ls1[:]), AX.X, ALU.max))
        s_dve.wait(e)
        s_dve.inc(e.tensor_tensor(gv(Ls2[:]), gv(Lsb[:]), mx1b, ALU.subtract))
    p.dve(_)
    dve_g1 = s_dve.val

    def _(e):
        s_dve.wait(e, dve_g1)
        s_act.inc(e.activation(Ls2[:], Ls2[:], AF.Exp))
    p.act(_)
    act_e = s_act.val

    def _(e):
        s_act.wait(e, act_e)
        e.tensor_tensor(gv(Ls1[:]), gv(Lsb[:]), mx2b, ALU.is_ge)
        e.tensor_tensor(Ls2[:], Ls2[:], Ls1[:], ALU.mult)
        s_dve.inc(e.tensor_reduce(den[:], gv(Ls2[:]), AX.X, ALU.add))
        s_dve.wait(e)
        s_dve.inc(e.reciprocal(den[:], den[:]))
        s_dve.wait(e)
        s_dve.inc(e.tensor_tensor(gv(reb[:]), gv(Ls2[:]), denb, ALU.mult))
    p.dve(_)
    dve_route = s_dve.val

    psTr = psD[0:8, 128:192].bitcast(BF16)
    tr_tok2 = [0] * 8
    ev_tok2 = [0] * 8

    for mt in range(8):
        def _(e, mt=mt):
            if mt == 0:
                s_dve.wait(e, dve_route)
                s_dve.wait(e, const_tok)
            if mt >= 1:
                s_act.wait(e, ev_tok2[mt - 1])
            tr = e.transpose(psTr, reb[:, mt * 8:(mt + 1) * 8], idn[:])
            tr_tok2[mt] = s_pe.inc(tr)
        p.te(_)

        def _(e, mt=mt):
            s_pe.wait(e, tr_tok2[mt])
            ev_tok2[mt] = s_act.inc(e.copy(res8[:, mt * 128:(mt + 1) * 128], psTr))
        p.act(_)
    route8_tok = ev_tok2[7]

    def _(e):
        s_act.wait(e, route8_tok)
        for tc in range(2):
            mm = e.matmul(psC[:, tc * 512:(tc + 1) * 512], P8[:],
                          res8[:, tc * 512:(tc + 1) * 512], start=True, stop=True)
        s_pe.inc(mm)
    p.te(_)
    pe_rfm = s_pe.val

    def _(e):
        s_pe.wait(e, pe_rfm)
        s_act.inc(e.copy(rb[:], psC[:]))
    p.act(_)
    act_rb = s_act.val

    def _(e):
        s_dve.wait(e, ln_tok)
        for tc in range(2):
            for kc in range(KC):
                mm = e.matmul(psB[:, tc * 512:(tc + 1) * 512],
                              aTs[:, kc * 128:(kc + 1) * 128],
                              xn[:, kc * T + tc * 512:kc * T + (tc + 1) * 512],
                              start=(kc == 0), stop=(kc == KC - 1))
        s_pe.inc(mm)
    p.te(_)
    pe_h = s_pe.val

    def _(e):
        s_pe.wait(e, pe_h)
        s_act.inc(e.copy(hb[:], psB[:]))
    p.act(_)
    act_hb = s_act.val

    def _(e):
        s_act.wait(e, act_hb)
        s_act.wait(e, act_rb)
        s_dve.inc(e.tensor_tensor(hw[:], hb[:], rb[:], ALU.mult))
    p.dve(_)
    hw_tok = s_dve.val

    wb_tok = [0] * KC
    pe_mb_tok = [0] * KC
    dve_mb_tok = [0] * KC
    out_tok = [0] * (KC * 2)
    xm_tok = [0] * KC

    for mb in range(KC):
        sl = mb % 2
        psW = psA if sl == 0 else psB

        def _(e, mb=mb, sl=sl):
            if mb >= 2:
                s_pe.wait(e, pe_mb_tok[mb - 2])
            wb_tok[mb] = s_ws[sl].inc(e.dma_start(
                wsl[:, sl, :], wbT.ap()[mb].rearrange("p a b -> p (a b)"),
                max_dma_last_dim=8192), 16)
        p.pool(_)

        def _(e, mb=mb, sl=sl, psW=psW):
            s_ws[sl].wait(e, wb_tok[mb])
            if mb == 0:
                s_dve.wait(e, hw_tok)
            if mb == 1:
                s_act.wait(e, act_hb)
            if mb >= 2:
                s_dve.wait(e, dve_mb_tok[mb - 2])
            for tc in range(2):
                for kc in range(KC):
                    e.matmul(psW[:, tc * 512:(tc + 1) * 512],
                             wsl[:, sl, kc * 128:(kc + 1) * 128],
                             xn[:, kc * T + tc * 512:kc * T + (tc + 1) * 512],
                             start=(kc == 0), stop=False)
                mm = e.matmul(psW[:, tc * 512:(tc + 1) * 512],
                              bTs[:, mb * 128:(mb + 1) * 128],
                              hw[:, tc * 512:(tc + 1) * 512],
                              start=False, stop=True)
            pe_mb_tok[mb] = s_pe.inc(mm)
        p.te(_)

        def _(e, mb=mb, sl=sl, dvg=s_dve.val):
            if mb >= 2:
                s_dve.wait(e, dve_mb_tok[mb - 2])
            else:
                s_dve.wait(e, dvg)
            xm_tok[mb] = s_xs[sl].inc(e.dma_start(xf[:, sl, :], xv[:, mb, :]), 16)
        p.sync(_)

        def _(e, mb=mb, sl=sl, psW=psW):
            s_xs[sl].wait(e, xm_tok[mb])
            s_pe.wait(e, pe_mb_tok[mb])
            for tc in range(2):
                oi = mb * 2 + tc
                if oi >= 2:
                    s_os[tc].wait(e, out_tok[oi - 2])
                e.tensor_scalar(outsb[:, tc, :], psW[:, tc * 512:(tc + 1) * 512],
                                bbc[:, mb:mb + 1], 0.5, ALU.add, ALU.mult)
                s_dve.inc(e.tensor_tensor(outsb[:, tc, :], outsb[:, tc, :],
                                          xf[:, sl, tc * 512:(tc + 1) * 512], ALU.add))
            dve_mb_tok[mb] = s_dve.val
        p.dve(_)

        def _(e, mb=mb):
            s_dve.wait(e, dve_mb_tok[mb])
            for tc in range(2):
                oi = mb * 2 + tc
                out_tok[oi] = s_os[tc].inc(e.dma_start(
                    ov[:, mb, tc * 512:(tc + 1) * 512], outsb[:, tc, :]), 16)
        p.sync(_)

    def _(e):
        for tc in range(2):
            s_os[tc].wait(e)
    p.sync(_)

    for (g2, _t) in reversed(pscope_a):
        g2.__exit__(None, None, None)
    emit_barrier(p)
    p.pop_scope()


def build_l1ffn_program():
    p = Prog()
    nc = p.nc
    D = declare_l1ffn_io(nc)
    D["x_in"] = nc.dram_tensor("x_in", [DM, T], F32, kind="ExternalInput")
    D["x_out"] = nc.dram_tensor("x_out", [DM, T], F32, kind="ExternalOutput")
    p.open_block()
    emit_l1ffn(p, D, D["x_in"], D["x_out"])
    p.close()
    return p


def run_l1ffn(x, w):
    p = _get_prog("l1ffn", build_l1ffn_program)
    shards = _x_to_shards(x)
    in_maps = [{"x_in": shards[c], **w} for c in range(NCORES)]
    res = run_bass_kernel_spmd(p.nc, in_maps, core_ids=list(range(NCORES)))
    return _shards_to_x([res.results[c]["x_out"] for c in range(NCORES)])



DEBUG_L1 = False
D1 = 512
ND0 = 4
DFF = 2048
NSUB = 4
EPS_L1 = 1e-5


def prep_l1attn(inp):
    d = {}
    d["winT"] = _lhsT_slabs(np.asarray(inp["L1_attn_Win"], np.float32))
    d["binc"] = np.ascontiguousarray(np.asarray(inp["L1_attn_bin"], np.float32).reshape(ND0, 128).T)
    for nm in ("ln1_w", "ln1_b", "phz_re", "phz_im", "lc_re", "lc_im",
               "sc_lin", "ln2_w", "ln2_b", "b2"):
        a = np.asarray(inp["L1_attn_" + nm], np.float32)
        d[nm] = np.ascontiguousarray(a.reshape(NSUB, ND0, 128).transpose(0, 2, 1))
    d["w1T"] = np.stack([_lhsT_slabs(np.asarray(inp["L1_attn_W1"], np.float32)[s])
                         for s in range(NSUB)])
    d["w2T"] = np.stack([_lhsT_slabs(np.asarray(inp["L1_attn_W2"], np.float32)[s])
                         for s in range(NSUB)])
    d["b1c"] = np.ascontiguousarray(np.asarray(inp["L1_attn_b1"], np.float32)
                                    .reshape(NSUB, 16, 128).transpose(0, 2, 1))
    d["lnlw"] = np.ascontiguousarray(np.asarray(inp["L1_attn_lnl_w"], np.float32).reshape(ND0, 128).T)
    d["lnlb"] = np.ascontiguousarray(np.asarray(inp["L1_attn_lnl_b"], np.float32).reshape(ND0, 128).T)
    d["woutT"] = _lhsT_slabs(np.asarray(inp["L1_attn_Wout"], np.float32))
    d["boutc"] = np.ascontiguousarray(np.asarray(inp["L1_attn_bout"], np.float32).reshape(KC, 128).T)
    return d


def declare_l1attn_io(nc):
    D = {}
    D["winT"] = nc.dram_tensor("winT", [ND0, 128, KC, 128], F32, kind="ExternalInput")
    D["binc"] = nc.dram_tensor("binc", [128, ND0], F32, kind="ExternalInput")
    for nm in ("ln1_w", "ln1_b", "phz_re", "phz_im", "lc_re", "lc_im",
               "sc_lin", "ln2_w", "ln2_b", "b2"):
        D[nm] = nc.dram_tensor(nm, [NSUB, 128, ND0], F32, kind="ExternalInput")
    D["w1T"] = nc.dram_tensor("w1T", [NSUB, 16, 128, ND0, 128], F32, kind="ExternalInput")
    D["w2T"] = nc.dram_tensor("w2T", [NSUB, ND0, 128, 16, 128], F32, kind="ExternalInput")
    D["b1c"] = nc.dram_tensor("b1c", [NSUB, 128, 16], F32, kind="ExternalInput")
    D["lnlw"] = nc.dram_tensor("lnlw", [128, ND0], F32, kind="ExternalInput")
    D["lnlb"] = nc.dram_tensor("lnlb", [128, ND0], F32, kind="ExternalInput")
    D["woutT"] = nc.dram_tensor("woutT", [KC, 128, ND0, 128], F32, kind="ExternalInput")
    D["boutc"] = nc.dram_tensor("boutc", [128, KC], F32, kind="ExternalInput")
    D["flagc"] = nc.dram_tensor("flagc", [128, 1], F32, kind="ExternalInput")
    D["mcol"] = nc.dram_tensor("mcol", [128, 8], F32, kind="ExternalInput")
    D["cc_st_in"] = nc.dram_tensor("cc_st_in", [128, 8], F32)
    D["cc_st_out"] = nc.dram_tensor("cc_st_out", [8 * 128, 8], F32)
    return D


def emit_l1attn(p, D, x_in_h, x_out_h):
    nc = p.nc
    winT = D["winT"]; binc = D["binc"]; w1T = D["w1T"]; w2T = D["w2T"]
    b1c = D["b1c"]; lnlw = D["lnlw"]; lnlb = D["lnlb"]
    woutT = D["woutT"]; boutc = D["boutc"]
    flagc_d = D["flagc"]; mcol_d = D["mcol"]
    cc_st_in = D["cc_st_in"]; cc_st_out = D["cc_st_out"]
    cols_in = {nm: D[nm] for nm in ("ln1_w", "ln1_b", "phz_re", "phz_im",
                                    "lc_re", "lc_im", "sc_lin", "ln2_w",
                                    "ln2_b", "b2")}
    _scope = p.push_scope()
    NCH = 16

    xv = x_in_h.ap().rearrange("(c p) t -> p c t", p=128)
    ov = x_out_h.ap().rearrange("(c p) t -> p c t", p=128)

    xf = p.sbuf("xf", [128, 2, T], F32)
    outsb = p.sbuf("outsb", [128, 2, 512], F32)
    ones_bf = p.sbuf("ones_bf", [128, 1], BF16)
    ones_f = p.sbuf("ones_f", [1, 128], F32)
    eps1 = p.sbuf("eps1", [1, 1], F32)
    h = p.sbuf("h", [128, ND0, T], F32)
    y = p.sbuf("y", [128, ND0, T], BF16)
    yc = p.sbuf("yc", [128, ND0, T], F32)
    lnq = p.sbuf("lnq", [128, ND0, 512], BF16)
    fq = p.sbuf("fq", [128, 16, 512], BF16)
    w1s = p.sbuf("w1s", [128, 16 * ND0 * 128], BF16)
    w2s = p.sbuf("w2s", [128, ND0 * 16 * 128], BF16)
    wsl = p.sbuf("wsl", [128, 2, ND0 * 128], BF16)
    rows5 = p.sbuf("rows5", [1, 2 * T], F32)
    mub = p.sbuf("mub", [128, 1024], F32)
    sdb = p.sbuf("sdb", [128, 1024], F32)
    xcF = p.sbuf("xcF", [128, 2, T], BF16)
    sqF = p.sbuf("sqF", [128, 2, T], BF16)
    cr = p.sbuf("cr", [128, 64], F32)
    ci = p.sbuf("ci", [128, 64], F32)
    Xs = p.sbuf("Xs", [128, 64], F32)
    tm1 = p.sbuf("tm1", [128, 64], F32)
    tm2 = p.sbuf("tm2", [128, 64], F32)
    tm3 = p.sbuf("tm3", [128, 64], F32)
    tm4 = p.sbuf("tm4", [128, 64], F32)
    shr = p.sbuf("shr", [128, 64], F32)
    shi = p.sbuf("shi", [128, 64], F32)
    sneg = p.sbuf("sneg", [128, 64], F32)
    pd1 = p.sbuf("pd1", [128, 256], F32)
    pd2 = p.sbuf("pd2", [128, 256], F32)
    prt = p.sbuf("prt", [128, ND0, NCH], F32)
    pit = p.sbuf("pit", [128, ND0, NCH], F32)
    P1r = p.sbuf("P1r", [128, ND0, 64], F32)
    P1i = p.sbuf("P1i", [128, ND0, 64], F32)
    P6r = p.sbuf("P6r", [128, ND0, NCH], F32)
    P6i = p.sbuf("P6i", [128, ND0, NCH], F32)
    SSr = p.sbuf("SSr", [128, ND0, NCH], F32)
    SSi = p.sbuf("SSi", [128, ND0, NCH], F32)
    p64k = p.sbuf("p64k", [128, ND0, 12], F32)
    ccols = p.sbuf("ccols", [128, ND0, 14], F32)
    subc = p.sbuf("subc", [128, ND0, 10], F32)
    b1s = p.sbuf("b1s", [128, 16], F32)
    binb = p.sbuf("binb", [128, ND0], F32)
    lnlwc = p.sbuf("lnlwc", [128, ND0], F32)
    lnlbc = p.sbuf("lnlbc", [128, ND0], F32)
    boutb = p.sbuf("boutb", [128, KC], F32)
    flagc = p.sbuf("flagc", [128, 1], F32)
    gflag = p.sbuf("gflag", [128, 1], F32)
    mcol = p.sbuf("mcol", [128, 8], F32)
    stq = p.sbuf("stq", [128, 8], F32)
    Sall = p.sbuf("Sall", [128, 64], F32)
    Srecv = p.sbuf("Srecv", [128, 8], F32)

    s_xs = [p.sem("s_x0"), p.sem("s_x1")]
    s_ws = [p.sem("s_w0"), p.sem("s_w1")]
    s_sm = p.sem("s_sm")
    s_os = [p.sem("s_o0"), p.sem("s_o1")]
    s_pe = p.sem("s_pe")
    s_act = p.sem("s_act")
    s_dve = p.sem("s_dve")
    s_cc = p.sem("s_cc")

    def _(e):
        s_sm.inc(e.dma_start(binb[:], binc.ap()), 16)
        s_sm.inc(e.dma_start(lnlwc[:], lnlw.ap()), 16)
        s_sm.inc(e.dma_start(lnlbc[:], lnlb.ap()), 16)
        s_sm.inc(e.dma_start(boutb[:], boutc.ap()), 16)
        s_sm.inc(e.dma_start(flagc[:], flagc_d.ap()), 16)
        s_sm.inc(e.dma_start(mcol[:], mcol_d.ap()), 16)
    p.pool(_)
    sm_base = s_sm.val

    def _(e):
        e.memset(ones_bf[:], 1.0)
        e.memset(eps1[:], EPS_LN)
        s_dve.inc(e.memset(ones_f[:], 1.0))
        s_sm.wait(e, sm_base)
        s_dve.inc(e.tensor_scalar(gflag[:], flagc[:], -1.0, 1.0, ALU.mult, ALU.add))
    p.dve(_)

    pscope = []
    for _nm in ("psA", "psB", "psC", "psD"):
        p._uid += 1
        g = nc.psum_tensor(f"{_nm}_l1a_u{p._uid}", [128, T], F32)
        pscope.append((g, g.__enter__()))
    psA, psB, psC, psD = [t for (_g, t) in pscope]
    psWin = [psA, psB, psC, psD]

    sa = []
    win_sl = p.sbuf("win_sl", [128, ND0 * KC * 128], BF16, sa)
    xnk = p.sbuf("xnk", [128, 2, T], BF16, sa)

    def _(e):
        for a in range(ND0):
            s_sm.inc(e.dma_start(win_sl[:, a * KC * 128:(a + 1) * KC * 128],
                                 winT.ap()[a].rearrange("p b c -> p (b c)"),
                                 max_dma_last_dim=8192), 16)
    p.pool(_)
    win_ld = s_sm.val

    actT_tok = [0] * KC
    peT_tok = [0] * KC
    xT_tok = [0] * KC
    for kc in range(KC):
        sl = kc % 2

        def _(e, kc=kc, sl=sl):
            if kc >= 2:
                s_act.wait(e, actT_tok[kc - 2])
            xT_tok[kc] = s_xs[sl].inc(e.dma_start(xf[:, sl, :], xv[:, kc, :]), 16)
        p.sync(_)

        def _(e, kc=kc, sl=sl):
            s_xs[sl].wait(e, xT_tok[kc])
            if kc >= 2:
                s_pe.wait(e, peT_tok[kc - 2])
            e.copy(xcF[:, sl, :], xf[:, sl, :])
            actT_tok[kc] = s_act.inc(e.square(sqF[:, sl, :], xf[:, sl, :]))
        p.act(_)

        def _(e, kc=kc, sl=sl):
            s_act.wait(e, actT_tok[kc])
            if kc == 0:
                s_dve.wait(e)
            for tc in range(2):
                e.matmul(psA[0:1, tc * 512:(tc + 1) * 512], ones_bf[:],
                         xcF[:, sl, tc * 512:(tc + 1) * 512],
                         start=(kc == 0), stop=(kc == KC - 1))
                mm = e.matmul(psC[0:1, tc * 512:(tc + 1) * 512], ones_bf[:],
                              sqF[:, sl, tc * 512:(tc + 1) * 512],
                              start=(kc == 0), stop=(kc == KC - 1))
            peT_tok[kc] = s_pe.inc(mm)
        p.te(_)
    pe_Tstats = s_pe.val

    def _(e, pe_Tstats=pe_Tstats):
        s_pe.wait(e, pe_Tstats)
        e.tensor_scalar(rows5[:, 0:T], psA[0:1, 0:T], 1.0 / DM, None, ALU.mult)
        s_dve.inc(e.tensor_scalar(rows5[:, T:2 * T], psC[0:1, 0:T], 1.0 / DM,
                                  None, ALU.mult))
        s_dve.inc(e.engine_nop())
        for seg in range(2):
            a, b = seg * 512, (seg + 1) * 512
            s_dve.wait(e)
            s_dve.inc(e.tensor_tensor(mub[0:1, 0:512], rows5[:, a:b],
                                      rows5[:, a:b], ALU.mult))
            s_dve.wait(e)
            s_dve.inc(e.tensor_tensor(rows5[:, T + a:T + b], rows5[:, T + a:T + b],
                                      mub[0:1, 0:512], ALU.subtract))
    p.dve(_)

    def _(e):
        s_dve.wait(e)
        s_act.inc(e.activation(rows5[:, T:2 * T], rows5[:, T:2 * T],
                               AF.Sqrt, bias=eps1[0:1, :], scale=1.0))
    p.act(_)

    def _(e):
        s_act.wait(e)
        s_dve.inc(e.reciprocal(rows5[:, T:2 * T], rows5[:, T:2 * T]))
    p.dve(_)
    tstats_tok = s_dve.val

    def _(e, tstats_tok=tstats_tok):
        s_dve.wait(e, tstats_tok)
        for tc in range(2):
            e.matmul(psA[:, tc * 512:(tc + 1) * 512], ones_f[:],
                     rows5[0:1, tc * 512:(tc + 1) * 512], start=True, stop=True)
            mm = e.matmul(psB[:, tc * 512:(tc + 1) * 512], ones_f[:],
                          rows5[0:1, T + tc * 512:T + (tc + 1) * 512],
                          start=True, stop=True)
        s_pe.inc(mm)
    p.te(_)
    pe_tbc = s_pe.val

    def _(e, pe_tbc=pe_tbc):
        s_pe.wait(e, pe_tbc)
        e.copy(mub[:], psA[:])
        s_act.inc(e.copy(sdb[:], psB[:]))
    p.act(_)
    act_tbc = s_act.val

    winpe_tok = [0] * KC
    xd_tok = [0] * KC
    for kc in range(KC):
        sl = kc % 2

        def _(e, kc=kc, sl=sl, act_tbc=act_tbc):
            if kc >= 2:
                s_dve.wait(e, xd_tok[kc - 2])
            else:
                s_act.wait(e, act_tbc)
            xT_tok[kc] = s_xs[sl].inc(e.dma_start(xf[:, sl, :], xv[:, kc, :]), 16)
        p.sync(_)

        def _(e, kc=kc, sl=sl, act_tbc=act_tbc):
            s_xs[sl].wait(e, xT_tok[kc])
            if kc == 0:
                s_act.wait(e, act_tbc)
            if kc >= 2:
                s_pe.wait(e, winpe_tok[kc - 2])
            e.tensor_tensor(outsb[:, 0, :], xf[:, sl, 0:512], mub[:, 0:512], ALU.subtract)
            e.tensor_tensor(xnk[:, sl, 0:512], outsb[:, 0, :], sdb[:, 0:512], ALU.mult)
            e.tensor_tensor(outsb[:, 0, :], xf[:, sl, 512:1024], mub[:, 512:1024], ALU.subtract)
            xd_tok[kc] = s_dve.inc(e.tensor_tensor(
                xnk[:, sl, 512:1024], outsb[:, 0, :], sdb[:, 512:1024], ALU.mult))
        p.dve(_)

        def _(e, kc=kc, sl=sl, win_ld=win_ld):
            s_dve.wait(e, xd_tok[kc])
            if kc == 0:
                s_sm.wait(e, win_ld)
            for mb in range(ND0):
                for tc in range(2):
                    mm = e.matmul(
                        psWin[mb][:, tc * 512:(tc + 1) * 512],
                        win_sl[:, (mb * KC + kc) * 128:(mb * KC + kc + 1) * 128],
                        xnk[:, sl, tc * 512:(tc + 1) * 512],
                        start=(kc == 0), stop=(kc == KC - 1))
            winpe_tok[kc] = s_pe.inc(mm)
        p.te(_)
    pe_win = s_pe.val

    def _(e, pe_win=pe_win):
        s_pe.wait(e, pe_win)
        for mb in range(ND0):
            s_dve.inc(e.tensor_scalar(h[:, mb, :], psWin[mb][:],
                                      binb[:, mb:mb + 1], None, ALU.add))
    p.dve(_)
    h_ready = s_dve.val
    p.close_scope(sa)

    def col_bc(buf, n):
        return bass.AP(buf.tensor, buf.offset, [buf.ap[0], buf.ap[1], [0, n]])

    CPR, CPI, CLCR, CLCI, CSCL, CL1W, CL1B, CL2W, CL2B, CB2 = range(10)

    def emit_sub_consts(s):
        def _(e, s=s, dvg=s_dve.val, peg=s_pe.val, acg=s_act.val):
            s_dve.wait(e, dvg)
            s_pe.wait(e, peg)
            s_act.wait(e, acg)
            s_sm.inc(e.dma_start(subc[:, :, CL1W:CL1W + 1], cols_in["ln1_w"].ap()[s].unsqueeze(2), allow_slow_non_contiguous=True), 16)
            s_sm.inc(e.dma_start(subc[:, :, CL1B:CL1B + 1], cols_in["ln1_b"].ap()[s].unsqueeze(2), allow_slow_non_contiguous=True), 16)
            s_sm.inc(e.dma_start(subc[:, :, CL2W:CL2W + 1], cols_in["ln2_w"].ap()[s].unsqueeze(2), allow_slow_non_contiguous=True), 16)
            s_sm.inc(e.dma_start(subc[:, :, CL2B:CL2B + 1], cols_in["ln2_b"].ap()[s].unsqueeze(2), allow_slow_non_contiguous=True), 16)
            s_sm.inc(e.dma_start(subc[:, :, CSCL:CSCL + 1], cols_in["sc_lin"].ap()[s].unsqueeze(2), allow_slow_non_contiguous=True), 16)
            s_sm.inc(e.dma_start(ccols[:, :, 0:1], cols_in["phz_re"].ap()[s].unsqueeze(2), allow_slow_non_contiguous=True), 16)
            s_sm.inc(e.dma_start(ccols[:, :, 1:2], cols_in["phz_im"].ap()[s].unsqueeze(2), allow_slow_non_contiguous=True), 16)
            s_sm.inc(e.dma_start(ccols[:, :, 2:3], cols_in["lc_re"].ap()[s].unsqueeze(2), allow_slow_non_contiguous=True), 16)
            s_sm.inc(e.dma_start(ccols[:, :, 3:4], cols_in["lc_im"].ap()[s].unsqueeze(2), allow_slow_non_contiguous=True), 16)
            s_sm.inc(e.dma_start(subc[:, :, CB2:CB2 + 1], cols_in["b2"].ap()[s].unsqueeze(2), allow_slow_non_contiguous=True), 16)
            s_sm.inc(e.dma_start(b1s[:], b1c.ap()[s]), 16)
            for a in range(16):
                s_sm.inc(e.dma_start(w1s[:, a * ND0 * 128:(a + 1) * ND0 * 128],
                                     w1T.ap()[s][a].rearrange("p b c -> p (b c)"),
                                     max_dma_last_dim=8192), 16)
            for a in range(ND0):
                s_sm.inc(e.dma_start(w2s[:, a * 16 * 128:(a + 1) * 16 * 128],
                                     w2T.ap()[s][a].rearrange("p b c -> p (b c)"),
                                     max_dma_last_dim=8192), 16)
        p.pool(_)
        lds = s_sm.val

        def _(e, s=s, lds=lds):
            s_sm.wait(e, lds)
            W = lambda: s_dve.wait(e)
            I = lambda ins: s_dve.inc(ins)
            phzr = ccols[:, :, 0:1]; phzi = ccols[:, :, 1:2]
            r2 = ccols[:, :, 4:5]; tmp = ccols[:, :, 5:6]
            I(e.tensor_tensor(r2, phzr, phzr, ALU.mult)); W()
            I(e.tensor_tensor(tmp, phzi, phzi, ALU.mult)); W()
            I(e.tensor_tensor(r2, r2, tmp, ALU.add)); W()
        p.dve(_)

        def _(e):
            s_dve.wait(e)
            s_act.inc(e.activation(ccols[:, :, 4:5], ccols[:, :, 4:5], AF.Sqrt,
                                   bias=0.0, scale=1.0))
        p.act(_)

        def _(e):
            s_act.wait(e)
            s_dve.inc(e.reciprocal(ccols[:, :, 5:6], ccols[:, :, 4:5]))
        p.dve(_)

        def _(e):
            s_dve.wait(e)
            s_act.inc(e.activation(ccols[:, :, 4:5], ccols[:, :, 4:5], AF.Tanh,
                                   bias=0.0, scale=1.0))
        p.act(_)

        def _(e, s=s):
            s_act.wait(e)
            W = lambda: s_dve.wait(e)
            I = lambda ins: s_dve.inc(ins)
            phzr = ccols[:, :, 0:1]; phzi = ccols[:, :, 1:2]
            lcr = ccols[:, :, 2:3]; lci = ccols[:, :, 3:4]
            th = ccols[:, :, 4:5]; rinv = ccols[:, :, 5:6]
            fac = ccols[:, :, 6:7]
            pr = subc[:, :, CPR:CPR + 1]; pi = subc[:, :, CPI:CPI + 1]
            lcpr = ccols[:, :, 7:8]; lcpi = ccols[:, :, 8:9]
            I(e.tensor_scalar(lcr, lcr, gflag[:], None, ALU.mult)); W()
            I(e.tensor_scalar(lci, lci, gflag[:], None, ALU.mult)); W()
            I(e.tensor_tensor(fac, th, rinv, ALU.mult)); W()
            I(e.tensor_tensor(pr, phzr, fac, ALU.mult)); W()
            I(e.tensor_tensor(pi, phzi, fac, ALU.mult)); W()
            I(e.tensor_copy(prt[:], col_bc(pr, NCH))); W()
            I(e.tensor_copy(pit[:], col_bc(pi, NCH))); W()
            I(e.tensor_copy(P1r[:, :, 0:1], pr)); W()
            I(e.tensor_copy(P1i[:, :, 0:1], pi)); W()
            L = 1
            while L < 64:
                ar = P1r[:, :, 0:L]; ai = P1i[:, :, 0:L]
                br = col_bc(P1r[:, :, L - 1:L], L); bi = col_bc(P1i[:, :, L - 1:L], L)
                s1 = pd1[:, 0:ND0 * L].rearrange("p (d l) -> p d l", d=ND0)
                s2 = pd2[:, 0:ND0 * L].rearrange("p (d l) -> p d l", d=ND0)
                I(e.tensor_tensor(s1, ar, br, ALU.mult)); W()
                I(e.tensor_tensor(s2, ai, bi, ALU.mult)); W()
                I(e.tensor_tensor(P1i[:, :, L:2 * L], ar, bi, ALU.mult)); W()
                I(e.tensor_tensor(s2, s1, s2, ALU.subtract)); W()
                I(e.tensor_tensor(s1, ai, br, ALU.mult)); W()
                I(e.tensor_tensor(P1i[:, :, L:2 * L], P1i[:, :, L:2 * L], s1, ALU.add)); W()
                I(e.tensor_copy(P1r[:, :, L:2 * L], s2)); W()
                L *= 2
            I(e.tensor_copy(p64k[:, :, 0:1], P1r[:, :, 63:64])); W()
            I(e.tensor_copy(p64k[:, :, 1:2], P1i[:, :, 63:64])); W()
            for k in range(1, 4):
                ar = p64k[:, :, 2 * k - 2:2 * k - 1]; ai = p64k[:, :, 2 * k - 1:2 * k]
                I(e.tensor_tensor(ccols[:, :, 10:11], ar, ar, ALU.mult)); W()
                I(e.tensor_tensor(ccols[:, :, 11:12], ai, ai, ALU.mult)); W()
                I(e.tensor_tensor(p64k[:, :, 2 * k:2 * k + 1], ccols[:, :, 10:11],
                                  ccols[:, :, 11:12], ALU.subtract)); W()
                I(e.tensor_tensor(ccols[:, :, 10:11], ar, ai, ALU.mult)); W()
                I(e.tensor_scalar(p64k[:, :, 2 * k + 1:2 * k + 2], ccols[:, :, 10:11],
                                  2.0, None, ALU.mult)); W()
            p64r = p64k[:, :, 0:1]; p64i = p64k[:, :, 1:2]
            I(e.tensor_tensor(ccols[:, :, 10:11], lcr, p64r, ALU.mult)); W()
            I(e.tensor_tensor(ccols[:, :, 11:12], lci, p64i, ALU.mult)); W()
            I(e.tensor_tensor(lcpr, ccols[:, :, 10:11], ccols[:, :, 11:12], ALU.subtract)); W()
            I(e.tensor_tensor(ccols[:, :, 10:11], lcr, p64i, ALU.mult)); W()
            I(e.tensor_tensor(ccols[:, :, 11:12], lci, p64r, ALU.mult)); W()
            I(e.tensor_tensor(lcpi, ccols[:, :, 10:11], ccols[:, :, 11:12], ALU.add)); W()
            I(e.tensor_scalar(ccols[:, :, 12:13], ccols[:, :, 3:4], -1.0, None, ALU.mult)); W()
            I(e.memset(P6r[:, :, 0:1], 1.0)); W()
            I(e.memset(P6i[:, :, 0:1], 0.0)); W()
            I(e.tensor_copy(P6r[:, :, 1:2], p64r)); W()
            I(e.tensor_copy(P6i[:, :, 1:2], p64i)); W()
            L = 2
            while L < NCH:
                mr = ccols[:, :, 10:11]; mi = ccols[:, :, 11:12]
                t1 = ccols[:, :, 13:14]
                I(e.tensor_tensor(mr, P6r[:, :, L - 1:L], p64r, ALU.mult)); W()
                I(e.tensor_tensor(t1, P6i[:, :, L - 1:L], p64i, ALU.mult)); W()
                I(e.tensor_tensor(mr, mr, t1, ALU.subtract)); W()
                I(e.tensor_tensor(mi, P6r[:, :, L - 1:L], p64i, ALU.mult)); W()
                I(e.tensor_tensor(t1, P6i[:, :, L - 1:L], p64r, ALU.mult)); W()
                I(e.tensor_tensor(mi, mi, t1, ALU.add)); W()
                n2 = min(L, NCH - L)
                ar = P6r[:, :, 0:n2]; ai = P6i[:, :, 0:n2]
                brc = col_bc(mr, n2); bic = col_bc(mi, n2)
                s1 = pd1[:, 0:ND0 * n2].rearrange("p (d l) -> p d l", d=ND0)
                s2 = pd2[:, 0:ND0 * n2].rearrange("p (d l) -> p d l", d=ND0)
                I(e.tensor_tensor(s1, ar, brc, ALU.mult)); W()
                I(e.tensor_tensor(s2, ai, bic, ALU.mult)); W()
                I(e.tensor_tensor(P6i[:, :, L:L + n2], ar, bic, ALU.mult)); W()
                I(e.tensor_tensor(s2, s1, s2, ALU.subtract)); W()
                I(e.tensor_tensor(s1, ai, brc, ALU.mult)); W()
                I(e.tensor_tensor(P6i[:, :, L:L + n2], P6i[:, :, L:L + n2], s1, ALU.add)); W()
                I(e.tensor_copy(P6r[:, :, L:L + n2], s2)); W()
                L *= 2
        p.dve(_)
        return s_dve.val

    def emit_inner_stats():
        tok_in = [0] * (ND0)
        for d0 in range(ND0):
            sl = d0 % 2

            def _(e, d0=d0, sl=sl, hg=s_dve.val):
                if d0 == 0:
                    s_dve.wait(e, hg)
                if d0 >= 2:
                    s_pe.wait(e, tok_in[d0 - 2])
                e.copy(xcF[:, sl, :], h[:, d0, :])
                s_act.inc(e.square(sqF[:, sl, :], h[:, d0, :]))
            p.act(_)
            a_tok = s_act.val

            def _(e, d0=d0, sl=sl, a_tok=a_tok):
                s_act.wait(e, a_tok)
                for tc in range(2):
                    e.matmul(psA[0:1, tc * 512:(tc + 1) * 512], ones_bf[:],
                             xcF[:, sl, tc * 512:(tc + 1) * 512],
                             start=(d0 == 0), stop=(d0 == ND0 - 1))
                    mm = e.matmul(psC[0:1, tc * 512:(tc + 1) * 512], ones_bf[:],
                                  sqF[:, sl, tc * 512:(tc + 1) * 512],
                                  start=(d0 == 0), stop=(d0 == ND0 - 1))
                tok_in[d0] = s_pe.inc(mm)
            p.te(_)
        pe_st = s_pe.val

        def _(e, pe_st=pe_st):
            s_pe.wait(e, pe_st)
            e.tensor_scalar(rows5[:, 0:T], psA[0:1, 0:T], 1.0 / D1, None, ALU.mult)
            s_dve.inc(e.tensor_scalar(rows5[:, T:2 * T], psC[0:1, 0:T], 1.0 / D1,
                                      None, ALU.mult))
            s_dve.inc(e.engine_nop())
            for seg in range(2):
                a, b = seg * 512, (seg + 1) * 512
                s_dve.wait(e)
                s_dve.inc(e.tensor_tensor(mub[0:1, 0:512], rows5[:, a:b],
                                          rows5[:, a:b], ALU.mult))
                s_dve.wait(e)
                s_dve.inc(e.tensor_tensor(rows5[:, T + a:T + b],
                                          rows5[:, T + a:T + b],
                                          mub[0:1, 0:512], ALU.subtract))
        p.dve(_)

        def _(e):
            s_dve.wait(e)
            s_act.inc(e.activation(rows5[:, T:2 * T], rows5[:, T:2 * T],
                                   AF.Sqrt, bias=eps1[0:1, :], scale=1.0))
        p.act(_)

        def _(e):
            s_act.wait(e)
            s_dve.inc(e.reciprocal(rows5[:, T:2 * T], rows5[:, T:2 * T]))
        p.dve(_)
        return s_dve.val

    def emit_quarter_bcast(q, guard_dve):
        def _(e, q=q, guard_dve=guard_dve):
            s_dve.wait(e, guard_dve)
            s_dve.wait(e)
            e.matmul(psA[:, 512:1024], ones_f[:], rows5[0:1, q * 512:(q + 1) * 512],
                     start=True, stop=True)
            mm = e.matmul(psB[:, 512:1024], ones_f[:],
                          rows5[0:1, T + q * 512:T + (q + 1) * 512],
                          start=True, stop=True)
            s_pe.inc(mm)
        p.te(_)
        bq = s_pe.val

        def _(e, bq=bq):
            s_pe.wait(e, bq)
            e.copy(mub[:, 0:512], psA[:, 512:1024])
            s_act.inc(e.copy(sdb[:, 0:512], psB[:, 512:1024]))
        p.act(_)
        return s_act.val

    NQ = 2
    for s in range(NSUB):
        cst_tok = emit_sub_consts(s)
        st_tok = emit_inner_stats()

        silu_tok = [0] * NQ
        for q in range(NQ):
            bc_tok = emit_quarter_bcast(q, st_tok)
            dv_toks = [0] * ND0
            for d0 in range(ND0):
                def _(e, q=q, d0=d0, bc_tok=bc_tok, cst_tok=cst_tok):
                    s_act.wait(e)
                    if q == 0 and d0 == 0:
                        s_dve.wait(e, cst_tok)
                    sl5 = h[:, d0, q * 512:(q + 1) * 512]
                    e.tensor_tensor(sdb[:, 512:1024], sl5, mub[:, 0:512], ALU.subtract)
                    e.tensor_tensor(sdb[:, 512:1024], sdb[:, 512:1024], sdb[:, 0:512], ALU.mult)
                    dv_toks[d0] = s_dve.inc(e.tensor_scalar(
                        sdb[:, 512:1024], sdb[:, 512:1024],
                        subc[:, d0][:, CL1W:CL1W + 1],
                        subc[:, d0][:, CL1B:CL1B + 1], ALU.mult, ALU.add))
                p.dve(_)

                def _(e, q=q, d0=d0):
                    s_dve.wait(e, dv_toks[d0])
                    silu_tok[q] = s_act.inc(e.activation(
                        y[:, d0, q * 512:(q + 1) * 512], sdb[:, 512:1024], AF.Silu))
                p.act(_)
        y_ready = silu_tok[NQ - 1]

        def _(e, s=s, y_ready=y_ready, cst_tok=cst_tok):
            s_act.wait(e, y_ready)
            s_dve.wait(e, cst_tok)
            e.memset(cr[:], 0.0)
            e.memset(ci[:], 0.0)
            yv = y.rearrange("p d (c u) -> p d c u", u=64)
            ycv = yc.rearrange("p d (c u) -> p d c u", u=64)
            for u in range(64):
                e.tensor_copy(Xs[:].rearrange("p (d c) -> p d c", d=ND0), yv[:, :, :, u])
                e.tensor_tensor(tm1[:], ci[:], pit[:].rearrange("p d c -> p (d c)"), ALU.mult)
                e.tensor_tensor(tm2[:], cr[:], prt[:].rearrange("p d c -> p (d c)"), ALU.mult)
                e.tensor_tensor(tm3[:], ci[:], prt[:].rearrange("p d c -> p (d c)"), ALU.mult)
                e.tensor_tensor(tm4[:], cr[:], pit[:].rearrange("p d c -> p (d c)"), ALU.mult)
                e.tensor_tensor(cr[:], tm2[:], tm1[:], ALU.subtract)
                e.tensor_tensor(cr[:], cr[:], Xs[:], ALU.add)
                e.tensor_tensor(ci[:], tm3[:], tm4[:], ALU.add)
                e.tensor_copy(ycv[:, :, :, u], cr[:].rearrange("p (d c) -> p d c", d=ND0))
            lcpr = ccols[:, :, 7:8]; lcpi = ccols[:, :, 8:9]
            for d0 in range(ND0):
                s_dve.inc(e.tensor_tensor(cr[:, d0 * NCH:d0 * NCH + 1],
                                          cr[:, d0 * NCH:d0 * NCH + 1],
                                          lcpr[:, d0, :], ALU.add))
                s_dve.wait(e)
                s_dve.inc(e.tensor_tensor(ci[:, d0 * NCH:d0 * NCH + 1],
                                          ci[:, d0 * NCH:d0 * NCH + 1],
                                          lcpi[:, d0, :], ALU.add))
                s_dve.wait(e)
            for ki, k in enumerate([1, 2, 4, 8]):
                pkr = col_bc(p64k[:, :, 2 * ki:2 * ki + 1], NCH - k)
                pki = col_bc(p64k[:, :, 2 * ki + 1:2 * ki + 2], NCH - k)
                e.memset(shr[:], 0.0)
                e.memset(shi[:], 0.0)
                for d0 in range(ND0):
                    e.tensor_copy(shr[:, d0 * NCH + k:(d0 + 1) * NCH],
                                  cr[:, d0 * NCH:(d0 + 1) * NCH - k])
                    e.tensor_copy(shi[:, d0 * NCH + k:(d0 + 1) * NCH],
                                  ci[:, d0 * NCH:(d0 + 1) * NCH - k])
                shrv = shr.rearrange("p (d c) -> p d c", d=ND0)[:, :, k:NCH]
                shiv = shi.rearrange("p (d c) -> p d c", d=ND0)[:, :, k:NCH]
                t1v = tm1.rearrange("p (d c) -> p d c", d=ND0)[:, :, k:NCH]
                t2v = tm2.rearrange("p (d c) -> p d c", d=ND0)[:, :, k:NCH]
                crv = cr.rearrange("p (d c) -> p d c", d=ND0)[:, :, k:NCH]
                civ = ci.rearrange("p (d c) -> p d c", d=ND0)[:, :, k:NCH]
                e.tensor_tensor(t1v, shrv, pkr, ALU.mult)
                e.tensor_tensor(t2v, shiv, pki, ALU.mult)
                e.tensor_tensor(t1v, t1v, t2v, ALU.subtract)
                e.tensor_tensor(crv, crv, t1v, ALU.add)
                e.tensor_tensor(t1v, shrv, pki, ALU.mult)
                e.tensor_tensor(t2v, shiv, pkr, ALU.mult)
                e.tensor_tensor(t1v, t1v, t2v, ALU.add)
                e.tensor_tensor(civ, civ, t1v, ALU.add)
            s_dve.inc(e.tensor_copy(stq[:, 0:4], cr[:, NCH - 1::NCH]))
            s_dve.wait(e)
            s_dve.inc(e.tensor_copy(stq[:, 4:8], ci[:, NCH - 1::NCH]))
        p.dve(_)
        hillis_tok = s_dve.val

        def _(e, s=s, hillis_tok=hillis_tok):
            s_dve.wait(e, hillis_tok)
            s_sm.inc(e.dma_start(cc_st_in.ap(), stq[:]), 16)
            s_sm.wait(e)
            e.collective_compute(
                "AllGather", ALU.bypass,
                replica_groups=[[0, 1, 2, 3, 4, 5, 6, 7]],
                ins=[cc_st_in.ap().opt()],
                outs=[cc_st_out.ap().opt()],
            ).then_inc(s_cc.h, 1)
            s_cc.val += 1
            e.wait_ge(s_cc.h, s_cc.val)
            for r in range(8):
                s_sm.inc(e.dma_start(Sall[:, r * 8:(r + 1) * 8],
                                     cc_st_out.ap()[r * 128:(r + 1) * 128, :]), 16)
        p.pool(_)
        exg_tok = s_sm.val

        def _(e, s=s, exg_tok=exg_tok):
            s_sm.wait(e, exg_tok)
            s_dve.inc(e.memset(Srecv[:], 0.0))
            for r in range(8):
                s_dve.wait(e)
                s_dve.inc(e.scalar_tensor_tensor(
                    Srecv[:], Sall[:, r * 8:(r + 1) * 8], mcol[:, r:r + 1],
                    Srecv[:], ALU.mult, ALU.add))
            s_dve.wait(e)
            s_dve.inc(e.tensor_scalar(Srecv[:], Srecv[:], flagc[:], None, ALU.mult))
            s_dve.wait(e)
            for d0 in range(ND0):
                sr = Srecv[:, d0:d0 + 1]
                si = Srecv[:, 4 + d0:5 + d0]
                e.tensor_scalar(tm1[:, 0:NCH], P6i[:, d0, :], si, None, ALU.mult)
                e.scalar_tensor_tensor(SSr[:, d0, :], P6r[:, d0, :], sr,
                                       tm1[:, 0:NCH], ALU.mult, ALU.subtract)
                e.tensor_scalar(tm1[:, 0:NCH], P6i[:, d0, :], sr, None, ALU.mult)
                e.scalar_tensor_tensor(SSi[:, d0, :], P6r[:, d0, :], si,
                                       tm1[:, 0:NCH], ALU.mult, ALU.add)
            s_dve.inc(e.engine_nop())
            s_dve.wait(e)
            s_dve.inc(e.tensor_scalar(SSi[:], SSi[:], -1.0, None, ALU.mult))
            s_dve.wait(e)
            s_dve.inc(e.tensor_scalar(sneg[:], ci[:], -1.0, None, ALU.mult))
            s_dve.wait(e)
            for d0 in range(ND0):
                for c in range(NCH):
                    if c == 0:
                        sr1 = ccols[:, d0][:, 2:3]
                        si1n = ccols[:, d0][:, 12:13]
                    else:
                        sr1 = cr[:, d0 * NCH + c - 1:d0 * NCH + c]
                        si1n = sneg[:, d0 * NCH + c - 1:d0 * NCH + c]
                    tgt = yc[:, d0, c * 64:(c + 1) * 64]
                    e.scalar_tensor_tensor(tgt, P1r[:, d0, :], sr1, tgt, ALU.mult, ALU.add)
                    e.scalar_tensor_tensor(tgt, P1i[:, d0, :], si1n, tgt, ALU.mult, ALU.add)
                    e.scalar_tensor_tensor(tgt, P1r[:, d0, :], SSr[:, d0, c:c + 1],
                                           tgt, ALU.mult, ALU.add)
                    e.scalar_tensor_tensor(tgt, P1i[:, d0, :], SSi[:, d0, c:c + 1],
                                           tgt, ALU.mult, ALU.add)
            for d0 in range(ND0):
                mmv = e.scalar_tensor_tensor(
                    h[:, d0, :], yc[:, d0, :], subc[:, d0][:, CSCL:CSCL + 1],
                    h[:, d0, :], ALU.mult, ALU.add)
            s_dve.inc(mmv)
        p.dve(_)
        scan_tok = s_dve.val

        st2_tok = emit_inner_stats()
        for q in range(NQ):
            bc2 = emit_quarter_bcast(q, st2_tok)
            dvq = [0] * ND0
            for d0 in range(ND0):
                def _(e, q=q, d0=d0, bc2=bc2, scan_tok=scan_tok, peg=s_pe.val):
                    s_act.wait(e, bc2)
                    if d0 == 0:
                        s_pe.wait(e, peg)
                    if q == 0 and d0 == 0:
                        s_dve.wait(e, scan_tok)
                    sl5 = h[:, d0, q * 512:(q + 1) * 512]
                    e.tensor_tensor(sdb[:, 512:1024], sl5, mub[:, 0:512], ALU.subtract)
                    e.tensor_tensor(sdb[:, 512:1024], sdb[:, 512:1024], sdb[:, 0:512], ALU.mult)
                    dvq[d0] = s_dve.inc(e.tensor_scalar(
                        lnq[:, d0, :], sdb[:, 512:1024],
                        subc[:, d0][:, CL2W:CL2W + 1],
                        subc[:, d0][:, CL2B:CL2B + 1], ALU.mult, ALU.add))
                p.dve(_)
            lnq_tok = dvq[ND0 - 1]

            fa_tok = [0] * 16
            sl_tok = [0] * 16
            for mb in range(16):
                def _(e, q=q, mb=mb, lnq_tok=lnq_tok):
                    s_dve.wait(e, lnq_tok)
                    if mb >= 2:
                        s_act.wait(e, sl_tok[mb - 2])
                    for kt in range(ND0):
                        mm = e.matmul(psD[:, (mb % 2) * 512:(mb % 2) * 512 + 512],
                                      w1s[:, (mb * ND0 + kt) * 128:(mb * ND0 + kt + 1) * 128],
                                      lnq[:, kt, :],
                                      start=(kt == 0), stop=(kt == ND0 - 1))
                    fa_tok[mb] = s_pe.inc(mm)
                p.te(_)

                def _(e, q=q, mb=mb):
                    s_pe.wait(e, fa_tok[mb])
                    sl_tok[mb] = s_act.inc(e.activation(
                        fq[:, mb, :], psD[:, (mb % 2) * 512:(mb % 2) * 512 + 512],
                        AF.Silu, bias=b1s[:, mb:mb + 1], scale=1.0))
                p.act(_)

            psY = [psA[:, 0:512], psA[:, 512:1024], psB[:, 0:512], psB[:, 512:1024]]

            def _(e, q=q):
                s_act.wait(e, sl_tok[15])
                for d0 in range(ND0):
                    for kt in range(16):
                        mm = e.matmul(psY[d0],
                                      w2s[:, (d0 * 16 + kt) * 128:(d0 * 16 + kt + 1) * 128],
                                      fq[:, kt, :],
                                      start=(kt == 0), stop=(kt == 15))
                s_pe.inc(mm)
            p.te(_)
            fb_tok = s_pe.val

            def _(e, q=q, fb_tok=fb_tok):
                s_pe.wait(e, fb_tok)
                for d0 in range(ND0):
                    mmv = e.scalar_tensor_tensor(
                        h[:, d0, q * 512:(q + 1) * 512], psY[d0],
                        subc[:, d0][:, CB2:CB2 + 1],
                        h[:, d0, q * 512:(q + 1) * 512], ALU.add, ALU.add)
                s_dve.inc(mmv)
            p.dve(_)

    stl_tok = emit_inner_stats()
    lnl_tok = [0] * NQ
    for q in range(NQ):
        bcl = emit_quarter_bcast(q, stl_tok)
        for d0 in range(ND0):
            def _(e, q=q, d0=d0, bcl=bcl):
                s_act.wait(e, bcl)
                sl5 = h[:, d0, q * 512:(q + 1) * 512]
                e.tensor_tensor(sdb[:, 512:1024], sl5, mub[:, 0:512], ALU.subtract)
                e.tensor_tensor(sdb[:, 512:1024], sdb[:, 512:1024], sdb[:, 0:512], ALU.mult)
                lnl_tok[q] = s_dve.inc(e.tensor_scalar(
                    y[:, d0, q * 512:(q + 1) * 512], sdb[:, 512:1024],
                    lnlwc[:, d0:d0 + 1], lnlbc[:, d0:d0 + 1], ALU.mult, ALU.add))
            p.dve(_)
    lnl_done = lnl_tok[NQ - 1]

    wo_tok = [0] * KC
    pe_wo = [0] * KC
    dve_wo = [0] * KC
    out_tok = [0] * (KC * 2)
    ev_q = [0] * (KC * 2)
    xw_tok = [0] * KC

    for mb in range(KC):
        sl = mb % 2
        psW = psA if sl == 0 else psB

        def _(e, mb=mb, sl=sl):
            if mb >= 2:
                s_pe.wait(e, pe_wo[mb - 2])
            wo_tok[mb] = s_ws[sl].inc(e.dma_start(
                wsl[:, sl, :], woutT.ap()[mb].rearrange("p a b -> p (a b)"),
                max_dma_last_dim=8192), 16)
        p.pool(_)

        def _(e, mb=mb, sl=sl, psW=psW):
            s_ws[sl].wait(e, wo_tok[mb])
            if mb == 0:
                s_dve.wait(e, lnl_done)
            if mb >= 2:
                s_dve.wait(e, dve_wo[mb - 2])
            for tc in range(2):
                for kt in range(ND0):
                    mm = e.matmul(psW[:, tc * 512:(tc + 1) * 512],
                                  wsl[:, sl, kt * 128:(kt + 1) * 128],
                                  y[:, kt, tc * 512:(tc + 1) * 512],
                                  start=(kt == 0), stop=(kt == ND0 - 1))
            pe_wo[mb] = s_pe.inc(mm)
        p.te(_)

        def _(e, mb=mb, sl=sl, dvg=s_dve.val):
            if mb >= 1:
                s_dve.wait(e, dve_wo[mb - 1])
            else:
                s_dve.wait(e, dvg)
            xw_tok[mb] = s_xs[sl].inc(e.dma_start(xf[:, sl, :], xv[:, mb, :]), 16)
        p.sync(_)

        for tc in range(2):
            oi = mb * 2 + tc

            def _(e, mb=mb, sl=sl, psW=psW, tc=tc, oi=oi):
                s_xs[sl].wait(e, xw_tok[mb])
                s_pe.wait(e, pe_wo[mb])
                if oi >= 2:
                    s_os[tc].wait(e, out_tok[oi - 2])
                e.tensor_scalar(outsb[:, tc, :], psW[:, tc * 512:(tc + 1) * 512],
                                boutb[:, mb:mb + 1], None, ALU.add)
                s_dve.inc(e.tensor_tensor(
                    outsb[:, tc, :], outsb[:, tc, :],
                    xf[:, sl, tc * 512:(tc + 1) * 512], ALU.add))
                ev_q[oi] = s_dve.val
            p.dve(_)

            def _(e, mb=mb, tc=tc, oi=oi):
                s_dve.wait(e, ev_q[oi])
                out_tok[oi] = s_os[tc].inc(e.dma_start(
                    ov[:, mb, tc * 512:(tc + 1) * 512], outsb[:, tc, :]), 16)
            p.sync(_)
        dve_wo[mb] = s_dve.val

    def _(e):
        for tc in range(2):
            s_os[tc].wait(e)
    p.sync(_)

    for (g, _t) in reversed(pscope):
        g.__exit__(None, None, None)
    emit_barrier(p)
    p.pop_scope()


def build_l1attn_program():
    p = Prog()
    nc = p.nc
    D = declare_l1attn_io(nc)
    D["x_in"] = nc.dram_tensor("x_in", [DM, T], F32, kind="ExternalInput")
    D["x_out"] = nc.dram_tensor("x_out", [DM, T], F32, kind="ExternalOutput")
    p.open_block()
    emit_l1attn(p, D, D["x_in"], D["x_out"])
    p.close()
    return p


def l1attn_meta(c):
    b, hh = c // 2, c % 2
    m = np.zeros((128, 8), np.float32)
    m[:, 2 * b] = 1.0
    return {
        "flagc": np.full((128, 1), float(hh), np.float32),
        "mcol": m,
    }


def run_l1attn(x, w):
    p = _get_prog("l1attn", build_l1attn_program)
    shards = _x_to_shards(x)
    in_maps = []
    for c in range(NCORES):
        in_maps.append({"x_in": shards[c], **l1attn_meta(c), **w})
    res = run_bass_kernel_spmd(p.nc, in_maps, core_ids=list(range(NCORES)))
    return _shards_to_x([res.results[c]["x_out"] for c in range(NCORES)])
